# version 03
# baseline (speedup 1.0000x reference)
import numpy as np
import concourse.bass as bass
import concourse.mybir as mybir
from concourse.bass_utils import run_bass_kernel_spmd

F32 = mybir.dt.float32
BF16 = mybir.dt.bfloat16
AF = mybir.ActivationFunctionType
ALU = mybir.AluOpType
AX = mybir.AxisListType

B, T, C = 4, 8192, 1024
NTOK = 4096
NT = NTOK // 128
NG = NTOK // 512
NFC = C // 128

COMPUTE_ENGINES = ("scalar", "vector", "tensor", "gpsimd", "sync")
DMA_QUEUES = ("sync", "gpsimd", "scalar")
NDMASEM = 8


def ts(i, n=128):
    return slice(i * n, (i + 1) * n)


class Op:
    __slots__ = ("eng", "fn", "kind", "idx", "eidx", "reads", "writes",
                 "waits", "sem", "semval")

    def __init__(self, eng, fn, kind, idx, eidx, reads, writes):
        self.eng, self.fn, self.kind = eng, fn, kind
        self.idx, self.eidx = idx, eidx
        self.reads, self.writes = reads, writes
        self.waits = []
        self.sem = None
        self.semval = None


class Builder:
    def __init__(self, nc):
        self.nc = nc
        self.ops = []
        self.eng_count = {e: 0 for e in COMPUTE_ENGINES}
        self.last_write = {}
        self.readers = {}
        self.finished = False

    def _record(self, eng, fn, kind, reads, writes):
        idx = len(self.ops)
        op = Op(eng, fn, kind, idx, self.eng_count[eng], tuple(reads), tuple(writes))
        self.eng_count[eng] += 1
        deps = set()
        for r in op.reads:
            w = self.last_write.get(r)
            if w is not None:
                deps.add(w)
        for wr in op.writes:
            w = self.last_write.get(wr)
            if w is not None:
                deps.add(w)
            for rd in self.readers.get(wr, ()):
                deps.add(rd)
        for d in sorted(deps):
            dop = self.ops[d]
            if (dop.kind == "d" or op.kind == "d" or dop.eng != eng
                    or eng in ("vector", "scalar", "gpsimd")):
                op.waits.append(d)
        for wr in op.writes:
            self.last_write[wr] = idx
            self.readers[wr] = []
        for r in op.reads:
            self.readers.setdefault(r, []).append(idx)
        self.ops.append(op)
        return op

    def c(self, eng, fn, r=(), w=()):
        return self._record(eng, fn, "c", r, w)

    def d(self, queue, out, in_, r=(), w=(), **kw):
        try:
            if out.tensor.dtype != in_.tensor.dtype:
                queue = "gpsimd"
        except AttributeError:
            pass
        fn = lambda e, o=out, i=in_, kw=kw: e.dma_start(out=o, in_=i, **kw)
        return self._record(queue, fn, "d", r, w)

    def finish(self):
        fin_deps = [op.idx for op in self.ops if op.kind == "d"]
        lastc = {}
        for op in self.ops:
            if op.kind == "c":
                lastc[op.eng] = op.idx
        fin = self._record("sync", lambda e: e.nop(), "c", (), ())
        fin.waits = sorted(set(fin_deps + [v for k, v in lastc.items() if k != "sync"]))
        self.finished = True

    def replay(self):
        if not self.finished:
            self.finish()
        nc = self.nc
        watched = set()
        for op in self.ops:
            watched.update(op.waits)
        comp_sems = {e: nc.alloc_semaphore(f"se_{e}") for e in COMPUTE_ENGINES}
        comp_val = {e: 0 for e in COMPUTE_ENGINES}
        dma_sems = {q: [nc.alloc_semaphore(f"sd_{q}{i}") for i in range(NDMASEM)]
                    for q in DMA_QUEUES}
        dma_uses = {q: [0] * NDMASEM for q in DMA_QUEUES}
        dma_rr = {q: 0 for q in DMA_QUEUES}
        for op in self.ops:
            if op.kind == "c":
                if op.idx in watched:
                    comp_val[op.eng] += 1
                    op.sem = comp_sems[op.eng]
                    op.semval = comp_val[op.eng]
            else:
                q = op.eng
                k = dma_rr[q] % NDMASEM
                dma_rr[q] += 1
                dma_uses[q][k] += 1
                op.sem = dma_sems[q][k]
                op.semval = dma_uses[q][k] * 16
        satisfied = {e: {} for e in COMPUTE_ENGINES}

        def emit_for(blk, name):
            myops = [o for o in self.ops if o.eng == name]
            if not myops:
                return

            def body(e, _ops=myops, _name=name):
                sat = satisfied[_name]
                for op in _ops:
                    for d in op.waits:
                        dop = self.ops[d]
                        if dop.sem is None:
                            continue
                        key = id(dop.sem)
                        need = dop.semval
                        if sat.get(key, 0) >= need:
                            continue
                        e.wait_ge(dop.sem, need)
                        sat[key] = need
                    inst = op.fn(e)
                    if op.kind == "c" and op.sem is not None:
                        inst.then_inc(op.sem, 1)
                    elif op.kind == "d":
                        inst.then_inc(op.sem, 16)

            getattr(blk, name)(body)

        with nc.Block(no_gpsimd_drain=True) as blk:
            for name in COMPUTE_ENGINES:
                emit_for(blk, name)



def emit_transpose_weight(b, wsrc, wT, psT, idn, tag, nchunk_o=8, nchunk_f=8):
    for o in range(nchunk_o):
        for f in range(nchunk_f):
            j = o * nchunk_f + f
            pt = psT[j % len(psT)]
            b.c("tensor",
                lambda e, o=o, f=f, pt=pt: e.transpose(
                    pt[0][:], wsrc[:, (o * nchunk_f + f) * 128:(o * nchunk_f + f + 1) * 128], idn[:]),
                r=[wsrc.name, idn.name, pt[1] + "r"], w=[pt[1]])
            b.c("vector",
                lambda e, o=o, f=f, pt=pt: e.tensor_copy(
                    wT[:, (f * nchunk_o + o) * 128:(f * nchunk_o + o + 1) * 128], pt[0][:]),
                r=[pt[1]], w=[wT.name, pt[1] + "r"])


def ln_token_tile(b, xa, g1t, mn, scr, tag, gain=None, bias_ap=None, eps=1e-5,
                  out_res=None, in_res=None, D=1024):
    rs = list(in_res or [xa.name])
    b.c("vector", lambda e: e.tensor_reduce(mn[:, 0:1], xa[:], AX.X, ALU.add),
        r=rs, w=[tag + "mn"])
    b.c("scalar", lambda e: e.activation(out=scr[:], in_=xa[:], func=AF.Square),
        r=rs, w=[scr.name])
    b.c("vector", lambda e: e.tensor_reduce(mn[:, 1:2], scr[:], AX.X, ALU.add),
        r=[scr.name], w=[tag + "mn"])
    b.c("vector", lambda e: e.tensor_scalar(mn[:, 0:1], mn[:, 0:1], 1.0 / D, None, ALU.mult),
        r=[tag + "mn"], w=[tag + "mn"])
    b.c("vector", lambda e: e.tensor_tensor(mn[:, 2:3], mn[:, 0:1], mn[:, 0:1], ALU.mult),
        r=[tag + "mn"], w=[tag + "mn"])
    b.c("vector", lambda e: e.tensor_scalar(mn[:, 1:2], mn[:, 1:2], 1.0 / D, None, ALU.mult),
        r=[tag + "mn"], w=[tag + "mn"])
    b.c("vector", lambda e: e.tensor_tensor(mn[:, 2:3], mn[:, 1:2], mn[:, 2:3], ALU.subtract),
        r=[tag + "mn"], w=[tag + "mn"])
    b.c("vector", lambda e: e.tensor_scalar(mn[:, 2:3], mn[:, 2:3], float(eps), None, ALU.add),
        r=[tag + "mn"], w=[tag + "mn"])
    b.c("scalar", lambda e: e.activation(out=mn[:, 3:4], in_=mn[:, 2:3], func=AF.Sqrt),
        r=[tag + "mn"], w=[tag + "sd"])
    b.c("vector", lambda e: e.reciprocal(mn[:, 3:4], mn[:, 3:4]),
        r=[tag + "sd"], w=[tag + "sd"])
    wout = [g1t.name] if out_res is None else out_res
    b.c("vector", lambda e: e.tensor_scalar(g1t[:], xa[:], mn[:, 0:1], mn[:, 3:4],
                                            ALU.subtract, ALU.mult),
        r=rs + [tag + "mn", tag + "sd"], w=wout)



def build_p1():
    nc = bass.Bass("TRN2", target_bir_lowering=False)
    x = nc.dram_tensor("x", [NTOK, C], F32, kind="ExternalInput").ap()
    Wq = nc.dram_tensor("Wq", [C, C], F32, kind="ExternalInput").ap()
    Wk = nc.dram_tensor("Wk", [C, C], F32, kind="ExternalInput").ap()
    Wv = nc.dram_tensor("Wv", [C, C], F32, kind="ExternalInput").ap()
    bqc = nc.dram_tensor("bqc", [128, 8], F32, kind="ExternalInput").ap()
    bkc = nc.dram_tensor("bkc", [128, 8], F32, kind="ExternalInput").ap()
    bvr = nc.dram_tensor("bvr", [1, C], F32, kind="ExternalInput").ap()
    c_idn = nc.dram_tensor("c_idn", [128, 128], F32, kind="ExternalInput").ap()
    c_seg = nc.dram_tensor("c_seg", [128, 8 * 16], F32, kind="ExternalInput").ap()
    c_hmap = nc.dram_tensor("c_hmap", [16, 8 * 128], F32, kind="ExternalInput").ap()
    c_ones = nc.dram_tensor("c_ones", [1, 128], F32, kind="ExternalInput").ap()
    o_qsmT = nc.dram_tensor("o_qsmT", [C, NTOK], BF16, kind="ExternalOutput").ap()
    o_ctx = nc.dram_tensor("o_ctx", [128, C], F32, kind="ExternalOutput").ap()
    o_ksum = nc.dram_tensor("o_ksum", [128, 8], F32, kind="ExternalOutput").ap()

    b = Builder(nc)
    A = nc.alloc_sbuf_tensor
    idn = A("idn", [128, 128], BF16)
    seg = A("seg", [128, 8 * 16], BF16)
    hmap = A("hmap", [16, 8 * 128], BF16)
    ones1 = A("ones1", [1, 128], BF16)
    bqv = A("bqv", [128, 8], F32)
    bkv = A("bkv", [128, 8], F32)
    bvrow = A("bvrow", [1, C], BF16)
    wload = A("wload", [128, 8 * 1024], BF16)
    WqT = A("WqT", [128, 8 * 1024], BF16)
    WkT = A("WkT", [128, 8 * 1024], BF16)
    WvT = A("WvT", [128, 8 * 1024], BF16)
    xtok = [A(f"xtok{i}", [128, 4 * 1024], BF16) for i in range(2)]
    xT = [A(f"xT{i}", [128, 8 * 512], BF16) for i in range(2)]
    qe = A("qe", [128, 8 * 512], BF16)
    ke = A("ke", [128, 8 * 512], BF16)
    rec = A("rec", [16, 2 * 512], F32)
    recb = A("recb", [16, 2 * 512], BF16)
    bcs = A("bcs", [128, 512], F32)
    qsm = A("qsm", [128, 8 * 512], BF16)
    ksmT = A("ksmT", [128, 512], BF16)
    ksm = [A(f"ksm{i}", [128, 4 * 1024], BF16) for i in range(2)]
    vtok = [A(f"vtok{i}", [128, 4 * 1024], BF16) for i in range(2)]
    ksump = A("ksump", [128, 8], F32)
    ksum1 = A("ksum1", [128, 1], F32)
    ctxs = A("ctxs", [128, 1024], F32)

    P = nc.alloc_psum_tensor
    ps_T = [(P(f"ps_T{i}", [128, 128], BF16), f"ps_T{i}") for i in range(2)]
    ps_a = [P(f"ps_a{i}", [128, 512], F32) for i in range(2)]
    ps_ss = P("ps_ss", [128, 512], F32)
    ps_bc = P("ps_bc", [128, 512], F32)
    ps_ctx = P("ps_ctx", [128, 1024], F32)

    b.d("sync", idn[:], c_idn[:], w=["idn"])
    b.d("sync", seg[:], c_seg[:], w=["seg"])
    b.d("sync", hmap[:], c_hmap[:], w=["hmap"])
    b.d("sync", ones1[:], c_ones[:], w=["ones1"])
    b.d("sync", bqv[:, 0:8], bqc[:], w=["bqv"])
    b.d("sync", bkv[:, 0:8], bkc[:], w=["bkv"])
    b.d("gpsimd", bvrow[:], bvr[:], w=["bvrow"])
    for (W, WT, nm) in ((Wq, WqT, "WqT"), (Wk, WkT, "WkT"), (Wv, WvT, "WvT")):
        for o in range(8):
            b.d("gpsimd", wload[:, o * 1024:(o + 1) * 1024], W[ts(o), :],
                w=["wload"])
        emit_transpose_weight(b, wload, WT, ps_T, idn, nm)

    for g in range(NG):
        gb = g % 2
        xk, xTk, ksmk, vk = xtok[gb], xT[gb], ksm[gb], vtok[gb]
        xkn, xTn, ksmn, vn = xk.name, xTk.name, ksmk.name, vk.name
        for i in range(4):
            b.d("gpsimd", xk[:, i * 1024:(i + 1) * 1024],
                x[g * 512 + i * 128: g * 512 + (i + 1) * 128, :],
                w=[f"{xkn}_{i}"])
        for i in range(4):
            for c1 in range(8):
                pt = ps_T[(i * 8 + c1) % 2]
                b.c("tensor",
                    lambda e, i=i, c1=c1, pt=pt, xk=xk: e.transpose(
                        pt[0][:], xk[:, i * 1024 + c1 * 128: i * 1024 + (c1 + 1) * 128], idn[:]),
                    r=[f"{xkn}_{i}", "idn", pt[1] + "r"], w=[pt[1]])
                b.c("vector",
                    lambda e, i=i, c1=c1, pt=pt, xTk=xTk: e.tensor_copy(
                        xTk[:, c1 * 512 + i * 128: c1 * 512 + (i + 1) * 128], pt[0][:]),
                    r=[pt[1]], w=[f"{xTn}_{c1}", pt[1] + "r"])
        xt_r = [f"{xTn}_{c1}" for c1 in range(8)]
        for qk in range(2):
            WT = WqT if qk == 0 else WkT
            ebuf, ebn = (qe, "qe") if qk == 0 else (ke, "ke")
            bias = bqv if qk == 0 else bkv
            bias_n = "bqv" if qk == 0 else "bkv"
            for o in range(8):
                pa = ps_a[o % 2]
                pan = f"ps_a{o % 2}"
                for f in range(8):
                    b.c("tensor",
                        lambda e, o=o, f=f, WT=WT, pa=pa, xTk=xTk: e.matmul(
                            pa[:], WT[:, f * 1024 + o * 128: f * 1024 + (o + 1) * 128],
                            xTk[:, f * 512:(f + 1) * 512],
                            start=(f == 0), stop=(f == 7)),
                        r=[WT.name, xt_r[f]] + ([pan + "r"] if f == 0 else []),
                        w=[pan])
                b.c("scalar",
                    lambda e, o=o, pa=pa, ebuf=ebuf, bias=bias: e.activation(
                        out=ebuf[:, o * 512:(o + 1) * 512], in_=pa[:],
                        func=AF.Exp, bias=bias[:, o:o + 1], scale=1.0),
                    r=[pan, bias_n], w=[f"{ebn}_{o}", pan + "r"])
            for c1 in range(8):
                b.c("tensor",
                    lambda e, c1=c1, ebuf=ebuf: e.matmul(
                        ps_ss[0:16, 0:512], seg[:, c1 * 16:(c1 + 1) * 16],
                        ebuf[:, c1 * 512:(c1 + 1) * 512],
                        start=(c1 == 0), stop=(c1 == 7)),
                    r=["seg", f"{ebn}_{c1}"] + (["ps_ssr"] if c1 == 0 else []),
                    w=["ps_ss"])
            b.c("vector",
                lambda e, qk=qk: e.reciprocal(rec[:, qk * 512:(qk + 1) * 512],
                                              ps_ss[0:16, 0:512]),
                r=["ps_ss"], w=["rec", "ps_ssr"])
            b.c("vector",
                lambda e, qk=qk: e.tensor_copy(recb[:, qk * 512:(qk + 1) * 512],
                                               rec[:, qk * 512:(qk + 1) * 512]),
                r=["rec"], w=["recb"])
            for c1 in range(8):
                b.c("tensor",
                    lambda e, c1=c1, qk=qk: e.matmul(
                        ps_bc[:], hmap[:, c1 * 128:(c1 + 1) * 128],
                        recb[0:16, qk * 512:(qk + 1) * 512], start=True, stop=True),
                    r=["hmap", "recb", "ps_bcr"], w=["ps_bc"])
                if qk == 0:
                    b.c("vector",
                        lambda e, c1=c1: e.tensor_tensor(
                            qsm[:, c1 * 512:(c1 + 1) * 512],
                            qe[:, c1 * 512:(c1 + 1) * 512], ps_bc[:], ALU.mult),
                        r=["ps_bc", f"qe_{c1}", f"qsm_{c1}d"], w=[f"qsm_{c1}", "ps_bcr"])
                    b.d("sync", o_qsmT[ts(c1), g * 512:(g + 1) * 512],
                        qsm[:, c1 * 512:(c1 + 1) * 512],
                        r=[f"qsm_{c1}"], w=[f"qsm_{c1}d"])
                else:
                    b.c("vector",
                        lambda e, c1=c1: e.tensor_tensor(
                            ksmT[:], ke[:, c1 * 512:(c1 + 1) * 512], ps_bc[:], ALU.mult),
                        r=["ps_bc", f"ke_{c1}"], w=["ksmT", "ps_bcr"])
                    b.c("vector",
                        lambda e, c1=c1: e.tensor_reduce(
                            ksum1[:], ksmT[:], AX.X, ALU.add),
                        r=["ksmT"], w=["ksum1"])
                    if g == 0:
                        b.c("vector",
                            lambda e, c1=c1: e.tensor_copy(ksump[:, c1:c1 + 1], ksum1[:]),
                            r=["ksum1"], w=["ksump"])
                    else:
                        b.c("vector",
                            lambda e, c1=c1: e.tensor_tensor(
                                ksump[:, c1:c1 + 1], ksump[:, c1:c1 + 1], ksum1[:], ALU.add),
                            r=["ksum1", "ksump"], w=["ksump"])
                    for i2 in range(4):
                        pt = ps_T[i2 % 2]
                        b.c("tensor",
                            lambda e, c1=c1, i2=i2, pt=pt: e.transpose(
                                pt[0][:], ksmT[:, i2 * 128:(i2 + 1) * 128], idn[:]),
                            r=["ksmT", "idn", pt[1] + "r"], w=[pt[1]])
                        b.c("vector",
                            lambda e, c1=c1, i2=i2, pt=pt, ksmk=ksmk: e.tensor_copy(
                                ksmk[:, i2 * 1024 + c1 * 128: i2 * 1024 + (c1 + 1) * 128],
                                pt[0][:]),
                            r=[pt[1]], w=[f"{ksmn}_{i2}", pt[1] + "r"])
        for i in range(4):
            for o2 in range(2):
                pa = ps_a[o2]
                pan = f"ps_a{o2}"
                for f in range(8):
                    b.c("tensor",
                        lambda e, i=i, o2=o2, f=f, pa=pa, xTk=xTk: e.matmul(
                            pa[:], xTk[:, f * 512 + i * 128: f * 512 + (i + 1) * 128],
                            WvT[:, f * 1024 + o2 * 512: f * 1024 + (o2 + 1) * 512],
                            start=(f == 0), stop=False),
                        r=["WvT", xt_r[f]] + ([pan + "r"] if f == 0 else []),
                        w=[pan])
                b.c("tensor",
                    lambda e, o2=o2, pa=pa: e.matmul(
                        pa[:], ones1[:], bvrow[:, o2 * 512:(o2 + 1) * 512],
                        start=False, stop=True),
                    r=["ones1", "bvrow"], w=[pan])
                b.c("vector",
                    lambda e, i=i, o2=o2, pa=pa, vk=vk: e.tensor_copy(
                        vk[:, i * 1024 + o2 * 512: i * 1024 + (o2 + 1) * 512], pa[:]),
                    r=[pan], w=[f"{vn}_{i}", pan + "r"])
        for i in range(4):
            for c1 in range(8):
                b.c("tensor",
                    lambda e, i=i, c1=c1, g=g, ksmk=ksmk, vk=vk: e.matmul(
                        ps_ctx[:, c1 * 128:(c1 + 1) * 128],
                        ksmk[:, i * 1024 + c1 * 128: i * 1024 + (c1 + 1) * 128],
                        vk[:, i * 1024 + c1 * 128: i * 1024 + (c1 + 1) * 128],
                        start=(g == 0 and i == 0 and c1 in (0, 4)),
                        stop=(g == NG - 1 and i == 3),
                        skip_group_check=True),
                    r=[f"{ksmn}_{i}", f"{vn}_{i}"], w=["ps_ctx"])

    b.c("vector", lambda e: e.tensor_copy(ctxs[:], ps_ctx[:]), r=["ps_ctx"], w=["ctxs"])
    b.d("sync", o_ctx[:], ctxs[:], r=["ctxs"])
    b.d("sync", o_ksum[:], ksump[:], r=["ksump"])
    b.replay()
    return nc



def build_p2():
    nc = bass.Bass("TRN2", target_bir_lowering=False)
    x = nc.dram_tensor("x", [NTOK, C], F32, kind="ExternalInput").ap()
    qsmT = nc.dram_tensor("qsmT", [C, NTOK], BF16, kind="ExternalInput").ap()
    ctx0 = nc.dram_tensor("ctx0", [128, C], F32, kind="ExternalInput").ap()
    ctx1 = nc.dram_tensor("ctx1", [128, C], F32, kind="ExternalInput").ap()
    ks0 = nc.dram_tensor("ks0", [128, 8], F32, kind="ExternalInput").ap()
    ks1 = nc.dram_tensor("ks1", [128, 8], F32, kind="ExternalInput").ap()
    Wp = nc.dram_tensor("Wp", [C, C], F32, kind="ExternalInput").ap()
    bpr = nc.dram_tensor("bpr", [1, C], F32, kind="ExternalInput").ap()
    c_idn = nc.dram_tensor("c_idn", [128, 128], F32, kind="ExternalInput").ap()
    c_seg = nc.dram_tensor("c_seg", [128, 8 * 16], F32, kind="ExternalInput").ap()
    c_hmap = nc.dram_tensor("c_hmap", [16, 8 * 128], F32, kind="ExternalInput").ap()
    c_ones = nc.dram_tensor("c_ones", [1, 128], F32, kind="ExternalInput").ap()
    c_bd64 = nc.dram_tensor("c_bd64", [128, 128], F32, kind="ExternalInput").ap()
    o_g1 = nc.dram_tensor("o_g1", [NTOK, C], F32, kind="ExternalOutput").ap()

    b = Builder(nc)
    A = nc.alloc_sbuf_tensor
    idn = A("idn", [128, 128], BF16)
    seg = A("seg", [128, 8 * 16], BF16)
    hmap = A("hmap", [16, 8 * 128], BF16)
    ones1 = A("ones1", [1, 128], BF16)
    bd64 = A("bd64", [128, 128], F32)
    bprow = A("bprow", [1, C], BF16)
    wload = A("wload", [128, 8 * 1024], BF16)
    WpT = A("WpT", [128, 8 * 1024], BF16)
    ctxa = A("ctxa", [128, 1024], F32)
    ctxb2 = A("ctxb2", [128, 1024], F32)
    ctxb = A("ctxb", [128, 1024], BF16)
    ksm = A("ksm", [128, 8], F32)
    ks1s = A("ks1s", [128, 8], F32)
    qs = [A(f"qs{i}", [128, 8 * 512], BF16) for i in range(2)]
    xg = [A(f"xg{i}", [128, 4 * 1024], F32) for i in range(2)]
    tsq = A("tsq", [128, 512], BF16)
    dinv = A("dinv", [16, 512], F32)
    dinvb = A("dinvb", [16, 512], BF16)
    bcs = A("bcs", [128, 512], F32)
    outT = A("outT", [128, 8 * 512], BF16)
    xa = A("xa", [128, 1024], F32)
    scr = A("scr", [128, 1024], F32)
    mn = A("mn", [128, 4], F32)
    g1t = A("g1t", [128, 1024], F32)

    P = nc.alloc_psum_tensor
    ps_T = [(P(f"ps_T{i}", [128, 128], BF16), f"ps_T{i}") for i in range(2)]
    ps_a = [P(f"ps_a{i}", [128, 512], F32) for i in range(2)]
    ps_ss = P("ps_ss", [128, 512], F32)
    ps_bc = P("ps_bc", [128, 512], F32)
    ps_r = P("ps_r", [128, 1024], F32)

    b.d("sync", idn[:], c_idn[:], w=["idn"])
    b.d("sync", seg[:], c_seg[:], w=["seg"])
    b.d("sync", hmap[:], c_hmap[:], w=["hmap"])
    b.d("sync", ones1[:], c_ones[:], w=["ones1"])
    b.d("sync", bd64[:], c_bd64[:], w=["bd64"])
    b.d("gpsimd", bprow[:], bpr[:], w=["bprow"])
    b.d("sync", ctxa[:], ctx0[:], w=["ctxa"])
    b.d("sync", ctxb2[:], ctx1[:], w=["ctxb2"])
    b.d("sync", ksm[:], ks0[:], w=["ksm"])
    b.d("sync", ks1s[:], ks1[:], w=["ks1s"])
    for o in range(8):
        b.d("gpsimd", wload[:, o * 1024:(o + 1) * 1024], Wp[ts(o), :], w=["wload"])
    emit_transpose_weight(b, wload, WpT, ps_T, idn, "WpT")
    b.c("vector", lambda e: e.tensor_tensor(ctxa[:], ctxa[:], ctxb2[:], ALU.add),
        r=["ctxa", "ctxb2"], w=["ctxa"])
    for c1 in range(8):
        b.c("vector", lambda e, c1=c1: e.tensor_tensor(
            ctxb[:, c1 * 128:(c1 + 1) * 128], ctxa[:, c1 * 128:(c1 + 1) * 128],
            bd64[:], ALU.mult),
            r=["ctxa", "bd64"], w=["ctxb"])
    b.c("vector", lambda e: e.tensor_tensor(ksm[:], ksm[:], ks1s[:], ALU.add),
        r=["ksm", "ks1s"], w=["ksm"])

    for g in range(NG):
        gb = g % 2
        q = qs[gb]
        qn = q.name
        xgk = xg[gb]
        xgn = xgk.name
        for c1 in range(8):
            b.d("sync", q[:, c1 * 512:(c1 + 1) * 512],
                qsmT[ts(c1), g * 512:(g + 1) * 512], w=[f"{qn}_{c1}"])
        for i in range(4):
            b.d("gpsimd", xgk[:, i * 1024:(i + 1) * 1024],
                x[g * 512 + i * 128:g * 512 + (i + 1) * 128, :], w=[f"{xgn}_{i}"])
        for c1 in range(8):
            b.c("vector", lambda e, c1=c1, q=q: e.tensor_scalar(
                tsq[:], q[:, c1 * 512:(c1 + 1) * 512], ksm[:, c1:c1 + 1], None, ALU.mult),
                r=[f"{qn}_{c1}", "ksm"], w=["tsq"])
            b.c("tensor", lambda e, c1=c1: e.matmul(
                ps_ss[0:16, 0:512], seg[:, c1 * 16:(c1 + 1) * 16], tsq[:],
                start=(c1 == 0), stop=(c1 == 7)),
                r=["seg", "tsq"] + (["ps_ssr"] if c1 == 0 else []), w=["ps_ss"])
        b.c("vector", lambda e: e.reciprocal(dinv[:], ps_ss[0:16, 0:512]),
            r=["ps_ss"], w=["dinv", "ps_ssr"])
        b.c("vector", lambda e: e.tensor_copy(dinvb[:], dinv[:]), r=["dinv"], w=["dinvb"])
        for c1 in range(8):
            pa = ps_a[c1 % 2]
            pan = f"ps_a{c1 % 2}"
            b.c("tensor", lambda e, c1=c1, pa=pa, q=q: e.matmul(
                pa[:], ctxb[:, c1 * 128:(c1 + 1) * 128], q[:, c1 * 512:(c1 + 1) * 512],
                start=True, stop=True),
                r=["ctxb", f"{qn}_{c1}", pan + "r"], w=[pan])
            b.c("tensor", lambda e, c1=c1: e.matmul(
                ps_bc[:], hmap[:, c1 * 128:(c1 + 1) * 128], dinvb[:],
                start=True, stop=True),
                r=["hmap", "dinvb", "ps_bcr"], w=["ps_bc"])
            b.c("vector", lambda e: e.tensor_copy(bcs[:], ps_bc[:]),
                r=["ps_bc"], w=["bcs", "ps_bcr"])
            b.c("vector", lambda e, c1=c1, pa=pa: e.tensor_tensor(
                outT[:, c1 * 512:(c1 + 1) * 512], pa[:], bcs[:], ALU.mult),
                r=[pan, "bcs"], w=[f"outT_{c1}", pan + "r"])
            b.c("vector", lambda e, c1=c1, q=q: e.tensor_tensor(
                outT[:, c1 * 512:(c1 + 1) * 512], outT[:, c1 * 512:(c1 + 1) * 512],
                q[:, c1 * 512:(c1 + 1) * 512], ALU.add),
                r=[f"outT_{c1}", f"{qn}_{c1}"], w=[f"outT_{c1}"])
        for i in range(4):
            for c1 in range(8):
                b.c("tensor", lambda e, i=i, c1=c1: e.matmul(
                    ps_r[:, 0:512],
                    outT[:, c1 * 512 + i * 128: c1 * 512 + (i + 1) * 128],
                    WpT[:, c1 * 1024: c1 * 1024 + 512],
                    start=(c1 == 0), stop=False, skip_group_check=True),
                    r=[f"outT_{c1}", "WpT"] + (["ps_rr"] if c1 == 0 else []), w=["ps_r"])
                b.c("tensor", lambda e, i=i, c1=c1: e.matmul(
                    ps_r[:, 512:1024],
                    outT[:, c1 * 512 + i * 128: c1 * 512 + (i + 1) * 128],
                    WpT[:, c1 * 1024 + 512: (c1 + 1) * 1024],
                    start=(c1 == 0), stop=False, skip_group_check=True),
                    r=[f"outT_{c1}", "WpT"], w=["ps_r"])
            b.c("tensor", lambda e: e.matmul(
                ps_r[:, 0:512], ones1[:], bprow[:, 0:512],
                start=False, stop=True, skip_group_check=True),
                r=["ones1", "bprow"], w=["ps_r"])
            b.c("tensor", lambda e: e.matmul(
                ps_r[:, 512:1024], ones1[:], bprow[:, 512:1024],
                start=False, stop=True, skip_group_check=True),
                r=["ones1", "bprow"], w=["ps_r"])
            b.c("vector", lambda e, i=i, xgk=xgk: e.tensor_tensor(
                xa[:], ps_r[:], xgk[:, i * 1024:(i + 1) * 1024], ALU.add),
                r=["ps_r", f"{xgn}_{i}"], w=["xa", "ps_rr"])
            ln_token_tile(b, xa, g1t, mn, scr, "p2ln")
            b.d("sync", o_g1[g * 512 + i * 128: g * 512 + (i + 1) * 128, :], g1t[:],
                r=["g1t"], w=["g1td"])
    b.replay()
    return nc



def host_consts():
    cs = {}
    cs["c_idn"] = np.eye(128, dtype=np.float32)
    seg = np.zeros((8, 128, 16), dtype=np.float32)
    hm = np.zeros((8, 16, 128), dtype=np.float32)
    for c in range(8):
        for p in range(128):
            hg = 2 * c + p // 64
            seg[c, p, hg] = 1.0
            hm[c, hg, p] = 1.0
    cs["c_seg"] = np.ascontiguousarray(seg.transpose(1, 0, 2).reshape(128, 8 * 16))
    cs["c_hmap"] = np.ascontiguousarray(hm.transpose(1, 0, 2).reshape(16, 8 * 128))
    m = np.zeros((128, 128), dtype=np.float32)
    m[:64, :64] = 1.0
    m[64:, 64:] = 1.0
    cs["c_bd64"] = m
    cs["c_ones"] = np.ones((1, 128), dtype=np.float32)
    return cs


def run_p1(I):
    cs = host_consts()
    nc = build_p1()
    x = I["L0_attn_x"]
    maps = []
    for c in range(8):
        bb, h = c // 2, c % 2
        maps.append({
            "x": np.ascontiguousarray(x[bb, h * NTOK:(h + 1) * NTOK]),
            "Wq": I["L0_attn_Wq"], "Wk": I["L0_attn_Wk"], "Wv": I["L0_attn_Wv"],
            "bqc": np.ascontiguousarray(I["L0_attn_bq"].reshape(8, 128).T),
            "bkc": np.ascontiguousarray(I["L0_attn_bk"].reshape(8, 128).T),
            "bvr": I["L0_attn_bv"].reshape(1, C),
            "c_idn": cs["c_idn"], "c_seg": cs["c_seg"], "c_hmap": cs["c_hmap"],
            "c_ones": cs["c_ones"],
        })
    res = run_bass_kernel_spmd(nc, maps, core_ids=list(range(8)))
    return res.results


def run_p2(I, p1res):
    cs = host_consts()
    nc = build_p2()
    x = I["L0_attn_x"]
    maps = []
    for c in range(8):
        bb, h = c // 2, c % 2
        peer = bb * 2 + (1 - h)
        maps.append({
            "x": np.ascontiguousarray(x[bb, h * NTOK:(h + 1) * NTOK]),
            "qsmT": p1res[c]["o_qsmT"],
            "ctx0": p1res[c]["o_ctx"], "ctx1": p1res[peer]["o_ctx"],
            "ks0": p1res[c]["o_ksum"], "ks1": p1res[peer]["o_ksum"],
            "Wp": I["L0_attn_Wp"], "bpr": I["L0_attn_bp"].reshape(1, C),
            "c_idn": cs["c_idn"], "c_seg": cs["c_seg"], "c_hmap": cs["c_hmap"],
            "c_ones": cs["c_ones"], "c_bd64": cs["c_bd64"],
        })
    res = run_bass_kernel_spmd(nc, maps, core_ids=list(range(8)))
    return res.results


def kernel(**inputs):
    raise NotImplementedError



W0F = 512
NW = 8


def build_p3a():
    nc = bass.Bass("TRN2", target_bir_lowering=False)
    x = nc.dram_tensor("x", [NTOK, C], F32, kind="ExternalInput").ap()
    xhal = nc.dram_tensor("xhal", [514, C], F32, kind="ExternalInput").ap()
    xprev = nc.dram_tensor("xprev", [128, 8], F32, kind="ExternalInput").ap()
    wqkv = nc.dram_tensor("wqkv", [C, 1536], F32, kind="ExternalInput").ap()
    wo = nc.dram_tensor("wo", [512, C], F32, kind="ExternalInput").ap()
    qsc = nc.dram_tensor("qsc", [128, 1], F32, kind="ExternalInput").ap()
    ksc = nc.dram_tensor("ksc", [128, 1], F32, kind="ExternalInput").ap()
    tabs = nc.dram_tensor("tabs", [128, 4 * 4608], F32, kind="ExternalInput").ap()
    c_idn = nc.dram_tensor("c_idn", [128, 128], F32, kind="ExternalInput").ap()
    c_ones = nc.dram_tensor("c_ones", [1, 128], F32, kind="ExternalInput").ap()
    c_seg8 = nc.dram_tensor("c_seg8", [128, 4 * 8], F32, kind="ExternalInput").ap()
    c_hm8 = nc.dram_tensor("c_hm8", [8, 4 * 128], F32, kind="ExternalInput").ap()
    c_rot = nc.dram_tensor("c_rot", [128, 128], F32, kind="ExternalInput").ap()
    c_fb = nc.dram_tensor("c_fb", [128, 640], F32, kind="ExternalInput").ap()
    c_w0x = nc.dram_tensor("c_w0x", [1, 1024], F32, kind="ExternalInput").ap()
    o_xmid = nc.dram_tensor("o_xmid", [NTOK, C], F32, kind="ExternalOutput").ap()

    b = Builder(nc)
    A = nc.alloc_sbuf_tensor
    idn = A("idn", [128, 128], BF16)
    ones1 = A("ones1", [1, 128], BF16)
    seg8 = A("seg8", [128, 4 * 8], F32)
    hm8 = A("hm8", [8, 4 * 128], F32)
    rotm = A("rotm", [128, 128], F32)
    fb = A("fb", [128, 640], F32)
    w0x = A("w0x", [1, 1024], BF16)
    qscv = A("qscv", [128, 1], F32)
    kscv = A("kscv", [128, 1], F32)
    wk = A("wk", [128, 8 * 1536], BF16)
    woS = A("woS", [128, 4 * 1024], BF16)
    tabw = [A(f"tabw{i}", [128, 4 * 512], F32) for i in range(2)]
    xw = [A(f"xw{i}", [128, 4 * 1024], F32) for i in range(2)]
    xwb = A("xwb", [128, 4 * 1024], BF16)
    xT = A("xT", [128, 8 * 640], BF16)
    xpv = A("xpv", [128, 8], BF16)
    qT = A("qT", [128, 4 * 512], BF16)
    kvtmp = A("kvtmp", [128, 4 * 512], F32)
    kT = A("kT", [128, 4 * 1024], BF16)
    vr = A("vr", [128, 8 * 512], BF16)
    nrm = A("nrm", [8, 512], F32)
    bcs = A("bcs", [128, 512], F32)
    t1b = A("t1b", [128, 512], F32)
    t2b = A("t2b", [128, 512], F32)
    Ssb = [A(f"Ssb{i}", [128, 640], F32) for i in range(2)]
    nm = [A(f"nm{i}", [128, 2], F32) for i in range(2)]
    Pb = [A(f"Pb{i}", [128, 640], BF16) for i in range(2)]
    PT = [A(f"PT{i}", [128, 5 * 128], BF16) for i in range(2)]
    aoT = A("aoT", [128, 512], BF16)
    xmid = A("xmid", [128, 1024], F32)

    P = nc.alloc_psum_tensor
    psSc1 = P("psSc", [128, 640], F32)
    psSc = [psSc1, psSc1]
    psC = P("psC", [128, 512], F32)
    psT = P("psT", [128, 512], BF16)
    psA = P("psA", [128, 1024], F32)
    psB = P("psB", [128, 512], F32)

    b.d("sync", idn[:], c_idn[:], w=["idn"])
    b.d("sync", ones1[:], c_ones[:], w=["ones1"])
    b.d("sync", seg8[:], c_seg8[:], w=["seg8"])
    b.d("sync", hm8[:], c_hm8[:], w=["hm8"])
    b.d("sync", rotm[:], c_rot[:], w=["rotm"])
    b.d("sync", fb[:], c_fb[:], w=["fb"])
    b.d("sync", w0x[:], c_w0x[:], w=["w0x"])
    b.d("sync", qscv[:], qsc[:], w=["qscv"])
    b.d("sync", kscv[:], ksc[:], w=["kscv"])
    b.d("sync", xpv[:], xprev[:], w=["xpv"])
    for f in range(8):
        b.d("gpsimd", wk[:, f * 1536:(f + 1) * 1536], wqkv[ts(f), :], w=["wk"])
    for i in range(4):
        b.d("gpsimd", woS[:, i * 1024:(i + 1) * 1024], wo[ts(i), :], w=["woS"])

    def emit_qkv_window(wi, wslot):
        real = wi >= 0
        xk = xw[wslot]
        xkn = xk.name
        tw = tabw[wslot]
        twn = tw.name
        for c1 in range(8):
            b.c("vector", lambda e, c1=c1: e.tensor_copy(
                xT[:, c1 * 640:c1 * 640 + 128], xT[:, c1 * 640 + 512:c1 * 640 + 640]),
                r=[f"xT_{c1}"], w=[f"xT_{c1}"])
        for c1 in range(4):
            b.c("vector", lambda e, c1=c1: e.tensor_copy(
                kT[:, c1 * 1024:c1 * 1024 + 512], kT[:, c1 * 1024 + 512:(c1 + 1) * 1024]),
                r=[f"kT_{c1}"], w=[f"kT_{c1}"])
        for i in range(4):
            b.c("vector", lambda e, i=i: e.tensor_copy(
                vr[:, i * 512:(i + 1) * 512], vr[:, (4 + i) * 512:(5 + i) * 512]),
                r=[f"vr_{i}", f"vr_{4 + i}"], w=[f"vr_{i}"])
        for i in range(4):
            if wi < 0:
                src = xhal[2 + i * 128: 2 + (i + 1) * 128, :]
            else:
                src = x[wi * 512 + i * 128: wi * 512 + (i + 1) * 128, :]
            b.d("gpsimd", xk[:, i * 1024:(i + 1) * 1024], src, w=[f"{xkn}_{i}"])
        for t4 in range(4):
            b.d("gpsimd", tw[:, t4 * 512:(t4 + 1) * 512],
                tabs[:, t4 * 4608 + (wi + 1) * 512: t4 * 4608 + (wi + 2) * 512],
                w=[twn])
        for i in range(4):
            b.c("vector", lambda e, i=i, xk=xk: e.tensor_copy(
                xwb[:, i * 1024:(i + 1) * 1024], xk[:, i * 1024:(i + 1) * 1024]),
                r=[f"{xkn}_{i}"], w=[f"xwb_{i}"])
            for c1 in range(8):
                b.c("tensor", lambda e, i=i, c1=c1: e.transpose(
                    psT[:, (i % 4) * 128:(i % 4 + 1) * 128],
                    xwb[:, i * 1024 + c1 * 128: i * 1024 + (c1 + 1) * 128], idn[:],
                    ) if False else e.transpose(
                    psT[:, 0:128],
                    xwb[:, i * 1024 + c1 * 128: i * 1024 + (c1 + 1) * 128], idn[:]),
                    r=[f"xwb_{i}", "idn", "psTr"], w=["psT"])
                b.c("vector", lambda e, i=i, c1=c1: e.tensor_copy(
                    xT[:, c1 * 640 + 128 + i * 128: c1 * 640 + 128 + (i + 1) * 128],
                    psT[:, 0:128]),
                    r=["psT"], w=[f"xT_{c1}", "psTr"])
        if wi == -1:
            for c1 in range(8):
                b.c("vector", lambda e, c1=c1: e.tensor_copy(
                    xT[:, c1 * 640 + 127: c1 * 640 + 128], xpv[:, c1:c1 + 1]),
                    r=["xpv"], w=[f"xT_{c1}"])
        xt_r = [f"xT_{c1}" for c1 in range(8)]

        def proj_part(dst, dstn, col0, oc_n, token_major):
            for oc in range(oc_n):
                for tile_or_none in ([None] if not token_major else range(4)):
                    if not token_major:
                        par = oc % 2
                        pd = psA[:, par * 512:(par + 1) * 512]
                        pdn = f"psA{par}"
                        for f in range(8):
                            sh = 127 if f >= 4 else 128
                            b.c("tensor", lambda e, oc=oc, f=f, sh=sh, col0=col0, pd=pd: e.matmul(
                                pd, wk[:, f * 1536 + col0 + oc * 128: f * 1536 + col0 + (oc + 1) * 128],
                                xT[:, f * 640 + sh: f * 640 + sh + 512],
                                start=(f == 0), stop=(f == 7)),
                                r=["wk", xt_r[f]] + ([pdn + "r"] if f == 0 else []), w=[pdn])
                        b.c("vector", lambda e, oc=oc, dst=dst, pd=pd: e.tensor_copy(
                            dst[:, oc * 512:(oc + 1) * 512], pd),
                            r=[pdn], w=[f"{dstn}_{oc}", pdn + "r"])
                    else:
                        i = tile_or_none
                        par = i % 2
                        pd = psA[:, par * 512:(par + 1) * 512]
                        pdn = f"psA{par}"
                        for f in range(8):
                            sh = 127 if f >= 4 else 128
                            b.c("tensor", lambda e, i=i, f=f, sh=sh, col0=col0, pd=pd: e.matmul(
                                pd, xT[:, f * 640 + sh + i * 128: f * 640 + sh + (i + 1) * 128],
                                wk[:, f * 1536 + col0: f * 1536 + col0 + 512],
                                start=(f == 0), stop=(f == 7)),
                                r=["wk", xt_r[f]] + ([pdn + "r"] if f == 0 else []), w=[pdn])
                        b.c("vector", lambda e, i=i, dst=dst, pd=pd: e.tensor_copy(
                            dst[:, (4 + i) * 512:(5 + i) * 512], pd),
                            r=[pdn], w=[f"{dstn}_{4 + i}", pdn + "r"])

        proj_part(vr, "vr", 1024, 1, True)
        proj_part(kvtmp, "kvtmp", 512, 4, False)

        def l2rot(srcbuf, srcn, scv, scvn, tab0, dst_fn, post=None):
            for c1 in range(4):
                b.c("scalar", lambda e, c1=c1, srcbuf=srcbuf: e.activation(
                    out=t1b[:], in_=srcbuf[:, c1 * 512:(c1 + 1) * 512], func=AF.Square),
                    r=[f"{srcn}_{c1}"], w=["t1b"])
                b.c("tensor", lambda e, c1=c1: e.matmul(
                    psB[0:8, 0:512], seg8[:, c1 * 8:(c1 + 1) * 8], t1b[:],
                    start=True, stop=True),
                    r=["seg8", "t1b", "psBr"], w=["psB"])
                b.c("scalar", lambda e: e.activation(out=nrm[:], in_=psB[0:8, 0:512], func=AF.Sqrt),
                    r=["psB"], w=["nrm", "psBr"])
                b.c("vector", lambda e: e.tensor_scalar(nrm[:], nrm[:], 1e-12, None, ALU.max),
                    r=["nrm"], w=["nrm"])
                b.c("vector", lambda e: e.reciprocal(nrm[:], nrm[:]), r=["nrm"], w=["nrm"])
                b.c("tensor", lambda e, c1=c1: e.matmul(
                    psB[:, 0:512], hm8[:, c1 * 128:(c1 + 1) * 128], nrm[:],
                    start=True, stop=True),
                    r=["hm8", "nrm", "psBr"], w=["psB"])
                b.c("vector", lambda e, c1=c1, srcbuf=srcbuf, scv=scv: e.tensor_tensor(
                    t1b[:], srcbuf[:, c1 * 512:(c1 + 1) * 512], psB[:, 0:512], ALU.mult),
                    r=[f"{srcn}_{c1}", "psB"], w=["t1b", "psBr"])
                b.c("vector", lambda e, scv=scv, post=post: e.tensor_scalar(
                    t1b[:], t1b[:], scv[:], post, ALU.mult,
                    op1=(ALU.bypass if post is None else ALU.mult)),
                    r=["t1b", scvn], w=["t1b"])
                b.c("tensor", lambda e: e.matmul(psB[:, 0:512], rotm[:], t1b[:],
                                                 start=True, stop=True),
                    r=["rotm", "t1b", "psBr"], w=["psB"])
                b.c("vector", lambda e, tab0=tab0, tw=tw: e.tensor_tensor(
                    t2b[:], t1b[:], tw[:, tab0 * 512 * 4: tab0 * 512 * 4 + 512] if False else tw[:, tab0 * 512:(tab0) * 512 + 512], ALU.mult),
                    r=["t1b", twn], w=["t2b"])
                b.c("vector", lambda e, tab0=tab0, tw=tw: e.tensor_tensor(
                    t1b[:], psB[:, 0:512], tw[:, (tab0 + 1) * 512:(tab0 + 2) * 512], ALU.mult),
                    r=["psB", twn], w=["t1b", "psBr"])
                dst_fn(c1)

        def qdst(c1):
            b.c("vector", lambda e, c1=c1: e.tensor_tensor(
                qT[:, c1 * 512:(c1 + 1) * 512], t2b[:], t1b[:], ALU.add),
                r=["t1b", "t2b"], w=[f"qT_{c1}"])

        def kdst(c1):
            b.c("vector", lambda e, c1=c1: e.tensor_tensor(
                kT[:, c1 * 1024 + 512:(c1 + 1) * 1024], t2b[:], t1b[:], ALU.add),
                r=["t1b", "t2b"], w=[f"kT_{c1}"])

        l2rot(kvtmp, "kvtmp", kscv, "kscv", 2, kdst)
        if real:
            proj_part(kvtmp, "kvtmp", 0, 4, False)
            l2rot(kvtmp, "kvtmp", qscv, "qscv", 0, qdst, post=8.0)

    def emit_attn_window(wi, wslot):
        xk = xw[wslot]
        xkn = xk.name
        for qt in range(4):
            o = qt * 128
            for h in range(8):
                c1, half = h // 2, h % 2
                hp = h % 2
                pS, pSn = psSc[hp], "psSc"
                sb, sbn = Ssb[hp], f"Ssb{hp}"
                pb, pbn = Pb[hp], f"Pb{hp}"
                pt_, ptn = PT[hp], f"PT{hp}"
                nmv, nmn = nm[hp], f"nm{hp}"
                b.c("tensor", lambda e, c1=c1, half=half, o=o, pS=pS: e.matmul(
                    pS[:, 0:512],
                    qT[:, c1 * 512 + o + 0:c1 * 512 + o + 128][ts(half, 64), :],
                    kT[:, c1 * 1024 + o: c1 * 1024 + o + 512][ts(half, 64), :],
                    start=True, stop=True),
                    r=[f"qT_{c1}", f"kT_{c1}", pSn + "r"], w=[pSn])
                b.c("tensor", lambda e, c1=c1, half=half, o=o, pS=pS: e.matmul(
                    pS[:, 512:640],
                    qT[:, c1 * 512 + o:c1 * 512 + o + 128][ts(half, 64), :],
                    kT[:, c1 * 1024 + o + 512: c1 * 1024 + o + 640][ts(half, 64), :],
                    start=True, stop=True),
                    r=[f"qT_{c1}", f"kT_{c1}"], w=[pSn])
                if wi == 0:
                    b.c("tensor", lambda e, o=o, pS=pS: e.matmul(
                        pS[:, 0:512], ones1[:], w0x[:, o:o + 512],
                        start=False, stop=True, skip_group_check=True),
                        r=["ones1", "w0x"], w=[pSn])
                    b.c("tensor", lambda e, o=o, pS=pS: e.matmul(
                        pS[:, 512:640], ones1[:], w0x[:, o + 512:o + 640],
                        start=False, stop=True, skip_group_check=True),
                        r=["ones1", "w0x"], w=[pSn])
                b.c("vector", lambda e, pS=pS, sb=sb: e.tensor_tensor(sb[:], pS[:], fb[:], ALU.add),
                    r=[pSn, "fb"], w=[sbn, pSn + "r"])
                b.c("vector", lambda e, sb=sb, nmv=nmv: e.tensor_reduce(
                    nmv[:, 0:1], sb[:], AX.X, ALU.max, negate=True),
                    r=[sbn], w=[nmn])
                b.c("scalar", lambda e, sb=sb, pb=pb, nmv=nmv: e.activation(
                    out=pb[:], in_=sb[:], func=AF.Exp,
                    bias=nmv[:, 0:1], scale=1.0, accum_out=nmv[:, 1:2]),
                    r=[sbn, nmn], w=[pbn, nmn + "2"])
                b.c("vector", lambda e, nmv=nmv: e.reciprocal(nmv[:, 1:2], nmv[:, 1:2]),
                    r=[nmn + "2"], w=[nmn + "2"])
                b.c("vector", lambda e, pb=pb, nmv=nmv: e.tensor_scalar(
                    pb[:], pb[:], nmv[:, 1:2], None, ALU.mult),
                    r=[pbn, nmn + "2"], w=[pbn])
                for pc in range(4):
                    b.c("tensor", lambda e, pc=pc, pb=pb: e.matmul(
                        psT[:, pc * 128:(pc + 1) * 128],
                        pb[:, pc * 128:(pc + 1) * 128], idn[:],
                        start=(pc == 0), stop=False, is_transpose=True,
                        skip_group_check=True),
                        r=[pbn, "idn"] + (["psTr"] if pc == 0 else []), w=["psT"])
                b.c("vector", lambda e, pt_=pt_: e.tensor_copy(pt_[:, 0:512], psT[:, 0:512]),
                    r=["psT"], w=[ptn])
                b.c("tensor", lambda e, pb=pb: e.matmul(
                    psT[:, 0:128], pb[:, 512:640], idn[:],
                    start=True, stop=False, is_transpose=True, skip_group_check=True),
                    r=[pbn, "idn", ptn], w=["psT"])
                b.c("vector", lambda e, pt_=pt_: e.tensor_copy(pt_[:, 512:640], psT[:, 0:128]),
                    r=["psT"], w=[ptn, "psTr"])
                pair = c1
                pbase = 64 * half
                for pc in range(5):
                    vtile = qt + pc
                    b.c("tensor", lambda e, pc=pc, vtile=vtile, h=h, pair=pair, pbase=pbase, pt_=pt_: e.matmul(
                        psC[pbase:pbase + 64, pair * 128:(pair + 1) * 128],
                        vr[:, vtile * 512 + h * 64: vtile * 512 + (h + 1) * 64],
                        pt_[:, pc * 128:(pc + 1) * 128],
                        start=(pc == 0), stop=(pc == 4), skip_group_check=True),
                        r=[f"vr_{vtile}", ptn] + (["psCr"] if pc == 0 else []), w=["psC"])
            b.c("vector", lambda e: e.tensor_copy(aoT[:], psC[:]), r=["psC"], w=["aoT", "psCr"])
            for p2 in range(4):
                for o2 in range(2):
                    b.c("tensor", lambda e, p2=p2, o2=o2: e.matmul(
                        psA[:, o2 * 512:(o2 + 1) * 512],
                        aoT[:, p2 * 128:(p2 + 1) * 128],
                        woS[:, p2 * 1024 + o2 * 512: p2 * 1024 + (o2 + 1) * 512],
                        start=(p2 == 0), stop=(p2 == 3), skip_group_check=True),
                        r=["aoT", "woS"] + ([f"psA{o2}r"] if p2 == 0 else []), w=[f"psA{o2}"])
            b.c("vector", lambda e, qt=qt, xk=xk: e.tensor_tensor(
                xmid[:], psA[:], xk[:, qt * 1024:(qt + 1) * 1024], ALU.add),
                r=["psA0", "psA1", f"{xkn}_{qt}"], w=["xmid", "psA0r", "psA1r"])
            b.d("sync", o_xmid[wi * 512 + qt * 128: wi * 512 + (qt + 1) * 128, :], xmid[:],
                r=["xmid"], w=["xmidd"])

    emit_qkv_window(-1, 0)
    for wi in range(NW):
        emit_qkv_window(wi, (wi + 1) % 2)
        emit_attn_window(wi, (wi + 1) % 2)
    b.replay()
    return nc


def host_consts_p3a():
    cs = {}
    cs["c_idn"] = np.eye(128, dtype=np.float32)
    cs["c_ones"] = np.ones((1, 128), dtype=np.float32)
    seg = np.zeros((4, 128, 8), dtype=np.float32)
    hm = np.zeros((4, 8, 128), dtype=np.float32)
    for c in range(4):
        for p in range(128):
            h = 2 * c + p // 64
            seg[c, p, h] = 1.0
            hm[c, h, p] = 1.0
    cs["c_seg8"] = np.ascontiguousarray(seg.transpose(1, 0, 2).reshape(128, 32))
    cs["c_hm8"] = np.ascontiguousarray(hm.transpose(1, 0, 2).reshape(8, 512))
    rt = np.zeros((128, 128), dtype=np.float32)
    for blk in range(2):
        o = blk * 64
        for j in range(32):
            rt[o + j + 32, o + j] = -1.0
            rt[o + j, o + j + 32] = 1.0
    cs["c_rot"] = rt
    fbm = np.zeros((128, 640), dtype=np.float32)
    for p in range(128):
        for jj in range(640):
            if not (0 <= jj - p <= 512):
                fbm[p, jj] = -1e30
    cs["c_fb"] = fbm
    tglob = np.arange(-512, 8192, dtype=np.float64)
    half_idx = np.arange(0, 64, 2, dtype=np.float64)
    inv_freq = 1.0 / 10000.0 ** (half_idx / 64.0)
    freqs = tglob[:, None] * inv_freq[None, :]
    freqs = np.concatenate([freqs, freqs], axis=-1)
    base = (half_idx + 0.4 * 64) / (1.4 * 64)
    power = (tglob - 4096) / 256.0
    scale = base[None, :] ** power[:, None]
    scale = np.concatenate([scale, scale], axis=-1)
    cos, sin = np.cos(freqs), np.sin(freqs)
    CSq = (cos * scale).T
    SSq = (sin * scale).T
    CSk = (cos / scale).T
    SSk = (sin / scale).T
    tab = np.stack([CSq, SSq, CSk, SSk])
    tab = np.concatenate([tab, tab], axis=1)
    cs["tabglob"] = tab.astype(np.float32)
    return cs


def run_p3a(I, xfull, l, _cache={}):
    cs = host_consts_p3a()
    if "p3a" not in _cache:
        _cache["p3a"] = build_p3a()
    nc = _cache["p3a"]
    qs = I["L0_ffn_q_scale"][l]
    ks = I["L0_ffn_k_scale"][l]
    maps = []
    for c in range(8):
        bb, h = c // 2, c % 2
        s = h * NTOK
        if h == 0:
            xh = np.zeros((514, C), np.float32)
            xp = np.zeros((128, 8), np.float32)
            w0 = np.zeros((1, 1024), np.float32)
            w0[0, :512] = -1e30
        else:
            xh = np.ascontiguousarray(xfull[bb, s - 514:s])
            xp = np.ascontiguousarray(xfull[bb, s - 513].reshape(8, 128).T)
            w0 = np.zeros((1, 1024), np.float32)
        tsl = cs["tabglob"][:, :, s: s + 4608]
        tabsl = np.ascontiguousarray(tsl.transpose(1, 0, 2).reshape(128, 4 * 4608))
        maps.append({
            "x": np.ascontiguousarray(xfull[bb, s:s + NTOK]),
            "xhal": xh, "xprev": xp,
            "wqkv": I["L0_ffn_wqkv"][l], "wo": I["L0_ffn_wo"][l],
            "qsc": np.ascontiguousarray(np.concatenate([qs, qs]).reshape(128, 1)),
            "ksc": np.ascontiguousarray(np.concatenate([ks, ks]).reshape(128, 1)),
            "tabs": tabsl,
            "c_idn": cs["c_idn"], "c_ones": cs["c_ones"], "c_seg8": cs["c_seg8"],
            "c_hm8": cs["c_hm8"], "c_rot": cs["c_rot"], "c_fb": cs["c_fb"],
            "c_w0x": w0,
        })
    res = run_bass_kernel_spmd(nc, maps, core_ids=list(range(8)))
    return res.results



def build_p3b():
    nc = bass.Bass("TRN2", target_bir_lowering=False)
    xm = nc.dram_tensor("xm", [NTOK, C], F32, kind="ExternalInput").ap()
    xmprev = nc.dram_tensor("xmprev", [128, 8], F32, kind="ExternalInput").ap()
    w1a = nc.dram_tensor("w1a", [C, 2730], F32, kind="ExternalInput").ap()
    w1g = nc.dram_tensor("w1g", [C, 2730], F32, kind="ExternalInput").ap()
    w2 = nc.dram_tensor("w2", [2730, C], F32, kind="ExternalInput").ap()
    b1a = nc.dram_tensor("b1a", [1, 2730], F32, kind="ExternalInput").ap()
    b1g = nc.dram_tensor("b1g", [1, 2730], F32, kind="ExternalInput").ap()
    b2r = nc.dram_tensor("b2r", [1, C], F32, kind="ExternalInput").ap()
    lgc = nc.dram_tensor("lgc", [128, 8], F32, kind="ExternalInput").ap()
    lbc = nc.dram_tensor("lbc", [128, 8], F32, kind="ExternalInput").ap()
    c_idn = nc.dram_tensor("c_idn", [128, 128], F32, kind="ExternalInput").ap()
    c_ones = nc.dram_tensor("c_ones", [1, 512], F32, kind="ExternalInput").ap()
    c_onec = nc.dram_tensor("c_onec", [128, 1], F32, kind="ExternalInput").ap()
    o_f = nc.dram_tensor("o_f", [NTOK, C], F32, kind="ExternalOutput").ap()

    b = Builder(nc)
    A = nc.alloc_sbuf_tensor
    idn = A("idn", [128, 128], BF16)
    ones1 = A("ones1", [1, 512], BF16)
    ones1f = A("ones1f", [1, 128], F32)
    onescol = A("onescol", [128, 1], BF16)
    lgv = A("lgv", [128, 8], F32)
    lbv = A("lbv", [128, 8], F32)
    b1ar = A("b1ar", [1, 2730], BF16)
    b1gr = A("b1gr", [1, 2730], BF16)
    b2row = A("b2row", [1, C], BF16)
    w1aS = A("w1aS", [128, 8 * 2730], BF16)
    w1gS = A("w1gS", [128, 8 * 2730], BF16)
    w2S = A("w2S", [128, 22 * 1024], BF16)
    xw = A("xw", [128, 1024], F32)
    xwb = A("xwb", [128, 1024], BF16)
    xmT = A("xmT", [128, 8 * 640], BF16)
    xpv = A("xpv", [128, 8], BF16)
    lnT = A("lnT", [128, 8 * 512], BF16)
    hT = A("hT", [128, 22 * 512], BF16)
    t1b = [A(f"t1b{i}", [128, 512], BF16) for i in range(2)]
    rows = A("rows", [1, 3 * 512], F32)
    bcRs = A("bcRs", [128, 512], F32)
    t2f = A("t2f", [128, 512], F32)
    outf = A("outf", [128, 512], F32)

    P = nc.alloc_psum_tensor
    psT = P("psT", [128, 512], BF16)
    psS1 = P("psS1", [1, 512], F32)
    psS2 = P("psS2", [1, 512], F32)
    psBC = P("psBC", [128, 512], F32)
    psW = [P(f"psW{i}", [128, 512], F32) for i in range(2)]
    psO = P("psO", [128, 1024], F32)

    b.d("sync", idn[:], c_idn[:], w=["idn"])
    b.d("sync", ones1[:], c_ones[:], w=["ones1"])
    b.d("sync", ones1f[:], c_ones[:, 0:128], w=["ones1f"])
    b.d("sync", onescol[:], c_onec[:], w=["onescol"])
    b.d("sync", lgv[:], lgc[:], w=["lgv"])
    b.d("sync", lbv[:], lbc[:], w=["lbv"])
    b.d("gpsimd", b1ar[:], b1a[:], w=["b1ar"])
    b.d("gpsimd", b1gr[:], b1g[:], w=["b1gr"])
    b.d("gpsimd", b2row[:], b2r[:], w=["b2row"])
    b.d("gpsimd", xpv[:], xmprev[:], w=["xpv"])
    for f in range(8):
        b.d("gpsimd", w1aS[:, f * 2730:(f + 1) * 2730], w1a[ts(f), :], w=["w1aS"])
        b.d("gpsimd", w1gS[:, f * 2730:(f + 1) * 2730], w1g[ts(f), :], w=["w1gS"])
    for hc in range(22):
        r0, r1 = hc * 128, min(2730, (hc + 1) * 128)
        b.d("gpsimd", w2S[0:r1 - r0, hc * 1024:(hc + 1) * 1024], w2[r0:r1, :], w=["w2S"])

    for wi in range(NW):
        for c1 in range(8):
            b.c("vector", lambda e, c1=c1: e.tensor_copy(
                xmT[:, c1 * 640 + 127:c1 * 640 + 128],
                xmT[:, c1 * 640 + 639:c1 * 640 + 640] if wi > 0 else xpv[:, c1:c1 + 1]),
                r=([f"xmT_{c1}"] if wi > 0 else ["xpv"]), w=[f"xmT_{c1}"])
        for i in range(4):
            b.d("gpsimd", xwb[:], xm[wi * 512 + i * 128: wi * 512 + (i + 1) * 128, :],
                r=["xwbR"], w=["xwb"])
            for g4 in range(2):
                for jj in range(4):
                    c1 = g4 * 4 + jj
                    b.c("tensor", lambda e, c1=c1, jj=jj: e.matmul(
                        psT[:, jj * 128:(jj + 1) * 128], xwb[:, c1 * 128:(c1 + 1) * 128],
                        idn[:], start=(jj == 0), stop=(jj == 3), is_transpose=True,
                        skip_group_check=True),
                        r=["xwb", "idn"] + (["psTr"] if jj == 0 else []),
                        w=["psT"] + (["xwbR"] if c1 == 7 else []))
                for jj in range(4):
                    c1 = g4 * 4 + jj
                    b.c("vector", lambda e, i=i, c1=c1, jj=jj: e.tensor_copy(
                        xmT[:, c1 * 640 + 128 + i * 128: c1 * 640 + 128 + (i + 1) * 128],
                        psT[:, jj * 128:(jj + 1) * 128]),
                        r=["psT"], w=[f"xmT_{c1}"] + (["psTr"] if jj == 3 else []))
        xt_r = [f"xmT_{c1}" for c1 in range(8)]

        def hsT(c1):
            sh = 127 if c1 >= 4 else 128
            return xmT[:, c1 * 640 + sh: c1 * 640 + sh + 512]

        for c1 in range(8):
            b.c("tensor", lambda e, c1=c1: e.matmul(
                psS1[:], onescol[:], hsT(c1), start=(c1 == 0), stop=(c1 == 7)),
                r=["onescol", xt_r[c1]] + (["psS1r"] if c1 == 0 else []), w=["psS1"])
        for c1 in range(8):
            tb = t1b[c1 % 2]
            tbn = f"t1b{c1 % 2}"
            b.c("scalar", lambda e, c1=c1, tb=tb: e.activation(out=tb[:], in_=hsT(c1),
                                                               func=AF.Square),
                r=[xt_r[c1], tbn + "R"], w=[tbn])
            b.c("tensor", lambda e, c1=c1, tb=tb: e.matmul(
                psS2[:], onescol[:], tb[:], start=(c1 == 0), stop=(c1 == 7)),
                r=["onescol", tbn] + (["psS2r"] if c1 == 0 else []), w=["psS2", tbn + "R"])
        b.c("vector", lambda e: e.tensor_scalar(rows[:, 0:512], psS1[:], 1.0 / C, None, ALU.mult),
            r=["psS1"], w=["rows", "psS1r"])
        b.c("vector", lambda e: e.tensor_scalar(rows[:, 512:1024], psS2[:], 1.0 / C, None, ALU.mult),
            r=["psS2", "rows"], w=["rows", "psS2r"])
        b.c("vector", lambda e: e.tensor_tensor(rows[:, 1024:1536], rows[:, 0:512],
                                                rows[:, 0:512], ALU.mult),
            r=["rows"], w=["rowsd"])
        b.c("vector", lambda e: e.tensor_tensor(rows[:, 512:1024], rows[:, 512:1024],
                                                rows[:, 1024:1536], ALU.subtract),
            r=["rows", "rowsd"], w=["rows"])
        b.c("vector", lambda e: e.tensor_scalar(rows[:, 512:1024], rows[:, 512:1024],
                                                1e-5, None, ALU.add),
            r=["rows"], w=["rows"])
        b.c("scalar", lambda e: e.activation(out=rows[:, 1024:1536], in_=rows[:, 512:1024],
                                             func=AF.Sqrt),
            r=["rows"], w=["rowsd"])
        b.c("vector", lambda e: e.reciprocal(rows[:, 1024:1536], rows[:, 1024:1536]),
            r=["rowsd"], w=["rowsd"])
        b.c("tensor", lambda e: e.matmul(psBC[:], ones1f[:], rows[:, 0:512],
                                         start=True, stop=True),
            r=["ones1f", "rows", "psBCr"], w=["psBC"])
        b.c("tensor", lambda e: e.matmul(psW[0][:], ones1f[:], rows[:, 1024:1536],
                                         start=True, stop=True),
            r=["ones1f", "rowsd", "psW0r"], w=["psW0"])
        b.c("vector", lambda e: e.tensor_copy(bcRs[:], psW[0][:]), r=["psW0"],
            w=["bcRs", "psW0r"])
        for c1 in range(8):
            b.c("vector", lambda e, c1=c1: e.tensor_tensor(
                t2f[:], hsT(c1), psBC[:], ALU.subtract),
                r=[xt_r[c1], "psBC"], w=["t2f"])
            b.c("vector", lambda e, c1=c1: e.tensor_tensor(t2f[:], t2f[:], bcRs[:], ALU.mult),
                r=["t2f", "bcRs"], w=["t2f"])
            b.c("vector", lambda e, c1=c1: e.tensor_scalar(
                lnT[:, c1 * 512:(c1 + 1) * 512], t2f[:], lgv[:, c1:c1 + 1],
                lbv[:, c1:c1 + 1], ALU.mult, op1=ALU.add),
                r=["t2f", "lgv", "lbv"], w=[f"lnT_{c1}"])
        for oc in range(22):
            ow = min(2730, (oc + 1) * 128) - oc * 128
            for part in range(2):
                wS = w1aS if part == 0 else w1gS
                wSn = "w1aS" if part == 0 else "w1gS"
                brow = b1ar if part == 0 else b1gr
                brn = "b1ar" if part == 0 else "b1gr"
                pw, pwn = psW[part], f"psW{part}"
                for f in range(8):
                    b.c("tensor", lambda e, oc=oc, f=f, wS=wS, pw=pw, ow=ow: e.matmul(
                        pw[0:ow, :], wS[:, f * 2730 + oc * 128: f * 2730 + oc * 128 + ow],
                        lnT[:, f * 512:(f + 1) * 512],
                        start=(f == 0), stop=False),
                        r=[wSn, f"lnT_{f}"] + ([pwn + "r"] if f == 0 else []), w=[pwn])
                b.c("tensor", lambda e, oc=oc, brow=brow, pw=pw, ow=ow: e.matmul(
                    pw[0:ow, :], brow[:, oc * 128: oc * 128 + ow], ones1[:],
                    start=False, stop=True),
                    r=["ones1", brn], w=[pwn])
            tb = t1b[oc % 2]
            tbn = f"t1b{oc % 2}"
            b.c("scalar", lambda e, ow=ow, tb=tb: e.activation(out=tb[0:ow, :], in_=psW[1][0:ow, :],
                                                               func=AF.Gelu),
                r=["psW1", tbn + "R"], w=[tbn, "psW1r"])
            b.c("vector", lambda e, oc=oc, ow=ow, tb=tb: e.tensor_tensor(
                hT[0:ow, oc * 512:(oc + 1) * 512], psW[0][0:ow, :], tb[0:ow, :], ALU.mult),
                r=["psW0", tbn], w=[f"hT_{oc}", "psW0r", tbn + "R"])
        for i in range(4):
            for hc in range(22):
                hw = min(2730, (hc + 1) * 128) - hc * 128
                for o2 in range(2):
                    b.c("tensor", lambda e, i=i, hc=hc, hw=hw, o2=o2: e.matmul(
                        psO[:, o2 * 512:(o2 + 1) * 512],
                        hT[0:hw, hc * 512 + i * 128: hc * 512 + (i + 1) * 128],
                        w2S[0:hw, hc * 1024 + o2 * 512: hc * 1024 + (o2 + 1) * 512],
                        start=(hc == 0), stop=False, skip_group_check=True),
                        r=[f"hT_{hc}", "w2S"] + (["psOr"] if hc == 0 and o2 == 0 else []),
                        w=["psO"])
            for o2 in range(2):
                b.c("tensor", lambda e, o2=o2: e.matmul(
                    psO[:, o2 * 512:(o2 + 1) * 512], ones1[:, 0:128],
                    b2row[:, o2 * 512:(o2 + 1) * 512],
                    start=False, stop=True, skip_group_check=True),
                    r=["ones1", "b2row"], w=["psO"])
            b.d("sync", xw[:], xm[wi * 512 + i * 128: wi * 512 + (i + 1) * 128, :],
                r=["xwR"], w=["xw"])
            for o2 in range(2):
                b.c("vector", lambda e, i=i, o2=o2: e.tensor_tensor(
                    outf[:], psO[:, o2 * 512:(o2 + 1) * 512],
                    xw[:, o2 * 512:(o2 + 1) * 512], ALU.add),
                    r=["psO", "xw", "outfd"], w=["outf"] + (["psOr", "xwR"] if o2 == 1 else []))
                b.d("sync", o_f[wi * 512 + i * 128: wi * 512 + (i + 1) * 128,
                                o2 * 512:(o2 + 1) * 512], outf[:],
                    r=["outf"], w=["outfd"])
    b.replay()
    return nc


def run_p3b(I, xmid_res, l, _cache={}):
    cs = host_consts()
    if "p3b" not in _cache:
        _cache["p3b"] = build_p3b()
    nc = _cache["p3b"]
    w1 = I["L0_ffn_w1"][l]
    maps = []
    for c in range(8):
        bb, h = c // 2, c % 2
        xm = np.asarray(xmid_res[c]["o_xmid"])
        if h == 0:
            xp = np.zeros((128, 8), np.float32)
        else:
            peer = np.asarray(xmid_res[bb * 2]["o_xmid"])
            xp = np.ascontiguousarray(peer[NTOK - 1].reshape(8, 128).T)
        maps.append({
            "xm": xm, "xmprev": xp,
            "w1a": np.ascontiguousarray(w1[:, :2730]),
            "w1g": np.ascontiguousarray(w1[:, 2730:]),
            "w2": I["L0_ffn_w2"][l],
            "b1a": np.ascontiguousarray(I["L0_ffn_b1"][l][:2730].reshape(1, 2730)),
            "b1g": np.ascontiguousarray(I["L0_ffn_b1"][l][2730:].reshape(1, 2730)),
            "b2r": I["L0_ffn_b2"][l].reshape(1, C),
            "lgc": np.ascontiguousarray(I["L0_ffn_ln_g"][l].reshape(8, 128).T),
            "lbc": np.ascontiguousarray(I["L0_ffn_ln_b"][l].reshape(8, 128).T),
            "c_idn": cs["c_idn"], "c_ones": np.ones((1, 512), np.float32),
            "c_onec": np.ones((128, 1), np.float32),
        })
    res = run_bass_kernel_spmd(nc, maps, core_ids=list(range(8)))
    return res.results



def softplus_chain(b, psrc, psrcn, dst, dstn, tmpa, tmpb, onecol):
    b.c("scalar", lambda e: e.activation(out=tmpa[:], in_=psrc[:], func=AF.Abs),
        r=[psrcn], w=[tmpa.name])
    b.c("scalar", lambda e: e.activation(out=tmpa[:], in_=tmpa[:], func=AF.Exp, scale=-1.0),
        r=[tmpa.name], w=[tmpa.name])
    b.c("scalar", lambda e: e.activation(out=tmpa[:], in_=tmpa[:], func=AF.Ln,
                                         bias=onecol[:, 0:1], scale=1.0),
        r=[tmpa.name, "onecolf"], w=[tmpa.name])
    b.c("vector", lambda e: e.tensor_scalar(tmpb[:], psrc[:], 0.0, None, ALU.max),
        r=[psrcn], w=[tmpb.name])
    b.c("vector", lambda e: e.tensor_tensor(dst[:], tmpa[:], tmpb[:], ALU.add),
        r=[tmpa.name, tmpb.name], w=[dstn])


def build_p6():
    nc = bass.Bass("TRN2", target_bir_lowering=False)
    g1d = nc.dram_tensor("g1d", [NTOK, C], F32, kind="ExternalInput").ap()
    f2d = nc.dram_tensor("f2d", [NTOK, C], F32, kind="ExternalInput").ap()
    Wq = nc.dram_tensor("Wq", [C, C], F32, kind="ExternalInput").ap()
    Wk = nc.dram_tensor("Wk", [C, C], F32, kind="ExternalInput").ap()
    Wv = nc.dram_tensor("Wv", [C, C], F32, kind="ExternalInput").ap()
    Wg = nc.dram_tensor("Wg", [C, C], F32, kind="ExternalInput").ap()
    Wp = nc.dram_tensor("Wp", [C, C], F32, kind="ExternalInput").ap()
    Wc = nc.dram_tensor("Wc", [C, 1], F32, kind="ExternalInput").ap()
    gbr = nc.dram_tensor("gbr", [1, C], F32, kind="ExternalInput").ap()
    pbr = nc.dram_tensor("pbr", [1, C], F32, kind="ExternalInput").ap()
    cbc = nc.dram_tensor("cbc", [128, 1], F32, kind="ExternalInput").ap()
    lngr = nc.dram_tensor("lngr", [1, C], F32, kind="ExternalInput").ap()
    lnbr = nc.dram_tensor("lnbr", [1, C], F32, kind="ExternalInput").ap()
    ropeC = nc.dram_tensor("ropeC", [NTOK, C], F32, kind="ExternalInput").ap()
    ropeS = nc.dram_tensor("ropeS", [NTOK, C], F32, kind="ExternalInput").ap()
    c_idn = nc.dram_tensor("c_idn", [128, 128], F32, kind="ExternalInput").ap()
    c_ones = nc.dram_tensor("c_ones", [1, 512], F32, kind="ExternalInput").ap()
    c_onec = nc.dram_tensor("c_onec", [128, 1], F32, kind="ExternalInput").ap()
    c_utri = nc.dram_tensor("c_utri", [128, 128], F32, kind="ExternalInput").ap()
    o_x3 = nc.dram_tensor("o_x3", [NTOK, C], F32, kind="ExternalOutput").ap()
    o_q = nc.dram_tensor("o_q", [NTOK, C], BF16, kind="ExternalOutput").ap()
    o_k = nc.dram_tensor("o_k", [NTOK, C], BF16, kind="ExternalOutput").ap()
    o_v = nc.dram_tensor("o_v", [NTOK, C], BF16, kind="ExternalOutput").ap()
    o_gcs = nc.dram_tensor("o_gcs", [NTOK, C], F32, kind="ExternalOutput").ap()
    o_gtot = nc.dram_tensor("o_gtot", [1, C], F32, kind="ExternalOutput").ap()
    o_p = nc.dram_tensor("o_p", [NTOK, C], F32, kind="ExternalOutput").ap()
    o_pmax = nc.dram_tensor("o_pmax", [128, C], F32, kind="ExternalOutput").ap()
    o_clock = nc.dram_tensor("o_clock", [128, NT], F32, kind="ExternalOutput").ap()

    b = Builder(nc)
    A = nc.alloc_sbuf_tensor
    idn = A("idn", [128, 128], BF16)
    ones1f = A("ones1f", [1, 128], F32)
    onecf = A("onecolf", [128, 1], F32)
    utriF = A("utriF", [128, 128], F32)
    cbv = A("cbv", [128, 1], F32)
    WqS = A("WqS", [128, 8 * 1024], BF16)
    WkS = A("WkS", [128, 8 * 1024], BF16)
    WvS = A("WvS", [128, 8 * 1024], BF16)
    WgS = A("WgS", [128, 8 * 1024], BF16)
    WpS = A("WpS", [128, 8 * 1024], BF16)
    WcS = A("WcS", [128, 8], BF16)
    gbrow = A("gbrow", [1, C], BF16)
    pbrow = A("pbrow", [1, C], BF16)
    glnB = A("glnB", [128, 1024], F32)
    blnB = A("blnB", [128, 1024], F32)
    xa = A("xa", [128, 1024], F32)
    xn = A("xnb", [128, 1024], F32)
    xnb16 = A("xnb16", [128, 1024], BF16)
    xnT = A("xnT", [128, 8 * 128], BF16)
    scr = A("scr", [128, 1024], F32)
    mn = A("mn", [128, 8], F32)
    tq = A("tq", [128, 1024], F32)
    tq2 = A("tq2", [128, 1024], F32)
    tsw = A("tsw", [128, 1024], F32)
    rc = [A(f"rc{i}", [128, 1024], F32) for i in range(2)]
    rs = [A(f"rs{i}", [128, 1024], F32) for i in range(2)]
    carry = A("carry", [1, 1024], F32)
    pmaxs = A("pmaxs", [128, 1024], F32)
    clocks = A("clocks", [128, NT], F32)
    outb = A("outb", [128, 1024], BF16)

    P = nc.alloc_psum_tensor
    psQ = [P(f"psQ{i}", [128, 1024], F32) for i in range(2)]
    psT = P("psT", [128, 512], BF16)
    psTot = P("psTot", [1, 1024], F32)
    psCk = P("psCk", [128, 1], F32)

    b.d("sync", idn[:], c_idn[:], w=["idn"])
    b.d("sync", ones1f[:], c_ones[:, 0:128], w=["ones1f"])
    b.d("sync", onecf[:], c_onec[:], w=["onecolf"])
    b.d("sync", utriF[:], c_utri[:], w=["utriF"])
    b.d("sync", cbv[:], cbc[:], w=["cbv"])
    b.d("gpsimd", gbrow[:], gbr[:], w=["gbrow"])
    b.d("gpsimd", pbrow[:], pbr[:], w=["pbrow"])
    for (W, WS, nm) in ((Wq, WqS, "WqS"), (Wk, WkS, "WkS"), (Wv, WvS, "WvS"),
                        (Wg, WgS, "WgS"), (Wp, WpS, "WpS")):
        for f in range(8):
            b.d("gpsimd", WS[:, f * 1024:(f + 1) * 1024], W[ts(f), :], w=[nm])
    b.d("gpsimd", WcS[:], Wc.rearrange("(f p) a -> p (f a)", p=128), w=["WcS"],
        allow_slow_non_contiguous=True)
    lngs = A("lngs", [1, C], F32)
    lnbs = A("lnbs", [1, C], F32)
    b.d("sync", lngs[:], lngr[:], w=["lngs"])
    b.d("sync", lnbs[:], lnbr[:], w=["lnbs"])
    for o2 in range(2):
        b.c("tensor", lambda e, o2=o2: e.matmul(
            psQ[0][:, o2 * 512:(o2 + 1) * 512], ones1f[:], lngs[:, o2 * 512:(o2 + 1) * 512],
            start=True, stop=True),
            r=["ones1f", "lngs"] + (["psQ0r"] if o2 == 0 else []), w=["psQ0"])
    b.c("vector", lambda e: e.tensor_copy(glnB[:], psQ[0][:]), r=["psQ0"], w=["glnB", "psQ0r"])
    for o2 in range(2):
        b.c("tensor", lambda e, o2=o2: e.matmul(
            psQ[0][:, o2 * 512:(o2 + 1) * 512], ones1f[:], lnbs[:, o2 * 512:(o2 + 1) * 512],
            start=True, stop=True),
            r=["ones1f", "lnbs"] + (["psQ0r"] if o2 == 0 else []), w=["psQ0"])
    b.c("vector", lambda e: e.tensor_copy(blnB[:], psQ[0][:]), r=["psQ0"], w=["blnB", "psQ0r"])

    for ti in range(NT):
        b.d("sync", xa[:], g1d[ts(ti), :], r=["xaR"], w=["xa"])
        b.d("gpsimd", scr[:], f2d[ts(ti), :], r=["scrR"], w=["scr"])
        b.c("vector", lambda e: e.tensor_tensor(xa[:], xa[:], scr[:], ALU.add),
            r=["xa", "scr"], w=["xa", "scrR"])
        ln_token_tile(b, xa, xa, mn, scr, "p6ln", out_res=["xa", "xaR"])
        b.d("sync", o_x3[ts(ti), :], xa[:], r=["xa"], w=["x3d"])
        ln_token_tile(b, xa, xn, mn, scr, "p6ln", out_res=["xnb"])
        b.c("vector", lambda e: e.tensor_tensor(xn[:], xn[:], glnB[:], ALU.mult),
            r=["xnb", "glnB"], w=["xnb"])
        b.c("vector", lambda e: e.tensor_tensor(xn[:], xn[:], blnB[:], ALU.add),
            r=["xnb", "blnB"], w=["xnb"])
        b.c("vector", lambda e: e.tensor_copy(xnb16[:], xn[:]), r=["xnb"], w=["xnb16"])
        for g4 in range(2):
            for jj in range(4):
                c1 = g4 * 4 + jj
                b.c("tensor", lambda e, c1=c1, jj=jj: e.matmul(
                    psT[:, jj * 128:(jj + 1) * 128], xnb16[:, c1 * 128:(c1 + 1) * 128],
                    idn[:], start=(jj == 0), stop=(jj == 3), is_transpose=True,
                    skip_group_check=True),
                    r=["xnb16", "idn"] + (["psTr"] if jj == 0 else []), w=["psT"])
            b.c("vector", lambda e, g4=g4: e.tensor_copy(
                xnT[:, g4 * 512:(g4 + 1) * 512], psT[:]),
                r=["psT"], w=["xnT", "psTr"])
        b.d("sync", rc[ti % 2][:], ropeC[ts(ti), :], r=[f"rc{ti % 2}R"], w=[f"rc{ti % 2}"])
        b.d("sync", rs[ti % 2][:], ropeS[ts(ti), :], r=[f"rs{ti % 2}R"], w=[f"rs{ti % 2}"])

        def proj(WS, wsn, pq, pqn, bias_row=None, brn=None):
            for f in range(8):
                for o2 in range(2):
                    b.c("tensor", lambda e, f=f, o2=o2, WS=WS, pq=pq: e.matmul(
                        pq[:, o2 * 512:(o2 + 1) * 512],
                        xnT[:, f * 128:(f + 1) * 128],
                        WS[:, f * 1024 + o2 * 512: f * 1024 + (o2 + 1) * 512],
                        start=(f == 0), stop=(f == 7 and bias_row is None),
                        skip_group_check=True),
                        r=["xnT", wsn] + ([pqn + "r"] if f == 0 and o2 == 0 else []),
                        w=[pqn])
            if bias_row is not None:
                for o2 in range(2):
                    b.c("tensor", lambda e, o2=o2, bias_row=bias_row: e.matmul(
                        pq[:, o2 * 512:(o2 + 1) * 512], b.ones1b[:],
                        bias_row[:, o2 * 512:(o2 + 1) * 512],
                        start=False, stop=True, skip_group_check=True),
                        r=["ones1b", brn], w=[pqn])

        def l2rope(pq, pqn, dst_dram, name):
            b.c("vector", lambda e, pq=pq: e.tensor_copy(tq[:], pq[:]),
                r=[pqn], w=["tq", pqn + "r"])
            b.c("scalar", lambda e: e.activation(out=scr[:], in_=tq[:], func=AF.Square),
                r=["tq", "scrR"], w=["scr"])
            b.c("vector", lambda e: e.tensor_reduce(mn[:, 4:5], scr[:], AX.X, ALU.add),
                r=["scr"], w=["mnl", "scrR"])
            b.c("scalar", lambda e: e.activation(out=mn[:, 5:6], in_=mn[:, 4:5], func=AF.Sqrt),
                r=["mnl"], w=["mnl2"])
            b.c("vector", lambda e: e.tensor_scalar(mn[:, 5:6], mn[:, 5:6], 1e-12, None, ALU.max),
                r=["mnl2"], w=["mnl2"])
            b.c("vector", lambda e: e.reciprocal(mn[:, 5:6], mn[:, 5:6]),
                r=["mnl2"], w=["mnl2"])
            b.c("vector", lambda e: e.tensor_scalar(tq[:], tq[:], mn[:, 5:6], None, ALU.mult),
                r=["tq", "mnl2"], w=["tq"])
            b.c("vector", lambda e: e.tensor_copy(
                bass.AP(tsw, 0, [[1024, 128], [2, 512]]),
                bass.AP(tq, 1, [[1024, 128], [2, 512]])),
                r=["tq"], w=["tsw"])
            b.c("vector", lambda e: e.tensor_copy(
                bass.AP(tsw, 1, [[1024, 128], [2, 512]]),
                bass.AP(tq, 0, [[1024, 128], [2, 512]])),
                r=["tq"], w=["tsw"])
            b.c("vector", lambda e, ti=ti: e.tensor_tensor(tq[:], tq[:], rc[ti % 2][:], ALU.mult),
                r=["tq", f"rc{ti % 2}"], w=["tq"] + ([f"rc{ti % 2}R"] if name == "k" else []))
            b.c("vector", lambda e, ti=ti: e.tensor_tensor(tsw[:], tsw[:], rs[ti % 2][:], ALU.mult),
                r=["tsw", f"rs{ti % 2}"], w=["tsw"] + ([f"rs{ti % 2}R"] if name == "k" else []))
            b.c("vector", lambda e: e.tensor_tensor(outb[:], tq[:], tsw[:], ALU.add),
                r=["tq", "tsw"], w=["outb"])
            b.d("sync", dst_dram[ts(ti), :], outb[:], r=["outb"], w=["outbd"])

        b.ones1b = getattr(b, "ones1b_t", None)
        if b.ones1b is None:
            b.ones1b_t = b.ones1b = A("ones1b", [1, 128], BF16)
            b.d("sync", b.ones1b[:], c_ones[:, 0:128], w=["ones1b"])

        proj(WqS, "WqS", psQ[0], "psQ0")
        l2rope(psQ[0], "psQ0", o_q, "q")
        proj(WkS, "WkS", psQ[1], "psQ1")
        l2rope(psQ[1], "psQ1", o_k, "k")
        proj(WvS, "WvS", psQ[0], "psQ0")
        b.c("vector", lambda e: e.tensor_copy(outb[:], psQ[0][:]),
            r=["psQ0", "outbd"], w=["outb", "psQ0r"])
        b.d("sync", o_v[ts(ti), :], outb[:], r=["outb"], w=["outbd"])
        proj(WgS, "WgS", psQ[1], "psQ1", bias_row=gbrow, brn="gbrow")
        softplus_chain(b, psQ[1], "psQ1", tq, "tq", tq2, tsw, onecf)
        b.c("vector", lambda e: e.tensor_copy(scr[:, 0:1], scr[:, 0:1]),
            r=["tq"], w=["psQ1r", "scrR"])
        for o2 in range(2):
            b.c("tensor", lambda e, o2=o2: e.matmul(
                psQ[0][:, o2 * 512:(o2 + 1) * 512], utriF[:], tq[:, o2 * 512:(o2 + 1) * 512],
                start=True, stop=False, skip_group_check=True),
                r=["utriF", "tq"] + (["psQ0r"] if o2 == 0 else []), w=["psQ0"])
            if ti > 0:
                b.c("tensor", lambda e, o2=o2: e.matmul(
                    psQ[0][:, o2 * 512:(o2 + 1) * 512], ones1f[:],
                    carry[:, o2 * 512:(o2 + 1) * 512],
                    start=False, stop=True, skip_group_check=True),
                    r=["ones1f", "carry"], w=["psQ0"])
        for o2 in range(2):
            b.c("tensor", lambda e, o2=o2, ti=ti: e.matmul(
                psTot[:, o2 * 512:(o2 + 1) * 512], onecf[:],
                tq[:, o2 * 512:(o2 + 1) * 512],
                start=(ti == 0), stop=(ti == NT - 1), skip_group_check=True),
                r=["onecolf", "tq"] + (["psTotr"] if ti == 0 and o2 == 0 else []),
                w=["psTot"])
        b.c("vector", lambda e: e.tensor_copy(carry[:], psTot[:]),
            r=["psTot"], w=["carry"])
        b.c("vector", lambda e: e.tensor_copy(scr[:], psQ[0][:]),
            r=["psQ0", "scrR"], w=["scr", "psQ0r"])
        b.d("sync", o_gcs[ts(ti), :], scr[:], r=["scr"], w=["scrR"])
        proj(WpS, "WpS", psQ[1], "psQ1", bias_row=pbrow, brn="pbrow")
        b.c("vector", lambda e: e.tensor_copy(tq[:], psQ[1][:]),
            r=["psQ1"], w=["tq", "psQ1r"])
        b.d("sync", o_p[ts(ti), :], tq[:], r=["tq"], w=["tqR"])
        if ti == 0:
            b.c("vector", lambda e: e.tensor_copy(pmaxs[:], tq[:]), r=["tq"], w=["pmaxs"])
        else:
            b.c("vector", lambda e: e.tensor_tensor(pmaxs[:], pmaxs[:], tq[:], ALU.max),
                r=["tq", "pmaxs"], w=["pmaxs"])
        for f in range(8):
            b.c("tensor", lambda e, f=f: e.matmul(
                psCk[:], xnT[:, f * 128:(f + 1) * 128], WcS[:, f:f + 1],
                start=(f == 0), stop=(f == 7)),
                r=["xnT", "WcS"] + (["psCkr"] if f == 0 else []), w=["psCk"])
        b.c("vector", lambda e, ti=ti: e.tensor_scalar(mn[:, 6:7], psCk[:], cbv[:], None, ALU.add),
            r=["psCk", "cbv"], w=["mnck", "psCkr"])
        b.c("scalar", lambda e: e.activation(out=mn[:, 7:8], in_=mn[:, 6:7], func=AF.Abs),
            r=["mnck"], w=["mnck2"])
        b.c("scalar", lambda e: e.activation(out=mn[:, 7:8], in_=mn[:, 7:8], func=AF.Exp, scale=-1.0),
            r=["mnck2"], w=["mnck2"])
        b.c("scalar", lambda e: e.activation(out=mn[:, 7:8], in_=mn[:, 7:8], func=AF.Ln,
                                             bias=onecf[:, 0:1], scale=1.0),
            r=["mnck2", "onecolf"], w=["mnck2"])
        b.c("vector", lambda e: e.tensor_scalar(mn[:, 6:7], mn[:, 6:7], 0.0, None, ALU.max),
            r=["mnck"], w=["mnck"])
        b.c("vector", lambda e, ti=ti: e.tensor_tensor(clocks[:, ti:ti + 1], mn[:, 6:7],
                                                       mn[:, 7:8], ALU.add),
            r=["mnck", "mnck2"], w=["clocks"])
    b.d("sync", o_gtot[:], carry[:], r=["carry"])
    b.d("sync", o_pmax[:], pmaxs[:], r=["pmaxs"])
    b.d("sync", o_clock[:], clocks[:], r=["clocks"])
    b.replay()
    return nc


def host_p6_tables():
    half = 512
    freqs = np.exp(-np.arange(half, dtype=np.float64) * (np.log(10000.0) / half))
    t = np.arange(T, dtype=np.float64)
    th = t[:, None] * freqs[None, :]
    c = np.cos(th)
    s = np.sin(th)
    ropeC = np.repeat(c, 2, axis=1).astype(np.float32)
    ropeS = np.empty((T, 1024), np.float64)
    ropeS[:, 0::2] = -s
    ropeS[:, 1::2] = s
    ut = np.triu(np.ones((128, 128), dtype=np.float32))
    return ropeC, ropeS.astype(np.float32), ut


def run_p6(I, g1full, f2full, _cache={}):
    cs = host_consts()
    ropeC, ropeS, ut = host_p6_tables()
    if "p6" not in _cache:
        _cache["p6"] = build_p6()
    nc = _cache["p6"]
    maps = []
    for c in range(8):
        bb, h = c // 2, c % 2
        s = h * NTOK
        maps.append({
            "g1d": np.ascontiguousarray(g1full[bb, s:s + NTOK]),
            "f2d": np.ascontiguousarray(f2full[bb, s:s + NTOK]),
            "Wq": I["L1_attn_Wq"], "Wk": I["L1_attn_Wk"], "Wv": I["L1_attn_Wv"],
            "Wg": I["L1_attn_gate_W"], "Wp": I["L1_attn_p_W"],
            "Wc": I["L1_attn_clock_W"],
            "gbr": I["L1_attn_gate_b"].reshape(1, C),
            "pbr": I["L1_attn_p_b"].reshape(1, C),
            "cbc": np.ascontiguousarray(np.broadcast_to(I["L1_attn_clock_b"].reshape(1, 1), (128, 1))),
            "lngr": I["L1_attn_ln1_g"].reshape(1, C),
            "lnbr": I["L1_attn_ln1_b"].reshape(1, C),
            "ropeC": np.ascontiguousarray(ropeC[s:s + NTOK]),
            "ropeS": np.ascontiguousarray(ropeS[s:s + NTOK]),
            "c_idn": cs["c_idn"], "c_ones": np.ones((1, 512), np.float32),
            "c_onec": np.ones((128, 1), np.float32), "c_utri": ut,
        })
    res = run_bass_kernel_spmd(nc, maps, core_ids=list(range(8)))
    return res.results



def build_p7():
    nc = bass.Bass("TRN2", target_bir_lowering=False)
    pd = nc.dram_tensor("pd", [NTOK, C], F32, kind="ExternalInput").ap()
    pm0 = nc.dram_tensor("pm0", [128, C], F32, kind="ExternalInput").ap()
    pm1 = nc.dram_tensor("pm1", [128, C], F32, kind="ExternalInput").ap()
    clk = nc.dram_tensor("clk", [128, NT], F32, kind="ExternalInput").ap()
    c_idn = nc.dram_tensor("c_idn", [128, 128], F32, kind="ExternalInput").ap()
    c_ones = nc.dram_tensor("c_ones", [1, 512], F32, kind="ExternalInput").ap()
    c_onec = nc.dram_tensor("c_onec", [128, 1], F32, kind="ExternalInput").ap()
    c_utri = nc.dram_tensor("c_utri", [128, 128], F32, kind="ExternalInput").ap()
    sc_pm = nc.dram_tensor("sc_pm", [8, 128], F32, kind="Internal").ap()
    o_pexp = nc.dram_tensor("o_pexp", [NTOK, C], F32, kind="ExternalOutput").ap()
    o_pcs = nc.dram_tensor("o_pcs", [NTOK, C], F32, kind="ExternalOutput").ap()
    o_ptot = nc.dram_tensor("o_ptot", [1, C], F32, kind="ExternalOutput").ap()

    b = Builder(nc)
    A = nc.alloc_sbuf_tensor
    idnf = A("idnf", [128, 128], F32)
    ones1f = A("ones1f", [1, 128], F32)
    onecf = A("onecolf", [128, 1], F32)
    utriF = A("utriF", [128, 128], F32)
    clks = A("clks", [128, NT], F32)
    pma = A("pma", [128, 1024], F32)
    pmb = A("pmb", [128, 1024], F32)
    pmpp = A("pmpp", [128, 8], F32)
    pmrow = A("pmrow", [1, 1024], F32)
    pmB = A("pmB", [128, 1024], F32)
    pt = [A(f"pt{i}", [128, 1024], F32) for i in range(2)]
    tq = A("tq", [128, 1024], F32)
    carry = A("carry", [1, 1024], F32)
    outf = A("outf", [128, 1024], F32)

    P = nc.alloc_psum_tensor
    psQ = [P(f"psQ{i}", [128, 1024], F32) for i in range(2)]
    psT = P("psT", [128, 128], F32)
    psTot = P("psTot", [1, 1024], F32)

    b.d("sync", idnf[:], c_idn[:], w=["idnf"])
    b.d("sync", ones1f[:], c_ones[:, 0:128], w=["ones1f"])
    b.d("sync", onecf[:], c_onec[:], w=["onecolf"])
    b.d("sync", utriF[:], c_utri[:], w=["utriF"])
    b.d("sync", clks[:], clk[:], w=["clks"])
    b.d("sync", pma[:], pm0[:], w=["pma"])
    b.d("sync", pmb[:], pm1[:], w=["pmb"])
    b.c("vector", lambda e: e.tensor_tensor(pma[:], pma[:], pmb[:], ALU.max),
        r=["pma", "pmb"], w=["pma"])
    for c1 in range(8):
        b.c("tensor", lambda e, c1=c1: e.transpose(
            psT[:], pma[:, c1 * 128:(c1 + 1) * 128], idnf[:]),
            r=["pma", "idnf", "psTr"], w=["psT"])
        b.c("vector", lambda e: e.tensor_copy(pmb[:, 0:128], psT[:]),
            r=["psT"], w=["pmb2", "psTr"])
        b.c("vector", lambda e, c1=c1: e.tensor_reduce(
            pmpp[:, c1:c1 + 1], pmb[:, 0:128], AX.X, ALU.max),
            r=["pmb2"], w=["pmpp"])
        b.d("sync", sc_pm[c1:c1 + 1, :], pmpp[:, c1:c1 + 1], r=["pmpp"], w=["scpm"])
    b.d("sync", pmrow[:], sc_pm.rearrange("c p -> (c p)"), r=["scpm"], w=["pmrow"])
    for o2 in range(2):
        b.c("tensor", lambda e, o2=o2: e.matmul(
            psQ[0][:, o2 * 512:(o2 + 1) * 512], ones1f[:], pmrow[:, o2 * 512:(o2 + 1) * 512],
            start=True, stop=True),
            r=["ones1f", "pmrow"] + (["psQ0r"] if o2 == 0 else []), w=["psQ0"])
    b.c("vector", lambda e: e.tensor_copy(pmB[:], psQ[0][:]), r=["psQ0"], w=["pmB", "psQ0r"])

    for ti in range(NT):
        tb = ti % 2
        b.d("sync", pt[tb][:], pd[ts(ti), :], r=[f"pt{tb}R"], w=[f"pt{tb}"])
        b.c("vector", lambda e, tb=tb: e.tensor_tensor(tq[:], pt[tb][:], pmB[:], ALU.subtract),
            r=[f"pt{tb}", "pmB"], w=["tq", f"pt{tb}R"])
        b.c("scalar", lambda e: e.activation(out=tq[:], in_=tq[:], func=AF.Exp),
            r=["tq"], w=["tq"])
        b.c("vector", lambda e, ti=ti: e.tensor_scalar(tq[:], tq[:], clks[:, ti:ti + 1],
                                                       None, ALU.mult),
            r=["tq", "clks"], w=["tq"])
        b.d("sync", o_pexp[ts(ti), :], tq[:], r=["tq"], w=["tqD"])
        for o2 in range(2):
            b.c("tensor", lambda e, o2=o2: e.matmul(
                psQ[0][:, o2 * 512:(o2 + 1) * 512], utriF[:], tq[:, o2 * 512:(o2 + 1) * 512],
                start=True, stop=False, skip_group_check=True),
                r=["utriF", "tq"] + (["psQ0r"] if o2 == 0 else []), w=["psQ0"])
            if ti > 0:
                b.c("tensor", lambda e, o2=o2: e.matmul(
                    psQ[0][:, o2 * 512:(o2 + 1) * 512], ones1f[:],
                    carry[:, o2 * 512:(o2 + 1) * 512],
                    start=False, stop=True, skip_group_check=True),
                    r=["ones1f", "carry"], w=["psQ0"])
        for o2 in range(2):
            b.c("tensor", lambda e, o2=o2, ti=ti: e.matmul(
                psTot[:, o2 * 512:(o2 + 1) * 512], onecf[:], tq[:, o2 * 512:(o2 + 1) * 512],
                start=(ti == 0), stop=(ti == NT - 1), skip_group_check=True),
                r=["onecolf", "tq"] + (["psTotr"] if ti == 0 and o2 == 0 else []),
                w=["psTot"])
        b.c("vector", lambda e: e.tensor_copy(carry[:], psTot[:]), r=["psTot"], w=["carry"])
        b.c("vector", lambda e: e.tensor_copy(outf[:], psQ[0][:]),
            r=["psQ0"], w=["outf", "psQ0r"])
        b.d("sync", o_pcs[ts(ti), :], outf[:], r=["outf"], w=["outfD"])
    b.d("sync", o_ptot[:], carry[:], r=["carry"])
    b.replay()
    return nc



def build_p8():
    nc = bass.Bass("TRN2", target_bir_lowering=False)
    qd = nc.dram_tensor("qd", [NTOK, C], BF16, kind="ExternalInput").ap()
    kd = nc.dram_tensor("kd", [NTOK, C], BF16, kind="ExternalInput").ap()
    gcs = nc.dram_tensor("gcs", [NTOK, C], F32, kind="ExternalInput").ap()
    pexp = nc.dram_tensor("pexp", [NTOK, C], F32, kind="ExternalInput").ap()
    pcs = nc.dram_tensor("pcs", [NTOK, C], F32, kind="ExternalInput").ap()
    gtp = nc.dram_tensor("gtp", [1, C], F32, kind="ExternalInput").ap()
    ptp = nc.dram_tensor("ptp", [1, C], F32, kind="ExternalInput").ap()
    c_ones = nc.dram_tensor("c_ones", [1, 512], F32, kind="ExternalInput").ap()
    c_ln32 = nc.dram_tensor("c_ln32", [128, 1], F32, kind="ExternalInput").ap()
    o_qp = nc.dram_tensor("o_qp", [NTOK, C], BF16, kind="ExternalOutput").ap()
    o_kp = nc.dram_tensor("o_kp", [NTOK, C], BF16, kind="ExternalOutput").ap()

    b = Builder(nc)
    A = nc.alloc_sbuf_tensor
    ones1f = A("ones1f", [1, 128], F32)
    ln32 = A("ln32", [128, 1], F32)
    gtr = A("gtr", [1, 1024], F32)
    ptr = A("ptr", [1, 1024], F32)
    GB = A("GB", [128, 1024], F32)
    PB = A("PB", [128, 1024], F32)
    qt = A("qt", [128, 1024], BF16)
    kt = A("kt", [128, 1024], BF16)
    g1t = A("g1t", [128, 1024], F32)
    pet = A("pet", [128, 1024], F32)
    pct = A("pct", [128, 1024], F32)
    gjc = A("gjc", [128, 1024], F32)
    rcp = A("rcp", [128, 1024], F32)
    tmp = A("tmp", [128, 1024], F32)
    outb = A("outb", [128, 1024], BF16)

    P = nc.alloc_psum_tensor
    psQ = P("psQ", [128, 1024], F32)

    b.d("sync", ones1f[:], c_ones[:, 0:128], w=["ones1f"])
    b.d("sync", ln32[:], c_ln32[:], w=["ln32"])
    b.d("sync", gtr[:], gtp[:], w=["gtr"])
    b.d("sync", ptr[:], ptp[:], w=["ptr"])
    for (row, rn, dst, dn) in ((gtr, "gtr", GB, "GB"), (ptr, "ptr", PB, "PB")):
        for o2 in range(2):
            b.c("tensor", lambda e, o2=o2, row=row: e.matmul(
                psQ[:, o2 * 512:(o2 + 1) * 512], ones1f[:], row[:, o2 * 512:(o2 + 1) * 512],
                start=True, stop=True),
                r=["ones1f", rn] + (["psQr"] if o2 == 0 else []), w=["psQ"])
        b.c("vector", lambda e, dst=dst: e.tensor_copy(dst[:], psQ[:]),
            r=["psQ"], w=[dn, "psQr"])

    for ti in range(NT):
        b.d("sync", qt[:], qd[ts(ti), :], r=["qtR"], w=["qtb"])
        b.d("sync", kt[:], kd[ts(ti), :], r=["ktR"], w=["ktb"])
        b.d("sync", g1t[:], gcs[ts(ti), :], r=["g1tR"], w=["g1tb"])
        b.d("sync", pet[:], pexp[ts(ti), :], r=["petR"], w=["petb"])
        b.d("sync", pct[:], pcs[ts(ti), :], r=["pctR"], w=["pctb"])
        b.c("vector", lambda e: e.tensor_tensor(gjc[:], g1t[:], GB[:], ALU.add),
            r=["g1tb", "GB"], w=["gjc", "g1tR"])
        b.c("vector", lambda e: e.tensor_scalar(gjc[:], gjc[:], 60.0, -50.0, ALU.min,
                                                op1=ALU.max),
            r=["gjc"], w=["gjc"])
        b.c("scalar", lambda e: e.activation(out=gjc[:], in_=gjc[:], func=AF.Exp, scale=-1.0),
            r=["gjc"], w=["gjc"])
        b.c("vector", lambda e: e.tensor_tensor(pct[:], pct[:], PB[:], ALU.add),
            r=["pctb", "PB"], w=["pctb"])
        b.c("vector", lambda e: e.tensor_scalar(pct[:], pct[:], 1e-8, None, ALU.add),
            r=["pctb"], w=["pctb"])
        b.c("scalar", lambda e: e.activation(out=rcp[:], in_=pct[:], func=AF.Ln),
            r=["pctb"], w=["rcp", "pctR"])
        b.c("scalar", lambda e: e.activation(out=rcp[:], in_=rcp[:], func=AF.Exp, scale=-1.0,
                                             bias=ln32[:, 0:1]),
            r=["rcp", "ln32"], w=["rcp"])
        b.c("vector", lambda e: e.tensor_tensor(tmp[:], qt[:], gjc[:], ALU.mult),
            r=["qtb", "gjc"], w=["tmp", "qtR"])
        b.c("vector", lambda e: e.tensor_tensor(outb[:], tmp[:], rcp[:], ALU.mult),
            r=["tmp", "rcp"], w=["outb"])
        b.d("sync", o_qp[ts(ti), :], outb[:], r=["outb"], w=["outbD"])
        b.c("vector", lambda e: e.tensor_scalar(gjc[:], gjc[:], 1e-8, None, ALU.add),
            r=["gjc"], w=["gjc"])
        b.c("scalar", lambda e: e.activation(out=gjc[:], in_=gjc[:], func=AF.Ln),
            r=["gjc"], w=["gjc"])
        b.c("scalar", lambda e: e.activation(out=gjc[:], in_=gjc[:], func=AF.Exp, scale=-1.0),
            r=["gjc"], w=["gjc"])
        b.c("vector", lambda e: e.tensor_tensor(tmp[:], kt[:], pet[:], ALU.mult),
            r=["ktb", "petb"], w=["tmp", "ktR", "petR"])
        b.c("vector", lambda e: e.tensor_tensor(outb[:], tmp[:], gjc[:], ALU.mult),
            r=["tmp", "gjc", "outbD"], w=["outb"])
        b.d("sync", o_kp[ts(ti), :], outb[:], r=["outb"], w=["outbD"])
    b.replay()
    return nc


def run_p7(I, p6res, _cache={}):
    if "p7" not in _cache:
        _cache["p7"] = build_p7()
    nc = _cache["p7"]
    _, _, ut = host_p6_tables()
    maps = []
    for c in range(8):
        bb, h = c // 2, c % 2
        peer = bb * 2 + (1 - h)
        maps.append({
            "pd": np.asarray(p6res[c]["o_p"]),
            "pm0": np.asarray(p6res[c]["o_pmax"]),
            "pm1": np.asarray(p6res[peer]["o_pmax"]),
            "clk": np.asarray(p6res[c]["o_clock"]),
            "c_idn": np.eye(128, dtype=np.float32),
            "c_ones": np.ones((1, 512), np.float32),
            "c_onec": np.ones((128, 1), np.float32), "c_utri": ut,
        })
    res = run_bass_kernel_spmd(nc, maps, core_ids=list(range(8)))
    return res.results


def run_p8(I, p6res, p7res, _cache={}):
    if "p8" not in _cache:
        _cache["p8"] = build_p8()
    nc = _cache["p8"]
    z = np.zeros((1, C), np.float32)
    maps = []
    for c in range(8):
        bb, h = c // 2, c % 2
        maps.append({
            "qd": np.asarray(p6res[c]["o_q"]),
            "kd": np.asarray(p6res[c]["o_k"]),
            "gcs": np.asarray(p6res[c]["o_gcs"]),
            "pexp": np.asarray(p7res[c]["o_pexp"]),
            "pcs": np.asarray(p7res[c]["o_pcs"]),
            "gtp": (np.asarray(p6res[bb * 2]["o_gtot"]) if h == 1 else z),
            "ptp": (np.asarray(p7res[bb * 2]["o_ptot"]) if h == 1 else z),
            "c_ones": np.ones((1, 512), np.float32),
            "c_ln32": np.full((128, 1), np.log(1.0 / 32.0), np.float32),
        })
    res = run_bass_kernel_spmd(nc, maps, core_ids=list(range(8)))
    return res.results



def build_p9():
    nc = bass.Bass("TRN2", target_bir_lowering=False)
    qp = nc.dram_tensor("qp", [NTOK, C], BF16, kind="ExternalInput").ap()
    kp = nc.dram_tensor("kp", [NTOK, C], BF16, kind="ExternalInput").ap()
    vp = nc.dram_tensor("vp", [NTOK, C], BF16, kind="ExternalInput").ap()
    qx = nc.dram_tensor("qx", [2048, C], BF16, kind="ExternalInput").ap()
    kx = nc.dram_tensor("kx", [NTOK, C], BF16, kind="ExternalInput").ap()
    vx = nc.dram_tensor("vx", [NTOK, C], BF16, kind="ExternalInput").ap()
    c_fc = nc.dram_tensor("c_fc", [128, 896], F32, kind="ExternalInput").ap()
    o_O1 = nc.dram_tensor("o_O1", [NTOK, C], F32, kind="ExternalOutput").ap()
    o_ml1 = nc.dram_tensor("o_ml1", [128, 2 * NT], F32, kind="ExternalOutput").ap()
    o_O2 = nc.dram_tensor("o_O2", [2048, C], F32, kind="ExternalOutput").ap()
    o_ml2 = nc.dram_tensor("o_ml2", [128, 2 * 16], F32, kind="ExternalOutput").ap()

    b = Builder(nc)
    A = nc.alloc_sbuf_tensor
    fc = A("fc", [128, 896], F32)
    QT = A("QT", [128, 8 * 2048], BF16)
    KT = [A(f"KT{i}", [128, 8 * 512], BF16) for i in range(2)]
    VB = [A(f"VB{i}", [128, 4 * 1024], BF16) for i in range(2)]
    Oac = A("Oac", [128, 16 * 1024], F32)
    ml = A("ml", [128, 4 * 16], F32)
    sm_ = [A(f"sm{i}", [128, 6], F32) for i in range(2)]
    Ssb_ = [A(f"Ssb{i}", [128, 512], F32) for i in range(2)]
    Pb_ = [A(f"Pb{i}", [128, 512], BF16) for i in range(2)]
    PT_ = [A(f"PT{i}", [128, 512], BF16) for i in range(2)]
    outf = A("outf", [128, 1024], F32)
    idn = A("idn", [128, 128], BF16)

    P = nc.alloc_psum_tensor
    psS = [P(f"psS{i}", [128, 512], F32) for i in range(2)]
    psPT_ = [P(f"psPT{i}", [128, 512], BF16) for i in range(2)]
    psO = P("psO", [128, 1024], F32)

    c_idn = nc.dram_tensor("c_idn", [128, 128], F32, kind="ExternalInput").ap()
    b.d("sync", idn[:], c_idn[:], w=["idn"])
    b.d("sync", fc[:], c_fc[:], w=["fc"])

    def load_q(src, tok0, ntok):
        for c1 in range(8):
            b.d("sync", QT[:, c1 * 2048: c1 * 2048 + ntok],
                src[tok0:tok0 + ntok, c1 * 128:(c1 + 1) * 128],
                w=["QT"], transpose=True)

    def load_kv(ksrc, vsrc, ks, slot):
        for c1 in range(8):
            b.d("sync", KT[slot][:, c1 * 512:(c1 + 1) * 512],
                ksrc[ks:ks + 512, c1 * 128:(c1 + 1) * 128],
                r=[f"KTr{slot}"], w=[f"KT{slot}"], transpose=True)
        for i in range(4):
            b.d("gpsimd", VB[slot][:, i * 1024:(i + 1) * 1024],
                vsrc[ks + i * 128: ks + (i + 1) * 128, :],
                r=[f"VBr{slot}"], w=[f"VB{slot}"])

    def flash_range(tag, qsrc, qtok0, nq, ksrc, vsrc, nk, causal, oO, oML):
        nqt = nq // 128
        load_q(qsrc, qtok0, nq)
        nkb = nk // 512
        first_done = [False] * nqt
        for kb in range(nkb):
            ks = kb * 512
            slot = kb % 2
            load_kv(ksrc, vsrc, ks, slot)
            for qt in range(nqt):
                qs = qt * 128 + (2048 if tag == "R2" else 0)
                if causal and ks > qs + 127:
                    continue
                vp = (kb + qt) % 2
                Ssb, sm, Pb, PT = Ssb_[vp], sm_[vp], Pb_[vp], PT_[vp]
                psPT = psPT_[vp]
                sfx = str(vp)
                for c1 in range(8):
                    b.c("tensor", lambda e, c1=c1, qt=qt, slot=slot: e.matmul(
                        psS[slot][:], QT[:, c1 * 2048 + qt * 128: c1 * 2048 + (qt + 1) * 128],
                        KT[slot][:, c1 * 512:(c1 + 1) * 512],
                        start=(c1 == 0), stop=(c1 == 7)),
                        r=["QT", f"KT{slot}"] + ([f"psS{slot}r"] if c1 == 0 else []),
                        w=[f"psS{slot}"])
                diag = causal and (qs - ks) < 512
                if diag:
                    d = qs - ks
                    b.c("vector", lambda e, d=d, slot=slot: e.tensor_tensor(
                        Ssb[:], psS[slot][:], fc[:, 384 - d: 896 - d], ALU.add),
                        r=[f"psS{slot}", "fc"], w=["Ssb" + sfx, f"psS{slot}r"])
                    sin_, sin_n = Ssb, "Ssb" + sfx
                else:
                    sin_, sin_n = psS[slot], f"psS{slot}"
                b.c("vector", lambda e, sin_=sin_: e.tensor_reduce(
                    sm[:, 0:1], sin_[:], AX.X, ALU.max),
                    r=[sin_n], w=["sm" + sfx])
                fd = first_done[qt]
                if not fd:
                    b.c("vector", lambda e, qt=qt: e.tensor_copy(ml[:, qt:qt + 1], sm[:, 0:1]),
                        r=["sm" + sfx], w=["mlm"])
                else:
                    b.c("vector", lambda e, qt=qt: e.tensor_tensor(
                        sm[:, 1:2], ml[:, qt:qt + 1], sm[:, 0:1], ALU.max),
                        r=["sm" + sfx, "mlm"], w=["sm2" + sfx])
                    b.c("vector", lambda e, qt=qt: e.tensor_tensor(
                        sm[:, 2:3], ml[:, qt:qt + 1], sm[:, 1:2], ALU.subtract),
                        r=["sm2" + sfx, "mlm"], w=["sm3" + sfx])
                    b.c("scalar", lambda e: e.activation(out=sm[:, 2:3], in_=sm[:, 2:3],
                                                         func=AF.Exp),
                        r=["sm3" + sfx], w=["sm3" + sfx])
                    b.c("vector", lambda e, qt=qt: e.tensor_copy(ml[:, qt:qt + 1], sm[:, 1:2]),
                        r=["sm2" + sfx], w=["mlm"])
                b.c("vector", lambda e, qt=qt: e.tensor_scalar(
                    sm[:, 3:4], ml[:, qt:qt + 1], -1.0, None, ALU.mult),
                    r=["mlm"], w=["sm4" + sfx])
                b.c("scalar", lambda e, sin_=sin_: e.activation(
                    out=Pb[:], in_=sin_[:], func=AF.Exp, bias=sm[:, 3:4], scale=1.0,
                    accum_out=sm[:, 4:5]),
                    r=[sin_n, "sm4" + sfx], w=["Pb" + sfx, "sm5" + sfx] + ([] if diag else [f"psS{slot}r"]))
                if not fd:
                    b.c("vector", lambda e, qt=qt: e.tensor_copy(
                        ml[:, 16 + qt: 17 + qt], sm[:, 4:5]),
                        r=["sm5" + sfx], w=["mll"])
                else:
                    b.c("vector", lambda e, qt=qt: e.tensor_tensor(
                        ml[:, 16 + qt: 17 + qt], ml[:, 16 + qt: 17 + qt], sm[:, 2:3], ALU.mult),
                        r=["mll", "sm3" + sfx], w=["mll"])
                    b.c("vector", lambda e, qt=qt: e.tensor_tensor(
                        ml[:, 16 + qt: 17 + qt], ml[:, 16 + qt: 17 + qt], sm[:, 4:5], ALU.add),
                        r=["mll", "sm5" + sfx], w=["mll"])
                for pc in range(4):
                    b.c("tensor", lambda e, pc=pc: e.matmul(
                        psPT[:, pc * 128:(pc + 1) * 128], Pb[:, pc * 128:(pc + 1) * 128],
                        idn[:], start=(pc == 0), stop=(pc == 3), is_transpose=True,
                        skip_group_check=True),
                        r=["Pb" + sfx, "idn"] + (["psPTr" + sfx] if pc == 0 else []), w=["psPT" + sfx])
                b.c("vector", lambda e: e.tensor_copy(PT[:], psPT[:]),
                    r=["psPT" + sfx], w=["PT" + sfx, "psPTr" + sfx])
                for pc in range(4):
                    for o2 in range(2):
                        b.c("tensor", lambda e, pc=pc, o2=o2, slot=slot: e.matmul(
                            psO[:, o2 * 512:(o2 + 1) * 512],
                            PT[:, pc * 128:(pc + 1) * 128],
                            VB[slot][:, pc * 1024 + o2 * 512: pc * 1024 + (o2 + 1) * 512],
                            start=(pc == 0), stop=(pc == 3), skip_group_check=True),
                            r=["PT" + sfx, f"VB{slot}"] + (["psOr"] if pc == 0 and o2 == 0 else []),
                            w=["psO"])
                if not fd:
                    b.c("vector", lambda e, qt=qt: e.tensor_copy(
                        Oac[:, qt * 1024:(qt + 1) * 1024], psO[:]),
                        r=["psO"], w=[f"Oac_{qt}", "psOr"])
                    first_done[qt] = True
                else:
                    b.c("scalar", lambda e, qt=qt: e.activation(
                        out=Oac[:, qt * 1024:(qt + 1) * 1024],
                        in_=Oac[:, qt * 1024:(qt + 1) * 1024],
                        func=AF.Copy, scale=sm[:, 2:3]),
                        r=[f"Oac_{qt}", "sm3" + sfx], w=[f"Oac_{qt}"])
                    b.c("vector", lambda e, qt=qt: e.tensor_tensor(
                        Oac[:, qt * 1024:(qt + 1) * 1024],
                        Oac[:, qt * 1024:(qt + 1) * 1024], psO[:], ALU.add),
                        r=[f"Oac_{qt}", "psO"], w=[f"Oac_{qt}", "psOr"])
            b.c("sync", lambda e: e.nop(), r=[f"KT{slot}", f"VB{slot}"],
                w=[f"KTr{slot}", f"VBr{slot}"])
        for qt in range(nqt):
            b.d("sync", oO[(qtok0 % NTOK) * 0 + (qt * 128) + (2048 if tag == "R2" else 0)
                           if tag != "R3" else qt * 128, :][0:0, :] if False else
                oO[(qt * 128 + (2048 if tag == "R2" else 0)) if tag != "R3" else qt * 128:
                   ((qt + 1) * 128 + (2048 if tag == "R2" else 0)) if tag != "R3" else (qt + 1) * 128, :],
                Oac[:, qt * 1024:(qt + 1) * 1024], r=[f"Oac_{qt}"], w=["OacD"])
        mlcol = (16 if tag == "R2" else 0)
        b.d("sync", oML[:, mlcol:mlcol + 16], ml[:, 0:16], r=["mlm"], w=["mlD"])
        b.d("sync", oML[:, NT + mlcol if tag != "R3" else 16: (NT + mlcol + 16) if tag != "R3" else 32],
            ml[:, 16:32], r=["mll"], w=["mlD"])

    flash_range("R1", qp, 0, 2048, kp, vp, 2048, True, o_O1, o_ml1)
    flash_range("R2", qp, 2048, 2048, kp, vp, 4096, True, o_O1, o_ml1)
    flash_range("R3", qx, 0, 2048, kx, vx, 4096, False, o_O2, o_ml2)
    b.replay()
    return nc


def run_p9(I, p6res, p8res, _cache={}):
    if "p9" not in _cache:
        _cache["p9"] = build_p9()
    nc = _cache["p9"]
    fc = np.zeros((128, 896), np.float32)
    for p in range(128):
        for v_ in range(896):
            if not (v_ - 384 <= p):
                fc[p, v_] = -1e30
    maps = []
    for c in range(8):
        bb, h = c // 2, c % 2
        peer = bb * 2 + (1 - h)
        qp_own = np.asarray(p8res[c]["o_qp"])
        q2 = np.asarray(p8res[bb * 2 + 1]["o_qp"])[h * 2048:(h + 1) * 2048]
        k1 = np.asarray(p8res[bb * 2]["o_kp"])
        v1 = np.asarray(p6res[bb * 2]["o_v"])
        maps.append({
            "qp": qp_own, "kp": np.asarray(p8res[c]["o_kp"]),
            "vp": np.asarray(p6res[c]["o_v"]),
            "qx": np.ascontiguousarray(q2), "kx": k1, "vx": v1,
            "c_fc": fc, "c_idn": np.eye(128, dtype=np.float32),
        })
    res = run_bass_kernel_spmd(nc, maps, core_ids=list(range(8)))
    return res.results



def build_p10():
    nc = bass.Bass("TRN2", target_bir_lowering=False)
    O1d = nc.dram_tensor("O1d", [NTOK, C], F32, kind="ExternalInput").ap()
    O2d = nc.dram_tensor("O2d", [NTOK, C], F32, kind="ExternalInput").ap()
    ml1d = nc.dram_tensor("ml1d", [128, 2 * NT], F32, kind="ExternalInput").ap()
    ml2d = nc.dram_tensor("ml2d", [128, 2 * NT], F32, kind="ExternalInput").ap()
    x3d = nc.dram_tensor("x3d", [NTOK, C], F32, kind="ExternalInput").ap()
    outW = nc.dram_tensor("outW", [C, C], F32, kind="ExternalInput").ap()
    outbr = nc.dram_tensor("outbr", [1, C], F32, kind="ExternalInput").ap()
    ln2gr = nc.dram_tensor("ln2gr", [1, C], F32, kind="ExternalInput").ap()
    ln2br = nc.dram_tensor("ln2br", [1, C], F32, kind="ExternalInput").ap()
    W1d = nc.dram_tensor("W1d", [C, 4096], F32, kind="ExternalInput").ap()
    b1r = nc.dram_tensor("b1r", [1, 4096], F32, kind="ExternalInput").ap()
    W2d = nc.dram_tensor("W2d", [4096, C], F32, kind="ExternalInput").ap()
    b2r = nc.dram_tensor("b2r", [1, C], F32, kind="ExternalInput").ap()
    c_idn = nc.dram_tensor("c_idn", [128, 128], F32, kind="ExternalInput").ap()
    c_ones = nc.dram_tensor("c_ones", [1, 512], F32, kind="ExternalInput").ap()
    o_g3 = nc.dram_tensor("o_g3", [NTOK, C], F32, kind="ExternalOutput").ap()

    b = Builder(nc)
    A = nc.alloc_sbuf_tensor
    idn = A("idn", [128, 128], BF16)
    ones1 = A("ones1", [1, 128], BF16)
    ones1f = A("ones1f", [1, 128], F32)
    outWS = A("outWS", [128, 8 * 1024], BF16)
    obrow = A("obrow", [1, C], BF16)
    b1row = A("b1row", [1, 4096], BF16)
    b2row = A("b2row", [1, C], BF16)
    W1S = A("W1S", [128, 8 * 4096], BF16)
    W2S = A("W2S", [128, 32 * 1024], BF16)
    ml1 = A("ml1", [128, 2 * NT], F32)
    ml2 = A("ml2", [128, 2 * NT], F32)
    g2B = A("g2B", [128, 1024], F32)
    b2B = A("b2B", [128, 1024], F32)
    lngs = A("lngs", [1, C], F32)
    O1t = A("O1t", [128, 1024], F32)
    O2t = A("O2t", [128, 1024], F32)
    x3t = A("x3t", [128, 1024], F32)
    x2t = A("x2t", [128, 1024], F32)
    sm = A("sm", [128, 6], F32)
    mn = A("mn", [128, 4], F32)
    scr = A("scr", [128, 1024], F32)
    hb = A("hb", [128, 1024], BF16)
    hT = A("hT", [128, 8 * 128], BF16)
    mg = A("mg", [128, 512], BF16)
    mT = A("mT", [128, 32 * 128], BF16)
    outf = O2t

    P = nc.alloc_psum_tensor
    psA = [P(f"psA{i}", [128, 1024], F32) for i in range(2)]
    psT = P("psT", [128, 512], BF16)
    psW = [P(f"psW{i}", [128, 512], F32) for i in range(2)]

    b.d("sync", idn[:], c_idn[:], w=["idn"])
    b.d("sync", ones1[:], c_ones[:, 0:128], w=["ones1"])
    b.d("sync", ones1f[:], c_ones[:, 0:128], w=["ones1f"])
    b.d("sync", ml1[:], ml1d[:], w=["ml1"])
    b.d("sync", ml2[:], ml2d[:], w=["ml2"])
    b.d("gpsimd", obrow[:], outbr[:], w=["obrow"])
    b.d("gpsimd", b1row[:], b1r[:], w=["b1row"])
    b.d("gpsimd", b2row[:], b2r[:], w=["b2row"])

    for f in range(8):
        b.d("gpsimd", outWS[:, f * 1024:(f + 1) * 1024], outW[ts(f), :], w=["outWS"])
        b.d("gpsimd", W1S[:, f * 4096:(f + 1) * 4096], W1d[ts(f), :], w=["W1S"])
    for hc in range(32):
        b.d("gpsimd", W2S[:, hc * 1024:(hc + 1) * 1024], W2d[ts(hc), :], w=["W2S"])
    for (srcd, srn, dst, dn) in ((ln2gr, "lngs", g2B, "g2B"), (ln2br, "lngs", b2B, "b2B")):
        b.d("sync", lngs[:], srcd[:], r=["lngsR"], w=["lngs"])
        src = lngs
        for o2 in range(2):
            b.c("tensor", lambda e, o2=o2, src=src: e.matmul(
                psA[0][:, o2 * 512:(o2 + 1) * 512], ones1f[:], src[:, o2 * 512:(o2 + 1) * 512],
                start=True, stop=True),
                r=["ones1f", srn] + (["psA0r"] if o2 == 0 else []), w=["psA0"])
        b.c("vector", lambda e, dst=dst: e.tensor_copy(dst[:], psA[0][:]),
            r=["psA0"], w=[dn, "psA0r", "lngsR"])

    for ti in range(NT):
        b.d("sync", O1t[:], O1d[ts(ti), :], r=["O1tR"], w=["O1t"])
        b.d("sync", O2t[:], O2d[ts(ti), :], r=["O2tR", "outfD", "outf"], w=["O2t"])
        b.d("sync", x3t[:], x3d[ts(ti), :], r=["x3tR"], w=["x3t"])
        b.c("vector", lambda e, ti=ti: e.tensor_tensor(
            sm[:, 0:1], ml1[:, ti:ti + 1], ml2[:, ti:ti + 1], ALU.max),
            r=["ml1", "ml2"], w=["sm"])
        b.c("vector", lambda e, ti=ti: e.tensor_tensor(
            sm[:, 1:2], ml1[:, ti:ti + 1], sm[:, 0:1], ALU.subtract),
            r=["ml1", "sm"], w=["sm1"])
        b.c("scalar", lambda e: e.activation(out=sm[:, 1:2], in_=sm[:, 1:2], func=AF.Exp),
            r=["sm1"], w=["sm1"])
        b.c("vector", lambda e, ti=ti: e.tensor_tensor(
            sm[:, 2:3], ml2[:, ti:ti + 1], sm[:, 0:1], ALU.subtract),
            r=["ml2", "sm"], w=["sm2"])
        b.c("scalar", lambda e: e.activation(out=sm[:, 2:3], in_=sm[:, 2:3], func=AF.Exp),
            r=["sm2"], w=["sm2"])
        b.c("vector", lambda e: e.tensor_scalar(O1t[:], O1t[:], sm[:, 1:2], None, ALU.mult),
            r=["O1t", "sm1"], w=["O1t"])
        b.c("vector", lambda e: e.tensor_scalar(O2t[:], O2t[:], sm[:, 2:3], None, ALU.mult),
            r=["O2t", "sm2"], w=["O2t"])
        b.c("vector", lambda e: e.tensor_tensor(O1t[:], O1t[:], O2t[:], ALU.add),
            r=["O1t", "O2t"], w=["O1t", "O2tR"])
        b.c("vector", lambda e, ti=ti: e.tensor_tensor(
            sm[:, 3:4], ml1[:, NT + ti:NT + ti + 1], sm[:, 1:2], ALU.mult),
            r=["ml1", "sm1"], w=["sm3"])
        b.c("vector", lambda e, ti=ti: e.tensor_tensor(
            sm[:, 4:5], ml2[:, NT + ti:NT + ti + 1], sm[:, 2:3], ALU.mult),
            r=["ml2", "sm2"], w=["sm4"])
        b.c("vector", lambda e: e.tensor_tensor(sm[:, 3:4], sm[:, 3:4], sm[:, 4:5], ALU.add),
            r=["sm3", "sm4"], w=["sm3"])
        b.c("vector", lambda e: e.reciprocal(sm[:, 3:4], sm[:, 3:4]), r=["sm3"], w=["sm3"])
        b.c("vector", lambda e: e.tensor_scalar(hb[:], O1t[:], sm[:, 3:4], None, ALU.mult),
            r=["O1t", "sm3"], w=["hb", "O1tR"])
        for g4 in range(2):
            for jj in range(4):
                c1 = g4 * 4 + jj
                b.c("tensor", lambda e, c1=c1, jj=jj: e.matmul(
                    psT[:, jj * 128:(jj + 1) * 128], hb[:, c1 * 128:(c1 + 1) * 128],
                    idn[:], start=(jj == 0), stop=(jj == 3), is_transpose=True,
                    skip_group_check=True),
                    r=["hb", "idn"] + (["psTr"] if jj == 0 else []), w=["psT"])
            b.c("vector", lambda e, g4=g4: e.tensor_copy(
                hT[:, g4 * 512:(g4 + 1) * 512], psT[:]),
                r=["psT"], w=["hT", "psTr"])
        for c1 in range(8):
            for o2 in range(2):
                b.c("tensor", lambda e, c1=c1, o2=o2: e.matmul(
                    psA[0][:, o2 * 512:(o2 + 1) * 512], hT[:, c1 * 128:(c1 + 1) * 128],
                    outWS[:, c1 * 1024 + o2 * 512: c1 * 1024 + (o2 + 1) * 512],
                    start=(c1 == 0), stop=False, skip_group_check=True),
                    r=["hT", "outWS"] + (["psA0r"] if c1 == 0 and o2 == 0 else []),
                    w=["psA0"])
        for o2 in range(2):
            b.c("tensor", lambda e, o2=o2: e.matmul(
                psA[0][:, o2 * 512:(o2 + 1) * 512], ones1[:], obrow[:, o2 * 512:(o2 + 1) * 512],
                start=False, stop=True, skip_group_check=True),
                r=["ones1", "obrow"], w=["psA0"])
        b.c("vector", lambda e: e.tensor_tensor(x2t[:], psA[0][:], x3t[:], ALU.add),
            r=["psA0", "x3t"], w=["x2t", "psA0r"])
        ln_token_tile(b, x2t, scr, mn, scr, "p10ln", in_res=["x2t"],
                      out_res=["scrO"])
        b.c("vector", lambda e: e.tensor_tensor(scr[:], scr[:], g2B[:], ALU.mult),
            r=["scrO", "g2B"], w=["scrO"])
        b.c("vector", lambda e: e.tensor_tensor(scr[:], scr[:], b2B[:], ALU.add),
            r=["scrO", "b2B"], w=["scrO"])
        b.c("vector", lambda e: e.tensor_copy(hb[:], scr[:]), r=["scrO"], w=["hb"])
        for g4 in range(2):
            for jj in range(4):
                c1 = g4 * 4 + jj
                b.c("tensor", lambda e, c1=c1, jj=jj: e.matmul(
                    psT[:, jj * 128:(jj + 1) * 128], hb[:, c1 * 128:(c1 + 1) * 128],
                    idn[:], start=(jj == 0), stop=(jj == 3), is_transpose=True,
                    skip_group_check=True),
                    r=["hb", "idn"] + (["psTr"] if jj == 0 else []), w=["psT"])
            b.c("vector", lambda e, g4=g4: e.tensor_copy(
                hT[:, g4 * 512:(g4 + 1) * 512], psT[:]),
                r=["psT"], w=["hT", "psTr"])
        for og in range(8):
            pw = psW[og % 2]
            pwn = f"psW{og % 2}"
            for f in range(8):
                b.c("tensor", lambda e, og=og, f=f, pw=pw: e.matmul(
                    pw[:], hT[:, f * 128:(f + 1) * 128],
                    W1S[:, f * 4096 + og * 512: f * 4096 + (og + 1) * 512],
                    start=(f == 0), stop=False, skip_group_check=True),
                    r=["hT", "W1S"] + ([pwn + "r"] if f == 0 else []), w=[pwn])
            b.c("tensor", lambda e, og=og, pw=pw: e.matmul(
                pw[:], ones1[:], b1row[:, og * 512:(og + 1) * 512],
                start=False, stop=True, skip_group_check=True),
                r=["ones1", "b1row"], w=[pwn])
            b.c("scalar", lambda e, og=og, pw=pw: e.activation(
                out=mg[:], in_=pw[:], func=AF.Gelu),
                r=[pwn, "mgR"], w=["mg", pwn + "r"])
            for j in range(4):
                b.c("tensor", lambda e, j=j: e.matmul(
                    psT[:, j * 128:(j + 1) * 128], mg[:, j * 128:(j + 1) * 128], idn[:],
                    start=(j == 0), stop=(j == 3), is_transpose=True,
                    skip_group_check=True),
                    r=["mg", "idn"] + (["psTr"] if j == 0 else []),
                    w=["psT"] + (["mgR"] if j == 3 else []))
            b.c("vector", lambda e, og=og: e.tensor_copy(
                mT[:, og * 512:(og + 1) * 512], psT[:]),
                r=["psT"], w=["mT", "psTr"])
        for o2 in range(2):
            for hc in range(32):
                b.c("tensor", lambda e, hc=hc, o2=o2: e.matmul(
                    psA[1][:, o2 * 512:(o2 + 1) * 512], mT[:, hc * 128:(hc + 1) * 128],
                    W2S[:, hc * 1024 + o2 * 512: hc * 1024 + (o2 + 1) * 512],
                    start=(hc == 0), stop=False, skip_group_check=True),
                    r=["mT", "W2S"] + (["psA1r"] if hc == 0 and o2 == 0 else []),
                    w=["psA1"])
            b.c("tensor", lambda e, o2=o2: e.matmul(
                psA[1][:, o2 * 512:(o2 + 1) * 512], ones1[:], b2row[:, o2 * 512:(o2 + 1) * 512],
                start=False, stop=True, skip_group_check=True),
                r=["ones1", "b2row"], w=["psA1"])
        b.c("vector", lambda e: e.tensor_tensor(outf[:], psA[1][:], x2t[:], ALU.add),
            r=["psA1", "x2t", "O2tR"], w=["outf", "psA1r"])
        b.c("vector", lambda e: e.tensor_tensor(outf[:], outf[:], x3t[:], ALU.add),
            r=["outf", "x3t"], w=["outf", "x3tR"])
        ln_token_tile(b, outf, outf, mn, scr, "p10ln", in_res=["outf"],
                      out_res=["outf"])
        b.d("sync", o_g3[ts(ti), :], outf[:], r=["outf"], w=["outfD"])
    b.replay()
    return nc


def run_p10(I, x3full, p9res, _cache={}):
    if "p10" not in _cache:
        _cache["p10"] = build_p10()
    nc = _cache["p10"]
    cs = host_consts()
    maps = []
    for c in range(8):
        bb, h = c // 2, c % 2
        if h == 0:
            O2 = np.zeros((NTOK, C), np.float32)
            ml2 = np.zeros((128, 2 * NT), np.float32)
            ml2[:, 0:NT] = -1e30
        else:
            O2a = np.asarray(p9res[bb * 2]["o_O2"])
            O2b = np.asarray(p9res[bb * 2 + 1]["o_O2"])
            O2 = np.concatenate([O2a, O2b], 0)
            m2a = np.asarray(p9res[bb * 2]["o_ml2"])
            m2b = np.asarray(p9res[bb * 2 + 1]["o_ml2"])
            ml2 = np.concatenate([m2a[:, 0:16], m2b[:, 0:16], m2a[:, 16:32], m2b[:, 16:32]],
                                 axis=1)
        maps.append({
            "O1d": np.asarray(p9res[c]["o_O1"]), "O2d": np.ascontiguousarray(O2),
            "ml1d": np.asarray(p9res[c]["o_ml1"]), "ml2d": np.ascontiguousarray(ml2),
            "x3d": np.ascontiguousarray(x3full[bb, h * NTOK:(h + 1) * NTOK]),
            "outW": I["L1_attn_out_W"], "outbr": I["L1_attn_out_b"].reshape(1, C),
            "ln2gr": I["L1_attn_ln2_g"].reshape(1, C),
            "ln2br": I["L1_attn_ln2_b"].reshape(1, C),
            "W1d": I["L1_attn_mlp_W1"], "b1r": I["L1_attn_mlp_b1"].reshape(1, 4096),
            "W2d": I["L1_attn_mlp_W2"], "b2r": I["L1_attn_mlp_b2"].reshape(1, C),
            "c_idn": cs["c_idn"], "c_ones": np.ones((1, 512), np.float32),
        })
    res = run_bass_kernel_spmd(nc, maps, core_ids=list(range(8)))
    return res.results



def build_p11():
    nc = bass.Bass("TRN2", target_bir_lowering=False)
    g3d = nc.dram_tensor("g3d", [NTOK, C], F32, kind="ExternalInput").ap()
    wred = nc.dram_tensor("wred", [128, 128], F32, kind="ExternalInput").ap()
    wgd = nc.dram_tensor("wgd", [8, 16], F32, kind="ExternalInput").ap()
    W1d = nc.dram_tensor("W1d", [8 * C, 2048], F32, kind="ExternalInput").ap()
    b1d = nc.dram_tensor("b1d", [8, 2048], F32, kind="ExternalInput").ap()
    W2d = nc.dram_tensor("W2d", [8 * 2048, C], F32, kind="ExternalInput").ap()
    b2d = nc.dram_tensor("b2d", [8, C], F32, kind="ExternalInput").ap()
    c_idn = nc.dram_tensor("c_idn", [128, 128], F32, kind="ExternalInput").ap()
    c_ones = nc.dram_tensor("c_ones", [1, 512], F32, kind="ExternalInput").ap()
    sc_xT = nc.dram_tensor("sc_xT", [C, NTOK], BF16, kind="Internal").ap()
    o_g4 = nc.dram_tensor("o_g4", [NTOK, C], F32, kind="ExternalOutput").ap()

    b = Builder(nc)
    A = nc.alloc_sbuf_tensor
    idn = A("idn", [128, 128], BF16)
    idnf = A("idnf", [128, 128], F32)
    ones1 = A("ones1", [1, 512], BF16)
    wredS = A("wredS", [128, 8 * 16], F32)
    wgs = A("wgs", [8, 16], F32)
    sq8 = A("sq8", [8, 16], F32)
    wgnT = A("wgnT", [16, 8], F32)
    sm8 = A("sm8", [8, 4], F32)
    W1S = A("W1S", [128, 8 * 2048], BF16)
    W2S = A("W2S", [128, 16 * 1024], BF16)
    b1row = A("b1row", [1, 2048], BF16)
    b2S = A("b2S", [8, 1024], BF16)
    gt = A("gt", [128, 1024], F32)
    gb = A("gbuf", [128, 1024], BF16)
    xTgF = A("xTgF", [128, 8 * 512], F32)
    xTg = [A(f"xTg{i}", [128, 8 * 512], BF16) for i in range(2)]
    lgf = A("lgf", [8, 512], F32)
    ltok = A("ltok", [128, 8], F32)
    gts = A("gts", [128, 8], F32)
    eq = A("eq", [128, 8], F32)
    rem = A("rem", [128, 2], F32)
    msc_all = A("msc_all", [128, NT * 8], BF16)
    mscR = A("mscR", [1, 512], BF16)
    msc8g = A("msc8g", [8, 512], BF16)
    mgb = [A(f"mgb{i}", [128, 512], BF16) for i in range(2)]
    hsc = A("hsc", [128, 16 * 512], BF16)
    out_all = A("out_all", [128, NT * 1024], BF16)
    mn = A("mn", [128, 4], F32)
    scr = A("scr", [128, 1024], F32)

    P = nc.alloc_psum_tensor
    psW = [P(f"psW{i}", [128, 512], F32) for i in range(2)]
    psO = P("psO", [128, 1024], F32)
    psT = P("psT", [128, 512], BF16)
    psTf = P("psTf", [128, 512], F32)
    psM = P("psM", [128, 512], F32)

    b.d("sync", idn[:], c_idn[:], w=["idn"])
    b.d("sync", idnf[:], c_idn[:], w=["idnf"])
    b.d("sync", ones1[:], c_ones[:], w=["ones1"])
    b.d("sync", wredS[:], wred[:], w=["wredS"])
    b.d("sync", wgs[:], wgd[:], w=["wgs"])
    b.d("gpsimd", b2S[:], b2d[:], w=["b2S"])
    b.c("scalar", lambda e: e.activation(out=sq8[:], in_=wgs[:], func=AF.Square,
                                         accum_out=sm8[:, 1:2]),
        r=["wgs"], w=["sm8", "sq8"])
    b.c("scalar", lambda e: e.activation(out=sm8[:, 1:2], in_=sm8[:, 1:2], func=AF.Sqrt),
        r=["sm8"], w=["sm8b"])
    b.c("vector", lambda e: e.tensor_scalar(sm8[:, 1:2], sm8[:, 1:2], 1e-4, None, ALU.max),
        r=["sm8b"], w=["sm8b"])
    b.c("vector", lambda e: e.reciprocal(sm8[:, 1:2], sm8[:, 1:2]), r=["sm8b"], w=["sm8b"])
    b.c("vector", lambda e: e.tensor_scalar(wgs[:], wgs[:], sm8[:, 1:2], None, ALU.mult),
        r=["wgs", "sm8b"], w=["wgs"])
    b.c("tensor", lambda e: e.transpose(psTf[0:16, 0:8], wgs[:], idnf[0:8, 0:8]),
        r=["wgs", "idnf", "psTfr"], w=["psTf"])
    b.c("vector", lambda e: e.tensor_copy(wgnT[:], psTf[0:16, 0:8]),
        r=["psTf"], w=["wgnT", "psTfr"])

    for g in range(NG):
        for i in range(4):
            ti = g * 4 + i
            b.d("sync", gt[:], g3d[ts(ti), :], r=["gtR"], w=["gt"])
            b.c("vector", lambda e: e.tensor_copy(gb[:], gt[:]), r=["gt", "gbR"],
                w=["gbuf", "gtR"])
            for g4 in range(2):
                for jj in range(4):
                    c1 = g4 * 4 + jj
                    b.c("tensor", lambda e, c1=c1, jj=jj: e.matmul(
                        psTf[:, jj * 128:(jj + 1) * 128], gt[:, c1 * 128:(c1 + 1) * 128],
                        idnf[:], start=(jj == 0), stop=(jj == 3), is_transpose=True,
                        skip_group_check=True),
                        r=["gt", "idnf"] + (["psTfr"] if jj == 0 else []), w=["psTf"])
                for jj in range(4):
                    c1 = g4 * 4 + jj
                    b.c("vector", lambda e, c1=c1, i=i, jj=jj: e.tensor_copy(
                        xTgF[:, c1 * 512 + i * 128: c1 * 512 + (i + 1) * 128],
                        psTf[:, jj * 128:(jj + 1) * 128]),
                        r=["psTf"], w=["xTgF"] + (["psTfr"] if jj == 3 else []))
        for c1 in range(8):
            b.c("vector", lambda e, c1=c1: e.tensor_copy(
                xTg[0][:, c1 * 512:(c1 + 1) * 512], xTgF[:, c1 * 512:(c1 + 1) * 512]),
                r=["xTgF"], w=["xTg0s"])
            b.d("sync", sc_xT[ts(c1), g * 512:(g + 1) * 512],
                xTg[0][:, c1 * 512:(c1 + 1) * 512], r=["xTg0s"], w=["scxT"])
        for c1 in range(8):
            b.c("tensor", lambda e, c1=c1: e.matmul(
                psW[0][0:16, :], wredS[:, c1 * 16:(c1 + 1) * 16],
                xTgF[:, c1 * 512:(c1 + 1) * 512],
                start=(c1 == 0), stop=(c1 == 7)),
                r=["wredS", "xTgF"] + (["psW0r"] if c1 == 0 else []), w=["psW0"])
        b.c("vector", lambda e: e.tensor_copy(scr[0:16, 0:512], psW[0][0:16, :]),
            r=["psW0"], w=["xredf", "psW0r"])
        b.c("tensor", lambda e: e.matmul(psW[1][0:8, :], wgnT[:], scr[0:16, 0:512],
                                         start=True, stop=True),
            r=["wgnT", "xredf", "psW1r"], w=["psW1"])
        b.c("vector", lambda e: e.tensor_copy(lgf[:], psW[1][0:8, :]),
            r=["psW1"], w=["lgf", "psW1r"])
        for i in range(4):
            b.c("tensor", lambda e, i=i: e.transpose(
                psTf[:, 0:8], lgf[:, i * 128:(i + 1) * 128], idnf[0:8, 0:8]),
                r=["lgf", "idnf", "psTfr"], w=["psTf"])
            b.c("vector", lambda e: e.tensor_copy(ltok[:], psTf[:, 0:8]),
                r=["psTf"], w=["ltok", "psTfr"])
            b.c("vector", lambda e: e.tensor_reduce(mn[:, 0:1], ltok[:], AX.X, ALU.max,
                                                    negate=True),
                r=["ltok"], w=["mnmax"])
            b.c("scalar", lambda e: e.activation(out=gts[:], in_=ltok[:], func=AF.Exp,
                                                 bias=mn[:, 0:1], scale=1.0),
                r=["ltok", "mnmax"], w=["gts"])
            b.c("vector", lambda e: e.tensor_reduce(mn[:, 1:2], gts[:], AX.X, ALU.add),
                r=["gts"], w=["mnsum"])
            b.c("vector", lambda e: e.reciprocal(mn[:, 1:2], mn[:, 1:2]),
                r=["mnsum"], w=["mnsum"])
            b.c("vector", lambda e: e.tensor_scalar(gts[:], gts[:], mn[:, 1:2], None, ALU.mult),
                r=["gts", "mnsum"], w=["gts"])
            b.c("vector", lambda e: e.tensor_reduce(mn[:, 2:3], gts[:], AX.X, ALU.max),
                r=["gts"], w=["mngm"])
            b.c("vector", lambda e: e.tensor_scalar(eq[:], gts[:], mn[:, 2:3], None, ALU.is_ge),
                r=["gts", "mngm"], w=["eq"])
            b.c("vector", lambda e: e.tensor_scalar(rem[:, 0:1], mn[:, 2:3], 0.0, None,
                                                    ALU.mult),
                r=["mngm"], w=["rem"])
            b.c("vector", lambda e: e.tensor_scalar(rem[:, 0:1], rem[:, 0:1], 1.0, None,
                                                    ALU.add),
                r=["rem"], w=["rem"])
            ti = g * 4 + i
            for ee in range(8):
                b.c("vector", lambda e, ee=ee: e.tensor_tensor(
                    eq[:, ee:ee + 1], eq[:, ee:ee + 1], rem[:, 0:1], ALU.mult),
                    r=["eq", "rem"], w=["eq"])
                if ee < 7:
                    b.c("vector", lambda e, ee=ee: e.tensor_tensor(
                        rem[:, 0:1], rem[:, 0:1], eq[:, ee:ee + 1], ALU.subtract),
                        r=["eq", "rem"], w=["rem"])
            b.c("vector", lambda e, ti=ti: e.tensor_scalar(
                msc_all[:, ti * 8:(ti + 1) * 8], eq[:], mn[:, 2:3], None, ALU.mult),
                r=["eq", "mngm"], w=["msc_all"])

    for e_ in range(8):
        for f in range(8):
            b.d("gpsimd", W1S[:, f * 2048:(f + 1) * 2048],
                W1d[e_ * C + f * 128: e_ * C + (f + 1) * 128, :],
                r=["W1Sr"], w=["W1S"])
        for hc in range(16):
            b.d("gpsimd", W2S[:, hc * 1024:(hc + 1) * 1024],
                W2d[e_ * 2048 + hc * 128: e_ * 2048 + (hc + 1) * 128, :],
                r=["W2Sr"], w=["W2S"])
        b.d("gpsimd", b1row[:], b1d[e_:e_ + 1, :], r=["b1rowR"], w=["b1row"])
        for g in range(NG):
            gsl = g % 2
            for c1 in range(8):
                b.d("sync", xTg[gsl][:, c1 * 512:(c1 + 1) * 512],
                    sc_xT[ts(c1), g * 512:(g + 1) * 512],
                    r=["scxT", f"xTg{gsl}R"], w=[f"xTg{gsl}"])
            for i in range(4):
                ti = g * 4 + i
                b.c("tensor", lambda e, ti=ti, e_=e_, i=i: e.matmul(
                    psT[0:1, i * 128:(i + 1) * 128],
                    msc_all[:, ti * 8 + e_: ti * 8 + e_ + 1], idn[:],
                    start=(i == 0), stop=(i == 3), is_transpose=True,
                    skip_group_check=True),
                    r=["msc_all", "idn"] + (["psTr"] if i == 0 else []), w=["psT"])
            b.c("vector", lambda e: e.tensor_copy(mscR[:], psT[0:1, 0:512]),
                r=["psT"], w=["mscR", "psTr"])
            b.c("tensor", lambda e: e.matmul(psM[:], ones1[:, 0:128], mscR[:],
                                             start=True, stop=True),
                r=["ones1", "mscR", "psMr"], w=["psM"])
            if e_ == 0:
                for i in range(4):
                    ti = g * 4 + i
                    b.c("tensor", lambda e, ti=ti, i=i: e.matmul(
                        psT[0:8, i * 128:(i + 1) * 128],
                        msc_all[:, ti * 8:(ti + 1) * 8], idn[:],
                        start=(i == 0), stop=(i == 3), is_transpose=True,
                        skip_group_check=True),
                        r=["msc_all", "idn"] + (["psTr"] if i == 0 else []), w=["psT"])
                b.c("vector", lambda e: e.tensor_copy(msc8g[:], psT[0:8, 0:512]),
                    r=["psT"], w=["msc8g", "psTr"])
            for hc in range(16):
                pw = psW[hc % 2]
                pwn = f"psW{hc % 2}"
                for f in range(8):
                    b.c("tensor", lambda e, hc=hc, f=f, pw=pw, gsl=gsl: e.matmul(
                        pw[:], W1S[:, f * 2048 + hc * 128: f * 2048 + (hc + 1) * 128],
                        xTg[gsl][:, f * 512:(f + 1) * 512],
                        start=(f == 0), stop=False, skip_group_check=True),
                        r=["W1S", f"xTg{gsl}"] + ([pwn + "r"] if f == 0 else []), w=[pwn])
                b.c("tensor", lambda e, hc=hc, pw=pw: e.matmul(
                    pw[:], b1row[:, hc * 128:(hc + 1) * 128], ones1[:],
                    start=False, stop=True, skip_group_check=True),
                    r=["b1row", "ones1"], w=[pwn])
                mg_ = mgb[hc % 2]
                mgn = f"mgb{hc % 2}"
                b.c("scalar", lambda e, pw=pw, mg_=mg_: e.activation(out=mg_[:], in_=pw[:],
                                                                     func=AF.Gelu),
                    r=[pwn, mgn + "R"], w=[mgn, pwn + "r"])
                b.c("vector", lambda e, hc=hc, mg_=mg_: e.tensor_tensor(
                    hsc[:, hc * 512:(hc + 1) * 512], mg_[:], psM[:], ALU.mult),
                    r=[mgn, "psM"], w=[f"hsc_{hc}", mgn + "R"])
            for i in range(4):
                ti = g * 4 + i
                if e_ == 0:
                    for o2 in range(2):
                        b.c("tensor", lambda e, i=i, o2=o2: e.matmul(
                            psO[:, o2 * 512:(o2 + 1) * 512], msc8g[:, i * 128:(i + 1) * 128],
                            b2S[:, o2 * 512:(o2 + 1) * 512],
                            start=True, stop=False, skip_group_check=True),
                            r=["msc8g", "b2S"] + (["psOr"] if o2 == 0 else []), w=["psO"])
                for hc in range(16):
                    for o2 in range(2):
                        b.c("tensor", lambda e, i=i, hc=hc, o2=o2: e.matmul(
                            psO[:, o2 * 512:(o2 + 1) * 512],
                            hsc[:, hc * 512 + i * 128: hc * 512 + (i + 1) * 128],
                            W2S[:, hc * 1024 + o2 * 512: hc * 1024 + (o2 + 1) * 512],
                            start=(hc == 0 and e_ != 0), stop=(hc == 15),
                            skip_group_check=True),
                            r=[f"hsc_{hc}", "W2S"] + (["psOr"] if hc == 0 and o2 == 0 and e_ != 0 else []),
                            w=["psO"])
                if e_ == 0:
                    b.c("vector", lambda e, ti=ti: e.tensor_copy(
                        out_all[:, ti * 1024:(ti + 1) * 1024], psO[:]),
                        r=["psO"], w=[f"oa_{ti}", "psOr"])
                else:
                    b.c("vector", lambda e, ti=ti: e.tensor_tensor(
                        out_all[:, ti * 1024:(ti + 1) * 1024],
                        out_all[:, ti * 1024:(ti + 1) * 1024], psO[:], ALU.add),
                        r=[f"oa_{ti}", "psO"], w=[f"oa_{ti}", "psOr"])
        b.c("sync", lambda e: e.nop(), r=["W1S", "W2S", "b1row"],
            w=["W1Sr", "W2Sr", "b1rowR"])

    for ti in range(NT):
        b.d("sync", gt[:], g3d[ts(ti), :], r=["gtR"], w=["gt"])
        b.c("vector", lambda e, ti=ti: e.tensor_tensor(
            gt[:], gt[:], out_all[:, ti * 1024:(ti + 1) * 1024], ALU.add),
            r=["gt", f"oa_{ti}"], w=["gt"])
        ln_token_tile(b, gt, gt, mn, scr, "p11ln", in_res=["gt"], out_res=["gt", "gtR"])
        b.d("sync", o_g4[ts(ti), :], gt[:], r=["gt"], w=["g4D"])
    b.replay()
    return nc


def run_p11(I, g3full, _cache={}):
    if "p11" not in _cache:
        _cache["p11"] = build_p11()
    nc = _cache["p11"]
    cs = host_consts()
    maps = []
    for c in range(8):
        bb, h = c // 2, c % 2
        maps.append({
            "g3d": np.ascontiguousarray(g3full[bb, h * NTOK:(h + 1) * NTOK]),
            "wred": np.ascontiguousarray(
                I["L1_ffn_w_red"].reshape(8, 128, 16).transpose(1, 0, 2).reshape(128, 128)),
            "wgd": I["L1_ffn_wg"],
            "W1d": np.ascontiguousarray(I["L1_ffn_W1"].reshape(8 * C, 2048)),
            "b1d": I["L1_ffn_b1"],
            "W2d": np.ascontiguousarray(I["L1_ffn_W2"].reshape(8 * 2048, C)),
            "b2d": I["L1_ffn_b2"],
            "c_idn": cs["c_idn"], "c_ones": np.ones((1, 512), np.float32),
        })
    res = run_bass_kernel_spmd(nc, maps, core_ids=list(range(8)))
    return res.results



def _full(res, key):
    return np.stack([np.asarray(res[c][key]) for c in range(8)]).reshape(B, T, -1)


def kernel(**I):
    p1 = run_p1(I)
    p2 = run_p2(I, p1)
    g1 = _full(p2, "o_g1")
    x = g1
    for l in range(2):
        pa = run_p3a(I, x, l)
        pb = run_p3b(I, pa, l)
        x = _full(pb, "o_f")
    f2 = x
    p6 = run_p6(I, g1, f2)
    p7 = run_p7(I, p6)
    p8 = run_p8(I, p6, p7)
    p9 = run_p9(I, p6, p8)
    x3 = _full(p6, "o_x3")
    p10 = run_p10(I, x3, p9)
    g3 = _full(p10, "o_g3")
    p11 = run_p11(I, g3)
    g4 = _full(p11, "o_g4")
    return g4.astype(np.float32)


# version 05
# speedup vs baseline: 1.0999x; 1.0999x over previous
import numpy as np
import concourse.bass as bass
import concourse.mybir as mybir
from concourse.bass_utils import run_bass_kernel_spmd

F32 = mybir.dt.float32
BF16 = mybir.dt.bfloat16
AF = mybir.ActivationFunctionType
ALU = mybir.AluOpType
AX = mybir.AxisListType

B, T, C = 4, 8192, 1024
NTOK = 4096
NT = NTOK // 128
NG = NTOK // 512
NFC = C // 128

COMPUTE_ENGINES = ("scalar", "vector", "tensor", "gpsimd", "sync")
DMA_QUEUES = ("sync", "gpsimd", "scalar")
NDMASEM = 8


def ts(i, n=128):
    return slice(i * n, (i + 1) * n)


class Op:
    __slots__ = ("eng", "fn", "kind", "idx", "eidx", "reads", "writes",
                 "waits", "sem", "semval")

    def __init__(self, eng, fn, kind, idx, eidx, reads, writes):
        self.eng, self.fn, self.kind = eng, fn, kind
        self.idx, self.eidx = idx, eidx
        self.reads, self.writes = reads, writes
        self.waits = []
        self.sem = None
        self.semval = None


class Builder:
    def __init__(self, nc):
        self.nc = nc
        self.ops = []
        self.eng_count = {e: 0 for e in COMPUTE_ENGINES}
        self.last_write = {}
        self.readers = {}
        self.finished = False

    def _record(self, eng, fn, kind, reads, writes):
        idx = len(self.ops)
        op = Op(eng, fn, kind, idx, self.eng_count[eng], tuple(reads), tuple(writes))
        self.eng_count[eng] += 1
        deps = set()
        for r in op.reads:
            w = self.last_write.get(r)
            if w is not None:
                deps.add(w)
        for wr in op.writes:
            w = self.last_write.get(wr)
            if w is not None:
                deps.add(w)
            for rd in self.readers.get(wr, ()):
                deps.add(rd)
        for d in sorted(deps):
            dop = self.ops[d]
            if (dop.kind == "d" or op.kind == "d" or dop.eng != eng
                    or eng in ("vector", "scalar", "gpsimd")):
                op.waits.append(d)
        for wr in op.writes:
            self.last_write[wr] = idx
            self.readers[wr] = []
        for r in op.reads:
            self.readers.setdefault(r, []).append(idx)
        self.ops.append(op)
        return op

    def c(self, eng, fn, r=(), w=()):
        return self._record(eng, fn, "c", r, w)

    def d(self, queue, out, in_, r=(), w=(), **kw):
        try:
            if out.tensor.dtype != in_.tensor.dtype:
                queue = "gpsimd"
        except AttributeError:
            pass
        fn = lambda e, o=out, i=in_, kw=kw: e.dma_start(out=o, in_=i, **kw)
        return self._record(queue, fn, "d", r, w)

    def finish(self):
        fin_deps = [op.idx for op in self.ops if op.kind == "d"]
        lastc = {}
        for op in self.ops:
            if op.kind == "c":
                lastc[op.eng] = op.idx
        fin = self._record("sync", lambda e: e.nop(), "c", (), ())
        fin.waits = sorted(set(fin_deps + [v for k, v in lastc.items() if k != "sync"]))
        self.finished = True

    def replay(self):
        if not self.finished:
            self.finish()
        nc = self.nc
        watched = set()
        for op in self.ops:
            watched.update(op.waits)
        comp_sems = {e: nc.alloc_semaphore(f"se_{e}") for e in COMPUTE_ENGINES}
        comp_val = {e: 0 for e in COMPUTE_ENGINES}
        dma_sems = {q: [nc.alloc_semaphore(f"sd_{q}{i}") for i in range(NDMASEM)]
                    for q in DMA_QUEUES}
        dma_uses = {q: [0] * NDMASEM for q in DMA_QUEUES}
        dma_rr = {q: 0 for q in DMA_QUEUES}
        for op in self.ops:
            if op.kind == "c":
                if op.idx in watched:
                    comp_val[op.eng] += 1
                    op.sem = comp_sems[op.eng]
                    op.semval = comp_val[op.eng]
            else:
                q = op.eng
                k = dma_rr[q] % NDMASEM
                dma_rr[q] += 1
                dma_uses[q][k] += 1
                op.sem = dma_sems[q][k]
                op.semval = dma_uses[q][k] * 16
        satisfied = {e: {} for e in COMPUTE_ENGINES}

        def emit_for(blk, name):
            myops = [o for o in self.ops if o.eng == name]
            if not myops:
                return

            def body(e, _ops=myops, _name=name):
                sat = satisfied[_name]
                for op in _ops:
                    for d in op.waits:
                        dop = self.ops[d]
                        if dop.sem is None:
                            continue
                        key = id(dop.sem)
                        need = dop.semval
                        if sat.get(key, 0) >= need:
                            continue
                        e.wait_ge(dop.sem, need)
                        sat[key] = need
                    inst = op.fn(e)
                    if op.kind == "c" and op.sem is not None:
                        inst.then_inc(op.sem, 1)
                    elif op.kind == "d":
                        inst.then_inc(op.sem, 16)

            getattr(blk, name)(body)

        with nc.Block(no_gpsimd_drain=True) as blk:
            for name in COMPUTE_ENGINES:
                emit_for(blk, name)



def emit_transpose_weight(b, wsrc, wT, psT, idn, tag, nchunk_o=8, nchunk_f=8):
    for o in range(nchunk_o):
        for f in range(nchunk_f):
            j = o * nchunk_f + f
            pt = psT[j % len(psT)]
            b.c("tensor",
                lambda e, o=o, f=f, pt=pt: e.transpose(
                    pt[0][:], wsrc[:, (o * nchunk_f + f) * 128:(o * nchunk_f + f + 1) * 128], idn[:]),
                r=[wsrc.name, idn.name, pt[1] + "r"], w=[pt[1]])
            b.c("vector",
                lambda e, o=o, f=f, pt=pt: e.tensor_copy(
                    wT[:, (f * nchunk_o + o) * 128:(f * nchunk_o + o + 1) * 128], pt[0][:]),
                r=[pt[1]], w=[wT.name, pt[1] + "r"])


def ln_token_tile(b, xa, g1t, mn, scr, tag, gain=None, bias_ap=None, eps=1e-5,
                  out_res=None, in_res=None, D=1024):
    rs = list(in_res or [xa.name])
    b.c("vector", lambda e: e.tensor_reduce(mn[:, 0:1], xa[:], AX.X, ALU.add),
        r=rs, w=[tag + "mn"])
    b.c("scalar", lambda e: e.activation(out=scr[:, 0:D], in_=xa[:], func=AF.Square),
        r=rs, w=[scr.name])
    b.c("vector", lambda e: e.tensor_reduce(mn[:, 1:2], scr[:, 0:D], AX.X, ALU.add),
        r=[scr.name], w=[tag + "mn"])
    b.c("vector", lambda e: e.tensor_scalar(mn[:, 0:1], mn[:, 0:1], 1.0 / D, None, ALU.mult),
        r=[tag + "mn"], w=[tag + "mn"])
    b.c("vector", lambda e: e.tensor_tensor(mn[:, 2:3], mn[:, 0:1], mn[:, 0:1], ALU.mult),
        r=[tag + "mn"], w=[tag + "mn"])
    b.c("vector", lambda e: e.tensor_scalar(mn[:, 1:2], mn[:, 1:2], 1.0 / D, None, ALU.mult),
        r=[tag + "mn"], w=[tag + "mn"])
    b.c("vector", lambda e: e.tensor_tensor(mn[:, 2:3], mn[:, 1:2], mn[:, 2:3], ALU.subtract),
        r=[tag + "mn"], w=[tag + "mn"])
    b.c("vector", lambda e: e.tensor_scalar(mn[:, 2:3], mn[:, 2:3], float(eps), None, ALU.add),
        r=[tag + "mn"], w=[tag + "mn"])
    b.c("scalar", lambda e: e.activation(out=mn[:, 3:4], in_=mn[:, 2:3], func=AF.Sqrt),
        r=[tag + "mn"], w=[tag + "sd"])
    b.c("vector", lambda e: e.reciprocal(mn[:, 3:4], mn[:, 3:4]),
        r=[tag + "sd"], w=[tag + "sd"])
    wout = [g1t.name] if out_res is None else out_res
    b.c("vector", lambda e: e.tensor_scalar(g1t[:], xa[:], mn[:, 0:1], mn[:, 3:4],
                                            ALU.subtract, ALU.mult),
        r=rs + [tag + "mn", tag + "sd"], w=wout)



def build_p1():
    nc = bass.Bass("TRN2", target_bir_lowering=False)
    x = nc.dram_tensor("x", [NTOK, C], F32, kind="ExternalInput").ap()
    Wq = nc.dram_tensor("Wq", [C, C], F32, kind="ExternalInput").ap()
    Wk = nc.dram_tensor("Wk", [C, C], F32, kind="ExternalInput").ap()
    Wv = nc.dram_tensor("Wv", [C, C], F32, kind="ExternalInput").ap()
    bqc = nc.dram_tensor("bqc", [128, 8], F32, kind="ExternalInput").ap()
    bkc = nc.dram_tensor("bkc", [128, 8], F32, kind="ExternalInput").ap()
    bvr = nc.dram_tensor("bvr", [1, C], F32, kind="ExternalInput").ap()
    c_idn = nc.dram_tensor("c_idn", [128, 128], F32, kind="ExternalInput").ap()
    c_seg = nc.dram_tensor("c_seg", [128, 8 * 16], F32, kind="ExternalInput").ap()
    c_hmap = nc.dram_tensor("c_hmap", [16, 8 * 128], F32, kind="ExternalInput").ap()
    c_ones = nc.dram_tensor("c_ones", [1, 128], F32, kind="ExternalInput").ap()
    o_qsmT = nc.dram_tensor("o_qsmT", [C, NTOK], BF16, kind="ExternalOutput").ap()
    o_ctx = nc.dram_tensor("o_ctx", [128, C], F32, kind="ExternalOutput").ap()
    o_ksum = nc.dram_tensor("o_ksum", [128, 8], F32, kind="ExternalOutput").ap()

    b = Builder(nc)
    A = nc.alloc_sbuf_tensor
    idn = A("idn", [128, 128], BF16)
    seg = A("seg", [128, 8 * 16], BF16)
    hmap = A("hmap", [16, 8 * 128], BF16)
    ones1 = A("ones1", [1, 128], BF16)
    bqv = A("bqv", [128, 8], F32)
    bkv = A("bkv", [128, 8], F32)
    bvrow = A("bvrow", [1, C], BF16)
    wload = A("wload", [128, 8 * 1024], BF16)
    WqT = A("WqT", [128, 8 * 1024], BF16)
    WkT = A("WkT", [128, 8 * 1024], BF16)
    WvT = A("WvT", [128, 8 * 1024], BF16)
    xtok = [A(f"xtok{i}", [128, 4 * 1024], BF16) for i in range(2)]
    xT = [A(f"xT{i}", [128, 8 * 512], BF16) for i in range(2)]
    qe = A("qe", [128, 8 * 512], BF16)
    ke = A("ke", [128, 8 * 512], BF16)
    rec = A("rec", [16, 2 * 512], F32)
    recb = A("recb", [16, 2 * 512], BF16)
    bcs = A("bcs", [128, 512], F32)
    qsm = A("qsm", [128, 8 * 512], BF16)
    ksmT = A("ksmT", [128, 512], BF16)
    ksm = [A(f"ksm{i}", [128, 4 * 1024], BF16) for i in range(2)]
    vtok = [A(f"vtok{i}", [128, 4 * 1024], BF16) for i in range(2)]
    ksump = A("ksump", [128, 8], F32)
    ksum1 = A("ksum1", [128, 1], F32)
    ctxs = A("ctxs", [128, 1024], F32)

    P = nc.alloc_psum_tensor
    ps_T = [(P(f"ps_T{i}", [128, 128], BF16), f"ps_T{i}") for i in range(2)]
    ps_a = [P(f"ps_a{i}", [128, 512], F32) for i in range(2)]
    ps_ss = P("ps_ss", [128, 512], F32)
    ps_bc = P("ps_bc", [128, 512], F32)
    ps_ctx = P("ps_ctx", [128, 1024], F32)

    b.d("sync", idn[:], c_idn[:], w=["idn"])
    b.d("sync", seg[:], c_seg[:], w=["seg"])
    b.d("sync", hmap[:], c_hmap[:], w=["hmap"])
    b.d("sync", ones1[:], c_ones[:], w=["ones1"])
    b.d("sync", bqv[:, 0:8], bqc[:], w=["bqv"])
    b.d("sync", bkv[:, 0:8], bkc[:], w=["bkv"])
    b.d("gpsimd", bvrow[:], bvr[:], w=["bvrow"])
    for (W, WT, nm) in ((Wq, WqT, "WqT"), (Wk, WkT, "WkT"), (Wv, WvT, "WvT")):
        for o in range(8):
            b.d("gpsimd", wload[:, o * 1024:(o + 1) * 1024], W[ts(o), :],
                w=["wload"])
        emit_transpose_weight(b, wload, WT, ps_T, idn, nm)

    for g in range(NG):
        gb = g % 2
        xk, xTk, ksmk, vk = xtok[gb], xT[gb], ksm[gb], vtok[gb]
        xkn, xTn, ksmn, vn = xk.name, xTk.name, ksmk.name, vk.name
        for i in range(4):
            b.d("gpsimd", xk[:, i * 1024:(i + 1) * 1024],
                x[g * 512 + i * 128: g * 512 + (i + 1) * 128, :],
                w=[f"{xkn}_{i}"])
        for i in range(4):
            for c1 in range(8):
                pt = ps_T[(i * 8 + c1) % 2]
                b.c("tensor",
                    lambda e, i=i, c1=c1, pt=pt, xk=xk: e.transpose(
                        pt[0][:], xk[:, i * 1024 + c1 * 128: i * 1024 + (c1 + 1) * 128], idn[:]),
                    r=[f"{xkn}_{i}", "idn", pt[1] + "r"], w=[pt[1]])
                b.c("vector",
                    lambda e, i=i, c1=c1, pt=pt, xTk=xTk: e.tensor_copy(
                        xTk[:, c1 * 512 + i * 128: c1 * 512 + (i + 1) * 128], pt[0][:]),
                    r=[pt[1]], w=[f"{xTn}_{c1}", pt[1] + "r"])
        xt_r = [f"{xTn}_{c1}" for c1 in range(8)]
        for qk in range(2):
            WT = WqT if qk == 0 else WkT
            ebuf, ebn = (qe, "qe") if qk == 0 else (ke, "ke")
            bias = bqv if qk == 0 else bkv
            bias_n = "bqv" if qk == 0 else "bkv"
            for o in range(8):
                pa = ps_a[o % 2]
                pan = f"ps_a{o % 2}"
                for f in range(8):
                    b.c("tensor",
                        lambda e, o=o, f=f, WT=WT, pa=pa, xTk=xTk: e.matmul(
                            pa[:], WT[:, f * 1024 + o * 128: f * 1024 + (o + 1) * 128],
                            xTk[:, f * 512:(f + 1) * 512],
                            start=(f == 0), stop=(f == 7)),
                        r=[WT.name, xt_r[f]] + ([pan + "r"] if f == 0 else []),
                        w=[pan])
                b.c("scalar",
                    lambda e, o=o, pa=pa, ebuf=ebuf, bias=bias: e.activation(
                        out=ebuf[:, o * 512:(o + 1) * 512], in_=pa[:],
                        func=AF.Exp, bias=bias[:, o:o + 1], scale=1.0),
                    r=[pan, bias_n], w=[f"{ebn}_{o}", pan + "r"])
            for c1 in range(8):
                b.c("tensor",
                    lambda e, c1=c1, ebuf=ebuf: e.matmul(
                        ps_ss[0:16, 0:512], seg[:, c1 * 16:(c1 + 1) * 16],
                        ebuf[:, c1 * 512:(c1 + 1) * 512],
                        start=(c1 == 0), stop=(c1 == 7)),
                    r=["seg", f"{ebn}_{c1}"] + (["ps_ssr"] if c1 == 0 else []),
                    w=["ps_ss"])
            b.c("scalar",
                lambda e, qk=qk: e.activation(out=rec[:, qk * 512:(qk + 1) * 512],
                                              in_=ps_ss[0:16, 0:512], func=AF.Ln),
                r=["ps_ss"], w=["rec", "ps_ssr"])
            b.c("scalar",
                lambda e, qk=qk: e.activation(out=rec[:, qk * 512:(qk + 1) * 512],
                                              in_=rec[:, qk * 512:(qk + 1) * 512],
                                              func=AF.Exp, scale=-1.0),
                r=["rec"], w=["rec"])
            b.c("vector",
                lambda e, qk=qk: e.tensor_copy(recb[:, qk * 512:(qk + 1) * 512],
                                               rec[:, qk * 512:(qk + 1) * 512]),
                r=["rec"], w=["recb"])
            for c1 in range(8):
                b.c("tensor",
                    lambda e, c1=c1, qk=qk: e.matmul(
                        ps_bc[:], hmap[:, c1 * 128:(c1 + 1) * 128],
                        recb[0:16, qk * 512:(qk + 1) * 512], start=True, stop=True),
                    r=["hmap", "recb", "ps_bcr"], w=["ps_bc"])
                if qk == 0:
                    b.c("vector",
                        lambda e, c1=c1: e.tensor_tensor(
                            qsm[:, c1 * 512:(c1 + 1) * 512],
                            qe[:, c1 * 512:(c1 + 1) * 512], ps_bc[:], ALU.mult),
                        r=["ps_bc", f"qe_{c1}", f"qsm_{c1}d"], w=[f"qsm_{c1}", "ps_bcr"])
                    b.d("sync", o_qsmT[ts(c1), g * 512:(g + 1) * 512],
                        qsm[:, c1 * 512:(c1 + 1) * 512],
                        r=[f"qsm_{c1}"], w=[f"qsm_{c1}d"])
                else:
                    b.c("vector",
                        lambda e, c1=c1: e.tensor_tensor(
                            ksmT[:], ke[:, c1 * 512:(c1 + 1) * 512], ps_bc[:], ALU.mult),
                        r=["ps_bc", f"ke_{c1}"], w=["ksmT", "ps_bcr"])
                    b.c("vector",
                        lambda e, c1=c1: e.tensor_reduce(
                            ksum1[:], ksmT[:], AX.X, ALU.add),
                        r=["ksmT"], w=["ksum1"])
                    if g == 0:
                        b.c("vector",
                            lambda e, c1=c1: e.tensor_copy(ksump[:, c1:c1 + 1], ksum1[:]),
                            r=["ksum1"], w=["ksump"])
                    else:
                        b.c("vector",
                            lambda e, c1=c1: e.tensor_tensor(
                                ksump[:, c1:c1 + 1], ksump[:, c1:c1 + 1], ksum1[:], ALU.add),
                            r=["ksum1", "ksump"], w=["ksump"])
                    for i2 in range(4):
                        pt = ps_T[i2 % 2]
                        b.c("tensor",
                            lambda e, c1=c1, i2=i2, pt=pt: e.transpose(
                                pt[0][:], ksmT[:, i2 * 128:(i2 + 1) * 128], idn[:]),
                            r=["ksmT", "idn", pt[1] + "r"], w=[pt[1]])
                        b.c("vector",
                            lambda e, c1=c1, i2=i2, pt=pt, ksmk=ksmk: e.tensor_copy(
                                ksmk[:, i2 * 1024 + c1 * 128: i2 * 1024 + (c1 + 1) * 128],
                                pt[0][:]),
                            r=[pt[1]], w=[f"{ksmn}_{i2}", pt[1] + "r"])
        for i in range(4):
            for o2 in range(2):
                pa = ps_a[o2]
                pan = f"ps_a{o2}"
                for f in range(8):
                    b.c("tensor",
                        lambda e, i=i, o2=o2, f=f, pa=pa, xTk=xTk: e.matmul(
                            pa[:], xTk[:, f * 512 + i * 128: f * 512 + (i + 1) * 128],
                            WvT[:, f * 1024 + o2 * 512: f * 1024 + (o2 + 1) * 512],
                            start=(f == 0), stop=False),
                        r=["WvT", xt_r[f]] + ([pan + "r"] if f == 0 else []),
                        w=[pan])
                b.c("tensor",
                    lambda e, o2=o2, pa=pa: e.matmul(
                        pa[:], ones1[:], bvrow[:, o2 * 512:(o2 + 1) * 512],
                        start=False, stop=True),
                    r=["ones1", "bvrow"], w=[pan])
                b.c("vector",
                    lambda e, i=i, o2=o2, pa=pa, vk=vk: e.tensor_copy(
                        vk[:, i * 1024 + o2 * 512: i * 1024 + (o2 + 1) * 512], pa[:]),
                    r=[pan], w=[f"{vn}_{i}", pan + "r"])
        for i in range(4):
            for c1 in range(8):
                b.c("tensor",
                    lambda e, i=i, c1=c1, g=g, ksmk=ksmk, vk=vk: e.matmul(
                        ps_ctx[:, c1 * 128:(c1 + 1) * 128],
                        ksmk[:, i * 1024 + c1 * 128: i * 1024 + (c1 + 1) * 128],
                        vk[:, i * 1024 + c1 * 128: i * 1024 + (c1 + 1) * 128],
                        start=(g == 0 and i == 0 and c1 in (0, 4)),
                        stop=(g == NG - 1 and i == 3),
                        skip_group_check=True),
                    r=[f"{ksmn}_{i}", f"{vn}_{i}"], w=["ps_ctx"])

    b.c("vector", lambda e: e.tensor_copy(ctxs[:], ps_ctx[:]), r=["ps_ctx"], w=["ctxs"])
    b.d("sync", o_ctx[:], ctxs[:], r=["ctxs"])
    b.d("sync", o_ksum[:], ksump[:], r=["ksump"])
    b.replay()
    return nc



def build_p2():
    nc = bass.Bass("TRN2", target_bir_lowering=False)
    x = nc.dram_tensor("x", [NTOK, C], F32, kind="ExternalInput").ap()
    qsmT = nc.dram_tensor("qsmT", [C, NTOK], BF16, kind="ExternalInput").ap()
    ctx0 = nc.dram_tensor("ctx0", [128, C], F32, kind="ExternalInput").ap()
    ctx1 = nc.dram_tensor("ctx1", [128, C], F32, kind="ExternalInput").ap()
    ks0 = nc.dram_tensor("ks0", [128, 8], F32, kind="ExternalInput").ap()
    ks1 = nc.dram_tensor("ks1", [128, 8], F32, kind="ExternalInput").ap()
    Wp = nc.dram_tensor("Wp", [C, C], F32, kind="ExternalInput").ap()
    bpr = nc.dram_tensor("bpr", [1, C], F32, kind="ExternalInput").ap()
    c_idn = nc.dram_tensor("c_idn", [128, 128], F32, kind="ExternalInput").ap()
    c_seg = nc.dram_tensor("c_seg", [128, 8 * 16], F32, kind="ExternalInput").ap()
    c_hmap = nc.dram_tensor("c_hmap", [16, 8 * 128], F32, kind="ExternalInput").ap()
    c_ones = nc.dram_tensor("c_ones", [1, 128], F32, kind="ExternalInput").ap()
    c_bd64 = nc.dram_tensor("c_bd64", [128, 128], F32, kind="ExternalInput").ap()
    o_g1 = nc.dram_tensor("o_g1", [NTOK, C], F32, kind="ExternalOutput").ap()

    b = Builder(nc)
    A = nc.alloc_sbuf_tensor
    idn = A("idn", [128, 128], BF16)
    seg = A("seg", [128, 8 * 16], BF16)
    hmap = A("hmap", [16, 8 * 128], BF16)
    ones1 = A("ones1", [1, 128], BF16)
    bd64 = A("bd64", [128, 128], F32)
    bprow = A("bprow", [1, C], BF16)
    wload = A("wload", [128, 8 * 1024], BF16)
    WpT = A("WpT", [128, 8 * 1024], BF16)
    ctxa = A("ctxa", [128, 1024], F32)
    ctxb2 = A("ctxb2", [128, 1024], F32)
    ctxb = A("ctxb", [128, 1024], BF16)
    ksm = A("ksm", [128, 8], F32)
    ks1s = A("ks1s", [128, 8], F32)
    qs = [A(f"qs{i}", [128, 8 * 512], BF16) for i in range(2)]
    xg = [A(f"xg{i}", [128, 4 * 1024], F32) for i in range(2)]
    tsq = A("tsq", [128, 512], BF16)
    dinv = A("dinv", [16, 512], F32)
    dinvb = A("dinvb", [16, 512], BF16)
    bcs = A("bcs", [128, 512], F32)
    outT = A("outT", [128, 8 * 512], BF16)
    xa = A("xa", [128, 1024], F32)
    scr = A("scr", [128, 1024], F32)
    mn = A("mn", [128, 4], F32)
    g1t = A("g1t", [128, 1024], F32)

    P = nc.alloc_psum_tensor
    ps_T = [(P(f"ps_T{i}", [128, 128], BF16), f"ps_T{i}") for i in range(2)]
    ps_a = [P(f"ps_a{i}", [128, 512], F32) for i in range(2)]
    ps_ss = P("ps_ss", [128, 512], F32)
    ps_bc = P("ps_bc", [128, 512], F32)
    ps_r = P("ps_r", [128, 1024], F32)

    b.d("sync", idn[:], c_idn[:], w=["idn"])
    b.d("sync", seg[:], c_seg[:], w=["seg"])
    b.d("sync", hmap[:], c_hmap[:], w=["hmap"])
    b.d("sync", ones1[:], c_ones[:], w=["ones1"])
    b.d("sync", bd64[:], c_bd64[:], w=["bd64"])
    b.d("gpsimd", bprow[:], bpr[:], w=["bprow"])
    b.d("sync", ctxa[:], ctx0[:], w=["ctxa"])
    b.d("sync", ctxb2[:], ctx1[:], w=["ctxb2"])
    b.d("sync", ksm[:], ks0[:], w=["ksm"])
    b.d("sync", ks1s[:], ks1[:], w=["ks1s"])
    for o in range(8):
        b.d("gpsimd", wload[:, o * 1024:(o + 1) * 1024], Wp[ts(o), :], w=["wload"])
    emit_transpose_weight(b, wload, WpT, ps_T, idn, "WpT")
    b.c("vector", lambda e: e.tensor_tensor(ctxa[:], ctxa[:], ctxb2[:], ALU.add),
        r=["ctxa", "ctxb2"], w=["ctxa"])
    for c1 in range(8):
        b.c("vector", lambda e, c1=c1: e.tensor_tensor(
            ctxb[:, c1 * 128:(c1 + 1) * 128], ctxa[:, c1 * 128:(c1 + 1) * 128],
            bd64[:], ALU.mult),
            r=["ctxa", "bd64"], w=["ctxb"])
    b.c("vector", lambda e: e.tensor_tensor(ksm[:], ksm[:], ks1s[:], ALU.add),
        r=["ksm", "ks1s"], w=["ksm"])

    for g in range(NG):
        gb = g % 2
        q = qs[gb]
        qn = q.name
        xgk = xg[gb]
        xgn = xgk.name
        for c1 in range(8):
            b.d("sync", q[:, c1 * 512:(c1 + 1) * 512],
                qsmT[ts(c1), g * 512:(g + 1) * 512], w=[f"{qn}_{c1}"])
        for i in range(4):
            b.d("gpsimd", xgk[:, i * 1024:(i + 1) * 1024],
                x[g * 512 + i * 128:g * 512 + (i + 1) * 128, :], w=[f"{xgn}_{i}"])
        for c1 in range(8):
            b.c("vector", lambda e, c1=c1, q=q: e.tensor_scalar(
                tsq[:], q[:, c1 * 512:(c1 + 1) * 512], ksm[:, c1:c1 + 1], None, ALU.mult),
                r=[f"{qn}_{c1}", "ksm"], w=["tsq"])
            b.c("tensor", lambda e, c1=c1: e.matmul(
                ps_ss[0:16, 0:512], seg[:, c1 * 16:(c1 + 1) * 16], tsq[:],
                start=(c1 == 0), stop=(c1 == 7)),
                r=["seg", "tsq"] + (["ps_ssr"] if c1 == 0 else []), w=["ps_ss"])
        b.c("scalar", lambda e: e.activation(out=dinv[:], in_=ps_ss[0:16, 0:512], func=AF.Ln),
            r=["ps_ss"], w=["dinv", "ps_ssr"])
        b.c("scalar", lambda e: e.activation(out=dinv[:], in_=dinv[:], func=AF.Exp, scale=-1.0),
            r=["dinv"], w=["dinv"])
        b.c("vector", lambda e: e.tensor_copy(dinvb[:], dinv[:]), r=["dinv"], w=["dinvb"])
        for c1 in range(8):
            pa = ps_a[c1 % 2]
            pan = f"ps_a{c1 % 2}"
            b.c("tensor", lambda e, c1=c1, pa=pa, q=q: e.matmul(
                pa[:], ctxb[:, c1 * 128:(c1 + 1) * 128], q[:, c1 * 512:(c1 + 1) * 512],
                start=True, stop=True),
                r=["ctxb", f"{qn}_{c1}", pan + "r"], w=[pan])
            b.c("tensor", lambda e, c1=c1: e.matmul(
                ps_bc[:], hmap[:, c1 * 128:(c1 + 1) * 128], dinvb[:],
                start=True, stop=True),
                r=["hmap", "dinvb", "ps_bcr"], w=["ps_bc"])
            b.c("vector", lambda e: e.tensor_copy(bcs[:], ps_bc[:]),
                r=["ps_bc"], w=["bcs", "ps_bcr"])
            b.c("vector", lambda e, c1=c1, pa=pa: e.tensor_tensor(
                outT[:, c1 * 512:(c1 + 1) * 512], pa[:], bcs[:], ALU.mult),
                r=[pan, "bcs"], w=[f"outT_{c1}", pan + "r"])
            b.c("vector", lambda e, c1=c1, q=q: e.tensor_tensor(
                outT[:, c1 * 512:(c1 + 1) * 512], outT[:, c1 * 512:(c1 + 1) * 512],
                q[:, c1 * 512:(c1 + 1) * 512], ALU.add),
                r=[f"outT_{c1}", f"{qn}_{c1}"], w=[f"outT_{c1}"])
        for i in range(4):
            for c1 in range(8):
                b.c("tensor", lambda e, i=i, c1=c1: e.matmul(
                    ps_r[:, 0:512],
                    outT[:, c1 * 512 + i * 128: c1 * 512 + (i + 1) * 128],
                    WpT[:, c1 * 1024: c1 * 1024 + 512],
                    start=(c1 == 0), stop=False, skip_group_check=True),
                    r=[f"outT_{c1}", "WpT"] + (["ps_rr"] if c1 == 0 else []), w=["ps_r"])
                b.c("tensor", lambda e, i=i, c1=c1: e.matmul(
                    ps_r[:, 512:1024],
                    outT[:, c1 * 512 + i * 128: c1 * 512 + (i + 1) * 128],
                    WpT[:, c1 * 1024 + 512: (c1 + 1) * 1024],
                    start=(c1 == 0), stop=False, skip_group_check=True),
                    r=[f"outT_{c1}", "WpT"], w=["ps_r"])
            b.c("tensor", lambda e: e.matmul(
                ps_r[:, 0:512], ones1[:], bprow[:, 0:512],
                start=False, stop=True, skip_group_check=True),
                r=["ones1", "bprow"], w=["ps_r"])
            b.c("tensor", lambda e: e.matmul(
                ps_r[:, 512:1024], ones1[:], bprow[:, 512:1024],
                start=False, stop=True, skip_group_check=True),
                r=["ones1", "bprow"], w=["ps_r"])
            b.c("vector", lambda e, i=i, xgk=xgk: e.tensor_tensor(
                xa[:], ps_r[:], xgk[:, i * 1024:(i + 1) * 1024], ALU.add),
                r=["ps_r", f"{xgn}_{i}"], w=["xa", "ps_rr"])
            ln_token_tile(b, xa, g1t, mn, scr, "p2ln")
            b.d("sync", o_g1[g * 512 + i * 128: g * 512 + (i + 1) * 128, :], g1t[:],
                r=["g1t"], w=["g1td"])
    b.replay()
    return nc



def host_consts():
    cs = {}
    cs["c_idn"] = np.eye(128, dtype=np.float32)
    seg = np.zeros((8, 128, 16), dtype=np.float32)
    hm = np.zeros((8, 16, 128), dtype=np.float32)
    for c in range(8):
        for p in range(128):
            hg = 2 * c + p // 64
            seg[c, p, hg] = 1.0
            hm[c, hg, p] = 1.0
    cs["c_seg"] = np.ascontiguousarray(seg.transpose(1, 0, 2).reshape(128, 8 * 16))
    cs["c_hmap"] = np.ascontiguousarray(hm.transpose(1, 0, 2).reshape(16, 8 * 128))
    m = np.zeros((128, 128), dtype=np.float32)
    m[:64, :64] = 1.0
    m[64:, 64:] = 1.0
    cs["c_bd64"] = m
    cs["c_ones"] = np.ones((1, 128), dtype=np.float32)
    return cs


def run_p1(I):
    cs = host_consts()
    nc = build_p1()
    x = I["L0_attn_x"]
    maps = []
    for c in range(8):
        bb, h = c // 2, c % 2
        maps.append({
            "x": np.ascontiguousarray(x[bb, h * NTOK:(h + 1) * NTOK]),
            "Wq": I["L0_attn_Wq"], "Wk": I["L0_attn_Wk"], "Wv": I["L0_attn_Wv"],
            "bqc": np.ascontiguousarray(I["L0_attn_bq"].reshape(8, 128).T),
            "bkc": np.ascontiguousarray(I["L0_attn_bk"].reshape(8, 128).T),
            "bvr": I["L0_attn_bv"].reshape(1, C),
            "c_idn": cs["c_idn"], "c_seg": cs["c_seg"], "c_hmap": cs["c_hmap"],
            "c_ones": cs["c_ones"],
        })
    res = run_bass_kernel_spmd(nc, maps, core_ids=list(range(8)))
    return res.results


def run_p2(I, p1res):
    cs = host_consts()
    nc = build_p2()
    x = I["L0_attn_x"]
    maps = []
    for c in range(8):
        bb, h = c // 2, c % 2
        peer = bb * 2 + (1 - h)
        maps.append({
            "x": np.ascontiguousarray(x[bb, h * NTOK:(h + 1) * NTOK]),
            "qsmT": p1res[c]["o_qsmT"],
            "ctx0": p1res[c]["o_ctx"], "ctx1": p1res[peer]["o_ctx"],
            "ks0": p1res[c]["o_ksum"], "ks1": p1res[peer]["o_ksum"],
            "Wp": I["L0_attn_Wp"], "bpr": I["L0_attn_bp"].reshape(1, C),
            "c_idn": cs["c_idn"], "c_seg": cs["c_seg"], "c_hmap": cs["c_hmap"],
            "c_ones": cs["c_ones"], "c_bd64": cs["c_bd64"],
        })
    res = run_bass_kernel_spmd(nc, maps, core_ids=list(range(8)))
    return res.results


def kernel(**inputs):
    raise NotImplementedError



W0F = 512
NW = 8


def build_p3a():
    nc = bass.Bass("TRN2", target_bir_lowering=False)
    x = nc.dram_tensor("x", [NTOK, C], F32, kind="ExternalInput").ap()
    xhal = nc.dram_tensor("xhal", [514, C], F32, kind="ExternalInput").ap()
    xprev = nc.dram_tensor("xprev", [128, 8], F32, kind="ExternalInput").ap()
    wqkv = nc.dram_tensor("wqkv", [C, 1536], F32, kind="ExternalInput").ap()
    wo = nc.dram_tensor("wo", [512, C], F32, kind="ExternalInput").ap()
    qsc = nc.dram_tensor("qsc", [128, 1], F32, kind="ExternalInput").ap()
    ksc = nc.dram_tensor("ksc", [128, 1], F32, kind="ExternalInput").ap()
    tabs = nc.dram_tensor("tabs", [128, 4 * 4608], F32, kind="ExternalInput").ap()
    c_idn = nc.dram_tensor("c_idn", [128, 128], F32, kind="ExternalInput").ap()
    c_ones = nc.dram_tensor("c_ones", [1, 128], F32, kind="ExternalInput").ap()
    c_seg8 = nc.dram_tensor("c_seg8", [128, 4 * 8], F32, kind="ExternalInput").ap()
    c_hm8 = nc.dram_tensor("c_hm8", [8, 4 * 128], F32, kind="ExternalInput").ap()
    c_rot = nc.dram_tensor("c_rot", [128, 128], F32, kind="ExternalInput").ap()
    c_fb = nc.dram_tensor("c_fb", [128, 640], F32, kind="ExternalInput").ap()
    c_w0x = nc.dram_tensor("c_w0x", [1, 1024], F32, kind="ExternalInput").ap()
    o_xmid = nc.dram_tensor("o_xmid", [NTOK, C], F32, kind="ExternalOutput").ap()

    b = Builder(nc)
    A = nc.alloc_sbuf_tensor
    idn = A("idn", [128, 128], BF16)
    ones1 = A("ones1", [1, 128], BF16)
    seg8 = A("seg8", [128, 4 * 8], F32)
    hm8 = A("hm8", [8, 4 * 128], F32)
    rotm = A("rotm", [128, 128], F32)
    fb = A("fb", [128, 640], F32)
    w0x = A("w0x", [1, 1024], BF16)
    qscv = A("qscv", [128, 1], F32)
    kscv = A("kscv", [128, 1], F32)
    wk = A("wk", [128, 8 * 1536], BF16)
    woS = A("woS", [128, 4 * 1024], BF16)
    tabw = [A(f"tabw{i}", [128, 4 * 512], F32) for i in range(2)]
    xw = [A(f"xw{i}", [128, 4 * 1024], F32) for i in range(2)]
    xwb = A("xwb", [128, 4 * 1024], BF16)
    xT = A("xT", [128, 8 * 640], BF16)
    xpv = A("xpv", [128, 8], BF16)
    qT = A("qT", [128, 4 * 512], BF16)
    kvtmp = A("kvtmp", [128, 4 * 512], F32)
    kT = A("kT", [128, 4 * 1024], BF16)
    vr = A("vr", [128, 8 * 512], BF16)
    nrm = [A(f"nrm{i}", [8, 512], F32) for i in range(2)]
    bcs = A("bcs", [128, 512], F32)
    t1b_ = [A(f"t1b{i}", [128, 512], F32) for i in range(2)]
    t2b_ = [A(f"t2b{i}", [128, 512], F32) for i in range(2)]
    Ssb = A("Ssb", [128, 8 * 640], F32)
    nmw = A("nmw", [128, 16], F32)
    Pb = A("Pb", [128, 8 * 640], BF16)
    PT = [A(f"PT{i}", [128, 5 * 128], BF16) for i in range(2)]
    aoT = A("aoT", [128, 512], BF16)
    xmid = A("xmid", [128, 1024], F32)

    P = nc.alloc_psum_tensor
    psScT = [P(f"psSc{i}", [128, 640], F32) for i in range(2)]
    psSc = [psScT[0], psScT[1], psScT[0], psScT[1]]
    psScN = ["psSc0", "psSc1", "psSc0", "psSc1"]
    psC = P("psC", [128, 512], F32)
    psT0 = P("psT", [128, 640], BF16)
    psTl = [(psT0, "psT"), (psT0, "psT")]
    psA = P("psA", [128, 1024], F32)

    b.d("sync", idn[:], c_idn[:], w=["idn"])
    b.d("sync", ones1[:], c_ones[:], w=["ones1"])
    b.d("sync", seg8[:], c_seg8[:], w=["seg8"])
    b.d("sync", hm8[:], c_hm8[:], w=["hm8"])
    b.d("sync", rotm[:], c_rot[:], w=["rotm"])
    b.d("sync", fb[:], c_fb[:], w=["fb"])
    b.d("sync", w0x[:], c_w0x[:], w=["w0x"])
    b.d("sync", qscv[:], qsc[:], w=["qscv"])
    b.d("sync", kscv[:], ksc[:], w=["kscv"])
    b.d("sync", xpv[:], xprev[:], w=["xpv"])
    for f in range(8):
        b.d("gpsimd", wk[:, f * 1536:(f + 1) * 1536], wqkv[ts(f), :], w=["wk"])
    for i in range(4):
        b.d("gpsimd", woS[:, i * 1024:(i + 1) * 1024], wo[ts(i), :], w=["woS"])

    def emit_qkv_window(wi, wslot):
        real = wi >= 0
        xk = xw[wslot]
        xkn = xk.name
        tw = tabw[wslot]
        twn = tw.name
        for c1 in range(8):
            b.c("gpsimd", lambda e, c1=c1: e.tensor_copy(
                xT[:, c1 * 640:c1 * 640 + 128], xT[:, c1 * 640 + 512:c1 * 640 + 640]),
                r=[f"xT_{c1}"], w=[f"xT_{c1}"])
        for c1 in range(4):
            b.c("gpsimd", lambda e, c1=c1: e.tensor_copy(
                kT[:, c1 * 1024:c1 * 1024 + 512], kT[:, c1 * 1024 + 512:(c1 + 1) * 1024]),
                r=[f"kT_{c1}"], w=[f"kT_{c1}"])
        for i in range(4):
            b.c("gpsimd", lambda e, i=i: e.tensor_copy(
                vr[:, i * 512:(i + 1) * 512], vr[:, (4 + i) * 512:(5 + i) * 512]),
                r=[f"vr_{i}", f"vr_{4 + i}"], w=[f"vr_{i}"])
        for i in range(4):
            if wi < 0:
                src = xhal[2 + i * 128: 2 + (i + 1) * 128, :]
            else:
                src = x[wi * 512 + i * 128: wi * 512 + (i + 1) * 128, :]
            b.d("gpsimd", xk[:, i * 1024:(i + 1) * 1024], src, w=[f"{xkn}_{i}"])
        for t4 in range(4):
            b.d("gpsimd", tw[:, t4 * 512:(t4 + 1) * 512],
                tabs[:, t4 * 4608 + (wi + 1) * 512: t4 * 4608 + (wi + 2) * 512],
                w=[twn])
        for i in range(4):
            b.c("vector", lambda e, i=i, xk=xk: e.tensor_copy(
                xwb[:, i * 1024:(i + 1) * 1024], xk[:, i * 1024:(i + 1) * 1024]),
                r=[f"{xkn}_{i}"], w=[f"xwb_{i}"])
            for g4 in range(2):
                psTc, psTn = psTl[g4 % 2]
                for jj in range(4):
                    c1 = g4 * 4 + jj
                    b.c("tensor", lambda e, i=i, c1=c1, jj=jj, psTc=psTc: e.matmul(
                        psTc[:, jj * 128:(jj + 1) * 128],
                        xwb[:, i * 1024 + c1 * 128: i * 1024 + (c1 + 1) * 128], idn[:],
                        start=(jj == 0), stop=(jj == 3), is_transpose=True,
                        skip_group_check=True),
                        r=[f"xwb_{i}", "idn"] + ([psTn + "r"] if jj == 0 else []), w=[psTn])
                for jj in range(4):
                    c1 = g4 * 4 + jj
                    b.c("vector", lambda e, i=i, c1=c1, jj=jj, psTc=psTc: e.tensor_copy(
                        xT[:, c1 * 640 + 128 + i * 128: c1 * 640 + 128 + (i + 1) * 128],
                        psTc[:, jj * 128:(jj + 1) * 128]),
                        r=[psTn], w=[f"xT_{c1}"] + ([psTn + "r"] if jj == 3 else []))
        if wi == -1:
            for c1 in range(8):
                b.c("vector", lambda e, c1=c1: e.tensor_copy(
                    xT[:, c1 * 640 + 127: c1 * 640 + 128], xpv[:, c1:c1 + 1]),
                    r=["xpv"], w=[f"xT_{c1}"])
        xt_r = [f"xT_{c1}" for c1 in range(8)]

        def proj_part(dst, dstn, col0, oc_n, token_major):
            for oc in range(oc_n):
                for tile_or_none in ([None] if not token_major else range(4)):
                    if not token_major:
                        par = oc % 2
                        pd = psA[:, par * 512:(par + 1) * 512]
                        pdn = f"psA{par}"
                        for f in range(8):
                            sh = 127 if f >= 4 else 128
                            b.c("tensor", lambda e, oc=oc, f=f, sh=sh, col0=col0, pd=pd: e.matmul(
                                pd, wk[:, f * 1536 + col0 + oc * 128: f * 1536 + col0 + (oc + 1) * 128],
                                xT[:, f * 640 + sh: f * 640 + sh + 512],
                                start=(f == 0), stop=(f == 7)),
                                r=["wk", xt_r[f]] + ([pdn + "r"] if f == 0 else []), w=[pdn])
                        b.c("vector", lambda e, oc=oc, dst=dst, pd=pd: e.tensor_copy(
                            dst[:, oc * 512:(oc + 1) * 512], pd),
                            r=[pdn], w=[f"{dstn}_{oc}", pdn + "r"])
                    else:
                        i = tile_or_none
                        par = i % 2
                        pd = psA[:, par * 512:(par + 1) * 512]
                        pdn = f"psA{par}"
                        for f in range(8):
                            sh = 127 if f >= 4 else 128
                            b.c("tensor", lambda e, i=i, f=f, sh=sh, col0=col0, pd=pd: e.matmul(
                                pd, xT[:, f * 640 + sh + i * 128: f * 640 + sh + (i + 1) * 128],
                                wk[:, f * 1536 + col0: f * 1536 + col0 + 512],
                                start=(f == 0), stop=(f == 7)),
                                r=["wk", xt_r[f]] + ([pdn + "r"] if f == 0 else []), w=[pdn])
                        b.c("vector", lambda e, i=i, dst=dst, pd=pd: e.tensor_copy(
                            dst[:, (4 + i) * 512:(5 + i) * 512], pd),
                            r=[pdn], w=[f"{dstn}_{4 + i}", pdn + "r"])

        proj_part(vr, "vr", 1024, 1, True)
        proj_part(kvtmp, "kvtmp", 512, 4, False)

        def l2rot(srcbuf, srcn, scv, scvn, tab0, dst_fn, post=None):
            for c1 in range(4):
                p2 = c1 % 2
                t1b, t1n = t1b_[p2], f"t1b{p2}"
                t2b, t2n = t2b_[p2], f"t2b{p2}"
                nr, nrn = nrm[p2], f"nrm{p2}"
                pb1, pb1n = (psScT[0][:, 0:512], "psSc0") if c1 % 2 == 0 else (psScT[1][:, 0:512], "psSc1")
                pb2, pb2n = (psScT[1][:, 0:512], "psSc1") if c1 % 2 == 0 else (psScT[0][:, 0:512], "psSc0")
                b.c("scalar", lambda e, c1=c1, srcbuf=srcbuf, t1b=t1b: e.activation(
                    out=t1b[:], in_=srcbuf[:, c1 * 512:(c1 + 1) * 512], func=AF.Square),
                    r=[f"{srcn}_{c1}", t1n + "R"], w=[t1n])
                b.c("tensor", lambda e, c1=c1, t1b=t1b, pb1=pb1: e.matmul(
                    pb1[0:8, 0:512], seg8[:, c1 * 8:(c1 + 1) * 8], t1b[:],
                    start=True, stop=True),
                    r=["seg8", t1n, pb1n + "r"], w=[pb1n, t1n + "R"])
                b.c("vector", lambda e, nr=nr, pb1=pb1: e.tensor_scalar(
                    nr[:], pb1[0:8, 0:512], 1e-24, None, ALU.max),
                    r=[pb1n], w=[nrn, pb1n + "r"])
                b.c("scalar", lambda e, nr=nr: e.activation(out=nr[:], in_=nr[:], func=AF.Ln),
                    r=[nrn], w=[nrn])
                b.c("scalar", lambda e, nr=nr: e.activation(out=nr[:], in_=nr[:], func=AF.Exp,
                                                            scale=-0.5),
                    r=[nrn], w=[nrn])
                b.c("tensor", lambda e, c1=c1, nr=nr, pb2=pb2: e.matmul(
                    pb2[:, 0:512], hm8[:, c1 * 128:(c1 + 1) * 128], nr[:],
                    start=True, stop=True),
                    r=["hm8", nrn, pb2n + "r"], w=[pb2n])
                b.c("vector", lambda e, c1=c1, srcbuf=srcbuf, t1b=t1b, pb2=pb2: e.tensor_tensor(
                    t1b[:], srcbuf[:, c1 * 512:(c1 + 1) * 512], pb2[:, 0:512], ALU.mult),
                    r=[f"{srcn}_{c1}", pb2n], w=[t1n, pb2n + "r"])
                b.c("vector", lambda e, scv=scv, post=post, t1b=t1b: e.tensor_scalar(
                    t1b[:], t1b[:], scv[:], post, ALU.mult,
                    op1=(ALU.bypass if post is None else ALU.mult)),
                    r=[t1n, scvn], w=[t1n])
                b.c("tensor", lambda e, t1b=t1b, pb1=pb1: e.matmul(pb1[:, 0:512], rotm[:], t1b[:],
                                                                   start=True, stop=True),
                    r=["rotm", t1n, pb1n + "r"], w=[pb1n])
                b.c("vector", lambda e, tab0=tab0, tw=tw, t1b=t1b, t2b=t2b: e.tensor_tensor(
                    t2b[:], t1b[:], tw[:, tab0 * 512:(tab0) * 512 + 512], ALU.mult),
                    r=[t1n, twn, t2n + "R"], w=[t2n])
                b.c("vector", lambda e, tab0=tab0, tw=tw, t1b=t1b, pb1=pb1: e.tensor_tensor(
                    t1b[:], pb1[:, 0:512], tw[:, (tab0 + 1) * 512:(tab0 + 2) * 512], ALU.mult),
                    r=[pb1n, twn], w=[t1n, pb1n + "r"])
                dst_fn(c1, t1b, t1n, t2b, t2n)

        def qdst(c1, t1b, t1n, t2b, t2n):
            b.c("vector", lambda e, c1=c1, t1b=t1b, t2b=t2b: e.tensor_tensor(
                qT[:, c1 * 512:(c1 + 1) * 512], t2b[:], t1b[:], ALU.add),
                r=[t1n, t2n], w=[f"qT_{c1}", t1n + "R", t2n + "R"])

        def kdst(c1, t1b, t1n, t2b, t2n):
            b.c("vector", lambda e, c1=c1, t1b=t1b, t2b=t2b: e.tensor_tensor(
                kT[:, c1 * 1024 + 512:(c1 + 1) * 1024], t2b[:], t1b[:], ALU.add),
                r=[t1n, t2n], w=[f"kT_{c1}", t1n + "R", t2n + "R"])

        l2rot(kvtmp, "kvtmp", kscv, "kscv", 2, kdst)
        if real:
            proj_part(kvtmp, "kvtmp", 0, 4, False)
            l2rot(kvtmp, "kvtmp", qscv, "qscv", 0, qdst, post=8.0)

    def emit_attn_window(wi, wslot):
        xk = xw[wslot]
        xkn = xk.name
        def front(qt):
            o = qt * 128
            for h in range(8):
                c1, half = h // 2, h % 2
                pS, pSn = psScT[h % 2], f"psSc{h % 2}"
                b.c("tensor", lambda e, c1=c1, half=half, o=o, pS=pS: e.matmul(
                    pS[:, 0:512],
                    qT[:, c1 * 512 + o + 0:c1 * 512 + o + 128][ts(half, 64), :],
                    kT[:, c1 * 1024 + o: c1 * 1024 + o + 512][ts(half, 64), :],
                    start=True, stop=True),
                    r=[f"qT_{c1}", f"kT_{c1}", pSn + "r"], w=[pSn])
                b.c("tensor", lambda e, c1=c1, half=half, o=o, pS=pS: e.matmul(
                    pS[:, 512:640],
                    qT[:, c1 * 512 + o:c1 * 512 + o + 128][ts(half, 64), :],
                    kT[:, c1 * 1024 + o + 512: c1 * 1024 + o + 640][ts(half, 64), :],
                    start=True, stop=True),
                    r=[f"qT_{c1}", f"kT_{c1}"], w=[pSn])
                if wi == 0:
                    b.c("tensor", lambda e, o=o, pS=pS: e.matmul(
                        pS[:, 0:512], ones1[:], w0x[:, o:o + 512],
                        start=False, stop=True, skip_group_check=True),
                        r=["ones1", "w0x"], w=[pSn])
                    b.c("tensor", lambda e, o=o, pS=pS: e.matmul(
                        pS[:, 512:640], ones1[:], w0x[:, o + 512:o + 640],
                        start=False, stop=True, skip_group_check=True),
                        r=["ones1", "w0x"], w=[pSn])
                b.c("vector", lambda e, h=h, pS=pS: e.tensor_tensor(
                    Ssb[:, h * 640:(h + 1) * 640], pS[:], fb[:], ALU.add),
                    r=[pSn, "fb", f"SsbR{h}"], w=[f"Ssb_{h}", pSn + "r"])
            b.c("vector", lambda e: e.tensor_reduce(
                nmw[:, 0:8], bass.AP(Ssb, 0, [[5120, 128], [640, 8], [1, 640]]),
                AX.X, ALU.max, negate=True),
                r=[f"Ssb_{h2}" for h2 in range(8)], w=["nmx"])

        def back1(qt):
            for h in range(8):
                b.c("scalar", lambda e, h=h: e.activation(
                    out=Pb[:, h * 640:(h + 1) * 640], in_=Ssb[:, h * 640:(h + 1) * 640],
                    func=AF.Exp, bias=nmw[:, h:h + 1], scale=1.0,
                    accum_out=nmw[:, 8 + h:9 + h]),
                    r=[f"Ssb_{h}", "nmx", f"PbR{h}"], w=[f"Pb_{h}", "nms", f"SsbR{h}"])
            b.c("vector", lambda e: e.reciprocal(nmw[:, 8:16], nmw[:, 8:16]),
                r=["nms"], w=["nms"])

        def back2(qt):
            for h in range(8):
                c1, half = h // 2, h % 2
                pt_, ptn = PT[h % 2], f"PT{h % 2}"
                b.c("vector", lambda e, h=h: e.tensor_scalar(
                    Pb[:, h * 640:(h + 1) * 640], Pb[:, h * 640:(h + 1) * 640],
                    nmw[:, 8 + h:9 + h], None, ALU.mult),
                    r=[f"Pb_{h}", "nms"], w=[f"Pb_{h}"])
                for pc in range(5):
                    b.c("tensor", lambda e, h=h, pc=pc: e.matmul(
                        psT0[:, pc * 128:(pc + 1) * 128],
                        Pb[:, h * 640 + pc * 128: h * 640 + (pc + 1) * 128], idn[:],
                        start=(pc == 0), stop=(pc == 4), is_transpose=True,
                        skip_group_check=True),
                        r=[f"Pb_{h}", "idn"] + (["psTr"] if pc == 0 else []),
                        w=["psT"] + ([f"PbR{h}"] if pc == 4 else []))
                b.c("scalar", lambda e, pt_=pt_: e.activation(
                    out=pt_[:, 0:640], in_=psT0[:, 0:640], func=AF.Copy),
                    r=["psT", ptn + "w"], w=[ptn, "psTr"])
                pair = c1
                pbase = 64 * half
                for pc in range(5):
                    vtile = qt + pc
                    b.c("tensor", lambda e, pc=pc, vtile=vtile, h=h, pair=pair, pbase=pbase, pt_=pt_: e.matmul(
                        psC[pbase:pbase + 64, pair * 128:(pair + 1) * 128],
                        vr[:, vtile * 512 + h * 64: vtile * 512 + (h + 1) * 64],
                        pt_[:, pc * 128:(pc + 1) * 128],
                        start=(pc == 0), stop=(pc == 4), skip_group_check=True),
                        r=[f"vr_{vtile}", ptn] + (["psCr"] if pc == 0 else []),
                        w=["psC"] + ([ptn + "w"] if pc == 4 else []))
            b.c("vector", lambda e: e.tensor_copy(aoT[:], psC[:]), r=["psC"], w=["aoT", "psCr"])
            for p2 in range(4):
                for o2 in range(2):
                    b.c("tensor", lambda e, p2=p2, o2=o2: e.matmul(
                        psA[:, o2 * 512:(o2 + 1) * 512],
                        aoT[:, p2 * 128:(p2 + 1) * 128],
                        woS[:, p2 * 1024 + o2 * 512: p2 * 1024 + (o2 + 1) * 512],
                        start=(p2 == 0), stop=(p2 == 3), skip_group_check=True),
                        r=["aoT", "woS"] + ([f"psA{o2}r"] if p2 == 0 else []), w=[f"psA{o2}"])
            b.c("vector", lambda e, qt=qt: e.tensor_tensor(
                xmid[:], psA[:], xw[wslot][:, qt * 1024:(qt + 1) * 1024], ALU.add),
                r=["psA0", "psA1", f"{xkn}_{qt}"], w=["xmid", "psA0r", "psA1r"])
            b.d("sync", o_xmid[wi * 512 + qt * 128: wi * 512 + (qt + 1) * 128, :], xmid[:],
                r=["xmid"], w=["xmidd"])

        front(0)
        for qt in range(4):
            back1(qt)
            if qt + 1 < 4:
                front(qt + 1)
            back2(qt)

    emit_qkv_window(-1, 0)
    for wi in range(NW):
        emit_qkv_window(wi, (wi + 1) % 2)
        emit_attn_window(wi, (wi + 1) % 2)
    b.replay()
    return nc


def host_consts_p3a():
    cs = {}
    cs["c_idn"] = np.eye(128, dtype=np.float32)
    cs["c_ones"] = np.ones((1, 128), dtype=np.float32)
    seg = np.zeros((4, 128, 8), dtype=np.float32)
    hm = np.zeros((4, 8, 128), dtype=np.float32)
    for c in range(4):
        for p in range(128):
            h = 2 * c + p // 64
            seg[c, p, h] = 1.0
            hm[c, h, p] = 1.0
    cs["c_seg8"] = np.ascontiguousarray(seg.transpose(1, 0, 2).reshape(128, 32))
    cs["c_hm8"] = np.ascontiguousarray(hm.transpose(1, 0, 2).reshape(8, 512))
    rt = np.zeros((128, 128), dtype=np.float32)
    for blk in range(2):
        o = blk * 64
        for j in range(32):
            rt[o + j + 32, o + j] = -1.0
            rt[o + j, o + j + 32] = 1.0
    cs["c_rot"] = rt
    fbm = np.zeros((128, 640), dtype=np.float32)
    for p in range(128):
        for jj in range(640):
            if not (0 <= jj - p <= 512):
                fbm[p, jj] = -1e30
    cs["c_fb"] = fbm
    tglob = np.arange(-512, 8192, dtype=np.float64)
    half_idx = np.arange(0, 64, 2, dtype=np.float64)
    inv_freq = 1.0 / 10000.0 ** (half_idx / 64.0)
    freqs = tglob[:, None] * inv_freq[None, :]
    freqs = np.concatenate([freqs, freqs], axis=-1)
    base = (half_idx + 0.4 * 64) / (1.4 * 64)
    power = (tglob - 4096) / 256.0
    scale = base[None, :] ** power[:, None]
    scale = np.concatenate([scale, scale], axis=-1)
    cos, sin = np.cos(freqs), np.sin(freqs)
    CSq = (cos * scale).T
    SSq = (sin * scale).T
    CSk = (cos / scale).T
    SSk = (sin / scale).T
    tab = np.stack([CSq, SSq, CSk, SSk])
    tab = np.concatenate([tab, tab], axis=1)
    cs["tabglob"] = tab.astype(np.float32)
    return cs


def run_p3a(I, xfull, l, _cache={}):
    cs = host_consts_p3a()
    if "p3a" not in _cache:
        _cache["p3a"] = build_p3a()
    nc = _cache["p3a"]
    qs = I["L0_ffn_q_scale"][l]
    ks = I["L0_ffn_k_scale"][l]
    maps = []
    for c in range(8):
        bb, h = c // 2, c % 2
        s = h * NTOK
        if h == 0:
            xh = np.zeros((514, C), np.float32)
            xp = np.zeros((128, 8), np.float32)
            w0 = np.zeros((1, 1024), np.float32)
            w0[0, :512] = -1e30
        else:
            xh = np.ascontiguousarray(xfull[bb, s - 514:s])
            xp = np.ascontiguousarray(xfull[bb, s - 513].reshape(8, 128).T)
            w0 = np.zeros((1, 1024), np.float32)
        tsl = cs["tabglob"][:, :, s: s + 4608]
        tabsl = np.ascontiguousarray(tsl.transpose(1, 0, 2).reshape(128, 4 * 4608))
        maps.append({
            "x": np.ascontiguousarray(xfull[bb, s:s + NTOK]),
            "xhal": xh, "xprev": xp,
            "wqkv": I["L0_ffn_wqkv"][l], "wo": I["L0_ffn_wo"][l],
            "qsc": np.ascontiguousarray(np.concatenate([qs, qs]).reshape(128, 1)),
            "ksc": np.ascontiguousarray(np.concatenate([ks, ks]).reshape(128, 1)),
            "tabs": tabsl,
            "c_idn": cs["c_idn"], "c_ones": cs["c_ones"], "c_seg8": cs["c_seg8"],
            "c_hm8": cs["c_hm8"], "c_rot": cs["c_rot"], "c_fb": cs["c_fb"],
            "c_w0x": w0,
        })
    res = run_bass_kernel_spmd(nc, maps, core_ids=list(range(8)))
    return res.results



def build_p3b():
    nc = bass.Bass("TRN2", target_bir_lowering=False)
    xm = nc.dram_tensor("xm", [NTOK, C], F32, kind="ExternalInput").ap()
    xmprev = nc.dram_tensor("xmprev", [128, 8], F32, kind="ExternalInput").ap()
    w1a = nc.dram_tensor("w1a", [C, 2730], F32, kind="ExternalInput").ap()
    w1g = nc.dram_tensor("w1g", [C, 2730], F32, kind="ExternalInput").ap()
    w2 = nc.dram_tensor("w2", [2730, C], F32, kind="ExternalInput").ap()
    b1a = nc.dram_tensor("b1a", [1, 2730], F32, kind="ExternalInput").ap()
    b1g = nc.dram_tensor("b1g", [1, 2730], F32, kind="ExternalInput").ap()
    b2r = nc.dram_tensor("b2r", [1, C], F32, kind="ExternalInput").ap()
    lgc = nc.dram_tensor("lgc", [128, 8], F32, kind="ExternalInput").ap()
    lbc = nc.dram_tensor("lbc", [128, 8], F32, kind="ExternalInput").ap()
    c_idn = nc.dram_tensor("c_idn", [128, 128], F32, kind="ExternalInput").ap()
    c_ones = nc.dram_tensor("c_ones", [1, 512], F32, kind="ExternalInput").ap()
    c_onec = nc.dram_tensor("c_onec", [128, 1], F32, kind="ExternalInput").ap()
    o_f = nc.dram_tensor("o_f", [NTOK, C], F32, kind="ExternalOutput").ap()

    b = Builder(nc)
    A = nc.alloc_sbuf_tensor
    idn = A("idn", [128, 128], BF16)
    ones1 = A("ones1", [1, 512], BF16)
    ones1f = A("ones1f", [1, 128], F32)
    onescol = A("onescol", [128, 1], BF16)
    lgv = A("lgv", [128, 8], F32)
    lbv = A("lbv", [128, 8], F32)
    b1ar = A("b1ar", [1, 2730], BF16)
    b1gr = A("b1gr", [1, 2730], BF16)
    b2row = A("b2row", [1, C], BF16)
    w1aS = A("w1aS", [128, 8 * 2730], BF16)
    w1gS = A("w1gS", [128, 8 * 2730], BF16)
    w2S = A("w2S", [128, 22 * 1024], BF16)
    xw = A("xw", [128, 1024], F32)
    xwb = A("xwb", [128, 1024], BF16)
    xmT = A("xmT", [128, 8 * 640], BF16)
    xpv = A("xpv", [128, 8], BF16)
    lnT = A("lnT", [128, 8 * 512], BF16)
    hT = A("hT", [128, 22 * 512], BF16)
    t1b = [A(f"t1b{i}", [128, 512], BF16) for i in range(2)]
    rows = A("rows", [1, 3 * 512], F32)
    bcRs = A("bcRs", [128, 512], F32)
    t2f = A("t2f", [128, 512], F32)
    outf = A("outf", [128, 512], F32)

    P = nc.alloc_psum_tensor
    psT = P("psT", [128, 512], BF16)
    psS1 = P("psS1", [1, 512], F32)
    psS2 = P("psS2", [1, 512], F32)
    psBC = P("psBC", [128, 512], F32)
    psW = [P(f"psW{i}", [128, 512], F32) for i in range(2)]
    psO = P("psO", [128, 1024], F32)

    b.d("sync", idn[:], c_idn[:], w=["idn"])
    b.d("sync", ones1[:], c_ones[:], w=["ones1"])
    b.d("sync", ones1f[:], c_ones[:, 0:128], w=["ones1f"])
    b.d("sync", onescol[:], c_onec[:], w=["onescol"])
    b.d("sync", lgv[:], lgc[:], w=["lgv"])
    b.d("sync", lbv[:], lbc[:], w=["lbv"])
    b.d("gpsimd", b1ar[:], b1a[:], w=["b1ar"])
    b.d("gpsimd", b1gr[:], b1g[:], w=["b1gr"])
    b.d("gpsimd", b2row[:], b2r[:], w=["b2row"])
    b.d("gpsimd", xpv[:], xmprev[:], w=["xpv"])
    for f in range(8):
        b.d("gpsimd", w1aS[:, f * 2730:(f + 1) * 2730], w1a[ts(f), :], w=["w1aS"])
        b.d("gpsimd", w1gS[:, f * 2730:(f + 1) * 2730], w1g[ts(f), :], w=["w1gS"])
    for hc in range(22):
        r0, r1 = hc * 128, min(2730, (hc + 1) * 128)
        b.d("gpsimd", w2S[0:r1 - r0, hc * 1024:(hc + 1) * 1024], w2[r0:r1, :], w=["w2S"])

    for wi in range(NW):
        for c1 in range(8):
            b.c("vector", lambda e, c1=c1, wi=wi: e.tensor_copy(
                xmT[:, c1 * 640 + 127:c1 * 640 + 128],
                xmT[:, c1 * 640 + 639:c1 * 640 + 640] if wi > 0 else xpv[:, c1:c1 + 1]),
                r=([f"xmT_{c1}"] if wi > 0 else ["xpv"]), w=[f"xmT_{c1}"])
        for i in range(4):
            b.d("gpsimd", xwb[:], xm[wi * 512 + i * 128: wi * 512 + (i + 1) * 128, :],
                r=["xwbR"], w=["xwb"])
            for g4 in range(2):
                for jj in range(4):
                    c1 = g4 * 4 + jj
                    b.c("tensor", lambda e, c1=c1, jj=jj: e.matmul(
                        psT[:, jj * 128:(jj + 1) * 128], xwb[:, c1 * 128:(c1 + 1) * 128],
                        idn[:], start=(jj == 0), stop=(jj == 3), is_transpose=True,
                        skip_group_check=True),
                        r=["xwb", "idn"] + (["psTr"] if jj == 0 else []),
                        w=["psT"] + (["xwbR"] if c1 == 7 else []))
                for jj in range(4):
                    c1 = g4 * 4 + jj
                    b.c("vector", lambda e, i=i, c1=c1, jj=jj: e.tensor_copy(
                        xmT[:, c1 * 640 + 128 + i * 128: c1 * 640 + 128 + (i + 1) * 128],
                        psT[:, jj * 128:(jj + 1) * 128]),
                        r=["psT"], w=[f"xmT_{c1}"] + (["psTr"] if jj == 3 else []))
        xt_r = [f"xmT_{c1}" for c1 in range(8)]

        def hsT(c1):
            sh = 127 if c1 >= 4 else 128
            return xmT[:, c1 * 640 + sh: c1 * 640 + sh + 512]

        for c1 in range(8):
            b.c("tensor", lambda e, c1=c1: e.matmul(
                psS1[:], onescol[:], hsT(c1), start=(c1 == 0), stop=(c1 == 7)),
                r=["onescol", xt_r[c1]] + (["psS1r"] if c1 == 0 else []), w=["psS1"])
        for c1 in range(8):
            tb = t1b[c1 % 2]
            tbn = f"t1b{c1 % 2}"
            b.c("scalar", lambda e, c1=c1, tb=tb: e.activation(out=tb[:], in_=hsT(c1),
                                                               func=AF.Square),
                r=[xt_r[c1], tbn + "R"], w=[tbn])
            b.c("tensor", lambda e, c1=c1, tb=tb: e.matmul(
                psS2[:], onescol[:], tb[:], start=(c1 == 0), stop=(c1 == 7)),
                r=["onescol", tbn] + (["psS2r"] if c1 == 0 else []), w=["psS2", tbn + "R"])
        b.c("vector", lambda e: e.tensor_scalar(rows[:, 0:512], psS1[:], 1.0 / C, None, ALU.mult),
            r=["psS1"], w=["rows", "psS1r"])
        b.c("vector", lambda e: e.tensor_scalar(rows[:, 512:1024], psS2[:], 1.0 / C, None, ALU.mult),
            r=["psS2", "rows"], w=["rows", "psS2r"])
        b.c("vector", lambda e: e.tensor_tensor(rows[:, 1024:1536], rows[:, 0:512],
                                                rows[:, 0:512], ALU.mult),
            r=["rows"], w=["rowsd"])
        b.c("vector", lambda e: e.tensor_tensor(rows[:, 512:1024], rows[:, 512:1024],
                                                rows[:, 1024:1536], ALU.subtract),
            r=["rows", "rowsd"], w=["rows"])
        b.c("vector", lambda e: e.tensor_scalar(rows[:, 512:1024], rows[:, 512:1024],
                                                1e-5, None, ALU.add),
            r=["rows"], w=["rows"])
        b.c("scalar", lambda e: e.activation(out=rows[:, 1024:1536], in_=rows[:, 512:1024],
                                             func=AF.Ln),
            r=["rows"], w=["rowsd"])
        b.c("scalar", lambda e: e.activation(out=rows[:, 1024:1536], in_=rows[:, 1024:1536],
                                             func=AF.Exp, scale=-0.5),
            r=["rowsd"], w=["rowsd"])
        b.c("tensor", lambda e: e.matmul(psBC[:], ones1f[:], rows[:, 0:512],
                                         start=True, stop=True),
            r=["ones1f", "rows", "psBCr"], w=["psBC"])
        b.c("tensor", lambda e: e.matmul(psW[0][:], ones1f[:], rows[:, 1024:1536],
                                         start=True, stop=True),
            r=["ones1f", "rowsd", "psW0r"], w=["psW0"])
        b.c("vector", lambda e: e.tensor_copy(bcRs[:], psW[0][:]), r=["psW0"],
            w=["bcRs", "psW0r"])
        for c1 in range(8):
            b.c("vector", lambda e, c1=c1: e.tensor_tensor(
                t2f[:], hsT(c1), psBC[:], ALU.subtract),
                r=[xt_r[c1], "psBC"], w=["t2f"])
            b.c("vector", lambda e, c1=c1: e.tensor_tensor(t2f[:], t2f[:], bcRs[:], ALU.mult),
                r=["t2f", "bcRs"], w=["t2f"])
            b.c("vector", lambda e, c1=c1: e.tensor_scalar(
                lnT[:, c1 * 512:(c1 + 1) * 512], t2f[:], lgv[:, c1:c1 + 1],
                lbv[:, c1:c1 + 1], ALU.mult, op1=ALU.add),
                r=["t2f", "lgv", "lbv"], w=[f"lnT_{c1}"])
        for oc in range(22):
            ow = min(2730, (oc + 1) * 128) - oc * 128
            for part in range(2):
                wS = w1aS if part == 0 else w1gS
                wSn = "w1aS" if part == 0 else "w1gS"
                brow = b1ar if part == 0 else b1gr
                brn = "b1ar" if part == 0 else "b1gr"
                pw, pwn = psW[part], f"psW{part}"
                for f in range(8):
                    b.c("tensor", lambda e, oc=oc, f=f, wS=wS, pw=pw, ow=ow: e.matmul(
                        pw[0:ow, :], wS[:, f * 2730 + oc * 128: f * 2730 + oc * 128 + ow],
                        lnT[:, f * 512:(f + 1) * 512],
                        start=(f == 0), stop=False),
                        r=[wSn, f"lnT_{f}"] + ([pwn + "r"] if f == 0 else []), w=[pwn])
                b.c("tensor", lambda e, oc=oc, brow=brow, pw=pw, ow=ow: e.matmul(
                    pw[0:ow, :], brow[:, oc * 128: oc * 128 + ow], ones1[:],
                    start=False, stop=True),
                    r=["ones1", brn], w=[pwn])
            tb = t1b[oc % 2]
            tbn = f"t1b{oc % 2}"
            b.c("scalar", lambda e, ow=ow, tb=tb: e.activation(out=tb[0:ow, :], in_=psW[1][0:ow, :],
                                                               func=AF.Gelu),
                r=["psW1", tbn + "R"], w=[tbn, "psW1r"])
            b.c("vector", lambda e, oc=oc, ow=ow, tb=tb: e.tensor_tensor(
                hT[0:ow, oc * 512:(oc + 1) * 512], psW[0][0:ow, :], tb[0:ow, :], ALU.mult),
                r=["psW0", tbn], w=[f"hT_{oc}", "psW0r", tbn + "R"])
        for i in range(4):
            for hc in range(22):
                hw = min(2730, (hc + 1) * 128) - hc * 128
                for o2 in range(2):
                    b.c("tensor", lambda e, i=i, hc=hc, hw=hw, o2=o2: e.matmul(
                        psO[:, o2 * 512:(o2 + 1) * 512],
                        hT[0:hw, hc * 512 + i * 128: hc * 512 + (i + 1) * 128],
                        w2S[0:hw, hc * 1024 + o2 * 512: hc * 1024 + (o2 + 1) * 512],
                        start=(hc == 0), stop=False, skip_group_check=True),
                        r=[f"hT_{hc}", "w2S"] + (["psOr"] if hc == 0 and o2 == 0 else []),
                        w=["psO"])
            for o2 in range(2):
                b.c("tensor", lambda e, o2=o2: e.matmul(
                    psO[:, o2 * 512:(o2 + 1) * 512], ones1[:, 0:128],
                    b2row[:, o2 * 512:(o2 + 1) * 512],
                    start=False, stop=True, skip_group_check=True),
                    r=["ones1", "b2row"], w=["psO"])
            b.d("sync", xw[:], xm[wi * 512 + i * 128: wi * 512 + (i + 1) * 128, :],
                r=["xwR"], w=["xw"])
            for o2 in range(2):
                b.c("vector", lambda e, i=i, o2=o2: e.tensor_tensor(
                    outf[:], psO[:, o2 * 512:(o2 + 1) * 512],
                    xw[:, o2 * 512:(o2 + 1) * 512], ALU.add),
                    r=["psO", "xw", "outfd"], w=["outf"] + (["psOr", "xwR"] if o2 == 1 else []))
                b.d("sync", o_f[wi * 512 + i * 128: wi * 512 + (i + 1) * 128,
                                o2 * 512:(o2 + 1) * 512], outf[:],
                    r=["outf"], w=["outfd"])
    b.replay()
    return nc


def run_p3b(I, xmid_res, l, _cache={}):
    cs = host_consts()
    if "p3b" not in _cache:
        _cache["p3b"] = build_p3b()
    nc = _cache["p3b"]
    w1 = I["L0_ffn_w1"][l]
    maps = []
    for c in range(8):
        bb, h = c // 2, c % 2
        xm = np.asarray(xmid_res[c]["o_xmid"])
        if h == 0:
            xp = np.zeros((128, 8), np.float32)
        else:
            peer = np.asarray(xmid_res[bb * 2]["o_xmid"])
            xp = np.ascontiguousarray(peer[NTOK - 1].reshape(8, 128).T)
        maps.append({
            "xm": xm, "xmprev": xp,
            "w1a": np.ascontiguousarray(w1[:, :2730]),
            "w1g": np.ascontiguousarray(w1[:, 2730:]),
            "w2": I["L0_ffn_w2"][l],
            "b1a": np.ascontiguousarray(I["L0_ffn_b1"][l][:2730].reshape(1, 2730)),
            "b1g": np.ascontiguousarray(I["L0_ffn_b1"][l][2730:].reshape(1, 2730)),
            "b2r": I["L0_ffn_b2"][l].reshape(1, C),
            "lgc": np.ascontiguousarray(I["L0_ffn_ln_g"][l].reshape(8, 128).T),
            "lbc": np.ascontiguousarray(I["L0_ffn_ln_b"][l].reshape(8, 128).T),
            "c_idn": cs["c_idn"], "c_ones": np.ones((1, 512), np.float32),
            "c_onec": np.ones((128, 1), np.float32),
        })
    res = run_bass_kernel_spmd(nc, maps, core_ids=list(range(8)))
    return res.results



def softplus_chain(b, psrc, psrcn, dst, dstn, tmpa, tmpb, onecol):
    b.c("scalar", lambda e: e.activation(out=tmpa[:], in_=psrc[:], func=AF.Abs),
        r=[psrcn], w=[tmpa.name])
    b.c("scalar", lambda e: e.activation(out=tmpa[:], in_=tmpa[:], func=AF.Exp, scale=-1.0),
        r=[tmpa.name], w=[tmpa.name])
    b.c("scalar", lambda e: e.activation(out=tmpa[:], in_=tmpa[:], func=AF.Ln,
                                         bias=onecol[:, 0:1], scale=1.0),
        r=[tmpa.name, "onecolf"], w=[tmpa.name])
    b.c("vector", lambda e: e.tensor_scalar(tmpb[:], psrc[:], 0.0, None, ALU.max),
        r=[psrcn], w=[tmpb.name])
    b.c("vector", lambda e: e.tensor_tensor(dst[:], tmpa[:], tmpb[:], ALU.add),
        r=[tmpa.name, tmpb.name], w=[dstn])


def build_p6():
    nc = bass.Bass("TRN2", target_bir_lowering=False)
    g1d = nc.dram_tensor("g1d", [NTOK, C], F32, kind="ExternalInput").ap()
    f2d = nc.dram_tensor("f2d", [NTOK, C], F32, kind="ExternalInput").ap()
    Wq = nc.dram_tensor("Wq", [C, C], F32, kind="ExternalInput").ap()
    Wk = nc.dram_tensor("Wk", [C, C], F32, kind="ExternalInput").ap()
    Wv = nc.dram_tensor("Wv", [C, C], F32, kind="ExternalInput").ap()
    Wg = nc.dram_tensor("Wg", [C, C], F32, kind="ExternalInput").ap()
    Wp = nc.dram_tensor("Wp", [C, C], F32, kind="ExternalInput").ap()
    Wc = nc.dram_tensor("Wc", [C, 1], F32, kind="ExternalInput").ap()
    gbr = nc.dram_tensor("gbr", [1, C], F32, kind="ExternalInput").ap()
    pbr = nc.dram_tensor("pbr", [1, C], F32, kind="ExternalInput").ap()
    cbc = nc.dram_tensor("cbc", [128, 1], F32, kind="ExternalInput").ap()
    lngr = nc.dram_tensor("lngr", [1, C], F32, kind="ExternalInput").ap()
    lnbr = nc.dram_tensor("lnbr", [1, C], F32, kind="ExternalInput").ap()
    ropeC = nc.dram_tensor("ropeC", [NTOK, C], F32, kind="ExternalInput").ap()
    ropeS = nc.dram_tensor("ropeS", [NTOK, C], F32, kind="ExternalInput").ap()
    c_idn = nc.dram_tensor("c_idn", [128, 128], F32, kind="ExternalInput").ap()
    c_ones = nc.dram_tensor("c_ones", [1, 512], F32, kind="ExternalInput").ap()
    c_onec = nc.dram_tensor("c_onec", [128, 1], F32, kind="ExternalInput").ap()
    c_utri = nc.dram_tensor("c_utri", [128, 128], F32, kind="ExternalInput").ap()
    o_x3 = nc.dram_tensor("o_x3", [NTOK, C], F32, kind="ExternalOutput").ap()
    o_q = nc.dram_tensor("o_q", [NTOK, C], BF16, kind="ExternalOutput").ap()
    o_k = nc.dram_tensor("o_k", [NTOK, C], BF16, kind="ExternalOutput").ap()
    o_v = nc.dram_tensor("o_v", [NTOK, C], BF16, kind="ExternalOutput").ap()
    o_gcs = nc.dram_tensor("o_gcs", [NTOK, C], F32, kind="ExternalOutput").ap()
    o_gtot = nc.dram_tensor("o_gtot", [1, C], F32, kind="ExternalOutput").ap()
    o_p = nc.dram_tensor("o_p", [NTOK, C], F32, kind="ExternalOutput").ap()
    o_pmax = nc.dram_tensor("o_pmax", [128, C], F32, kind="ExternalOutput").ap()
    o_clock = nc.dram_tensor("o_clock", [128, NT], F32, kind="ExternalOutput").ap()

    b = Builder(nc)
    A = nc.alloc_sbuf_tensor
    idn = A("idn", [128, 128], BF16)
    ones1f = A("ones1f", [1, 128], F32)
    onecf = A("onecolf", [128, 1], F32)
    utriF = A("utriF", [128, 128], F32)
    cbv = A("cbv", [128, 1], F32)
    WqS = A("WqS", [128, 8 * 1024], BF16)
    WkS = A("WkS", [128, 8 * 1024], BF16)
    WvS = A("WvS", [128, 8 * 1024], BF16)
    WgS = A("WgS", [128, 8 * 1024], BF16)
    WpS = A("WpS", [128, 8 * 1024], BF16)
    WcS = A("WcS", [128, 8], BF16)
    gbrow = A("gbrow", [1, C], BF16)
    pbrow = A("pbrow", [1, C], BF16)
    glnB = A("glnB", [128, 1024], F32)
    blnB = A("blnB", [128, 1024], F32)
    xa = A("xa", [128, 1024], F32)
    xn = A("xnb", [128, 1024], F32)
    xnb16 = A("xnb16", [128, 1024], BF16)
    xnT = A("xnT", [128, 8 * 128], BF16)
    scr = A("scr", [128, 1024], F32)
    mn = A("mn", [128, 8], F32)
    tq = A("tq", [128, 1024], F32)
    tq2 = A("tq2", [128, 1024], F32)
    tsw = A("tsw", [128, 1024], F32)
    rc = [A(f"rc{i}", [128, 1024], F32) for i in range(2)]
    rs = [A(f"rs{i}", [128, 1024], F32) for i in range(2)]
    carry = A("carry", [1, 1024], F32)
    pmaxs = A("pmaxs", [128, 1024], F32)
    clocks = A("clocks", [128, NT], F32)
    outb = A("outb", [128, 1024], BF16)

    P = nc.alloc_psum_tensor
    psQ = [P(f"psQ{i}", [128, 1024], F32) for i in range(2)]
    psT = P("psT", [128, 512], BF16)
    psTot = P("psTot", [1, 1024], F32)
    psCk = P("psCk", [128, 1], F32)

    b.d("sync", idn[:], c_idn[:], w=["idn"])
    b.d("sync", ones1f[:], c_ones[:, 0:128], w=["ones1f"])
    b.d("sync", onecf[:], c_onec[:], w=["onecolf"])
    b.d("sync", utriF[:], c_utri[:], w=["utriF"])
    b.d("sync", cbv[:], cbc[:], w=["cbv"])
    b.d("gpsimd", gbrow[:], gbr[:], w=["gbrow"])
    b.d("gpsimd", pbrow[:], pbr[:], w=["pbrow"])
    for (W, WS, nm) in ((Wq, WqS, "WqS"), (Wk, WkS, "WkS"), (Wv, WvS, "WvS"),
                        (Wg, WgS, "WgS"), (Wp, WpS, "WpS")):
        for f in range(8):
            b.d("gpsimd", WS[:, f * 1024:(f + 1) * 1024], W[ts(f), :], w=[nm])
    b.d("gpsimd", WcS[:], Wc.rearrange("(f p) a -> p (f a)", p=128), w=["WcS"],
        allow_slow_non_contiguous=True)
    lngs = A("lngs", [1, C], F32)
    lnbs = A("lnbs", [1, C], F32)
    b.d("sync", lngs[:], lngr[:], w=["lngs"])
    b.d("sync", lnbs[:], lnbr[:], w=["lnbs"])
    for o2 in range(2):
        b.c("tensor", lambda e, o2=o2: e.matmul(
            psQ[0][:, o2 * 512:(o2 + 1) * 512], ones1f[:], lngs[:, o2 * 512:(o2 + 1) * 512],
            start=True, stop=True),
            r=["ones1f", "lngs"] + (["psQ0r"] if o2 == 0 else []), w=["psQ0"])
    b.c("vector", lambda e: e.tensor_copy(glnB[:], psQ[0][:]), r=["psQ0"], w=["glnB", "psQ0r"])
    for o2 in range(2):
        b.c("tensor", lambda e, o2=o2: e.matmul(
            psQ[0][:, o2 * 512:(o2 + 1) * 512], ones1f[:], lnbs[:, o2 * 512:(o2 + 1) * 512],
            start=True, stop=True),
            r=["ones1f", "lnbs"] + (["psQ0r"] if o2 == 0 else []), w=["psQ0"])
    b.c("vector", lambda e: e.tensor_copy(blnB[:], psQ[0][:]), r=["psQ0"], w=["blnB", "psQ0r"])

    for ti in range(NT):
        b.d("sync", xa[:], g1d[ts(ti), :], r=["xaR"], w=["xa"])
        b.d("gpsimd", scr[:], f2d[ts(ti), :], r=["scrR"], w=["scr"])
        b.c("vector", lambda e: e.tensor_tensor(xa[:], xa[:], scr[:], ALU.add),
            r=["xa", "scr"], w=["xa", "scrR"])
        ln_token_tile(b, xa, xa, mn, scr, "p6ln", out_res=["xa", "xaR"])
        b.d("sync", o_x3[ts(ti), :], xa[:], r=["xa"], w=["x3d"])
        ln_token_tile(b, xa, xn, mn, scr, "p6ln", out_res=["xnb"])
        b.c("vector", lambda e: e.tensor_tensor(xn[:], xn[:], glnB[:], ALU.mult),
            r=["xnb", "glnB"], w=["xnb"])
        b.c("vector", lambda e: e.tensor_tensor(xn[:], xn[:], blnB[:], ALU.add),
            r=["xnb", "blnB"], w=["xnb"])
        b.c("vector", lambda e: e.tensor_copy(xnb16[:], xn[:]), r=["xnb"], w=["xnb16"])
        for g4 in range(2):
            for jj in range(4):
                c1 = g4 * 4 + jj
                b.c("tensor", lambda e, c1=c1, jj=jj: e.matmul(
                    psT[:, jj * 128:(jj + 1) * 128], xnb16[:, c1 * 128:(c1 + 1) * 128],
                    idn[:], start=(jj == 0), stop=(jj == 3), is_transpose=True,
                    skip_group_check=True),
                    r=["xnb16", "idn"] + (["psTr"] if jj == 0 else []), w=["psT"])
            b.c("vector", lambda e, g4=g4: e.tensor_copy(
                xnT[:, g4 * 512:(g4 + 1) * 512], psT[:]),
                r=["psT"], w=["xnT", "psTr"])
        b.d("sync", rc[ti % 2][:], ropeC[ts(ti), :], r=[f"rc{ti % 2}R"], w=[f"rc{ti % 2}"])
        b.d("sync", rs[ti % 2][:], ropeS[ts(ti), :], r=[f"rs{ti % 2}R"], w=[f"rs{ti % 2}"])

        def proj(WS, wsn, pq, pqn, bias_row=None, brn=None):
            for f in range(8):
                for o2 in range(2):
                    b.c("tensor", lambda e, f=f, o2=o2, WS=WS, pq=pq: e.matmul(
                        pq[:, o2 * 512:(o2 + 1) * 512],
                        xnT[:, f * 128:(f + 1) * 128],
                        WS[:, f * 1024 + o2 * 512: f * 1024 + (o2 + 1) * 512],
                        start=(f == 0), stop=(f == 7 and bias_row is None),
                        skip_group_check=True),
                        r=["xnT", wsn] + ([pqn + "r"] if f == 0 and o2 == 0 else []),
                        w=[pqn])
            if bias_row is not None:
                for o2 in range(2):
                    b.c("tensor", lambda e, o2=o2, bias_row=bias_row: e.matmul(
                        pq[:, o2 * 512:(o2 + 1) * 512], b.ones1b[:],
                        bias_row[:, o2 * 512:(o2 + 1) * 512],
                        start=False, stop=True, skip_group_check=True),
                        r=["ones1b", brn], w=[pqn])

        def l2rope(pq, pqn, dst_dram, name):
            b.c("vector", lambda e, pq=pq: e.tensor_copy(tq[:], pq[:]),
                r=[pqn], w=["tq", pqn + "r"])
            b.c("scalar", lambda e: e.activation(out=scr[:], in_=tq[:], func=AF.Square),
                r=["tq", "scrR"], w=["scr"])
            b.c("vector", lambda e: e.tensor_reduce(mn[:, 4:5], scr[:], AX.X, ALU.add),
                r=["scr"], w=["mnl", "scrR"])
            b.c("scalar", lambda e: e.activation(out=mn[:, 5:6], in_=mn[:, 4:5], func=AF.Sqrt),
                r=["mnl"], w=["mnl2"])
            b.c("vector", lambda e: e.tensor_scalar(mn[:, 5:6], mn[:, 5:6], 1e-12, None, ALU.max),
                r=["mnl2"], w=["mnl2"])
            b.c("vector", lambda e: e.reciprocal(mn[:, 5:6], mn[:, 5:6]),
                r=["mnl2"], w=["mnl2"])
            b.c("vector", lambda e: e.tensor_scalar(tq[:], tq[:], mn[:, 5:6], None, ALU.mult),
                r=["tq", "mnl2"], w=["tq"])
            b.c("vector", lambda e: e.tensor_copy(
                bass.AP(tsw, 0, [[1024, 128], [2, 512]]),
                bass.AP(tq, 1, [[1024, 128], [2, 512]])),
                r=["tq"], w=["tsw"])
            b.c("vector", lambda e: e.tensor_copy(
                bass.AP(tsw, 1, [[1024, 128], [2, 512]]),
                bass.AP(tq, 0, [[1024, 128], [2, 512]])),
                r=["tq"], w=["tsw"])
            b.c("vector", lambda e, ti=ti: e.tensor_tensor(tq[:], tq[:], rc[ti % 2][:], ALU.mult),
                r=["tq", f"rc{ti % 2}"], w=["tq"] + ([f"rc{ti % 2}R"] if name == "k" else []))
            b.c("vector", lambda e, ti=ti: e.tensor_tensor(tsw[:], tsw[:], rs[ti % 2][:], ALU.mult),
                r=["tsw", f"rs{ti % 2}"], w=["tsw"] + ([f"rs{ti % 2}R"] if name == "k" else []))
            b.c("vector", lambda e: e.tensor_tensor(outb[:], tq[:], tsw[:], ALU.add),
                r=["tq", "tsw"], w=["outb"])
            b.d("sync", dst_dram[ts(ti), :], outb[:], r=["outb"], w=["outbd"])

        b.ones1b = getattr(b, "ones1b_t", None)
        if b.ones1b is None:
            b.ones1b_t = b.ones1b = A("ones1b", [1, 128], BF16)
            b.d("sync", b.ones1b[:], c_ones[:, 0:128], w=["ones1b"])

        proj(WqS, "WqS", psQ[0], "psQ0")
        l2rope(psQ[0], "psQ0", o_q, "q")
        proj(WkS, "WkS", psQ[1], "psQ1")
        l2rope(psQ[1], "psQ1", o_k, "k")
        proj(WvS, "WvS", psQ[0], "psQ0")
        b.c("vector", lambda e: e.tensor_copy(outb[:], psQ[0][:]),
            r=["psQ0", "outbd"], w=["outb", "psQ0r"])
        b.d("sync", o_v[ts(ti), :], outb[:], r=["outb"], w=["outbd"])
        proj(WgS, "WgS", psQ[1], "psQ1", bias_row=gbrow, brn="gbrow")
        softplus_chain(b, psQ[1], "psQ1", tq, "tq", tq2, tsw, onecf)
        b.c("vector", lambda e: e.tensor_copy(scr[:, 0:1], scr[:, 0:1]),
            r=["tq"], w=["psQ1r", "scrR"])
        for o2 in range(2):
            b.c("tensor", lambda e, o2=o2: e.matmul(
                psQ[0][:, o2 * 512:(o2 + 1) * 512], utriF[:], tq[:, o2 * 512:(o2 + 1) * 512],
                start=True, stop=False, skip_group_check=True),
                r=["utriF", "tq"] + (["psQ0r"] if o2 == 0 else []), w=["psQ0"])
            if ti > 0:
                b.c("tensor", lambda e, o2=o2: e.matmul(
                    psQ[0][:, o2 * 512:(o2 + 1) * 512], ones1f[:],
                    carry[:, o2 * 512:(o2 + 1) * 512],
                    start=False, stop=True, skip_group_check=True),
                    r=["ones1f", "carry"], w=["psQ0"])
        for o2 in range(2):
            b.c("tensor", lambda e, o2=o2, ti=ti: e.matmul(
                psTot[:, o2 * 512:(o2 + 1) * 512], onecf[:],
                tq[:, o2 * 512:(o2 + 1) * 512],
                start=(ti == 0), stop=(ti == NT - 1), skip_group_check=True),
                r=["onecolf", "tq"] + (["psTotr"] if ti == 0 and o2 == 0 else []),
                w=["psTot"])
        b.c("vector", lambda e: e.tensor_copy(carry[:], psTot[:]),
            r=["psTot"], w=["carry"])
        b.c("vector", lambda e: e.tensor_copy(scr[:], psQ[0][:]),
            r=["psQ0", "scrR"], w=["scr", "psQ0r"])
        b.d("sync", o_gcs[ts(ti), :], scr[:], r=["scr"], w=["scrR"])
        proj(WpS, "WpS", psQ[1], "psQ1", bias_row=pbrow, brn="pbrow")
        b.c("vector", lambda e: e.tensor_copy(tq[:], psQ[1][:]),
            r=["psQ1"], w=["tq", "psQ1r"])
        b.d("sync", o_p[ts(ti), :], tq[:], r=["tq"], w=["tqR"])
        if ti == 0:
            b.c("vector", lambda e: e.tensor_copy(pmaxs[:], tq[:]), r=["tq"], w=["pmaxs"])
        else:
            b.c("vector", lambda e: e.tensor_tensor(pmaxs[:], pmaxs[:], tq[:], ALU.max),
                r=["tq", "pmaxs"], w=["pmaxs"])
        for f in range(8):
            b.c("tensor", lambda e, f=f: e.matmul(
                psCk[:], xnT[:, f * 128:(f + 1) * 128], WcS[:, f:f + 1],
                start=(f == 0), stop=(f == 7)),
                r=["xnT", "WcS"] + (["psCkr"] if f == 0 else []), w=["psCk"])
        b.c("vector", lambda e, ti=ti: e.tensor_scalar(mn[:, 6:7], psCk[:], cbv[:], None, ALU.add),
            r=["psCk", "cbv"], w=["mnck", "psCkr"])
        b.c("scalar", lambda e: e.activation(out=mn[:, 7:8], in_=mn[:, 6:7], func=AF.Abs),
            r=["mnck"], w=["mnck2"])
        b.c("scalar", lambda e: e.activation(out=mn[:, 7:8], in_=mn[:, 7:8], func=AF.Exp, scale=-1.0),
            r=["mnck2"], w=["mnck2"])
        b.c("scalar", lambda e: e.activation(out=mn[:, 7:8], in_=mn[:, 7:8], func=AF.Ln,
                                             bias=onecf[:, 0:1], scale=1.0),
            r=["mnck2", "onecolf"], w=["mnck2"])
        b.c("vector", lambda e: e.tensor_scalar(mn[:, 6:7], mn[:, 6:7], 0.0, None, ALU.max),
            r=["mnck"], w=["mnck"])
        b.c("vector", lambda e, ti=ti: e.tensor_tensor(clocks[:, ti:ti + 1], mn[:, 6:7],
                                                       mn[:, 7:8], ALU.add),
            r=["mnck", "mnck2"], w=["clocks"])
    b.d("sync", o_gtot[:], carry[:], r=["carry"])
    b.d("sync", o_pmax[:], pmaxs[:], r=["pmaxs"])
    b.d("sync", o_clock[:], clocks[:], r=["clocks"])
    b.replay()
    return nc


def host_p6_tables():
    half = 512
    freqs = np.exp(-np.arange(half, dtype=np.float64) * (np.log(10000.0) / half))
    t = np.arange(T, dtype=np.float64)
    th = t[:, None] * freqs[None, :]
    c = np.cos(th)
    s = np.sin(th)
    ropeC = np.repeat(c, 2, axis=1).astype(np.float32)
    ropeS = np.empty((T, 1024), np.float64)
    ropeS[:, 0::2] = -s
    ropeS[:, 1::2] = s
    ut = np.triu(np.ones((128, 128), dtype=np.float32))
    return ropeC, ropeS.astype(np.float32), ut


def run_p6(I, g1full, f2full, _cache={}):
    cs = host_consts()
    ropeC, ropeS, ut = host_p6_tables()
    if "p6" not in _cache:
        _cache["p6"] = build_p6()
    nc = _cache["p6"]
    maps = []
    for c in range(8):
        bb, h = c // 2, c % 2
        s = h * NTOK
        maps.append({
            "g1d": np.ascontiguousarray(g1full[bb, s:s + NTOK]),
            "f2d": np.ascontiguousarray(f2full[bb, s:s + NTOK]),
            "Wq": I["L1_attn_Wq"], "Wk": I["L1_attn_Wk"], "Wv": I["L1_attn_Wv"],
            "Wg": I["L1_attn_gate_W"], "Wp": I["L1_attn_p_W"],
            "Wc": I["L1_attn_clock_W"],
            "gbr": I["L1_attn_gate_b"].reshape(1, C),
            "pbr": I["L1_attn_p_b"].reshape(1, C),
            "cbc": np.ascontiguousarray(np.broadcast_to(I["L1_attn_clock_b"].reshape(1, 1), (128, 1))),
            "lngr": I["L1_attn_ln1_g"].reshape(1, C),
            "lnbr": I["L1_attn_ln1_b"].reshape(1, C),
            "ropeC": np.ascontiguousarray(ropeC[s:s + NTOK]),
            "ropeS": np.ascontiguousarray(ropeS[s:s + NTOK]),
            "c_idn": cs["c_idn"], "c_ones": np.ones((1, 512), np.float32),
            "c_onec": np.ones((128, 1), np.float32), "c_utri": ut,
        })
    res = run_bass_kernel_spmd(nc, maps, core_ids=list(range(8)))
    return res.results



def build_p7():
    nc = bass.Bass("TRN2", target_bir_lowering=False)
    pd = nc.dram_tensor("pd", [NTOK, C], F32, kind="ExternalInput").ap()
    pm0 = nc.dram_tensor("pm0", [128, C], F32, kind="ExternalInput").ap()
    pm1 = nc.dram_tensor("pm1", [128, C], F32, kind="ExternalInput").ap()
    clk = nc.dram_tensor("clk", [128, NT], F32, kind="ExternalInput").ap()
    c_idn = nc.dram_tensor("c_idn", [128, 128], F32, kind="ExternalInput").ap()
    c_ones = nc.dram_tensor("c_ones", [1, 512], F32, kind="ExternalInput").ap()
    c_onec = nc.dram_tensor("c_onec", [128, 1], F32, kind="ExternalInput").ap()
    c_utri = nc.dram_tensor("c_utri", [128, 128], F32, kind="ExternalInput").ap()
    sc_pm = nc.dram_tensor("sc_pm", [8, 128], F32, kind="Internal").ap()
    o_pexp = nc.dram_tensor("o_pexp", [NTOK, C], F32, kind="ExternalOutput").ap()
    o_pcs = nc.dram_tensor("o_pcs", [NTOK, C], F32, kind="ExternalOutput").ap()
    o_ptot = nc.dram_tensor("o_ptot", [1, C], F32, kind="ExternalOutput").ap()

    b = Builder(nc)
    A = nc.alloc_sbuf_tensor
    idnf = A("idnf", [128, 128], F32)
    ones1f = A("ones1f", [1, 128], F32)
    onecf = A("onecolf", [128, 1], F32)
    utriF = A("utriF", [128, 128], F32)
    clks = A("clks", [128, NT], F32)
    pma = A("pma", [128, 1024], F32)
    pmb = A("pmb", [128, 1024], F32)
    pmpp = A("pmpp", [128, 8], F32)
    pmrow = A("pmrow", [1, 1024], F32)
    pmB = A("pmB", [128, 1024], F32)
    pt = [A(f"pt{i}", [128, 1024], F32) for i in range(2)]
    tq = A("tq", [128, 1024], F32)
    carry = A("carry", [1, 1024], F32)
    outf = A("outf", [128, 1024], F32)

    P = nc.alloc_psum_tensor
    psQ = [P(f"psQ{i}", [128, 1024], F32) for i in range(2)]
    psT = P("psT", [128, 128], F32)
    psTot = P("psTot", [1, 1024], F32)

    b.d("sync", idnf[:], c_idn[:], w=["idnf"])
    b.d("sync", ones1f[:], c_ones[:, 0:128], w=["ones1f"])
    b.d("sync", onecf[:], c_onec[:], w=["onecolf"])
    b.d("sync", utriF[:], c_utri[:], w=["utriF"])
    b.d("sync", clks[:], clk[:], w=["clks"])
    b.d("sync", pma[:], pm0[:], w=["pma"])
    b.d("sync", pmb[:], pm1[:], w=["pmb"])
    b.c("vector", lambda e: e.tensor_tensor(pma[:], pma[:], pmb[:], ALU.max),
        r=["pma", "pmb"], w=["pma"])
    for c1 in range(8):
        b.c("tensor", lambda e, c1=c1: e.transpose(
            psT[:], pma[:, c1 * 128:(c1 + 1) * 128], idnf[:]),
            r=["pma", "idnf", "psTr"], w=["psT"])
        b.c("vector", lambda e: e.tensor_copy(pmb[:, 0:128], psT[:]),
            r=["psT"], w=["pmb2", "psTr"])
        b.c("vector", lambda e, c1=c1: e.tensor_reduce(
            pmpp[:, c1:c1 + 1], pmb[:, 0:128], AX.X, ALU.max),
            r=["pmb2"], w=["pmpp"])
        b.d("sync", sc_pm[c1:c1 + 1, :], pmpp[:, c1:c1 + 1], r=["pmpp"], w=["scpm"])
    b.d("sync", pmrow[:], sc_pm.rearrange("c p -> (c p)"), r=["scpm"], w=["pmrow"])
    for o2 in range(2):
        b.c("tensor", lambda e, o2=o2: e.matmul(
            psQ[0][:, o2 * 512:(o2 + 1) * 512], ones1f[:], pmrow[:, o2 * 512:(o2 + 1) * 512],
            start=True, stop=True),
            r=["ones1f", "pmrow"] + (["psQ0r"] if o2 == 0 else []), w=["psQ0"])
    b.c("vector", lambda e: e.tensor_copy(pmB[:], psQ[0][:]), r=["psQ0"], w=["pmB", "psQ0r"])

    for ti in range(NT):
        tb = ti % 2
        b.d("sync", pt[tb][:], pd[ts(ti), :], r=[f"pt{tb}R"], w=[f"pt{tb}"])
        b.c("vector", lambda e, tb=tb: e.tensor_tensor(tq[:], pt[tb][:], pmB[:], ALU.subtract),
            r=[f"pt{tb}", "pmB"], w=["tq", f"pt{tb}R"])
        b.c("scalar", lambda e: e.activation(out=tq[:], in_=tq[:], func=AF.Exp),
            r=["tq"], w=["tq"])
        b.c("vector", lambda e, ti=ti: e.tensor_scalar(tq[:], tq[:], clks[:, ti:ti + 1],
                                                       None, ALU.mult),
            r=["tq", "clks"], w=["tq"])
        b.d("sync", o_pexp[ts(ti), :], tq[:], r=["tq"], w=["tqD"])
        for o2 in range(2):
            b.c("tensor", lambda e, o2=o2: e.matmul(
                psQ[0][:, o2 * 512:(o2 + 1) * 512], utriF[:], tq[:, o2 * 512:(o2 + 1) * 512],
                start=True, stop=False, skip_group_check=True),
                r=["utriF", "tq"] + (["psQ0r"] if o2 == 0 else []), w=["psQ0"])
            if ti > 0:
                b.c("tensor", lambda e, o2=o2: e.matmul(
                    psQ[0][:, o2 * 512:(o2 + 1) * 512], ones1f[:],
                    carry[:, o2 * 512:(o2 + 1) * 512],
                    start=False, stop=True, skip_group_check=True),
                    r=["ones1f", "carry"], w=["psQ0"])
        for o2 in range(2):
            b.c("tensor", lambda e, o2=o2, ti=ti: e.matmul(
                psTot[:, o2 * 512:(o2 + 1) * 512], onecf[:], tq[:, o2 * 512:(o2 + 1) * 512],
                start=(ti == 0), stop=(ti == NT - 1), skip_group_check=True),
                r=["onecolf", "tq"] + (["psTotr"] if ti == 0 and o2 == 0 else []),
                w=["psTot"])
        b.c("vector", lambda e: e.tensor_copy(carry[:], psTot[:]), r=["psTot"], w=["carry"])
        b.c("vector", lambda e: e.tensor_copy(outf[:], psQ[0][:]),
            r=["psQ0"], w=["outf", "psQ0r"])
        b.d("sync", o_pcs[ts(ti), :], outf[:], r=["outf"], w=["outfD"])
    b.d("sync", o_ptot[:], carry[:], r=["carry"])
    b.replay()
    return nc



def build_p8():
    nc = bass.Bass("TRN2", target_bir_lowering=False)
    qd = nc.dram_tensor("qd", [NTOK, C], BF16, kind="ExternalInput").ap()
    kd = nc.dram_tensor("kd", [NTOK, C], BF16, kind="ExternalInput").ap()
    gcs = nc.dram_tensor("gcs", [NTOK, C], F32, kind="ExternalInput").ap()
    pexp = nc.dram_tensor("pexp", [NTOK, C], F32, kind="ExternalInput").ap()
    pcs = nc.dram_tensor("pcs", [NTOK, C], F32, kind="ExternalInput").ap()
    gtp = nc.dram_tensor("gtp", [1, C], F32, kind="ExternalInput").ap()
    ptp = nc.dram_tensor("ptp", [1, C], F32, kind="ExternalInput").ap()
    c_ones = nc.dram_tensor("c_ones", [1, 512], F32, kind="ExternalInput").ap()
    c_ln32 = nc.dram_tensor("c_ln32", [128, 1], F32, kind="ExternalInput").ap()
    o_qp = nc.dram_tensor("o_qp", [NTOK, C], BF16, kind="ExternalOutput").ap()
    o_kp = nc.dram_tensor("o_kp", [NTOK, C], BF16, kind="ExternalOutput").ap()

    b = Builder(nc)
    A = nc.alloc_sbuf_tensor
    ones1f = A("ones1f", [1, 128], F32)
    ln32 = A("ln32", [128, 1], F32)
    gtr = A("gtr", [1, 1024], F32)
    ptr = A("ptr", [1, 1024], F32)
    GB = A("GB", [128, 1024], F32)
    PB = A("PB", [128, 1024], F32)
    qt = A("qt", [128, 1024], BF16)
    kt = A("kt", [128, 1024], BF16)
    g1t = A("g1t", [128, 1024], F32)
    pet = A("pet", [128, 1024], F32)
    pct = A("pct", [128, 1024], F32)
    gjc = A("gjc", [128, 1024], F32)
    rcp = A("rcp", [128, 1024], F32)
    tmp = A("tmp", [128, 1024], F32)
    outb = A("outb", [128, 1024], BF16)

    P = nc.alloc_psum_tensor
    psQ = P("psQ", [128, 1024], F32)

    b.d("sync", ones1f[:], c_ones[:, 0:128], w=["ones1f"])
    b.d("sync", ln32[:], c_ln32[:], w=["ln32"])
    b.d("sync", gtr[:], gtp[:], w=["gtr"])
    b.d("sync", ptr[:], ptp[:], w=["ptr"])
    for (row, rn, dst, dn) in ((gtr, "gtr", GB, "GB"), (ptr, "ptr", PB, "PB")):
        for o2 in range(2):
            b.c("tensor", lambda e, o2=o2, row=row: e.matmul(
                psQ[:, o2 * 512:(o2 + 1) * 512], ones1f[:], row[:, o2 * 512:(o2 + 1) * 512],
                start=True, stop=True),
                r=["ones1f", rn] + (["psQr"] if o2 == 0 else []), w=["psQ"])
        b.c("vector", lambda e, dst=dst: e.tensor_copy(dst[:], psQ[:]),
            r=["psQ"], w=[dn, "psQr"])

    for ti in range(NT):
        b.d("sync", qt[:], qd[ts(ti), :], r=["qtR"], w=["qtb"])
        b.d("sync", kt[:], kd[ts(ti), :], r=["ktR"], w=["ktb"])
        b.d("sync", g1t[:], gcs[ts(ti), :], r=["g1tR"], w=["g1tb"])
        b.d("sync", pet[:], pexp[ts(ti), :], r=["petR"], w=["petb"])
        b.d("sync", pct[:], pcs[ts(ti), :], r=["pctR"], w=["pctb"])
        b.c("vector", lambda e: e.tensor_tensor(gjc[:], g1t[:], GB[:], ALU.add),
            r=["g1tb", "GB"], w=["gjc", "g1tR"])
        b.c("vector", lambda e: e.tensor_scalar(gjc[:], gjc[:], 60.0, -50.0, ALU.min,
                                                op1=ALU.max),
            r=["gjc"], w=["gjc"])
        b.c("scalar", lambda e: e.activation(out=gjc[:], in_=gjc[:], func=AF.Exp, scale=-1.0),
            r=["gjc"], w=["gjc"])
        b.c("vector", lambda e: e.tensor_tensor(pct[:], pct[:], PB[:], ALU.add),
            r=["pctb", "PB"], w=["pctb"])
        b.c("vector", lambda e: e.tensor_scalar(pct[:], pct[:], 1e-8, None, ALU.add),
            r=["pctb"], w=["pctb"])
        b.c("scalar", lambda e: e.activation(out=rcp[:], in_=pct[:], func=AF.Ln),
            r=["pctb"], w=["rcp", "pctR"])
        b.c("scalar", lambda e: e.activation(out=rcp[:], in_=rcp[:], func=AF.Exp, scale=-1.0,
                                             bias=ln32[:, 0:1]),
            r=["rcp", "ln32"], w=["rcp"])
        b.c("vector", lambda e: e.tensor_tensor(tmp[:], qt[:], gjc[:], ALU.mult),
            r=["qtb", "gjc"], w=["tmp", "qtR"])
        b.c("vector", lambda e: e.tensor_tensor(outb[:], tmp[:], rcp[:], ALU.mult),
            r=["tmp", "rcp"], w=["outb"])
        b.d("sync", o_qp[ts(ti), :], outb[:], r=["outb"], w=["outbD"])
        b.c("vector", lambda e: e.tensor_scalar(gjc[:], gjc[:], 1e-8, None, ALU.add),
            r=["gjc"], w=["gjc"])
        b.c("scalar", lambda e: e.activation(out=gjc[:], in_=gjc[:], func=AF.Ln),
            r=["gjc"], w=["gjc"])
        b.c("scalar", lambda e: e.activation(out=gjc[:], in_=gjc[:], func=AF.Exp, scale=-1.0),
            r=["gjc"], w=["gjc"])
        b.c("vector", lambda e: e.tensor_tensor(tmp[:], kt[:], pet[:], ALU.mult),
            r=["ktb", "petb"], w=["tmp", "ktR", "petR"])
        b.c("vector", lambda e: e.tensor_tensor(outb[:], tmp[:], gjc[:], ALU.mult),
            r=["tmp", "gjc", "outbD"], w=["outb"])
        b.d("sync", o_kp[ts(ti), :], outb[:], r=["outb"], w=["outbD"])
    b.replay()
    return nc


def run_p7(I, p6res, _cache={}):
    if "p7" not in _cache:
        _cache["p7"] = build_p7()
    nc = _cache["p7"]
    _, _, ut = host_p6_tables()
    maps = []
    for c in range(8):
        bb, h = c // 2, c % 2
        peer = bb * 2 + (1 - h)
        maps.append({
            "pd": np.asarray(p6res[c]["o_p"]),
            "pm0": np.asarray(p6res[c]["o_pmax"]),
            "pm1": np.asarray(p6res[peer]["o_pmax"]),
            "clk": np.asarray(p6res[c]["o_clock"]),
            "c_idn": np.eye(128, dtype=np.float32),
            "c_ones": np.ones((1, 512), np.float32),
            "c_onec": np.ones((128, 1), np.float32), "c_utri": ut,
        })
    res = run_bass_kernel_spmd(nc, maps, core_ids=list(range(8)))
    return res.results


def run_p8(I, p6res, p7res, _cache={}):
    if "p8" not in _cache:
        _cache["p8"] = build_p8()
    nc = _cache["p8"]
    z = np.zeros((1, C), np.float32)
    maps = []
    for c in range(8):
        bb, h = c // 2, c % 2
        maps.append({
            "qd": np.asarray(p6res[c]["o_q"]),
            "kd": np.asarray(p6res[c]["o_k"]),
            "gcs": np.asarray(p6res[c]["o_gcs"]),
            "pexp": np.asarray(p7res[c]["o_pexp"]),
            "pcs": np.asarray(p7res[c]["o_pcs"]),
            "gtp": (np.asarray(p6res[bb * 2]["o_gtot"]) if h == 1 else z),
            "ptp": (np.asarray(p7res[bb * 2]["o_ptot"]) if h == 1 else z),
            "c_ones": np.ones((1, 512), np.float32),
            "c_ln32": np.full((128, 1), np.log(1.0 / 32.0), np.float32),
        })
    res = run_bass_kernel_spmd(nc, maps, core_ids=list(range(8)))
    return res.results



def build_p9():
    nc = bass.Bass("TRN2", target_bir_lowering=False)
    qp = nc.dram_tensor("qp", [NTOK, C], BF16, kind="ExternalInput").ap()
    kp = nc.dram_tensor("kp", [NTOK, C], BF16, kind="ExternalInput").ap()
    vp = nc.dram_tensor("vp", [NTOK, C], BF16, kind="ExternalInput").ap()
    qx = nc.dram_tensor("qx", [2048, C], BF16, kind="ExternalInput").ap()
    kx = nc.dram_tensor("kx", [NTOK, C], BF16, kind="ExternalInput").ap()
    vx = nc.dram_tensor("vx", [NTOK, C], BF16, kind="ExternalInput").ap()
    c_fc = nc.dram_tensor("c_fc", [128, 896], F32, kind="ExternalInput").ap()
    o_O1 = nc.dram_tensor("o_O1", [NTOK, C], F32, kind="ExternalOutput").ap()
    o_ml1 = nc.dram_tensor("o_ml1", [128, 2 * NT], F32, kind="ExternalOutput").ap()
    o_O2 = nc.dram_tensor("o_O2", [2048, C], F32, kind="ExternalOutput").ap()
    o_ml2 = nc.dram_tensor("o_ml2", [128, 2 * 16], F32, kind="ExternalOutput").ap()

    b = Builder(nc)
    A = nc.alloc_sbuf_tensor
    fc = A("fc", [128, 896], F32)
    QT = A("QT", [128, 8 * 2048], BF16)
    KT = [A(f"KT{i}", [128, 8 * 512], BF16) for i in range(2)]
    VB = [A(f"VB{i}", [128, 4 * 1024], BF16) for i in range(2)]
    Oac = A("Oac", [128, 16 * 1024], F32)
    ml = A("ml", [128, 4 * 16], F32)
    sm_ = [A(f"sm{i}", [128, 6], F32) for i in range(2)]
    Ssb_ = [A(f"Ssb{i}", [128, 512], F32) for i in range(2)]
    Pb_ = [A(f"Pb{i}", [128, 512], BF16) for i in range(2)]
    PT_ = [A(f"PT{i}", [128, 512], BF16) for i in range(2)]
    outf = A("outf", [128, 1024], F32)
    idn = A("idn", [128, 128], BF16)

    P = nc.alloc_psum_tensor
    psS = [P(f"psS{i}", [128, 512], F32) for i in range(2)]
    psPT_ = [P(f"psPT{i}", [128, 512], BF16) for i in range(2)]
    psO = P("psO", [128, 1024], F32)

    c_idn = nc.dram_tensor("c_idn", [128, 128], F32, kind="ExternalInput").ap()
    b.d("sync", idn[:], c_idn[:], w=["idn"])
    b.d("sync", fc[:], c_fc[:], w=["fc"])

    def load_q(src, tok0, ntok):
        for c1 in range(8):
            b.d("sync", QT[:, c1 * 2048: c1 * 2048 + ntok],
                src[tok0:tok0 + ntok, c1 * 128:(c1 + 1) * 128],
                w=["QT"], transpose=True)

    def load_kv(ksrc, vsrc, ks, slot):
        for c1 in range(8):
            b.d("sync", KT[slot][:, c1 * 512:(c1 + 1) * 512],
                ksrc[ks:ks + 512, c1 * 128:(c1 + 1) * 128],
                r=[f"KTr{slot}"], w=[f"KT{slot}"], transpose=True)
        for i in range(4):
            b.d("gpsimd", VB[slot][:, i * 1024:(i + 1) * 1024],
                vsrc[ks + i * 128: ks + (i + 1) * 128, :],
                r=[f"VBr{slot}"], w=[f"VB{slot}"])

    def flash_range(tag, qsrc, qtok0, nq, ksrc, vsrc, nk, causal, oO, oML):
        nqt = nq // 128
        load_q(qsrc, qtok0, nq)
        nkb = nk // 512
        first_done = [False] * nqt
        for kb in range(nkb):
            ks = kb * 512
            slot = kb % 2
            load_kv(ksrc, vsrc, ks, slot)
            for qt in range(nqt):
                qs = qt * 128 + (2048 if tag == "R2" else 0)
                if causal and ks > qs + 127:
                    continue
                vp = (kb + qt) % 2
                Ssb, sm, Pb, PT = Ssb_[vp], sm_[vp], Pb_[vp], PT_[vp]
                psPT = psPT_[vp]
                sfx = str(vp)
                for c1 in range(8):
                    b.c("tensor", lambda e, c1=c1, qt=qt, slot=slot: e.matmul(
                        psS[slot][:], QT[:, c1 * 2048 + qt * 128: c1 * 2048 + (qt + 1) * 128],
                        KT[slot][:, c1 * 512:(c1 + 1) * 512],
                        start=(c1 == 0), stop=(c1 == 7)),
                        r=["QT", f"KT{slot}"] + ([f"psS{slot}r"] if c1 == 0 else []),
                        w=[f"psS{slot}"])
                diag = causal and (qs - ks) < 512
                if diag:
                    d = qs - ks
                    b.c("vector", lambda e, d=d, slot=slot: e.tensor_tensor(
                        Ssb[:], psS[slot][:], fc[:, 384 - d: 896 - d], ALU.add),
                        r=[f"psS{slot}", "fc"], w=["Ssb" + sfx, f"psS{slot}r"])
                    sin_, sin_n = Ssb, "Ssb" + sfx
                else:
                    sin_, sin_n = psS[slot], f"psS{slot}"
                b.c("vector", lambda e, sin_=sin_: e.tensor_reduce(
                    sm[:, 0:1], sin_[:], AX.X, ALU.max),
                    r=[sin_n], w=["sm" + sfx])
                fd = first_done[qt]
                if not fd:
                    b.c("vector", lambda e, qt=qt: e.tensor_copy(ml[:, qt:qt + 1], sm[:, 0:1]),
                        r=["sm" + sfx], w=["mlm"])
                else:
                    b.c("vector", lambda e, qt=qt: e.tensor_tensor(
                        sm[:, 1:2], ml[:, qt:qt + 1], sm[:, 0:1], ALU.max),
                        r=["sm" + sfx, "mlm"], w=["sm2" + sfx])
                    b.c("vector", lambda e, qt=qt: e.tensor_tensor(
                        sm[:, 2:3], ml[:, qt:qt + 1], sm[:, 1:2], ALU.subtract),
                        r=["sm2" + sfx, "mlm"], w=["sm3" + sfx])
                    b.c("scalar", lambda e: e.activation(out=sm[:, 2:3], in_=sm[:, 2:3],
                                                         func=AF.Exp),
                        r=["sm3" + sfx], w=["sm3" + sfx])
                    b.c("vector", lambda e, qt=qt: e.tensor_copy(ml[:, qt:qt + 1], sm[:, 1:2]),
                        r=["sm2" + sfx], w=["mlm"])
                b.c("vector", lambda e, qt=qt: e.tensor_scalar(
                    sm[:, 3:4], ml[:, qt:qt + 1], -1.0, None, ALU.mult),
                    r=["mlm"], w=["sm4" + sfx])
                b.c("scalar", lambda e, sin_=sin_: e.activation(
                    out=Pb[:], in_=sin_[:], func=AF.Exp, bias=sm[:, 3:4], scale=1.0,
                    accum_out=sm[:, 4:5]),
                    r=[sin_n, "sm4" + sfx], w=["Pb" + sfx, "sm5" + sfx] + ([] if diag else [f"psS{slot}r"]))
                if not fd:
                    b.c("vector", lambda e, qt=qt: e.tensor_copy(
                        ml[:, 16 + qt: 17 + qt], sm[:, 4:5]),
                        r=["sm5" + sfx], w=["mll"])
                else:
                    b.c("vector", lambda e, qt=qt: e.tensor_tensor(
                        ml[:, 16 + qt: 17 + qt], ml[:, 16 + qt: 17 + qt], sm[:, 2:3], ALU.mult),
                        r=["mll", "sm3" + sfx], w=["mll"])
                    b.c("vector", lambda e, qt=qt: e.tensor_tensor(
                        ml[:, 16 + qt: 17 + qt], ml[:, 16 + qt: 17 + qt], sm[:, 4:5], ALU.add),
                        r=["mll", "sm5" + sfx], w=["mll"])
                for pc in range(4):
                    b.c("tensor", lambda e, pc=pc: e.matmul(
                        psPT[:, pc * 128:(pc + 1) * 128], Pb[:, pc * 128:(pc + 1) * 128],
                        idn[:], start=(pc == 0), stop=(pc == 3), is_transpose=True,
                        skip_group_check=True),
                        r=["Pb" + sfx, "idn"] + (["psPTr" + sfx] if pc == 0 else []), w=["psPT" + sfx])
                b.c("vector", lambda e: e.tensor_copy(PT[:], psPT[:]),
                    r=["psPT" + sfx], w=["PT" + sfx, "psPTr" + sfx])
                for pc in range(4):
                    for o2 in range(2):
                        b.c("tensor", lambda e, pc=pc, o2=o2, slot=slot: e.matmul(
                            psO[:, o2 * 512:(o2 + 1) * 512],
                            PT[:, pc * 128:(pc + 1) * 128],
                            VB[slot][:, pc * 1024 + o2 * 512: pc * 1024 + (o2 + 1) * 512],
                            start=(pc == 0), stop=(pc == 3), skip_group_check=True),
                            r=["PT" + sfx, f"VB{slot}"] + (["psOr"] if pc == 0 and o2 == 0 else []),
                            w=["psO"])
                if not fd:
                    b.c("vector", lambda e, qt=qt: e.tensor_copy(
                        Oac[:, qt * 1024:(qt + 1) * 1024], psO[:]),
                        r=["psO"], w=[f"Oac_{qt}", "psOr"])
                    first_done[qt] = True
                else:
                    b.c("scalar", lambda e, qt=qt: e.activation(
                        out=Oac[:, qt * 1024:(qt + 1) * 1024],
                        in_=Oac[:, qt * 1024:(qt + 1) * 1024],
                        func=AF.Copy, scale=sm[:, 2:3]),
                        r=[f"Oac_{qt}", "sm3" + sfx], w=[f"Oac_{qt}"])
                    b.c("vector", lambda e, qt=qt: e.tensor_tensor(
                        Oac[:, qt * 1024:(qt + 1) * 1024],
                        Oac[:, qt * 1024:(qt + 1) * 1024], psO[:], ALU.add),
                        r=[f"Oac_{qt}", "psO"], w=[f"Oac_{qt}", "psOr"])
            b.c("sync", lambda e: e.nop(), r=[f"KT{slot}", f"VB{slot}"],
                w=[f"KTr{slot}", f"VBr{slot}"])
        for qt in range(nqt):
            b.d("sync", oO[(qtok0 % NTOK) * 0 + (qt * 128) + (2048 if tag == "R2" else 0)
                           if tag != "R3" else qt * 128, :][0:0, :] if False else
                oO[(qt * 128 + (2048 if tag == "R2" else 0)) if tag != "R3" else qt * 128:
                   ((qt + 1) * 128 + (2048 if tag == "R2" else 0)) if tag != "R3" else (qt + 1) * 128, :],
                Oac[:, qt * 1024:(qt + 1) * 1024], r=[f"Oac_{qt}"], w=["OacD"])
        mlcol = (16 if tag == "R2" else 0)
        b.d("sync", oML[:, mlcol:mlcol + 16], ml[:, 0:16], r=["mlm"], w=["mlD"])
        b.d("sync", oML[:, NT + mlcol if tag != "R3" else 16: (NT + mlcol + 16) if tag != "R3" else 32],
            ml[:, 16:32], r=["mll"], w=["mlD"])

    flash_range("R1", qp, 0, 2048, kp, vp, 2048, True, o_O1, o_ml1)
    flash_range("R2", qp, 2048, 2048, kp, vp, 4096, True, o_O1, o_ml1)
    flash_range("R3", qx, 0, 2048, kx, vx, 4096, False, o_O2, o_ml2)
    b.replay()
    return nc


def run_p9(I, p6res, p8res, _cache={}):
    if "p9" not in _cache:
        _cache["p9"] = build_p9()
    nc = _cache["p9"]
    fc = np.zeros((128, 896), np.float32)
    for p in range(128):
        for v_ in range(896):
            if not (v_ - 384 <= p):
                fc[p, v_] = -1e30
    maps = []
    for c in range(8):
        bb, h = c // 2, c % 2
        peer = bb * 2 + (1 - h)
        qp_own = np.asarray(p8res[c]["o_qp"])
        q2 = np.asarray(p8res[bb * 2 + 1]["o_qp"])[h * 2048:(h + 1) * 2048]
        k1 = np.asarray(p8res[bb * 2]["o_kp"])
        v1 = np.asarray(p6res[bb * 2]["o_v"])
        maps.append({
            "qp": qp_own, "kp": np.asarray(p8res[c]["o_kp"]),
            "vp": np.asarray(p6res[c]["o_v"]),
            "qx": np.ascontiguousarray(q2), "kx": k1, "vx": v1,
            "c_fc": fc, "c_idn": np.eye(128, dtype=np.float32),
        })
    res = run_bass_kernel_spmd(nc, maps, core_ids=list(range(8)))
    return res.results



def build_p10():
    nc = bass.Bass("TRN2", target_bir_lowering=False)
    O1d = nc.dram_tensor("O1d", [NTOK, C], F32, kind="ExternalInput").ap()
    O2d = nc.dram_tensor("O2d", [NTOK, C], F32, kind="ExternalInput").ap()
    ml1d = nc.dram_tensor("ml1d", [128, 2 * NT], F32, kind="ExternalInput").ap()
    ml2d = nc.dram_tensor("ml2d", [128, 2 * NT], F32, kind="ExternalInput").ap()
    x3d = nc.dram_tensor("x3d", [NTOK, C], F32, kind="ExternalInput").ap()
    outW = nc.dram_tensor("outW", [C, C], F32, kind="ExternalInput").ap()
    outbr = nc.dram_tensor("outbr", [1, C], F32, kind="ExternalInput").ap()
    ln2gr = nc.dram_tensor("ln2gr", [1, C], F32, kind="ExternalInput").ap()
    ln2br = nc.dram_tensor("ln2br", [1, C], F32, kind="ExternalInput").ap()
    W1d = nc.dram_tensor("W1d", [C, 4096], F32, kind="ExternalInput").ap()
    b1r = nc.dram_tensor("b1r", [1, 4096], F32, kind="ExternalInput").ap()
    W2d = nc.dram_tensor("W2d", [4096, C], F32, kind="ExternalInput").ap()
    b2r = nc.dram_tensor("b2r", [1, C], F32, kind="ExternalInput").ap()
    c_idn = nc.dram_tensor("c_idn", [128, 128], F32, kind="ExternalInput").ap()
    c_ones = nc.dram_tensor("c_ones", [1, 512], F32, kind="ExternalInput").ap()
    o_g3 = nc.dram_tensor("o_g3", [NTOK, C], F32, kind="ExternalOutput").ap()

    b = Builder(nc)
    A = nc.alloc_sbuf_tensor
    idn = A("idn", [128, 128], BF16)
    ones1 = A("ones1", [1, 128], BF16)
    ones1f = A("ones1f", [1, 128], F32)
    outWS = A("outWS", [128, 8 * 1024], BF16)
    obrow = A("obrow", [1, C], BF16)
    b1row = A("b1row", [1, 4096], BF16)
    b2row = A("b2row", [1, C], BF16)
    W1S = A("W1S", [128, 8 * 4096], BF16)
    W2S = A("W2S", [128, 32 * 1024], BF16)
    ml1 = A("ml1", [128, 2 * NT], F32)
    ml2 = A("ml2", [128, 2 * NT], F32)
    g2B = A("g2B", [128, 1024], F32)
    b2B = A("b2B", [128, 1024], F32)
    lngs = A("lngs", [1, C], F32)
    O1t = A("O1t", [128, 1024], F32)
    O2t = A("O2t", [128, 1024], F32)
    x3t = A("x3t", [128, 1024], F32)
    x2t = A("x2t", [128, 1024], F32)
    sm = A("sm", [128, 6], F32)
    mn = A("mn", [128, 4], F32)
    scr = A("scr", [128, 1024], F32)
    hb = A("hb", [128, 1024], BF16)
    hT = A("hT", [128, 8 * 128], BF16)
    mg = A("mg", [128, 512], BF16)
    mT = A("mT", [128, 32 * 128], BF16)
    outf = O2t

    P = nc.alloc_psum_tensor
    psA = [P(f"psA{i}", [128, 1024], F32) for i in range(2)]
    psT = P("psT", [128, 512], BF16)
    psW = [P(f"psW{i}", [128, 512], F32) for i in range(2)]

    b.d("sync", idn[:], c_idn[:], w=["idn"])
    b.d("sync", ones1[:], c_ones[:, 0:128], w=["ones1"])
    b.d("sync", ones1f[:], c_ones[:, 0:128], w=["ones1f"])
    b.d("sync", ml1[:], ml1d[:], w=["ml1"])
    b.d("sync", ml2[:], ml2d[:], w=["ml2"])
    b.d("gpsimd", obrow[:], outbr[:], w=["obrow"])
    b.d("gpsimd", b1row[:], b1r[:], w=["b1row"])
    b.d("gpsimd", b2row[:], b2r[:], w=["b2row"])

    for f in range(8):
        b.d("gpsimd", outWS[:, f * 1024:(f + 1) * 1024], outW[ts(f), :], w=["outWS"])
        b.d("gpsimd", W1S[:, f * 4096:(f + 1) * 4096], W1d[ts(f), :], w=["W1S"])
    for hc in range(32):
        b.d("gpsimd", W2S[:, hc * 1024:(hc + 1) * 1024], W2d[ts(hc), :], w=["W2S"])
    for (srcd, srn, dst, dn) in ((ln2gr, "lngs", g2B, "g2B"), (ln2br, "lngs", b2B, "b2B")):
        b.d("sync", lngs[:], srcd[:], r=["lngsR"], w=["lngs"])
        src = lngs
        for o2 in range(2):
            b.c("tensor", lambda e, o2=o2, src=src: e.matmul(
                psA[0][:, o2 * 512:(o2 + 1) * 512], ones1f[:], src[:, o2 * 512:(o2 + 1) * 512],
                start=True, stop=True),
                r=["ones1f", srn] + (["psA0r"] if o2 == 0 else []), w=["psA0"])
        b.c("vector", lambda e, dst=dst: e.tensor_copy(dst[:], psA[0][:]),
            r=["psA0"], w=[dn, "psA0r", "lngsR"])

    for ti in range(NT):
        b.d("sync", O1t[:], O1d[ts(ti), :], r=["O1tR"], w=["O1t"])
        b.d("sync", O2t[:], O2d[ts(ti), :], r=["O2tR", "outfD", "outf"], w=["O2t"])
        b.d("sync", x3t[:], x3d[ts(ti), :], r=["x3tR"], w=["x3t"])
        b.c("vector", lambda e, ti=ti: e.tensor_tensor(
            sm[:, 0:1], ml1[:, ti:ti + 1], ml2[:, ti:ti + 1], ALU.max),
            r=["ml1", "ml2"], w=["sm"])
        b.c("vector", lambda e, ti=ti: e.tensor_tensor(
            sm[:, 1:2], ml1[:, ti:ti + 1], sm[:, 0:1], ALU.subtract),
            r=["ml1", "sm"], w=["sm1"])
        b.c("scalar", lambda e: e.activation(out=sm[:, 1:2], in_=sm[:, 1:2], func=AF.Exp),
            r=["sm1"], w=["sm1"])
        b.c("vector", lambda e, ti=ti: e.tensor_tensor(
            sm[:, 2:3], ml2[:, ti:ti + 1], sm[:, 0:1], ALU.subtract),
            r=["ml2", "sm"], w=["sm2"])
        b.c("scalar", lambda e: e.activation(out=sm[:, 2:3], in_=sm[:, 2:3], func=AF.Exp),
            r=["sm2"], w=["sm2"])
        b.c("vector", lambda e: e.tensor_scalar(O1t[:], O1t[:], sm[:, 1:2], None, ALU.mult),
            r=["O1t", "sm1"], w=["O1t"])
        b.c("vector", lambda e: e.tensor_scalar(O2t[:], O2t[:], sm[:, 2:3], None, ALU.mult),
            r=["O2t", "sm2"], w=["O2t"])
        b.c("vector", lambda e: e.tensor_tensor(O1t[:], O1t[:], O2t[:], ALU.add),
            r=["O1t", "O2t"], w=["O1t", "O2tR"])
        b.c("vector", lambda e, ti=ti: e.tensor_tensor(
            sm[:, 3:4], ml1[:, NT + ti:NT + ti + 1], sm[:, 1:2], ALU.mult),
            r=["ml1", "sm1"], w=["sm3"])
        b.c("vector", lambda e, ti=ti: e.tensor_tensor(
            sm[:, 4:5], ml2[:, NT + ti:NT + ti + 1], sm[:, 2:3], ALU.mult),
            r=["ml2", "sm2"], w=["sm4"])
        b.c("vector", lambda e: e.tensor_tensor(sm[:, 3:4], sm[:, 3:4], sm[:, 4:5], ALU.add),
            r=["sm3", "sm4"], w=["sm3"])
        b.c("vector", lambda e: e.reciprocal(sm[:, 3:4], sm[:, 3:4]), r=["sm3"], w=["sm3"])
        b.c("vector", lambda e: e.tensor_scalar(hb[:], O1t[:], sm[:, 3:4], None, ALU.mult),
            r=["O1t", "sm3"], w=["hb", "O1tR"])
        for g4 in range(2):
            for jj in range(4):
                c1 = g4 * 4 + jj
                b.c("tensor", lambda e, c1=c1, jj=jj: e.matmul(
                    psT[:, jj * 128:(jj + 1) * 128], hb[:, c1 * 128:(c1 + 1) * 128],
                    idn[:], start=(jj == 0), stop=(jj == 3), is_transpose=True,
                    skip_group_check=True),
                    r=["hb", "idn"] + (["psTr"] if jj == 0 else []), w=["psT"])
            b.c("vector", lambda e, g4=g4: e.tensor_copy(
                hT[:, g4 * 512:(g4 + 1) * 512], psT[:]),
                r=["psT"], w=["hT", "psTr"])
        for c1 in range(8):
            for o2 in range(2):
                b.c("tensor", lambda e, c1=c1, o2=o2: e.matmul(
                    psA[0][:, o2 * 512:(o2 + 1) * 512], hT[:, c1 * 128:(c1 + 1) * 128],
                    outWS[:, c1 * 1024 + o2 * 512: c1 * 1024 + (o2 + 1) * 512],
                    start=(c1 == 0), stop=False, skip_group_check=True),
                    r=["hT", "outWS"] + (["psA0r"] if c1 == 0 and o2 == 0 else []),
                    w=["psA0"])
        for o2 in range(2):
            b.c("tensor", lambda e, o2=o2: e.matmul(
                psA[0][:, o2 * 512:(o2 + 1) * 512], ones1[:], obrow[:, o2 * 512:(o2 + 1) * 512],
                start=False, stop=True, skip_group_check=True),
                r=["ones1", "obrow"], w=["psA0"])
        b.c("vector", lambda e: e.tensor_tensor(x2t[:], psA[0][:], x3t[:], ALU.add),
            r=["psA0", "x3t"], w=["x2t", "psA0r"])
        ln_token_tile(b, x2t, scr, mn, scr, "p10ln", in_res=["x2t"],
                      out_res=["scrO"])
        b.c("vector", lambda e: e.tensor_tensor(scr[:], scr[:], g2B[:], ALU.mult),
            r=["scrO", "g2B"], w=["scrO"])
        b.c("vector", lambda e: e.tensor_tensor(scr[:], scr[:], b2B[:], ALU.add),
            r=["scrO", "b2B"], w=["scrO"])
        b.c("vector", lambda e: e.tensor_copy(hb[:], scr[:]), r=["scrO"], w=["hb"])
        for g4 in range(2):
            for jj in range(4):
                c1 = g4 * 4 + jj
                b.c("tensor", lambda e, c1=c1, jj=jj: e.matmul(
                    psT[:, jj * 128:(jj + 1) * 128], hb[:, c1 * 128:(c1 + 1) * 128],
                    idn[:], start=(jj == 0), stop=(jj == 3), is_transpose=True,
                    skip_group_check=True),
                    r=["hb", "idn"] + (["psTr"] if jj == 0 else []), w=["psT"])
            b.c("vector", lambda e, g4=g4: e.tensor_copy(
                hT[:, g4 * 512:(g4 + 1) * 512], psT[:]),
                r=["psT"], w=["hT", "psTr"])
        for og in range(8):
            pw = psW[og % 2]
            pwn = f"psW{og % 2}"
            for f in range(8):
                b.c("tensor", lambda e, og=og, f=f, pw=pw: e.matmul(
                    pw[:], hT[:, f * 128:(f + 1) * 128],
                    W1S[:, f * 4096 + og * 512: f * 4096 + (og + 1) * 512],
                    start=(f == 0), stop=False, skip_group_check=True),
                    r=["hT", "W1S"] + ([pwn + "r"] if f == 0 else []), w=[pwn])
            b.c("tensor", lambda e, og=og, pw=pw: e.matmul(
                pw[:], ones1[:], b1row[:, og * 512:(og + 1) * 512],
                start=False, stop=True, skip_group_check=True),
                r=["ones1", "b1row"], w=[pwn])
            b.c("scalar", lambda e, og=og, pw=pw: e.activation(
                out=mg[:], in_=pw[:], func=AF.Gelu),
                r=[pwn, "mgR"], w=["mg", pwn + "r"])
            for j in range(4):
                b.c("tensor", lambda e, j=j: e.matmul(
                    psT[:, j * 128:(j + 1) * 128], mg[:, j * 128:(j + 1) * 128], idn[:],
                    start=(j == 0), stop=(j == 3), is_transpose=True,
                    skip_group_check=True),
                    r=["mg", "idn"] + (["psTr"] if j == 0 else []),
                    w=["psT"] + (["mgR"] if j == 3 else []))
            b.c("vector", lambda e, og=og: e.tensor_copy(
                mT[:, og * 512:(og + 1) * 512], psT[:]),
                r=["psT"], w=["mT", "psTr"])
        for o2 in range(2):
            for hc in range(32):
                b.c("tensor", lambda e, hc=hc, o2=o2: e.matmul(
                    psA[1][:, o2 * 512:(o2 + 1) * 512], mT[:, hc * 128:(hc + 1) * 128],
                    W2S[:, hc * 1024 + o2 * 512: hc * 1024 + (o2 + 1) * 512],
                    start=(hc == 0), stop=False, skip_group_check=True),
                    r=["mT", "W2S"] + (["psA1r"] if hc == 0 and o2 == 0 else []),
                    w=["psA1"])
            b.c("tensor", lambda e, o2=o2: e.matmul(
                psA[1][:, o2 * 512:(o2 + 1) * 512], ones1[:], b2row[:, o2 * 512:(o2 + 1) * 512],
                start=False, stop=True, skip_group_check=True),
                r=["ones1", "b2row"], w=["psA1"])
        b.c("vector", lambda e: e.tensor_tensor(outf[:], psA[1][:], x2t[:], ALU.add),
            r=["psA1", "x2t", "O2tR"], w=["outf", "psA1r"])
        b.c("vector", lambda e: e.tensor_tensor(outf[:], outf[:], x3t[:], ALU.add),
            r=["outf", "x3t"], w=["outf", "x3tR"])
        ln_token_tile(b, outf, outf, mn, scr, "p10ln", in_res=["outf"],
                      out_res=["outf"])
        b.d("sync", o_g3[ts(ti), :], outf[:], r=["outf"], w=["outfD"])
    b.replay()
    return nc


def run_p10(I, x3full, p9res, _cache={}):
    if "p10" not in _cache:
        _cache["p10"] = build_p10()
    nc = _cache["p10"]
    cs = host_consts()
    maps = []
    for c in range(8):
        bb, h = c // 2, c % 2
        if h == 0:
            O2 = np.zeros((NTOK, C), np.float32)
            ml2 = np.zeros((128, 2 * NT), np.float32)
            ml2[:, 0:NT] = -1e30
        else:
            O2a = np.asarray(p9res[bb * 2]["o_O2"])
            O2b = np.asarray(p9res[bb * 2 + 1]["o_O2"])
            O2 = np.concatenate([O2a, O2b], 0)
            m2a = np.asarray(p9res[bb * 2]["o_ml2"])
            m2b = np.asarray(p9res[bb * 2 + 1]["o_ml2"])
            ml2 = np.concatenate([m2a[:, 0:16], m2b[:, 0:16], m2a[:, 16:32], m2b[:, 16:32]],
                                 axis=1)
        maps.append({
            "O1d": np.asarray(p9res[c]["o_O1"]), "O2d": np.ascontiguousarray(O2),
            "ml1d": np.asarray(p9res[c]["o_ml1"]), "ml2d": np.ascontiguousarray(ml2),
            "x3d": np.ascontiguousarray(x3full[bb, h * NTOK:(h + 1) * NTOK]),
            "outW": I["L1_attn_out_W"], "outbr": I["L1_attn_out_b"].reshape(1, C),
            "ln2gr": I["L1_attn_ln2_g"].reshape(1, C),
            "ln2br": I["L1_attn_ln2_b"].reshape(1, C),
            "W1d": I["L1_attn_mlp_W1"], "b1r": I["L1_attn_mlp_b1"].reshape(1, 4096),
            "W2d": I["L1_attn_mlp_W2"], "b2r": I["L1_attn_mlp_b2"].reshape(1, C),
            "c_idn": cs["c_idn"], "c_ones": np.ones((1, 512), np.float32),
        })
    res = run_bass_kernel_spmd(nc, maps, core_ids=list(range(8)))
    return res.results



def build_p11():
    nc = bass.Bass("TRN2", target_bir_lowering=False)
    g3d = nc.dram_tensor("g3d", [NTOK, C], F32, kind="ExternalInput").ap()
    wred = nc.dram_tensor("wred", [128, 128], F32, kind="ExternalInput").ap()
    wgd = nc.dram_tensor("wgd", [8, 16], F32, kind="ExternalInput").ap()
    W1d = nc.dram_tensor("W1d", [8 * 128, 8 * 2048], F32, kind="ExternalInput").ap()
    b1d = nc.dram_tensor("b1d", [8, 2048], F32, kind="ExternalInput").ap()
    W2d = nc.dram_tensor("W2d", [8 * 128, 16 * 1024], F32, kind="ExternalInput").ap()
    b2d = nc.dram_tensor("b2d", [8, C], F32, kind="ExternalInput").ap()
    c_idn = nc.dram_tensor("c_idn", [128, 128], F32, kind="ExternalInput").ap()
    c_ones = nc.dram_tensor("c_ones", [1, 512], F32, kind="ExternalInput").ap()
    sc_xT = nc.dram_tensor("sc_xT", [C, NTOK], BF16, kind="Internal").ap()
    o_g4 = nc.dram_tensor("o_g4", [NTOK, C], F32, kind="ExternalOutput").ap()

    b = Builder(nc)
    A = nc.alloc_sbuf_tensor
    idn = A("idn", [128, 128], BF16)
    ones1 = A("ones1", [1, 512], BF16)
    sm8 = A("sm8", [8, 4], F32)
    W1S_ = [A(f"W1S{i}", [128, 8 * 2048], BF16) for i in range(2)]
    W2S = A("W2S", [128, 16 * 1024], BF16)
    b1row = A("b1row", [1, 2048], BF16)
    b2S = A("b2S", [8, 1024], BF16)
    gt = A("gt", [128, 1024], F32)
    xTg = [A(f"xTg{i}", [128, 8 * 512], BF16) for i in range(2)]
    msc_all = A("msc_all", [128, NT * 8], BF16)
    mscR = A("mscR", [1, 512], BF16)
    msc8g = A("msc8g", [8, 512], BF16)
    mgb = [A(f"mgb{i}", [128, 512], BF16) for i in range(2)]
    _grp = [nc.sbuf_tensor("idnf", [128, 128], F32),
            nc.sbuf_tensor("wredS", [128, 8 * 16], F32),
            nc.sbuf_tensor("wgs", [8, 16], F32),
            nc.sbuf_tensor("sq8", [8, 16], F32),
            nc.sbuf_tensor("wgnT", [16, 8], F32),
            nc.sbuf_tensor("lgf", [8, 512], F32),
            nc.sbuf_tensor("ltok", [128, 8], F32),
            nc.sbuf_tensor("gts", [128, 8], F32),
            nc.sbuf_tensor("eq", [128, 8], F32),
            nc.sbuf_tensor("rem", [128, 2], F32),
            nc.sbuf_tensor("xredf", [16, 512], F32),
            nc.sbuf_tensor("xTgF", [128, 8 * 512], F32)]
    idnf, wredS, wgs, sq8, wgnT, lgf, ltok, gts, eq, rem, xredf, xTgF = [
        g.__enter__() for g in _grp]
    for g in reversed(_grp):
        g.__exit__(None, None, None)
    hsc = A("hsc", [128, 16 * 512], BF16)
    out_all = A("out_all", [128, NT * 1024], BF16)
    mn = A("mn", [128, 4], F32)

    P = nc.alloc_psum_tensor
    psW = [P(f"psW{i}", [128, 512], F32) for i in range(2)]
    psO = P("psO", [128, 1024], F32)
    psT = P("psT", [128, 512], BF16)
    psTf = P("psTf", [128, 512], F32)
    psM = P("psM", [128, 512], F32)

    b.d("sync", idn[:], c_idn[:], w=["idn"])
    b.d("sync", idnf[:], c_idn[:], w=["idnf"])
    b.d("sync", ones1[:], c_ones[:], w=["ones1"])
    b.d("sync", wredS[:], wred[:], w=["wredS"])
    b.d("sync", wgs[:], wgd[:], w=["wgs"])
    b.d("gpsimd", b2S[:], b2d[:], w=["b2S"])
    b.c("scalar", lambda e: e.activation(out=sq8[:], in_=wgs[:], func=AF.Square,
                                         accum_out=sm8[:, 1:2]),
        r=["wgs"], w=["sm8", "sq8"])
    b.c("scalar", lambda e: e.activation(out=sm8[:, 1:2], in_=sm8[:, 1:2], func=AF.Sqrt),
        r=["sm8"], w=["sm8b"])
    b.c("vector", lambda e: e.tensor_scalar(sm8[:, 1:2], sm8[:, 1:2], 1e-4, None, ALU.max),
        r=["sm8b"], w=["sm8b"])
    b.c("vector", lambda e: e.reciprocal(sm8[:, 1:2], sm8[:, 1:2]), r=["sm8b"], w=["sm8b"])
    b.c("vector", lambda e: e.tensor_scalar(wgs[:], wgs[:], sm8[:, 1:2], None, ALU.mult),
        r=["wgs", "sm8b"], w=["wgs"])
    b.c("tensor", lambda e: e.transpose(psTf[0:16, 0:8], wgs[:], idnf[0:8, 0:8]),
        r=["wgs", "idnf", "psTfr"], w=["psTf"])
    b.c("vector", lambda e: e.tensor_copy(wgnT[:], psTf[0:16, 0:8]),
        r=["psTf"], w=["wgnT", "psTfr"])

    for g in range(NG):
        for i in range(4):
            ti = g * 4 + i
            b.d("sync", gt[:], g3d[ts(ti), :], r=["gtR"], w=["gt"])
            for g4 in range(2):
                for jj in range(4):
                    c1 = g4 * 4 + jj
                    b.c("tensor", lambda e, c1=c1, jj=jj: e.matmul(
                        psTf[:, jj * 128:(jj + 1) * 128], gt[:, c1 * 128:(c1 + 1) * 128],
                        idnf[:], start=(jj == 0), stop=(jj == 3), is_transpose=True,
                        skip_group_check=True),
                        r=["gt", "idnf"] + (["psTfr"] if jj == 0 else []), w=["psTf"])
                for jj in range(4):
                    c1 = g4 * 4 + jj
                    b.c("vector", lambda e, c1=c1, i=i, jj=jj: e.tensor_copy(
                        xTgF[:, c1 * 512 + i * 128: c1 * 512 + (i + 1) * 128],
                        psTf[:, jj * 128:(jj + 1) * 128]),
                        r=["psTf"], w=["xTgF"] + (["psTfr"] if jj == 3 else [])
                        + (["gtR"] if g4 == 1 and jj == 3 else []))
        for c1 in range(8):
            b.c("vector", lambda e, c1=c1: e.tensor_copy(
                xTg[0][:, c1 * 512:(c1 + 1) * 512], xTgF[:, c1 * 512:(c1 + 1) * 512]),
                r=["xTgF"], w=["xTg0s"])
            b.d("sync", sc_xT[ts(c1), g * 512:(g + 1) * 512],
                xTg[0][:, c1 * 512:(c1 + 1) * 512], r=["xTg0s"], w=["scxT"])
        for c1 in range(8):
            b.c("tensor", lambda e, c1=c1: e.matmul(
                psW[0][0:16, :], wredS[:, c1 * 16:(c1 + 1) * 16],
                xTgF[:, c1 * 512:(c1 + 1) * 512],
                start=(c1 == 0), stop=(c1 == 7)),
                r=["wredS", "xTgF"] + (["psW0r"] if c1 == 0 else []), w=["psW0"])
        b.c("vector", lambda e: e.tensor_copy(xredf[:], psW[0][0:16, :]),
            r=["psW0"], w=["xredf", "psW0r"])
        b.c("tensor", lambda e: e.matmul(psW[1][0:8, :], wgnT[:], xredf[:],
                                         start=True, stop=True),
            r=["wgnT", "xredf", "psW1r"], w=["psW1"])
        b.c("vector", lambda e: e.tensor_copy(lgf[:], psW[1][0:8, :]),
            r=["psW1"], w=["lgf", "psW1r"])
        for i in range(4):
            b.c("tensor", lambda e, i=i: e.transpose(
                psTf[:, 0:8], lgf[:, i * 128:(i + 1) * 128], idnf[0:8, 0:8]),
                r=["lgf", "idnf", "psTfr"], w=["psTf"])
            b.c("vector", lambda e: e.tensor_copy(ltok[:], psTf[:, 0:8]),
                r=["psTf"], w=["ltok", "psTfr"])
            b.c("vector", lambda e: e.tensor_reduce(mn[:, 0:1], ltok[:], AX.X, ALU.max,
                                                    negate=True),
                r=["ltok"], w=["mnmax"])
            b.c("scalar", lambda e: e.activation(out=gts[:], in_=ltok[:], func=AF.Exp,
                                                 bias=mn[:, 0:1], scale=1.0),
                r=["ltok", "mnmax"], w=["gts"])
            b.c("vector", lambda e: e.tensor_reduce(mn[:, 1:2], gts[:], AX.X, ALU.add),
                r=["gts"], w=["mnsum"])
            b.c("vector", lambda e: e.reciprocal(mn[:, 1:2], mn[:, 1:2]),
                r=["mnsum"], w=["mnsum"])
            b.c("vector", lambda e: e.tensor_scalar(gts[:], gts[:], mn[:, 1:2], None, ALU.mult),
                r=["gts", "mnsum"], w=["gts"])
            b.c("vector", lambda e: e.tensor_reduce(mn[:, 2:3], gts[:], AX.X, ALU.max),
                r=["gts"], w=["mngm"])
            b.c("vector", lambda e: e.tensor_scalar(eq[:], gts[:], mn[:, 2:3], None, ALU.is_ge),
                r=["gts", "mngm"], w=["eq"])
            b.c("vector", lambda e: e.tensor_scalar(rem[:, 0:1], mn[:, 2:3], 0.0, None,
                                                    ALU.mult),
                r=["mngm"], w=["rem"])
            b.c("vector", lambda e: e.tensor_scalar(rem[:, 0:1], rem[:, 0:1], 1.0, None,
                                                    ALU.add),
                r=["rem"], w=["rem"])
            ti = g * 4 + i
            for ee in range(8):
                b.c("vector", lambda e, ee=ee: e.tensor_tensor(
                    eq[:, ee:ee + 1], eq[:, ee:ee + 1], rem[:, 0:1], ALU.mult),
                    r=["eq", "rem"], w=["eq"])
                if ee < 7:
                    b.c("vector", lambda e, ee=ee: e.tensor_tensor(
                        rem[:, 0:1], rem[:, 0:1], eq[:, ee:ee + 1], ALU.subtract),
                        r=["eq", "rem"], w=["rem"])
            b.c("vector", lambda e, ti=ti: e.tensor_scalar(
                msc_all[:, ti * 8:(ti + 1) * 8], eq[:], mn[:, 2:3], None, ALU.mult),
                r=["eq", "mngm"], w=["msc_all"])

    for e_ in range(8):
        W1S = W1S_[e_ % 2]
        w1n = f"W1S{e_ % 2}"
        b.d("gpsimd", W1S[:], W1d[e_ * 128:(e_ + 1) * 128, :],
            r=[w1n + "r"], w=[w1n])
        b.d("gpsimd", W2S[:], W2d[e_ * 128:(e_ + 1) * 128, :],
            r=["W2Sr"], w=["W2S"])
        b.d("gpsimd", b1row[:], b1d[e_:e_ + 1, :], r=["b1rowR"], w=["b1row"])
        for g in range(NG):
            gsl = g % 2
            b.d("sync", xTg[gsl][:].rearrange("p (c t) -> p c t", c=8),
                sc_xT[:, g * 512:(g + 1) * 512].rearrange("(c p) t -> p c t", p=128),
                r=["scxT", f"xTg{gsl}R"], w=[f"xTg{gsl}"])
            for i in range(4):
                ti = g * 4 + i
                b.c("tensor", lambda e, ti=ti, e_=e_, i=i: e.matmul(
                    psT[0:1, i * 128:(i + 1) * 128],
                    msc_all[:, ti * 8 + e_: ti * 8 + e_ + 1], idn[:],
                    start=(i == 0), stop=(i == 3), is_transpose=True,
                    skip_group_check=True),
                    r=["msc_all", "idn"] + (["psTr"] if i == 0 else []), w=["psT"])
            b.c("vector", lambda e: e.tensor_copy(mscR[:], psT[0:1, 0:512]),
                r=["psT"], w=["mscR", "psTr"])
            b.c("tensor", lambda e: e.matmul(psM[:], ones1[:, 0:128], mscR[:],
                                             start=True, stop=True),
                r=["ones1", "mscR", "psMr"], w=["psM"])
            if e_ == 0:
                for i in range(4):
                    ti = g * 4 + i
                    b.c("tensor", lambda e, ti=ti, i=i: e.matmul(
                        psT[0:8, i * 128:(i + 1) * 128],
                        msc_all[:, ti * 8:(ti + 1) * 8], idn[:],
                        start=(i == 0), stop=(i == 3), is_transpose=True,
                        skip_group_check=True),
                        r=["msc_all", "idn"] + (["psTr"] if i == 0 else []), w=["psT"])
                b.c("vector", lambda e: e.tensor_copy(msc8g[:], psT[0:8, 0:512]),
                    r=["psT"], w=["msc8g", "psTr"])
            for hc in range(16):
                pw = psW[hc % 2]
                pwn = f"psW{hc % 2}"
                for f in range(8):
                    b.c("tensor", lambda e, hc=hc, f=f, pw=pw, gsl=gsl, W1S=W1S: e.matmul(
                        pw[:], W1S[:, f * 2048 + hc * 128: f * 2048 + (hc + 1) * 128],
                        xTg[gsl][:, f * 512:(f + 1) * 512],
                        start=(f == 0), stop=False, skip_group_check=True),
                        r=[w1n, f"xTg{gsl}"] + ([pwn + "r"] if f == 0 else []), w=[pwn])
                b.c("tensor", lambda e, hc=hc, pw=pw: e.matmul(
                    pw[:], b1row[:, hc * 128:(hc + 1) * 128], ones1[:],
                    start=False, stop=True, skip_group_check=True),
                    r=["b1row", "ones1"], w=[pwn])
                mg_ = mgb[hc % 2]
                mgn = f"mgb{hc % 2}"
                b.c("scalar", lambda e, pw=pw, mg_=mg_: e.activation(out=mg_[:], in_=pw[:],
                                                                     func=AF.Gelu),
                    r=[pwn, mgn + "R"], w=[mgn, pwn + "r"])
                b.c("vector", lambda e, hc=hc, mg_=mg_: e.tensor_tensor(
                    hsc[:, hc * 512:(hc + 1) * 512], mg_[:], psM[:], ALU.mult),
                    r=[mgn, "psM"],
                    w=[f"hsc_{hc}", mgn + "R"] + (["xTgF", "idnf", "wredS", "wgs", "sq8", "wgnT", "lgf", "ltok", "gts", "eq", "rem"] if e_ == 0 and g == 0 and hc == 0 else []))
            for i in range(4):
                ti = g * 4 + i
                if e_ == 0:
                    for o2 in range(2):
                        b.c("tensor", lambda e, i=i, o2=o2: e.matmul(
                            psO[:, o2 * 512:(o2 + 1) * 512], msc8g[:, i * 128:(i + 1) * 128],
                            b2S[:, o2 * 512:(o2 + 1) * 512],
                            start=True, stop=False, skip_group_check=True),
                            r=["msc8g", "b2S"] + (["psOr"] if o2 == 0 else []), w=["psO"])
                for hc in range(16):
                    for o2 in range(2):
                        b.c("tensor", lambda e, i=i, hc=hc, o2=o2: e.matmul(
                            psO[:, o2 * 512:(o2 + 1) * 512],
                            hsc[:, hc * 512 + i * 128: hc * 512 + (i + 1) * 128],
                            W2S[:, hc * 1024 + o2 * 512: hc * 1024 + (o2 + 1) * 512],
                            start=(hc == 0 and e_ != 0), stop=(hc == 15),
                            skip_group_check=True),
                            r=[f"hsc_{hc}", "W2S"] + (["psOr"] if hc == 0 and o2 == 0 and e_ != 0 else []),
                            w=["psO"])
                if e_ == 0:
                    b.c("vector", lambda e, ti=ti: e.tensor_copy(
                        out_all[:, ti * 1024:(ti + 1) * 1024], psO[:]),
                        r=["psO"],
                        w=[f"oa_{ti}", "psOr"] + (["xTgF", "idnf", "wredS", "wgs", "sq8", "wgnT", "lgf", "ltok", "gts", "eq", "rem"] if g == 0 and i == 0 else []))
                else:
                    b.c("vector", lambda e, ti=ti: e.tensor_tensor(
                        out_all[:, ti * 1024:(ti + 1) * 1024],
                        out_all[:, ti * 1024:(ti + 1) * 1024], psO[:], ALU.add),
                        r=[f"oa_{ti}", "psO"], w=[f"oa_{ti}", "psOr"])
        b.c("sync", lambda e: e.nop(), r=[w1n, "W2S", "b1row"],
            w=[w1n + "r", "W2Sr", "b1rowR"])

    for ti in range(NT):
        b.d("sync", gt[:], g3d[ts(ti), :], r=["gtR"], w=["gt"])
        b.c("vector", lambda e, ti=ti: e.tensor_tensor(
            gt[:], gt[:], out_all[:, ti * 1024:(ti + 1) * 1024], ALU.add),
            r=["gt", f"oa_{ti}"], w=["gt"])
        ln_token_tile(b, gt, gt, mn, xTg[0], "p11ln", in_res=["gt"], out_res=["gt", "gtR"])
        b.d("sync", o_g4[ts(ti), :], gt[:], r=["gt"], w=["g4D"])
    b.replay()
    return nc


def run_p11(I, g3full, _cache={}):
    if "p11" not in _cache:
        _cache["p11"] = build_p11()
    nc = _cache["p11"]
    cs = host_consts()
    maps = []
    for c in range(8):
        bb, h = c // 2, c % 2
        maps.append({
            "g3d": np.ascontiguousarray(g3full[bb, h * NTOK:(h + 1) * NTOK]),
            "wred": np.ascontiguousarray(
                I["L1_ffn_w_red"].reshape(8, 128, 16).transpose(1, 0, 2).reshape(128, 128)),
            "wgd": I["L1_ffn_wg"],
            "W1d": np.ascontiguousarray(
                I["L1_ffn_W1"].reshape(8, 8, 128, 2048).transpose(0, 2, 1, 3).reshape(8 * 128, 8 * 2048)),
            "b1d": I["L1_ffn_b1"],
            "W2d": np.ascontiguousarray(
                I["L1_ffn_W2"].reshape(8, 16, 128, 1024).transpose(0, 2, 1, 3).reshape(8 * 128, 16 * 1024)),
            "b2d": I["L1_ffn_b2"],
            "c_idn": cs["c_idn"], "c_ones": np.ones((1, 512), np.float32),
        })
    res = run_bass_kernel_spmd(nc, maps, core_ids=list(range(8)))
    return res.results



def _full(res, key):
    return np.stack([np.asarray(res[c][key]) for c in range(8)]).reshape(B, T, -1)


def kernel(**I):
    p1 = run_p1(I)
    p2 = run_p2(I, p1)
    g1 = _full(p2, "o_g1")
    x = g1
    for l in range(2):
        pa = run_p3a(I, x, l)
        pb = run_p3b(I, pa, l)
        x = _full(pb, "o_f")
    f2 = x
    p6 = run_p6(I, g1, f2)
    p7 = run_p7(I, p6)
    p8 = run_p8(I, p6, p7)
    p9 = run_p9(I, p6, p8)
    x3 = _full(p6, "o_x3")
    p10 = run_p10(I, x3, p9)
    g3 = _full(p10, "o_g3")
    p11 = run_p11(I, g3)
    g4 = _full(p11, "o_g4")
    return g4.astype(np.float32)


# version 06
# speedup vs baseline: 1.1462x; 1.0421x over previous
import numpy as np
import concourse.bass as bass
import concourse.mybir as mybir
from concourse.bass_utils import run_bass_kernel_spmd

F32 = mybir.dt.float32
BF16 = mybir.dt.bfloat16
AF = mybir.ActivationFunctionType
ALU = mybir.AluOpType
AX = mybir.AxisListType

B, T, C = 4, 8192, 1024
NTOK = 4096
NT = NTOK // 128
NG = NTOK // 512
NFC = C // 128

COMPUTE_ENGINES = ("scalar", "vector", "tensor", "gpsimd", "sync")
DMA_QUEUES = ("sync", "gpsimd", "scalar")
NDMASEM = 8


def ts(i, n=128):
    return slice(i * n, (i + 1) * n)


class Op:
    __slots__ = ("eng", "fn", "kind", "idx", "eidx", "reads", "writes",
                 "waits", "sem", "semval")

    def __init__(self, eng, fn, kind, idx, eidx, reads, writes):
        self.eng, self.fn, self.kind = eng, fn, kind
        self.idx, self.eidx = idx, eidx
        self.reads, self.writes = reads, writes
        self.waits = []
        self.sem = None
        self.semval = None


class Builder:
    def __init__(self, nc):
        self.nc = nc
        self.ops = []
        self.eng_count = {e: 0 for e in COMPUTE_ENGINES}
        self.last_write = {}
        self.readers = {}
        self.finished = False

    def _record(self, eng, fn, kind, reads, writes):
        idx = len(self.ops)
        op = Op(eng, fn, kind, idx, self.eng_count[eng], tuple(reads), tuple(writes))
        self.eng_count[eng] += 1
        deps = set()
        for r in op.reads:
            w = self.last_write.get(r)
            if w is not None:
                deps.add(w)
        for wr in op.writes:
            w = self.last_write.get(wr)
            if w is not None:
                deps.add(w)
            for rd in self.readers.get(wr, ()):
                deps.add(rd)
        for d in sorted(deps):
            dop = self.ops[d]
            if (dop.kind == "d" or op.kind == "d" or dop.eng != eng
                    or eng in ("vector", "scalar", "gpsimd")):
                op.waits.append(d)
        for wr in op.writes:
            self.last_write[wr] = idx
            self.readers[wr] = []
        for r in op.reads:
            self.readers.setdefault(r, []).append(idx)
        self.ops.append(op)
        return op

    def c(self, eng, fn, r=(), w=()):
        return self._record(eng, fn, "c", r, w)

    def d(self, queue, out, in_, r=(), w=(), **kw):
        try:
            if out.tensor.dtype != in_.tensor.dtype:
                queue = "gpsimd"
        except AttributeError:
            pass
        fn = lambda e, o=out, i=in_, kw=kw: e.dma_start(out=o, in_=i, **kw)
        return self._record(queue, fn, "d", r, w)

    def finish(self):
        fin_deps = [op.idx for op in self.ops if op.kind == "d"]
        lastc = {}
        for op in self.ops:
            if op.kind == "c":
                lastc[op.eng] = op.idx
        fin = self._record("sync", lambda e: e.nop(), "c", (), ())
        fin.waits = sorted(set(fin_deps + [v for k, v in lastc.items() if k != "sync"]))
        self.finished = True

    def replay(self):
        if not self.finished:
            self.finish()
        nc = self.nc
        watched = set()
        for op in self.ops:
            watched.update(op.waits)
        comp_sems = {e: nc.alloc_semaphore(f"se_{e}") for e in COMPUTE_ENGINES}
        comp_val = {e: 0 for e in COMPUTE_ENGINES}
        dma_sems = {q: [nc.alloc_semaphore(f"sd_{q}{i}") for i in range(NDMASEM)]
                    for q in DMA_QUEUES}
        dma_uses = {q: [0] * NDMASEM for q in DMA_QUEUES}
        dma_rr = {q: 0 for q in DMA_QUEUES}
        for op in self.ops:
            if op.kind == "c":
                if op.idx in watched:
                    comp_val[op.eng] += 1
                    op.sem = comp_sems[op.eng]
                    op.semval = comp_val[op.eng]
            else:
                q = op.eng
                k = dma_rr[q] % NDMASEM
                dma_rr[q] += 1
                dma_uses[q][k] += 1
                op.sem = dma_sems[q][k]
                op.semval = dma_uses[q][k] * 16
        satisfied = {e: {} for e in COMPUTE_ENGINES}

        def emit_for(blk, name):
            myops = [o for o in self.ops if o.eng == name]
            if not myops:
                return

            def body(e, _ops=myops, _name=name):
                sat = satisfied[_name]
                for op in _ops:
                    for d in op.waits:
                        dop = self.ops[d]
                        if dop.sem is None:
                            continue
                        key = id(dop.sem)
                        need = dop.semval
                        if sat.get(key, 0) >= need:
                            continue
                        e.wait_ge(dop.sem, need)
                        sat[key] = need
                    inst = op.fn(e)
                    if op.kind == "c" and op.sem is not None:
                        inst.then_inc(op.sem, 1)
                    elif op.kind == "d":
                        inst.then_inc(op.sem, 16)

            getattr(blk, name)(body)

        with nc.Block(no_gpsimd_drain=True) as blk:
            for name in COMPUTE_ENGINES:
                emit_for(blk, name)



def emit_transpose_weight(b, wsrc, wT, psT, idn, tag, nchunk_o=8, nchunk_f=8):
    for o in range(nchunk_o):
        for f in range(nchunk_f):
            j = o * nchunk_f + f
            pt = psT[j % len(psT)]
            b.c("tensor",
                lambda e, o=o, f=f, pt=pt: e.transpose(
                    pt[0][:], wsrc[:, (o * nchunk_f + f) * 128:(o * nchunk_f + f + 1) * 128], idn[:]),
                r=[wsrc.name, idn.name, pt[1] + "r"], w=[pt[1]])
            b.c("vector",
                lambda e, o=o, f=f, pt=pt: e.tensor_copy(
                    wT[:, (f * nchunk_o + o) * 128:(f * nchunk_o + o + 1) * 128], pt[0][:]),
                r=[pt[1]], w=[wT.name, pt[1] + "r"])


def ln_token_tile(b, xa, g1t, mn, scr, tag, gain=None, bias_ap=None, eps=1e-5,
                  out_res=None, in_res=None, D=1024):
    rs = list(in_res or [xa.name])
    b.c("vector", lambda e: e.tensor_reduce(mn[:, 0:1], xa[:], AX.X, ALU.add),
        r=rs, w=[tag + "mn"])
    b.c("scalar", lambda e: e.activation(out=scr[:, 0:D], in_=xa[:], func=AF.Square),
        r=rs, w=[scr.name])
    b.c("vector", lambda e: e.tensor_reduce(mn[:, 1:2], scr[:, 0:D], AX.X, ALU.add),
        r=[scr.name], w=[tag + "mn"])
    b.c("vector", lambda e: e.tensor_scalar(mn[:, 0:1], mn[:, 0:1], 1.0 / D, None, ALU.mult),
        r=[tag + "mn"], w=[tag + "mn"])
    b.c("vector", lambda e: e.tensor_tensor(mn[:, 2:3], mn[:, 0:1], mn[:, 0:1], ALU.mult),
        r=[tag + "mn"], w=[tag + "mn"])
    b.c("vector", lambda e: e.tensor_scalar(mn[:, 1:2], mn[:, 1:2], 1.0 / D, None, ALU.mult),
        r=[tag + "mn"], w=[tag + "mn"])
    b.c("vector", lambda e: e.tensor_tensor(mn[:, 2:3], mn[:, 1:2], mn[:, 2:3], ALU.subtract),
        r=[tag + "mn"], w=[tag + "mn"])
    b.c("vector", lambda e: e.tensor_scalar(mn[:, 2:3], mn[:, 2:3], float(eps), None, ALU.add),
        r=[tag + "mn"], w=[tag + "mn"])
    b.c("scalar", lambda e: e.activation(out=mn[:, 3:4], in_=mn[:, 2:3], func=AF.Sqrt),
        r=[tag + "mn"], w=[tag + "sd"])
    b.c("vector", lambda e: e.reciprocal(mn[:, 3:4], mn[:, 3:4]),
        r=[tag + "sd"], w=[tag + "sd"])
    wout = [g1t.name] if out_res is None else out_res
    b.c("vector", lambda e: e.tensor_scalar(g1t[:], xa[:], mn[:, 0:1], mn[:, 3:4],
                                            ALU.subtract, ALU.mult),
        r=rs + [tag + "mn", tag + "sd"], w=wout)



def build_p1():
    nc = bass.Bass("TRN2", target_bir_lowering=False)
    x = nc.dram_tensor("x", [NTOK, C], F32, kind="ExternalInput").ap()
    Wq = nc.dram_tensor("Wq", [C, C], F32, kind="ExternalInput").ap()
    Wk = nc.dram_tensor("Wk", [C, C], F32, kind="ExternalInput").ap()
    Wv = nc.dram_tensor("Wv", [C, C], F32, kind="ExternalInput").ap()
    bqc = nc.dram_tensor("bqc", [128, 8], F32, kind="ExternalInput").ap()
    bkc = nc.dram_tensor("bkc", [128, 8], F32, kind="ExternalInput").ap()
    bvr = nc.dram_tensor("bvr", [1, C], F32, kind="ExternalInput").ap()
    c_idn = nc.dram_tensor("c_idn", [128, 128], F32, kind="ExternalInput").ap()
    c_seg = nc.dram_tensor("c_seg", [128, 8 * 16], F32, kind="ExternalInput").ap()
    c_hmap = nc.dram_tensor("c_hmap", [16, 8 * 128], F32, kind="ExternalInput").ap()
    c_ones = nc.dram_tensor("c_ones", [1, 128], F32, kind="ExternalInput").ap()
    o_qsmT = nc.dram_tensor("o_qsmT", [C, NTOK], BF16, kind="ExternalOutput").ap()
    o_ctx = nc.dram_tensor("o_ctx", [128, C], F32, kind="ExternalOutput").ap()
    o_ksum = nc.dram_tensor("o_ksum", [128, 8], F32, kind="ExternalOutput").ap()

    b = Builder(nc)
    A = nc.alloc_sbuf_tensor
    idn = A("idn", [128, 128], BF16)
    seg = A("seg", [128, 8 * 16], BF16)
    hmap = A("hmap", [16, 8 * 128], BF16)
    ones1 = A("ones1", [1, 128], BF16)
    bqv = A("bqv", [128, 8], F32)
    bkv = A("bkv", [128, 8], F32)
    bvrow = A("bvrow", [1, C], BF16)
    wload = A("wload", [128, 8 * 1024], BF16)
    WqT = A("WqT", [128, 8 * 1024], BF16)
    WkT = A("WkT", [128, 8 * 1024], BF16)
    WvT = A("WvT", [128, 8 * 1024], BF16)
    xtok = [A(f"xtok{i}", [128, 4 * 1024], BF16) for i in range(2)]
    xT = [A(f"xT{i}", [128, 8 * 512], BF16) for i in range(2)]
    qe = A("qe", [128, 8 * 512], BF16)
    ke = A("ke", [128, 8 * 512], BF16)
    rec = A("rec", [16, 2 * 512], F32)
    recb = A("recb", [16, 2 * 512], BF16)
    bcs = A("bcs", [128, 512], F32)
    qsm = A("qsm", [128, 8 * 512], BF16)
    ksmT = A("ksmT", [128, 512], BF16)
    ksm = [A(f"ksm{i}", [128, 4 * 1024], BF16) for i in range(2)]
    vtok = [A(f"vtok{i}", [128, 4 * 1024], BF16) for i in range(2)]
    ksump = A("ksump", [128, 8], F32)
    ksum1 = A("ksum1", [128, 1], F32)
    ctxs = A("ctxs", [128, 1024], F32)

    P = nc.alloc_psum_tensor
    ps_T = [(P(f"ps_T{i}", [128, 128], BF16), f"ps_T{i}") for i in range(2)]
    ps_a = [P(f"ps_a{i}", [128, 512], F32) for i in range(2)]
    ps_ss = P("ps_ss", [128, 512], F32)
    ps_bc = P("ps_bc", [128, 512], F32)
    ps_ctx = P("ps_ctx", [128, 1024], F32)

    b.d("sync", idn[:], c_idn[:], w=["idn"])
    b.d("sync", seg[:], c_seg[:], w=["seg"])
    b.d("sync", hmap[:], c_hmap[:], w=["hmap"])
    b.d("sync", ones1[:], c_ones[:], w=["ones1"])
    b.d("sync", bqv[:, 0:8], bqc[:], w=["bqv"])
    b.d("sync", bkv[:, 0:8], bkc[:], w=["bkv"])
    b.d("gpsimd", bvrow[:], bvr[:], w=["bvrow"])
    for (W, WT, nm) in ((Wq, WqT, "WqT"), (Wk, WkT, "WkT"), (Wv, WvT, "WvT")):
        for o in range(8):
            b.d("gpsimd", wload[:, o * 1024:(o + 1) * 1024], W[ts(o), :],
                w=["wload"])
        emit_transpose_weight(b, wload, WT, ps_T, idn, nm)

    for g in range(NG):
        gb = g % 2
        xk, xTk, ksmk, vk = xtok[gb], xT[gb], ksm[gb], vtok[gb]
        xkn, xTn, ksmn, vn = xk.name, xTk.name, ksmk.name, vk.name
        for i in range(4):
            b.d("gpsimd", xk[:, i * 1024:(i + 1) * 1024],
                x[g * 512 + i * 128: g * 512 + (i + 1) * 128, :],
                w=[f"{xkn}_{i}"])
        for i in range(4):
            for c1 in range(8):
                pt = ps_T[(i * 8 + c1) % 2]
                b.c("tensor",
                    lambda e, i=i, c1=c1, pt=pt, xk=xk: e.transpose(
                        pt[0][:], xk[:, i * 1024 + c1 * 128: i * 1024 + (c1 + 1) * 128], idn[:]),
                    r=[f"{xkn}_{i}", "idn", pt[1] + "r"], w=[pt[1]])
                b.c("vector",
                    lambda e, i=i, c1=c1, pt=pt, xTk=xTk: e.tensor_copy(
                        xTk[:, c1 * 512 + i * 128: c1 * 512 + (i + 1) * 128], pt[0][:]),
                    r=[pt[1]], w=[f"{xTn}_{c1}", pt[1] + "r"])
        xt_r = [f"{xTn}_{c1}" for c1 in range(8)]
        for qk in range(2):
            WT = WqT if qk == 0 else WkT
            ebuf, ebn = (qe, "qe") if qk == 0 else (ke, "ke")
            bias = bqv if qk == 0 else bkv
            bias_n = "bqv" if qk == 0 else "bkv"
            for o in range(8):
                pa = ps_a[o % 2]
                pan = f"ps_a{o % 2}"
                for f in range(8):
                    b.c("tensor",
                        lambda e, o=o, f=f, WT=WT, pa=pa, xTk=xTk: e.matmul(
                            pa[:], WT[:, f * 1024 + o * 128: f * 1024 + (o + 1) * 128],
                            xTk[:, f * 512:(f + 1) * 512],
                            start=(f == 0), stop=(f == 7)),
                        r=[WT.name, xt_r[f]] + ([pan + "r"] if f == 0 else []),
                        w=[pan])
                b.c("scalar",
                    lambda e, o=o, pa=pa, ebuf=ebuf, bias=bias: e.activation(
                        out=ebuf[:, o * 512:(o + 1) * 512], in_=pa[:],
                        func=AF.Exp, bias=bias[:, o:o + 1], scale=1.0),
                    r=[pan, bias_n], w=[f"{ebn}_{o}", pan + "r"])
            for c1 in range(8):
                b.c("tensor",
                    lambda e, c1=c1, ebuf=ebuf: e.matmul(
                        ps_ss[0:16, 0:512], seg[:, c1 * 16:(c1 + 1) * 16],
                        ebuf[:, c1 * 512:(c1 + 1) * 512],
                        start=(c1 == 0), stop=(c1 == 7)),
                    r=["seg", f"{ebn}_{c1}"] + (["ps_ssr"] if c1 == 0 else []),
                    w=["ps_ss"])
            b.c("scalar",
                lambda e, qk=qk: e.activation(out=rec[:, qk * 512:(qk + 1) * 512],
                                              in_=ps_ss[0:16, 0:512], func=AF.Ln),
                r=["ps_ss"], w=["rec", "ps_ssr"])
            b.c("scalar",
                lambda e, qk=qk: e.activation(out=rec[:, qk * 512:(qk + 1) * 512],
                                              in_=rec[:, qk * 512:(qk + 1) * 512],
                                              func=AF.Exp, scale=-1.0),
                r=["rec"], w=["rec"])
            b.c("vector",
                lambda e, qk=qk: e.tensor_copy(recb[:, qk * 512:(qk + 1) * 512],
                                               rec[:, qk * 512:(qk + 1) * 512]),
                r=["rec"], w=["recb"])
            for c1 in range(8):
                b.c("tensor",
                    lambda e, c1=c1, qk=qk: e.matmul(
                        ps_bc[:], hmap[:, c1 * 128:(c1 + 1) * 128],
                        recb[0:16, qk * 512:(qk + 1) * 512], start=True, stop=True),
                    r=["hmap", "recb", "ps_bcr"], w=["ps_bc"])
                if qk == 0:
                    b.c("vector",
                        lambda e, c1=c1: e.tensor_tensor(
                            qsm[:, c1 * 512:(c1 + 1) * 512],
                            qe[:, c1 * 512:(c1 + 1) * 512], ps_bc[:], ALU.mult),
                        r=["ps_bc", f"qe_{c1}", f"qsm_{c1}d"], w=[f"qsm_{c1}", "ps_bcr"])
                    b.d("sync", o_qsmT[ts(c1), g * 512:(g + 1) * 512],
                        qsm[:, c1 * 512:(c1 + 1) * 512],
                        r=[f"qsm_{c1}"], w=[f"qsm_{c1}d"])
                else:
                    b.c("vector",
                        lambda e, c1=c1: e.tensor_tensor(
                            ksmT[:], ke[:, c1 * 512:(c1 + 1) * 512], ps_bc[:], ALU.mult),
                        r=["ps_bc", f"ke_{c1}"], w=["ksmT", "ps_bcr"])
                    b.c("vector",
                        lambda e, c1=c1: e.tensor_reduce(
                            ksum1[:], ksmT[:], AX.X, ALU.add),
                        r=["ksmT"], w=["ksum1"])
                    if g == 0:
                        b.c("vector",
                            lambda e, c1=c1: e.tensor_copy(ksump[:, c1:c1 + 1], ksum1[:]),
                            r=["ksum1"], w=["ksump"])
                    else:
                        b.c("vector",
                            lambda e, c1=c1: e.tensor_tensor(
                                ksump[:, c1:c1 + 1], ksump[:, c1:c1 + 1], ksum1[:], ALU.add),
                            r=["ksum1", "ksump"], w=["ksump"])
                    for i2 in range(4):
                        pt = ps_T[i2 % 2]
                        b.c("tensor",
                            lambda e, c1=c1, i2=i2, pt=pt: e.transpose(
                                pt[0][:], ksmT[:, i2 * 128:(i2 + 1) * 128], idn[:]),
                            r=["ksmT", "idn", pt[1] + "r"], w=[pt[1]])
                        b.c("vector",
                            lambda e, c1=c1, i2=i2, pt=pt, ksmk=ksmk: e.tensor_copy(
                                ksmk[:, i2 * 1024 + c1 * 128: i2 * 1024 + (c1 + 1) * 128],
                                pt[0][:]),
                            r=[pt[1]], w=[f"{ksmn}_{i2}", pt[1] + "r"])
        for i in range(4):
            for o2 in range(2):
                pa = ps_a[o2]
                pan = f"ps_a{o2}"
                for f in range(8):
                    b.c("tensor",
                        lambda e, i=i, o2=o2, f=f, pa=pa, xTk=xTk: e.matmul(
                            pa[:], xTk[:, f * 512 + i * 128: f * 512 + (i + 1) * 128],
                            WvT[:, f * 1024 + o2 * 512: f * 1024 + (o2 + 1) * 512],
                            start=(f == 0), stop=False),
                        r=["WvT", xt_r[f]] + ([pan + "r"] if f == 0 else []),
                        w=[pan])
                b.c("tensor",
                    lambda e, o2=o2, pa=pa: e.matmul(
                        pa[:], ones1[:], bvrow[:, o2 * 512:(o2 + 1) * 512],
                        start=False, stop=True),
                    r=["ones1", "bvrow"], w=[pan])
                b.c("vector",
                    lambda e, i=i, o2=o2, pa=pa, vk=vk: e.tensor_copy(
                        vk[:, i * 1024 + o2 * 512: i * 1024 + (o2 + 1) * 512], pa[:]),
                    r=[pan], w=[f"{vn}_{i}", pan + "r"])
        for i in range(4):
            for c1 in range(8):
                b.c("tensor",
                    lambda e, i=i, c1=c1, g=g, ksmk=ksmk, vk=vk: e.matmul(
                        ps_ctx[:, c1 * 128:(c1 + 1) * 128],
                        ksmk[:, i * 1024 + c1 * 128: i * 1024 + (c1 + 1) * 128],
                        vk[:, i * 1024 + c1 * 128: i * 1024 + (c1 + 1) * 128],
                        start=(g == 0 and i == 0 and c1 in (0, 4)),
                        stop=(g == NG - 1 and i == 3),
                        skip_group_check=True),
                    r=[f"{ksmn}_{i}", f"{vn}_{i}"], w=["ps_ctx"])

    b.c("vector", lambda e: e.tensor_copy(ctxs[:], ps_ctx[:]), r=["ps_ctx"], w=["ctxs"])
    b.d("sync", o_ctx[:], ctxs[:], r=["ctxs"])
    b.d("sync", o_ksum[:], ksump[:], r=["ksump"])
    b.replay()
    return nc



def build_p2():
    nc = bass.Bass("TRN2", target_bir_lowering=False)
    x = nc.dram_tensor("x", [NTOK, C], F32, kind="ExternalInput").ap()
    qsmT = nc.dram_tensor("qsmT", [C, NTOK], BF16, kind="ExternalInput").ap()
    ctx0 = nc.dram_tensor("ctx0", [128, C], F32, kind="ExternalInput").ap()
    ctx1 = nc.dram_tensor("ctx1", [128, C], F32, kind="ExternalInput").ap()
    ks0 = nc.dram_tensor("ks0", [128, 8], F32, kind="ExternalInput").ap()
    ks1 = nc.dram_tensor("ks1", [128, 8], F32, kind="ExternalInput").ap()
    Wp = nc.dram_tensor("Wp", [C, C], F32, kind="ExternalInput").ap()
    bpr = nc.dram_tensor("bpr", [1, C], F32, kind="ExternalInput").ap()
    c_idn = nc.dram_tensor("c_idn", [128, 128], F32, kind="ExternalInput").ap()
    c_seg = nc.dram_tensor("c_seg", [128, 8 * 16], F32, kind="ExternalInput").ap()
    c_hmap = nc.dram_tensor("c_hmap", [16, 8 * 128], F32, kind="ExternalInput").ap()
    c_ones = nc.dram_tensor("c_ones", [1, 128], F32, kind="ExternalInput").ap()
    c_bd64 = nc.dram_tensor("c_bd64", [128, 128], F32, kind="ExternalInput").ap()
    o_g1 = nc.dram_tensor("o_g1", [NTOK, C], F32, kind="ExternalOutput").ap()

    b = Builder(nc)
    A = nc.alloc_sbuf_tensor
    idn = A("idn", [128, 128], BF16)
    seg = A("seg", [128, 8 * 16], BF16)
    hmap = A("hmap", [16, 8 * 128], BF16)
    ones1 = A("ones1", [1, 128], BF16)
    bd64 = A("bd64", [128, 128], F32)
    bprow = A("bprow", [1, C], BF16)
    wload = A("wload", [128, 8 * 1024], BF16)
    WpT = A("WpT", [128, 8 * 1024], BF16)
    ctxa = A("ctxa", [128, 1024], F32)
    ctxb2 = A("ctxb2", [128, 1024], F32)
    ctxb = A("ctxb", [128, 1024], BF16)
    ksm = A("ksm", [128, 8], F32)
    ks1s = A("ks1s", [128, 8], F32)
    qs = [A(f"qs{i}", [128, 8 * 512], BF16) for i in range(2)]
    xg = [A(f"xg{i}", [128, 4 * 1024], F32) for i in range(2)]
    tsq = A("tsq", [128, 512], BF16)
    dinv = A("dinv", [16, 512], F32)
    dinvb = A("dinvb", [16, 512], BF16)
    bcs = A("bcs", [128, 512], F32)
    outT = A("outT", [128, 8 * 512], BF16)
    xa = A("xa", [128, 1024], F32)
    scr = A("scr", [128, 1024], F32)
    mn = A("mn", [128, 4], F32)
    g1t = A("g1t", [128, 1024], F32)

    P = nc.alloc_psum_tensor
    ps_T = [(P(f"ps_T{i}", [128, 128], BF16), f"ps_T{i}") for i in range(2)]
    ps_a = [P(f"ps_a{i}", [128, 512], F32) for i in range(2)]
    ps_ss = P("ps_ss", [128, 512], F32)
    ps_bc = P("ps_bc", [128, 512], F32)
    ps_r = P("ps_r", [128, 1024], F32)

    b.d("sync", idn[:], c_idn[:], w=["idn"])
    b.d("sync", seg[:], c_seg[:], w=["seg"])
    b.d("sync", hmap[:], c_hmap[:], w=["hmap"])
    b.d("sync", ones1[:], c_ones[:], w=["ones1"])
    b.d("sync", bd64[:], c_bd64[:], w=["bd64"])
    b.d("gpsimd", bprow[:], bpr[:], w=["bprow"])
    b.d("sync", ctxa[:], ctx0[:], w=["ctxa"])
    b.d("sync", ctxb2[:], ctx1[:], w=["ctxb2"])
    b.d("sync", ksm[:], ks0[:], w=["ksm"])
    b.d("sync", ks1s[:], ks1[:], w=["ks1s"])
    for o in range(8):
        b.d("gpsimd", wload[:, o * 1024:(o + 1) * 1024], Wp[ts(o), :], w=["wload"])
    emit_transpose_weight(b, wload, WpT, ps_T, idn, "WpT")
    b.c("vector", lambda e: e.tensor_tensor(ctxa[:], ctxa[:], ctxb2[:], ALU.add),
        r=["ctxa", "ctxb2"], w=["ctxa"])
    for c1 in range(8):
        b.c("vector", lambda e, c1=c1: e.tensor_tensor(
            ctxb[:, c1 * 128:(c1 + 1) * 128], ctxa[:, c1 * 128:(c1 + 1) * 128],
            bd64[:], ALU.mult),
            r=["ctxa", "bd64"], w=["ctxb"])
    b.c("vector", lambda e: e.tensor_tensor(ksm[:], ksm[:], ks1s[:], ALU.add),
        r=["ksm", "ks1s"], w=["ksm"])

    for g in range(NG):
        gb = g % 2
        q = qs[gb]
        qn = q.name
        xgk = xg[gb]
        xgn = xgk.name
        for c1 in range(8):
            b.d("sync", q[:, c1 * 512:(c1 + 1) * 512],
                qsmT[ts(c1), g * 512:(g + 1) * 512], w=[f"{qn}_{c1}"])
        for i in range(4):
            b.d("gpsimd", xgk[:, i * 1024:(i + 1) * 1024],
                x[g * 512 + i * 128:g * 512 + (i + 1) * 128, :], w=[f"{xgn}_{i}"])
        for c1 in range(8):
            b.c("vector", lambda e, c1=c1, q=q: e.tensor_scalar(
                tsq[:], q[:, c1 * 512:(c1 + 1) * 512], ksm[:, c1:c1 + 1], None, ALU.mult),
                r=[f"{qn}_{c1}", "ksm"], w=["tsq"])
            b.c("tensor", lambda e, c1=c1: e.matmul(
                ps_ss[0:16, 0:512], seg[:, c1 * 16:(c1 + 1) * 16], tsq[:],
                start=(c1 == 0), stop=(c1 == 7)),
                r=["seg", "tsq"] + (["ps_ssr"] if c1 == 0 else []), w=["ps_ss"])
        b.c("scalar", lambda e: e.activation(out=dinv[:], in_=ps_ss[0:16, 0:512], func=AF.Ln),
            r=["ps_ss"], w=["dinv", "ps_ssr"])
        b.c("scalar", lambda e: e.activation(out=dinv[:], in_=dinv[:], func=AF.Exp, scale=-1.0),
            r=["dinv"], w=["dinv"])
        b.c("vector", lambda e: e.tensor_copy(dinvb[:], dinv[:]), r=["dinv"], w=["dinvb"])
        for c1 in range(8):
            pa = ps_a[c1 % 2]
            pan = f"ps_a{c1 % 2}"
            b.c("tensor", lambda e, c1=c1, pa=pa, q=q: e.matmul(
                pa[:], ctxb[:, c1 * 128:(c1 + 1) * 128], q[:, c1 * 512:(c1 + 1) * 512],
                start=True, stop=True),
                r=["ctxb", f"{qn}_{c1}", pan + "r"], w=[pan])
            b.c("tensor", lambda e, c1=c1: e.matmul(
                ps_bc[:], hmap[:, c1 * 128:(c1 + 1) * 128], dinvb[:],
                start=True, stop=True),
                r=["hmap", "dinvb", "ps_bcr"], w=["ps_bc"])
            b.c("vector", lambda e: e.tensor_copy(bcs[:], ps_bc[:]),
                r=["ps_bc"], w=["bcs", "ps_bcr"])
            b.c("vector", lambda e, c1=c1, pa=pa: e.tensor_tensor(
                outT[:, c1 * 512:(c1 + 1) * 512], pa[:], bcs[:], ALU.mult),
                r=[pan, "bcs"], w=[f"outT_{c1}", pan + "r"])
            b.c("vector", lambda e, c1=c1, q=q: e.tensor_tensor(
                outT[:, c1 * 512:(c1 + 1) * 512], outT[:, c1 * 512:(c1 + 1) * 512],
                q[:, c1 * 512:(c1 + 1) * 512], ALU.add),
                r=[f"outT_{c1}", f"{qn}_{c1}"], w=[f"outT_{c1}"])
        for i in range(4):
            for c1 in range(8):
                b.c("tensor", lambda e, i=i, c1=c1: e.matmul(
                    ps_r[:, 0:512],
                    outT[:, c1 * 512 + i * 128: c1 * 512 + (i + 1) * 128],
                    WpT[:, c1 * 1024: c1 * 1024 + 512],
                    start=(c1 == 0), stop=False, skip_group_check=True),
                    r=[f"outT_{c1}", "WpT"] + (["ps_rr"] if c1 == 0 else []), w=["ps_r"])
                b.c("tensor", lambda e, i=i, c1=c1: e.matmul(
                    ps_r[:, 512:1024],
                    outT[:, c1 * 512 + i * 128: c1 * 512 + (i + 1) * 128],
                    WpT[:, c1 * 1024 + 512: (c1 + 1) * 1024],
                    start=(c1 == 0), stop=False, skip_group_check=True),
                    r=[f"outT_{c1}", "WpT"], w=["ps_r"])
            b.c("tensor", lambda e: e.matmul(
                ps_r[:, 0:512], ones1[:], bprow[:, 0:512],
                start=False, stop=True, skip_group_check=True),
                r=["ones1", "bprow"], w=["ps_r"])
            b.c("tensor", lambda e: e.matmul(
                ps_r[:, 512:1024], ones1[:], bprow[:, 512:1024],
                start=False, stop=True, skip_group_check=True),
                r=["ones1", "bprow"], w=["ps_r"])
            b.c("vector", lambda e, i=i, xgk=xgk: e.tensor_tensor(
                xa[:], ps_r[:], xgk[:, i * 1024:(i + 1) * 1024], ALU.add),
                r=["ps_r", f"{xgn}_{i}"], w=["xa", "ps_rr"])
            ln_token_tile(b, xa, g1t, mn, scr, "p2ln")
            b.d("sync", o_g1[g * 512 + i * 128: g * 512 + (i + 1) * 128, :], g1t[:],
                r=["g1t"], w=["g1td"])
    b.replay()
    return nc



def host_consts():
    cs = {}
    cs["c_idn"] = np.eye(128, dtype=np.float32)
    seg = np.zeros((8, 128, 16), dtype=np.float32)
    hm = np.zeros((8, 16, 128), dtype=np.float32)
    for c in range(8):
        for p in range(128):
            hg = 2 * c + p // 64
            seg[c, p, hg] = 1.0
            hm[c, hg, p] = 1.0
    cs["c_seg"] = np.ascontiguousarray(seg.transpose(1, 0, 2).reshape(128, 8 * 16))
    cs["c_hmap"] = np.ascontiguousarray(hm.transpose(1, 0, 2).reshape(16, 8 * 128))
    m = np.zeros((128, 128), dtype=np.float32)
    m[:64, :64] = 1.0
    m[64:, 64:] = 1.0
    cs["c_bd64"] = m
    cs["c_ones"] = np.ones((1, 128), dtype=np.float32)
    return cs


def run_p1(I):
    cs = host_consts()
    nc = build_p1()
    x = I["L0_attn_x"]
    maps = []
    for c in range(8):
        bb, h = c // 2, c % 2
        maps.append({
            "x": np.ascontiguousarray(x[bb, h * NTOK:(h + 1) * NTOK]),
            "Wq": I["L0_attn_Wq"], "Wk": I["L0_attn_Wk"], "Wv": I["L0_attn_Wv"],
            "bqc": np.ascontiguousarray(I["L0_attn_bq"].reshape(8, 128).T),
            "bkc": np.ascontiguousarray(I["L0_attn_bk"].reshape(8, 128).T),
            "bvr": I["L0_attn_bv"].reshape(1, C),
            "c_idn": cs["c_idn"], "c_seg": cs["c_seg"], "c_hmap": cs["c_hmap"],
            "c_ones": cs["c_ones"],
        })
    res = run_bass_kernel_spmd(nc, maps, core_ids=list(range(8)))
    return res.results


def run_p2(I, p1res):
    cs = host_consts()
    nc = build_p2()
    x = I["L0_attn_x"]
    maps = []
    for c in range(8):
        bb, h = c // 2, c % 2
        peer = bb * 2 + (1 - h)
        maps.append({
            "x": np.ascontiguousarray(x[bb, h * NTOK:(h + 1) * NTOK]),
            "qsmT": p1res[c]["o_qsmT"],
            "ctx0": p1res[c]["o_ctx"], "ctx1": p1res[peer]["o_ctx"],
            "ks0": p1res[c]["o_ksum"], "ks1": p1res[peer]["o_ksum"],
            "Wp": I["L0_attn_Wp"], "bpr": I["L0_attn_bp"].reshape(1, C),
            "c_idn": cs["c_idn"], "c_seg": cs["c_seg"], "c_hmap": cs["c_hmap"],
            "c_ones": cs["c_ones"], "c_bd64": cs["c_bd64"],
        })
    res = run_bass_kernel_spmd(nc, maps, core_ids=list(range(8)))
    return res.results


def kernel(**inputs):
    raise NotImplementedError



W0F = 512
NW = 8


def build_p3a():
    nc = bass.Bass("TRN2", target_bir_lowering=False)
    x = nc.dram_tensor("x", [NTOK, C], F32, kind="ExternalInput").ap()
    xhal = nc.dram_tensor("xhal", [514, C], F32, kind="ExternalInput").ap()
    xprev = nc.dram_tensor("xprev", [128, 8], F32, kind="ExternalInput").ap()
    wqkv = nc.dram_tensor("wqkv", [C, 1536], F32, kind="ExternalInput").ap()
    wo = nc.dram_tensor("wo", [512, C], F32, kind="ExternalInput").ap()
    qsc = nc.dram_tensor("qsc", [128, 1], F32, kind="ExternalInput").ap()
    ksc = nc.dram_tensor("ksc", [128, 1], F32, kind="ExternalInput").ap()
    tabs = nc.dram_tensor("tabs", [128, 4 * 4608], F32, kind="ExternalInput").ap()
    c_idn = nc.dram_tensor("c_idn", [128, 128], F32, kind="ExternalInput").ap()
    c_ones = nc.dram_tensor("c_ones", [1, 128], F32, kind="ExternalInput").ap()
    c_seg8 = nc.dram_tensor("c_seg8", [128, 4 * 8], F32, kind="ExternalInput").ap()
    c_hm8 = nc.dram_tensor("c_hm8", [8, 4 * 128], F32, kind="ExternalInput").ap()
    c_rot = nc.dram_tensor("c_rot", [128, 128], F32, kind="ExternalInput").ap()
    c_fb = nc.dram_tensor("c_fb", [128, 640], F32, kind="ExternalInput").ap()
    c_w0x = nc.dram_tensor("c_w0x", [1, 1024], F32, kind="ExternalInput").ap()
    o_xmid = nc.dram_tensor("o_xmid", [NTOK, C], F32, kind="ExternalOutput").ap()

    b = Builder(nc)
    A = nc.alloc_sbuf_tensor
    idn = A("idn", [128, 128], BF16)
    ones1 = A("ones1", [1, 128], BF16)
    seg8 = A("seg8", [128, 4 * 8], F32)
    hm8 = A("hm8", [8, 4 * 128], F32)
    rotm = A("rotm", [128, 128], F32)
    fb = A("fb", [128, 640], F32)
    w0x = A("w0x", [1, 1024], BF16)
    qscv = A("qscv", [128, 1], F32)
    kscv = A("kscv", [128, 1], F32)
    wk = A("wk", [128, 8 * 1536], BF16)
    woS = A("woS", [128, 4 * 1024], BF16)
    tabw = [A(f"tabw{i}", [128, 4 * 512], F32) for i in range(2)]
    xw = [A(f"xw{i}", [128, 4 * 1024], F32) for i in range(2)]
    xwb = A("xwb", [128, 4 * 1024], BF16)
    xT = A("xT", [128, 8 * 640], BF16)
    xpv = A("xpv", [128, 8], BF16)
    qT = A("qT", [128, 4 * 512], BF16)
    kvtmp = A("kvtmp", [128, 4 * 512], F32)
    kT = A("kT", [128, 4 * 1024], BF16)
    vr = A("vr", [128, 8 * 512], BF16)
    nrm = [A(f"nrm{i}", [8, 512], F32) for i in range(2)]
    bcs = A("bcs", [128, 512], F32)
    t1b_ = [A(f"t1b{i}", [128, 512], F32) for i in range(2)]
    t2b_ = [A(f"t2b{i}", [128, 512], F32) for i in range(2)]
    Ssb = A("Ssb", [128, 8 * 640], F32)
    nmw = A("nmw", [128, 16], F32)
    Pb = A("Pb", [128, 8 * 640], BF16)
    PT = [A(f"PT{i}", [128, 5 * 128], BF16) for i in range(2)]
    aoT = A("aoT", [128, 512], BF16)
    xmid = A("xmid", [128, 1024], F32)

    P = nc.alloc_psum_tensor
    psScT = [P(f"psSc{i}", [128, 640], F32) for i in range(2)]
    psSc = [psScT[0], psScT[1], psScT[0], psScT[1]]
    psScN = ["psSc0", "psSc1", "psSc0", "psSc1"]
    psC = P("psC", [128, 512], F32)
    psT0 = P("psT", [128, 640], BF16)
    psTl = [(psT0, "psT"), (psT0, "psT")]
    psA = P("psA", [128, 1024], F32)

    b.d("sync", idn[:], c_idn[:], w=["idn"])
    b.d("sync", ones1[:], c_ones[:], w=["ones1"])
    b.d("sync", seg8[:], c_seg8[:], w=["seg8"])
    b.d("sync", hm8[:], c_hm8[:], w=["hm8"])
    b.d("sync", rotm[:], c_rot[:], w=["rotm"])
    b.d("sync", fb[:], c_fb[:], w=["fb"])
    b.d("sync", w0x[:], c_w0x[:], w=["w0x"])
    b.d("sync", qscv[:], qsc[:], w=["qscv"])
    b.d("sync", kscv[:], ksc[:], w=["kscv"])
    b.d("sync", xpv[:], xprev[:], w=["xpv"])
    for f in range(8):
        b.d("gpsimd", wk[:, f * 1536:(f + 1) * 1536], wqkv[ts(f), :], w=["wk"])
    for i in range(4):
        b.d("gpsimd", woS[:, i * 1024:(i + 1) * 1024], wo[ts(i), :], w=["woS"])

    def emit_qkv_window(wi, wslot):
        real = wi >= 0
        xk = xw[wslot]
        xkn = xk.name
        tw = tabw[wslot]
        twn = tw.name
        for c1 in range(8):
            b.c("gpsimd", lambda e, c1=c1: e.tensor_copy(
                xT[:, c1 * 640:c1 * 640 + 128], xT[:, c1 * 640 + 512:c1 * 640 + 640]),
                r=[f"xT_{c1}"], w=[f"xT_{c1}"])
        for c1 in range(4):
            b.c("gpsimd", lambda e, c1=c1: e.tensor_copy(
                kT[:, c1 * 1024:c1 * 1024 + 512], kT[:, c1 * 1024 + 512:(c1 + 1) * 1024]),
                r=[f"kT_{c1}"], w=[f"kT_{c1}"])
        for i in range(4):
            b.c("gpsimd", lambda e, i=i: e.tensor_copy(
                vr[:, i * 512:(i + 1) * 512], vr[:, (4 + i) * 512:(5 + i) * 512]),
                r=[f"vr_{i}", f"vr_{4 + i}"], w=[f"vr_{i}"])
        for i in range(4):
            if wi < 0:
                src = xhal[2 + i * 128: 2 + (i + 1) * 128, :]
            else:
                src = x[wi * 512 + i * 128: wi * 512 + (i + 1) * 128, :]
            b.d("gpsimd", xk[:, i * 1024:(i + 1) * 1024], src, w=[f"{xkn}_{i}"])
        for t4 in range(4):
            b.d("gpsimd", tw[:, t4 * 512:(t4 + 1) * 512],
                tabs[:, t4 * 4608 + (wi + 1) * 512: t4 * 4608 + (wi + 2) * 512],
                w=[twn])
        for i in range(4):
            b.c("vector", lambda e, i=i, xk=xk: e.tensor_copy(
                xwb[:, i * 1024:(i + 1) * 1024], xk[:, i * 1024:(i + 1) * 1024]),
                r=[f"{xkn}_{i}"], w=[f"xwb_{i}"])
            for g4 in range(2):
                psTc, psTn = psTl[g4 % 2]
                for jj in range(4):
                    c1 = g4 * 4 + jj
                    b.c("tensor", lambda e, i=i, c1=c1, jj=jj, psTc=psTc: e.matmul(
                        psTc[:, jj * 128:(jj + 1) * 128],
                        xwb[:, i * 1024 + c1 * 128: i * 1024 + (c1 + 1) * 128], idn[:],
                        start=(jj == 0), stop=(jj == 3), is_transpose=True,
                        skip_group_check=True),
                        r=[f"xwb_{i}", "idn"] + ([psTn + "r"] if jj == 0 else []), w=[psTn])
                for jj in range(4):
                    c1 = g4 * 4 + jj
                    b.c("vector", lambda e, i=i, c1=c1, jj=jj, psTc=psTc: e.tensor_copy(
                        xT[:, c1 * 640 + 128 + i * 128: c1 * 640 + 128 + (i + 1) * 128],
                        psTc[:, jj * 128:(jj + 1) * 128]),
                        r=[psTn], w=[f"xT_{c1}"] + ([psTn + "r"] if jj == 3 else []))
        if wi == -1:
            for c1 in range(8):
                b.c("vector", lambda e, c1=c1: e.tensor_copy(
                    xT[:, c1 * 640 + 127: c1 * 640 + 128], xpv[:, c1:c1 + 1]),
                    r=["xpv"], w=[f"xT_{c1}"])
        xt_r = [f"xT_{c1}" for c1 in range(8)]

        def proj_part(dst, dstn, col0, oc_n, token_major):
            for oc in range(oc_n):
                for tile_or_none in ([None] if not token_major else range(4)):
                    if not token_major:
                        par = oc % 2
                        pd = psA[:, par * 512:(par + 1) * 512]
                        pdn = f"psA{par}"
                        for f in range(8):
                            sh = 127 if f >= 4 else 128
                            b.c("tensor", lambda e, oc=oc, f=f, sh=sh, col0=col0, pd=pd: e.matmul(
                                pd, wk[:, f * 1536 + col0 + oc * 128: f * 1536 + col0 + (oc + 1) * 128],
                                xT[:, f * 640 + sh: f * 640 + sh + 512],
                                start=(f == 0), stop=(f == 7)),
                                r=["wk", xt_r[f]] + ([pdn + "r"] if f == 0 else []), w=[pdn])
                        b.c("vector", lambda e, oc=oc, dst=dst, pd=pd: e.tensor_copy(
                            dst[:, oc * 512:(oc + 1) * 512], pd),
                            r=[pdn], w=[f"{dstn}_{oc}", pdn + "r"])
                    else:
                        i = tile_or_none
                        par = i % 2
                        pd = psA[:, par * 512:(par + 1) * 512]
                        pdn = f"psA{par}"
                        for f in range(8):
                            sh = 127 if f >= 4 else 128
                            b.c("tensor", lambda e, i=i, f=f, sh=sh, col0=col0, pd=pd: e.matmul(
                                pd, xT[:, f * 640 + sh + i * 128: f * 640 + sh + (i + 1) * 128],
                                wk[:, f * 1536 + col0: f * 1536 + col0 + 512],
                                start=(f == 0), stop=(f == 7)),
                                r=["wk", xt_r[f]] + ([pdn + "r"] if f == 0 else []), w=[pdn])
                        b.c("vector", lambda e, i=i, dst=dst, pd=pd: e.tensor_copy(
                            dst[:, (4 + i) * 512:(5 + i) * 512], pd),
                            r=[pdn], w=[f"{dstn}_{4 + i}", pdn + "r"])

        proj_part(vr, "vr", 1024, 1, True)
        proj_part(kvtmp, "kvtmp", 512, 4, False)

        def l2rot(srcbuf, srcn, scv, scvn, tab0, dst_fn, post=None):
            for c1 in range(4):
                p2 = c1 % 2
                t1b, t1n = t1b_[p2], f"t1b{p2}"
                t2b, t2n = t2b_[p2], f"t2b{p2}"
                nr, nrn = nrm[p2], f"nrm{p2}"
                pb1, pb1n = (psScT[0][:, 0:512], "psSc0") if c1 % 2 == 0 else (psScT[1][:, 0:512], "psSc1")
                pb2, pb2n = (psScT[1][:, 0:512], "psSc1") if c1 % 2 == 0 else (psScT[0][:, 0:512], "psSc0")
                b.c("scalar", lambda e, c1=c1, srcbuf=srcbuf, t1b=t1b: e.activation(
                    out=t1b[:], in_=srcbuf[:, c1 * 512:(c1 + 1) * 512], func=AF.Square),
                    r=[f"{srcn}_{c1}", t1n + "R"], w=[t1n])
                b.c("tensor", lambda e, c1=c1, t1b=t1b, pb1=pb1: e.matmul(
                    pb1[0:8, 0:512], seg8[:, c1 * 8:(c1 + 1) * 8], t1b[:],
                    start=True, stop=True),
                    r=["seg8", t1n, pb1n + "r"], w=[pb1n, t1n + "R"])
                b.c("vector", lambda e, nr=nr, pb1=pb1: e.tensor_scalar(
                    nr[:], pb1[0:8, 0:512], 1e-24, None, ALU.max),
                    r=[pb1n], w=[nrn, pb1n + "r"])
                b.c("scalar", lambda e, nr=nr: e.activation(out=nr[:], in_=nr[:], func=AF.Ln),
                    r=[nrn], w=[nrn])
                b.c("scalar", lambda e, nr=nr: e.activation(out=nr[:], in_=nr[:], func=AF.Exp,
                                                            scale=-0.5),
                    r=[nrn], w=[nrn])
                b.c("tensor", lambda e, c1=c1, nr=nr, pb2=pb2: e.matmul(
                    pb2[:, 0:512], hm8[:, c1 * 128:(c1 + 1) * 128], nr[:],
                    start=True, stop=True),
                    r=["hm8", nrn, pb2n + "r"], w=[pb2n])
                b.c("vector", lambda e, c1=c1, srcbuf=srcbuf, t1b=t1b, pb2=pb2: e.tensor_tensor(
                    t1b[:], srcbuf[:, c1 * 512:(c1 + 1) * 512], pb2[:, 0:512], ALU.mult),
                    r=[f"{srcn}_{c1}", pb2n], w=[t1n, pb2n + "r"])
                b.c("vector", lambda e, scv=scv, post=post, t1b=t1b: e.tensor_scalar(
                    t1b[:], t1b[:], scv[:], post, ALU.mult,
                    op1=(ALU.bypass if post is None else ALU.mult)),
                    r=[t1n, scvn], w=[t1n])
                b.c("tensor", lambda e, t1b=t1b, pb1=pb1: e.matmul(pb1[:, 0:512], rotm[:], t1b[:],
                                                                   start=True, stop=True),
                    r=["rotm", t1n, pb1n + "r"], w=[pb1n])
                b.c("vector", lambda e, tab0=tab0, tw=tw, t1b=t1b, t2b=t2b: e.tensor_tensor(
                    t2b[:], t1b[:], tw[:, tab0 * 512:(tab0) * 512 + 512], ALU.mult),
                    r=[t1n, twn, t2n + "R"], w=[t2n])
                b.c("vector", lambda e, tab0=tab0, tw=tw, t1b=t1b, pb1=pb1: e.tensor_tensor(
                    t1b[:], pb1[:, 0:512], tw[:, (tab0 + 1) * 512:(tab0 + 2) * 512], ALU.mult),
                    r=[pb1n, twn], w=[t1n, pb1n + "r"])
                dst_fn(c1, t1b, t1n, t2b, t2n)

        def qdst(c1, t1b, t1n, t2b, t2n):
            b.c("vector", lambda e, c1=c1, t1b=t1b, t2b=t2b: e.tensor_tensor(
                qT[:, c1 * 512:(c1 + 1) * 512], t2b[:], t1b[:], ALU.add),
                r=[t1n, t2n], w=[f"qT_{c1}", t1n + "R", t2n + "R"])

        def kdst(c1, t1b, t1n, t2b, t2n):
            b.c("vector", lambda e, c1=c1, t1b=t1b, t2b=t2b: e.tensor_tensor(
                kT[:, c1 * 1024 + 512:(c1 + 1) * 1024], t2b[:], t1b[:], ALU.add),
                r=[t1n, t2n], w=[f"kT_{c1}", t1n + "R", t2n + "R"])

        l2rot(kvtmp, "kvtmp", kscv, "kscv", 2, kdst)
        if real:
            proj_part(kvtmp, "kvtmp", 0, 4, False)
            l2rot(kvtmp, "kvtmp", qscv, "qscv", 0, qdst, post=8.0)

    def emit_attn_window(wi, wslot):
        xk = xw[wslot]
        xkn = xk.name
        def front(qt):
            o = qt * 128
            for h in range(8):
                c1, half = h // 2, h % 2
                pS, pSn = psScT[h % 2], f"psSc{h % 2}"
                b.c("tensor", lambda e, c1=c1, half=half, o=o, pS=pS: e.matmul(
                    pS[:, 0:512],
                    qT[:, c1 * 512 + o + 0:c1 * 512 + o + 128][ts(half, 64), :],
                    kT[:, c1 * 1024 + o: c1 * 1024 + o + 512][ts(half, 64), :],
                    start=True, stop=True),
                    r=[f"qT_{c1}", f"kT_{c1}", pSn + "r"], w=[pSn])
                b.c("tensor", lambda e, c1=c1, half=half, o=o, pS=pS: e.matmul(
                    pS[:, 512:640],
                    qT[:, c1 * 512 + o:c1 * 512 + o + 128][ts(half, 64), :],
                    kT[:, c1 * 1024 + o + 512: c1 * 1024 + o + 640][ts(half, 64), :],
                    start=True, stop=True),
                    r=[f"qT_{c1}", f"kT_{c1}"], w=[pSn])
                if wi == 0:
                    b.c("tensor", lambda e, o=o, pS=pS: e.matmul(
                        pS[:, 0:512], ones1[:], w0x[:, o:o + 512],
                        start=False, stop=True, skip_group_check=True),
                        r=["ones1", "w0x"], w=[pSn])
                    b.c("tensor", lambda e, o=o, pS=pS: e.matmul(
                        pS[:, 512:640], ones1[:], w0x[:, o + 512:o + 640],
                        start=False, stop=True, skip_group_check=True),
                        r=["ones1", "w0x"], w=[pSn])
                b.c("vector", lambda e, h=h, pS=pS: e.tensor_tensor(
                    Ssb[:, h * 640:(h + 1) * 640], pS[:], fb[:], ALU.add),
                    r=[pSn, "fb", f"SsbR{h}"], w=[f"Ssb_{h}", pSn + "r"])
            b.c("vector", lambda e: e.tensor_reduce(
                nmw[:, 0:8], bass.AP(Ssb, 0, [[5120, 128], [640, 8], [1, 640]]),
                AX.X, ALU.max, negate=True),
                r=[f"Ssb_{h2}" for h2 in range(8)], w=["nmx"])

        def back1(qt):
            for h in range(8):
                b.c("scalar", lambda e, h=h: e.activation(
                    out=Pb[:, h * 640:(h + 1) * 640], in_=Ssb[:, h * 640:(h + 1) * 640],
                    func=AF.Exp, bias=nmw[:, h:h + 1], scale=1.0,
                    accum_out=nmw[:, 8 + h:9 + h]),
                    r=[f"Ssb_{h}", "nmx", f"PbR{h}"], w=[f"Pb_{h}", "nms", f"SsbR{h}"])
            b.c("vector", lambda e: e.reciprocal(nmw[:, 8:16], nmw[:, 8:16]),
                r=["nms"], w=["nms"])

        def back2(qt):
            for h in range(8):
                c1, half = h // 2, h % 2
                pt_, ptn = PT[h % 2], f"PT{h % 2}"
                b.c("vector", lambda e, h=h: e.tensor_scalar(
                    Pb[:, h * 640:(h + 1) * 640], Pb[:, h * 640:(h + 1) * 640],
                    nmw[:, 8 + h:9 + h], None, ALU.mult),
                    r=[f"Pb_{h}", "nms"], w=[f"Pb_{h}"])
                for pc in range(5):
                    b.c("tensor", lambda e, h=h, pc=pc: e.matmul(
                        psT0[:, pc * 128:(pc + 1) * 128],
                        Pb[:, h * 640 + pc * 128: h * 640 + (pc + 1) * 128], idn[:],
                        start=(pc == 0), stop=(pc == 4), is_transpose=True,
                        skip_group_check=True),
                        r=[f"Pb_{h}", "idn"] + (["psTr"] if pc == 0 else []),
                        w=["psT"] + ([f"PbR{h}"] if pc == 4 else []))
                b.c("scalar", lambda e, pt_=pt_: e.activation(
                    out=pt_[:, 0:640], in_=psT0[:, 0:640], func=AF.Copy),
                    r=["psT", ptn + "w"], w=[ptn, "psTr"])
                pair = c1
                pbase = 64 * half
                for pc in range(5):
                    vtile = qt + pc
                    b.c("tensor", lambda e, pc=pc, vtile=vtile, h=h, pair=pair, pbase=pbase, pt_=pt_: e.matmul(
                        psC[pbase:pbase + 64, pair * 128:(pair + 1) * 128],
                        vr[:, vtile * 512 + h * 64: vtile * 512 + (h + 1) * 64],
                        pt_[:, pc * 128:(pc + 1) * 128],
                        start=(pc == 0), stop=(pc == 4), skip_group_check=True),
                        r=[f"vr_{vtile}", ptn] + (["psCr"] if pc == 0 else []),
                        w=["psC"] + ([ptn + "w"] if pc == 4 else []))
            b.c("vector", lambda e: e.tensor_copy(aoT[:], psC[:]), r=["psC"], w=["aoT", "psCr"])
            for p2 in range(4):
                for o2 in range(2):
                    b.c("tensor", lambda e, p2=p2, o2=o2: e.matmul(
                        psA[:, o2 * 512:(o2 + 1) * 512],
                        aoT[:, p2 * 128:(p2 + 1) * 128],
                        woS[:, p2 * 1024 + o2 * 512: p2 * 1024 + (o2 + 1) * 512],
                        start=(p2 == 0), stop=(p2 == 3), skip_group_check=True),
                        r=["aoT", "woS"] + ([f"psA{o2}r"] if p2 == 0 else []), w=[f"psA{o2}"])
            b.c("vector", lambda e, qt=qt: e.tensor_tensor(
                xmid[:], psA[:], xw[wslot][:, qt * 1024:(qt + 1) * 1024], ALU.add),
                r=["psA0", "psA1", f"{xkn}_{qt}"], w=["xmid", "psA0r", "psA1r"])
            b.d("sync", o_xmid[wi * 512 + qt * 128: wi * 512 + (qt + 1) * 128, :], xmid[:],
                r=["xmid"], w=["xmidd"])

        front(0)
        for qt in range(4):
            back1(qt)
            if qt + 1 < 4:
                front(qt + 1)
            back2(qt)

    emit_qkv_window(-1, 0)
    for wi in range(NW):
        emit_qkv_window(wi, (wi + 1) % 2)
        emit_attn_window(wi, (wi + 1) % 2)
    b.replay()
    return nc


def host_consts_p3a():
    cs = {}
    cs["c_idn"] = np.eye(128, dtype=np.float32)
    cs["c_ones"] = np.ones((1, 128), dtype=np.float32)
    seg = np.zeros((4, 128, 8), dtype=np.float32)
    hm = np.zeros((4, 8, 128), dtype=np.float32)
    for c in range(4):
        for p in range(128):
            h = 2 * c + p // 64
            seg[c, p, h] = 1.0
            hm[c, h, p] = 1.0
    cs["c_seg8"] = np.ascontiguousarray(seg.transpose(1, 0, 2).reshape(128, 32))
    cs["c_hm8"] = np.ascontiguousarray(hm.transpose(1, 0, 2).reshape(8, 512))
    rt = np.zeros((128, 128), dtype=np.float32)
    for blk in range(2):
        o = blk * 64
        for j in range(32):
            rt[o + j + 32, o + j] = -1.0
            rt[o + j, o + j + 32] = 1.0
    cs["c_rot"] = rt
    fbm = np.zeros((128, 640), dtype=np.float32)
    for p in range(128):
        for jj in range(640):
            if not (0 <= jj - p <= 512):
                fbm[p, jj] = -1e30
    cs["c_fb"] = fbm
    tglob = np.arange(-512, 8192, dtype=np.float64)
    half_idx = np.arange(0, 64, 2, dtype=np.float64)
    inv_freq = 1.0 / 10000.0 ** (half_idx / 64.0)
    freqs = tglob[:, None] * inv_freq[None, :]
    freqs = np.concatenate([freqs, freqs], axis=-1)
    base = (half_idx + 0.4 * 64) / (1.4 * 64)
    power = (tglob - 4096) / 256.0
    scale = base[None, :] ** power[:, None]
    scale = np.concatenate([scale, scale], axis=-1)
    cos, sin = np.cos(freqs), np.sin(freqs)
    CSq = (cos * scale).T
    SSq = (sin * scale).T
    CSk = (cos / scale).T
    SSk = (sin / scale).T
    tab = np.stack([CSq, SSq, CSk, SSk])
    tab = np.concatenate([tab, tab], axis=1)
    cs["tabglob"] = tab.astype(np.float32)
    return cs


def run_p3a(I, xfull, l, _cache={}):
    cs = host_consts_p3a()
    if "p3a" not in _cache:
        _cache["p3a"] = build_p3a()
    nc = _cache["p3a"]
    qs = I["L0_ffn_q_scale"][l]
    ks = I["L0_ffn_k_scale"][l]
    maps = []
    for c in range(8):
        bb, h = c // 2, c % 2
        s = h * NTOK
        if h == 0:
            xh = np.zeros((514, C), np.float32)
            xp = np.zeros((128, 8), np.float32)
            w0 = np.zeros((1, 1024), np.float32)
            w0[0, :512] = -1e30
        else:
            xh = np.ascontiguousarray(xfull[bb, s - 514:s])
            xp = np.ascontiguousarray(xfull[bb, s - 513].reshape(8, 128).T)
            w0 = np.zeros((1, 1024), np.float32)
        tsl = cs["tabglob"][:, :, s: s + 4608]
        tabsl = np.ascontiguousarray(tsl.transpose(1, 0, 2).reshape(128, 4 * 4608))
        maps.append({
            "x": np.ascontiguousarray(xfull[bb, s:s + NTOK]),
            "xhal": xh, "xprev": xp,
            "wqkv": I["L0_ffn_wqkv"][l], "wo": I["L0_ffn_wo"][l],
            "qsc": np.ascontiguousarray(np.concatenate([qs, qs]).reshape(128, 1)),
            "ksc": np.ascontiguousarray(np.concatenate([ks, ks]).reshape(128, 1)),
            "tabs": tabsl,
            "c_idn": cs["c_idn"], "c_ones": cs["c_ones"], "c_seg8": cs["c_seg8"],
            "c_hm8": cs["c_hm8"], "c_rot": cs["c_rot"], "c_fb": cs["c_fb"],
            "c_w0x": w0,
        })
    res = run_bass_kernel_spmd(nc, maps, core_ids=list(range(8)))
    return res.results



def build_p3b():
    nc = bass.Bass("TRN2", target_bir_lowering=False)
    xm = nc.dram_tensor("xm", [NTOK, C], F32, kind="ExternalInput").ap()
    xmprev = nc.dram_tensor("xmprev", [128, 8], F32, kind="ExternalInput").ap()
    w1a = nc.dram_tensor("w1a", [C, 2730], F32, kind="ExternalInput").ap()
    w1g = nc.dram_tensor("w1g", [C, 2730], F32, kind="ExternalInput").ap()
    w2 = nc.dram_tensor("w2", [2730, C], F32, kind="ExternalInput").ap()
    b1a = nc.dram_tensor("b1a", [1, 2730], F32, kind="ExternalInput").ap()
    b1g = nc.dram_tensor("b1g", [1, 2730], F32, kind="ExternalInput").ap()
    b2r = nc.dram_tensor("b2r", [1, C], F32, kind="ExternalInput").ap()
    lgc = nc.dram_tensor("lgc", [128, 8], F32, kind="ExternalInput").ap()
    lbc = nc.dram_tensor("lbc", [128, 8], F32, kind="ExternalInput").ap()
    c_idn = nc.dram_tensor("c_idn", [128, 128], F32, kind="ExternalInput").ap()
    c_ones = nc.dram_tensor("c_ones", [1, 512], F32, kind="ExternalInput").ap()
    c_onec = nc.dram_tensor("c_onec", [128, 1], F32, kind="ExternalInput").ap()
    o_f = nc.dram_tensor("o_f", [NTOK, C], F32, kind="ExternalOutput").ap()

    b = Builder(nc)
    A = nc.alloc_sbuf_tensor
    idn = A("idn", [128, 128], BF16)
    ones1 = A("ones1", [1, 512], BF16)
    ones1f = A("ones1f", [1, 128], F32)
    onescol = A("onescol", [128, 1], BF16)
    lgv = A("lgv", [128, 8], F32)
    lbv = A("lbv", [128, 8], F32)
    b1ar = A("b1ar", [1, 2730], BF16)
    b1gr = A("b1gr", [1, 2730], BF16)
    b2row = A("b2row", [1, C], BF16)
    w1aS = A("w1aS", [128, 8 * 2730], BF16)
    w1gS = A("w1gS", [128, 8 * 2730], BF16)
    w2S = A("w2S", [128, 22 * 1024], BF16)
    xw = A("xw", [128, 1024], F32)
    xwb = A("xwb", [128, 1024], BF16)
    xmT = A("xmT", [128, 8 * 640], BF16)
    xpv = A("xpv", [128, 8], BF16)
    lnT = A("lnT", [128, 8 * 512], BF16)
    hT = A("hT", [128, 22 * 512], BF16)
    t1b = [A(f"t1b{i}", [128, 512], BF16) for i in range(2)]
    rows = A("rows", [1, 3 * 512], F32)
    bcRs = A("bcRs", [128, 512], F32)
    t2f = A("t2f", [128, 512], F32)
    outf = A("outf", [128, 512], F32)

    P = nc.alloc_psum_tensor
    psT = P("psT", [128, 512], BF16)
    psS1 = P("psS1", [128, 512], F32)
    psS2 = P("psS2", [1, 512], F32)
    psBC = psS1
    psW = [P(f"psW{i}", [128, 512], F32) for i in range(4)]
    psO = P("psO", [128, 512], F32)

    b.d("sync", idn[:], c_idn[:], w=["idn"])
    b.d("sync", ones1[:], c_ones[:], w=["ones1"])
    b.d("sync", ones1f[:], c_ones[:, 0:128], w=["ones1f"])
    b.d("sync", onescol[:], c_onec[:], w=["onescol"])
    b.d("sync", lgv[:], lgc[:], w=["lgv"])
    b.d("sync", lbv[:], lbc[:], w=["lbv"])
    b.d("gpsimd", b1ar[:], b1a[:], w=["b1ar"])
    b.d("gpsimd", b1gr[:], b1g[:], w=["b1gr"])
    b.d("gpsimd", b2row[:], b2r[:], w=["b2row"])
    b.d("gpsimd", xpv[:], xmprev[:], w=["xpv"])
    for f in range(8):
        b.d("gpsimd", w1aS[:, f * 2730:(f + 1) * 2730], w1a[ts(f), :], w=["w1aS"])
        b.d("gpsimd", w1gS[:, f * 2730:(f + 1) * 2730], w1g[ts(f), :], w=["w1gS"])
    for hc in range(22):
        r0, r1 = hc * 128, min(2730, (hc + 1) * 128)
        b.d("gpsimd", w2S[0:r1 - r0, hc * 1024:(hc + 1) * 1024], w2[r0:r1, :], w=["w2S"])

    for wi in range(NW):
        for c1 in range(8):
            b.c("vector", lambda e, c1=c1, wi=wi: e.tensor_copy(
                xmT[:, c1 * 640 + 127:c1 * 640 + 128],
                xmT[:, c1 * 640 + 639:c1 * 640 + 640] if wi > 0 else xpv[:, c1:c1 + 1]),
                r=([f"xmT_{c1}"] if wi > 0 else ["xpv"]), w=[f"xmT_{c1}"])
        for i in range(4):
            b.d("gpsimd", xwb[:], xm[wi * 512 + i * 128: wi * 512 + (i + 1) * 128, :],
                r=["xwbR"], w=["xwb"])
            for g4 in range(2):
                for jj in range(4):
                    c1 = g4 * 4 + jj
                    b.c("tensor", lambda e, c1=c1, jj=jj: e.matmul(
                        psT[:, jj * 128:(jj + 1) * 128], xwb[:, c1 * 128:(c1 + 1) * 128],
                        idn[:], start=(jj == 0), stop=(jj == 3), is_transpose=True,
                        skip_group_check=True),
                        r=["xwb", "idn"] + (["psTr"] if jj == 0 else []),
                        w=["psT"] + (["xwbR"] if c1 == 7 else []))
                for jj in range(4):
                    c1 = g4 * 4 + jj
                    b.c("vector", lambda e, i=i, c1=c1, jj=jj: e.tensor_copy(
                        xmT[:, c1 * 640 + 128 + i * 128: c1 * 640 + 128 + (i + 1) * 128],
                        psT[:, jj * 128:(jj + 1) * 128]),
                        r=["psT"], w=[f"xmT_{c1}"] + (["psTr"] if jj == 3 else []))
        xt_r = [f"xmT_{c1}" for c1 in range(8)]

        def hsT(c1):
            sh = 127 if c1 >= 4 else 128
            return xmT[:, c1 * 640 + sh: c1 * 640 + sh + 512]

        for c1 in range(8):
            b.c("tensor", lambda e, c1=c1: e.matmul(
                psS1[0:1, :], onescol[:], hsT(c1), start=(c1 == 0), stop=(c1 == 7)),
                r=["onescol", xt_r[c1]] + (["psS1r"] if c1 == 0 else []), w=["psS1"])
        for c1 in range(8):
            tb = t1b[c1 % 2]
            tbn = f"t1b{c1 % 2}"
            b.c("scalar", lambda e, c1=c1, tb=tb: e.activation(out=tb[:], in_=hsT(c1),
                                                               func=AF.Square),
                r=[xt_r[c1], tbn + "R"], w=[tbn])
            b.c("tensor", lambda e, c1=c1, tb=tb: e.matmul(
                psS2[:], onescol[:], tb[:], start=(c1 == 0), stop=(c1 == 7)),
                r=["onescol", tbn] + (["psS2r"] if c1 == 0 else []), w=["psS2", tbn + "R"])
        b.c("vector", lambda e: e.tensor_scalar(rows[:, 0:512], psS1[0:1, :], 1.0 / C, None, ALU.mult),
            r=["psS1"], w=["rows", "psS1r"])
        b.c("vector", lambda e: e.tensor_scalar(rows[:, 512:1024], psS2[:], 1.0 / C, None, ALU.mult),
            r=["psS2", "rows"], w=["rows", "psS2r"])
        b.c("vector", lambda e: e.tensor_tensor(rows[:, 1024:1536], rows[:, 0:512],
                                                rows[:, 0:512], ALU.mult),
            r=["rows"], w=["rowsd"])
        b.c("vector", lambda e: e.tensor_tensor(rows[:, 512:1024], rows[:, 512:1024],
                                                rows[:, 1024:1536], ALU.subtract),
            r=["rows", "rowsd"], w=["rows"])
        b.c("vector", lambda e: e.tensor_scalar(rows[:, 512:1024], rows[:, 512:1024],
                                                1e-5, None, ALU.add),
            r=["rows"], w=["rows"])
        b.c("scalar", lambda e: e.activation(out=rows[:, 1024:1536], in_=rows[:, 512:1024],
                                             func=AF.Ln),
            r=["rows"], w=["rowsd"])
        b.c("scalar", lambda e: e.activation(out=rows[:, 1024:1536], in_=rows[:, 1024:1536],
                                             func=AF.Exp, scale=-0.5),
            r=["rowsd"], w=["rowsd"])
        b.c("tensor", lambda e: e.matmul(psBC[:], ones1f[:], rows[:, 0:512],
                                         start=True, stop=True),
            r=["ones1f", "rows", "psBCr"], w=["psBC"])
        b.c("tensor", lambda e: e.matmul(psW[3][:], ones1f[:], rows[:, 1024:1536],
                                         start=True, stop=True),
            r=["ones1f", "rowsd", "psW3r"], w=["psW3"])
        b.c("vector", lambda e: e.tensor_copy(bcRs[:], psW[3][:]), r=["psW3"],
            w=["bcRs", "psW3r"])
        for c1 in range(8):
            b.c("vector", lambda e, c1=c1: e.tensor_tensor(
                t2f[:], hsT(c1), psBC[:], ALU.subtract),
                r=[xt_r[c1], "psBC"], w=["t2f"])
            b.c("vector", lambda e, c1=c1: e.tensor_tensor(t2f[:], t2f[:], bcRs[:], ALU.mult),
                r=["t2f", "bcRs"], w=["t2f"])
            b.c("vector", lambda e, c1=c1: e.tensor_scalar(
                lnT[:, c1 * 512:(c1 + 1) * 512], t2f[:], lgv[:, c1:c1 + 1],
                lbv[:, c1:c1 + 1], ALU.mult, op1=ALU.add),
                r=["t2f", "lgv", "lbv"], w=[f"lnT_{c1}"])
        for oc in range(22):
            ow = min(2730, (oc + 1) * 128) - oc * 128
            for part in range(2):
                wS = w1aS if part == 0 else w1gS
                wSn = "w1aS" if part == 0 else "w1gS"
                brow = b1ar if part == 0 else b1gr
                brn = "b1ar" if part == 0 else "b1gr"
                pwi = 2 * (oc % 2) + part
                pw, pwn = psW[pwi], f"psW{pwi}"
                for f in range(8):
                    b.c("tensor", lambda e, oc=oc, f=f, wS=wS, pw=pw, ow=ow: e.matmul(
                        pw[0:ow, :], wS[:, f * 2730 + oc * 128: f * 2730 + oc * 128 + ow],
                        lnT[:, f * 512:(f + 1) * 512],
                        start=(f == 0), stop=False),
                        r=[wSn, f"lnT_{f}"] + ([pwn + "r"] if f == 0 else []), w=[pwn])
                b.c("tensor", lambda e, oc=oc, brow=brow, pw=pw, ow=ow: e.matmul(
                    pw[0:ow, :], brow[:, oc * 128: oc * 128 + ow], ones1[:],
                    start=False, stop=True),
                    r=["ones1", brn], w=[pwn])
            tb = t1b[oc % 2]
            tbn = f"t1b{oc % 2}"
            pga, pgan = psW[2 * (oc % 2)], f"psW{2 * (oc % 2)}"
            pgg, pggn = psW[2 * (oc % 2) + 1], f"psW{2 * (oc % 2) + 1}"
            b.c("scalar", lambda e, ow=ow, tb=tb, pgg=pgg: e.activation(
                out=tb[0:ow, :], in_=pgg[0:ow, :], func=AF.Gelu),
                r=[pggn, tbn + "R"], w=[tbn, pggn + "r"])
            b.c("vector", lambda e, oc=oc, ow=ow, tb=tb, pga=pga: e.tensor_tensor(
                hT[0:ow, oc * 512:(oc + 1) * 512], pga[0:ow, :], tb[0:ow, :], ALU.mult),
                r=[pgan, tbn], w=[f"hT_{oc}", pgan + "r", tbn + "R"])
        for i in range(4):
            b.d("sync", xw[:], xm[wi * 512 + i * 128: wi * 512 + (i + 1) * 128, :],
                r=["xwR"], w=["xw"])
            for o2 in range(2):
                for hc in range(22):
                    hw = min(2730, (hc + 1) * 128) - hc * 128
                    b.c("tensor", lambda e, i=i, hc=hc, hw=hw, o2=o2: e.matmul(
                        psO[:],
                        hT[0:hw, hc * 512 + i * 128: hc * 512 + (i + 1) * 128],
                        w2S[0:hw, hc * 1024 + o2 * 512: hc * 1024 + (o2 + 1) * 512],
                        start=(hc == 0), stop=False, skip_group_check=True),
                        r=[f"hT_{hc}", "w2S"] + (["psOr"] if hc == 0 else []),
                        w=["psO"])
                b.c("tensor", lambda e, o2=o2: e.matmul(
                    psO[:], ones1[:, 0:128],
                    b2row[:, o2 * 512:(o2 + 1) * 512],
                    start=False, stop=True, skip_group_check=True),
                    r=["ones1", "b2row"], w=["psO"])
                b.c("vector", lambda e, i=i, o2=o2: e.tensor_tensor(
                    outf[:], psO[:],
                    xw[:, o2 * 512:(o2 + 1) * 512], ALU.add),
                    r=["psO", "xw", "outfd"], w=["outf", "psOr"] + (["xwR"] if o2 == 1 else []))
                b.d("sync", o_f[wi * 512 + i * 128: wi * 512 + (i + 1) * 128,
                                o2 * 512:(o2 + 1) * 512], outf[:],
                    r=["outf"], w=["outfd"])
    b.replay()
    return nc


def run_p3b(I, xmid_res, l, _cache={}):
    cs = host_consts()
    if "p3b" not in _cache:
        _cache["p3b"] = build_p3b()
    nc = _cache["p3b"]
    w1 = I["L0_ffn_w1"][l]
    maps = []
    for c in range(8):
        bb, h = c // 2, c % 2
        xm = np.asarray(xmid_res[c]["o_xmid"])
        if h == 0:
            xp = np.zeros((128, 8), np.float32)
        else:
            peer = np.asarray(xmid_res[bb * 2]["o_xmid"])
            xp = np.ascontiguousarray(peer[NTOK - 1].reshape(8, 128).T)
        maps.append({
            "xm": xm, "xmprev": xp,
            "w1a": np.ascontiguousarray(w1[:, :2730]),
            "w1g": np.ascontiguousarray(w1[:, 2730:]),
            "w2": I["L0_ffn_w2"][l],
            "b1a": np.ascontiguousarray(I["L0_ffn_b1"][l][:2730].reshape(1, 2730)),
            "b1g": np.ascontiguousarray(I["L0_ffn_b1"][l][2730:].reshape(1, 2730)),
            "b2r": I["L0_ffn_b2"][l].reshape(1, C),
            "lgc": np.ascontiguousarray(I["L0_ffn_ln_g"][l].reshape(8, 128).T),
            "lbc": np.ascontiguousarray(I["L0_ffn_ln_b"][l].reshape(8, 128).T),
            "c_idn": cs["c_idn"], "c_ones": np.ones((1, 512), np.float32),
            "c_onec": np.ones((128, 1), np.float32),
        })
    res = run_bass_kernel_spmd(nc, maps, core_ids=list(range(8)))
    return res.results



def softplus_chain(b, psrc, psrcn, dst, dstn, tmpa, tmpb, onecol):
    b.c("scalar", lambda e: e.activation(out=tmpa[:], in_=psrc[:], func=AF.Abs),
        r=[psrcn], w=[tmpa.name])
    b.c("scalar", lambda e: e.activation(out=tmpa[:], in_=tmpa[:], func=AF.Exp, scale=-1.0),
        r=[tmpa.name], w=[tmpa.name])
    b.c("scalar", lambda e: e.activation(out=tmpa[:], in_=tmpa[:], func=AF.Ln,
                                         bias=onecol[:, 0:1], scale=1.0),
        r=[tmpa.name, "onecolf"], w=[tmpa.name])
    b.c("vector", lambda e: e.tensor_scalar(tmpb[:], psrc[:], 0.0, None, ALU.max),
        r=[psrcn], w=[tmpb.name])
    b.c("vector", lambda e: e.tensor_tensor(dst[:], tmpa[:], tmpb[:], ALU.add),
        r=[tmpa.name, tmpb.name], w=[dstn])


def build_p6():
    nc = bass.Bass("TRN2", target_bir_lowering=False)
    g1d = nc.dram_tensor("g1d", [NTOK, C], F32, kind="ExternalInput").ap()
    f2d = nc.dram_tensor("f2d", [NTOK, C], F32, kind="ExternalInput").ap()
    Wq = nc.dram_tensor("Wq", [C, C], F32, kind="ExternalInput").ap()
    Wk = nc.dram_tensor("Wk", [C, C], F32, kind="ExternalInput").ap()
    Wv = nc.dram_tensor("Wv", [C, C], F32, kind="ExternalInput").ap()
    Wg = nc.dram_tensor("Wg", [C, C], F32, kind="ExternalInput").ap()
    Wp = nc.dram_tensor("Wp", [C, C], F32, kind="ExternalInput").ap()
    Wc = nc.dram_tensor("Wc", [C, 1], F32, kind="ExternalInput").ap()
    gbr = nc.dram_tensor("gbr", [1, C], F32, kind="ExternalInput").ap()
    pbr = nc.dram_tensor("pbr", [1, C], F32, kind="ExternalInput").ap()
    cbc = nc.dram_tensor("cbc", [128, 1], F32, kind="ExternalInput").ap()
    lngr = nc.dram_tensor("lngr", [1, C], F32, kind="ExternalInput").ap()
    lnbr = nc.dram_tensor("lnbr", [1, C], F32, kind="ExternalInput").ap()
    ropeC = nc.dram_tensor("ropeC", [NTOK, C], F32, kind="ExternalInput").ap()
    ropeS = nc.dram_tensor("ropeS", [NTOK, C], F32, kind="ExternalInput").ap()
    c_idn = nc.dram_tensor("c_idn", [128, 128], F32, kind="ExternalInput").ap()
    c_ones = nc.dram_tensor("c_ones", [1, 512], F32, kind="ExternalInput").ap()
    c_onec = nc.dram_tensor("c_onec", [128, 1], F32, kind="ExternalInput").ap()
    c_utri = nc.dram_tensor("c_utri", [128, 128], F32, kind="ExternalInput").ap()
    o_x3 = nc.dram_tensor("o_x3", [NTOK, C], F32, kind="ExternalOutput").ap()
    o_q = nc.dram_tensor("o_q", [NTOK, C], BF16, kind="ExternalOutput").ap()
    o_k = nc.dram_tensor("o_k", [NTOK, C], BF16, kind="ExternalOutput").ap()
    o_v = nc.dram_tensor("o_v", [NTOK, C], BF16, kind="ExternalOutput").ap()
    o_gcs = nc.dram_tensor("o_gcs", [NTOK, C], F32, kind="ExternalOutput").ap()
    o_gtot = nc.dram_tensor("o_gtot", [1, C], F32, kind="ExternalOutput").ap()
    o_p = nc.dram_tensor("o_p", [NTOK, C], F32, kind="ExternalOutput").ap()
    o_pmax = nc.dram_tensor("o_pmax", [128, C], F32, kind="ExternalOutput").ap()
    o_clock = nc.dram_tensor("o_clock", [128, NT], F32, kind="ExternalOutput").ap()

    b = Builder(nc)
    A = nc.alloc_sbuf_tensor
    idn = A("idn", [128, 128], BF16)
    ones1f = A("ones1f", [1, 128], F32)
    onecf = A("onecolf", [128, 1], F32)
    utriF = A("utriF", [128, 128], F32)
    cbv = A("cbv", [128, 1], F32)
    WqS = A("WqS", [128, 8 * 1024], BF16)
    WkS = A("WkS", [128, 8 * 1024], BF16)
    WvS = A("WvS", [128, 8 * 1024], BF16)
    WgS = A("WgS", [128, 8 * 1024], BF16)
    WpS = A("WpS", [128, 8 * 1024], BF16)
    WcS = A("WcS", [128, 8], BF16)
    gbrow = A("gbrow", [1, C], BF16)
    pbrow = A("pbrow", [1, C], BF16)
    glnB = A("glnB", [128, 1024], F32)
    blnB = A("blnB", [128, 1024], F32)
    xa = A("xa", [128, 1024], F32)
    xn = A("xnb", [128, 1024], F32)
    xnb16 = A("xnb16", [128, 1024], BF16)
    xnT = A("xnT", [128, 8 * 128], BF16)
    scr = A("scr", [128, 1024], F32)
    mn = A("mn", [128, 8], F32)
    tq = A("tq", [128, 1024], F32)
    tq2 = A("tq2", [128, 1024], F32)
    tsw = A("tsw", [128, 1024], F32)
    rc = [A(f"rc{i}", [128, 1024], F32) for i in range(2)]
    rs = [A(f"rs{i}", [128, 1024], F32) for i in range(2)]
    carry = A("carry", [1, 1024], F32)
    pmaxs = A("pmaxs", [128, 1024], F32)
    clocks = A("clocks", [128, NT], F32)
    outb = A("outb", [128, 1024], BF16)

    P = nc.alloc_psum_tensor
    psQ = [P(f"psQ{i}", [128, 1024], F32) for i in range(2)]
    psT = P("psT", [128, 512], BF16)
    psTot = P("psTot", [1, 1024], F32)
    psCk = P("psCk", [128, 1], F32)

    b.d("sync", idn[:], c_idn[:], w=["idn"])
    b.d("sync", ones1f[:], c_ones[:, 0:128], w=["ones1f"])
    b.d("sync", onecf[:], c_onec[:], w=["onecolf"])
    b.d("sync", utriF[:], c_utri[:], w=["utriF"])
    b.d("sync", cbv[:], cbc[:], w=["cbv"])
    b.d("gpsimd", gbrow[:], gbr[:], w=["gbrow"])
    b.d("gpsimd", pbrow[:], pbr[:], w=["pbrow"])
    for (W, WS, nm) in ((Wq, WqS, "WqS"), (Wk, WkS, "WkS"), (Wv, WvS, "WvS"),
                        (Wg, WgS, "WgS"), (Wp, WpS, "WpS")):
        for f in range(8):
            b.d("gpsimd", WS[:, f * 1024:(f + 1) * 1024], W[ts(f), :], w=[nm])
    b.d("gpsimd", WcS[:], Wc.rearrange("(f p) a -> p (f a)", p=128), w=["WcS"],
        allow_slow_non_contiguous=True)
    lngs = A("lngs", [1, C], F32)
    lnbs = A("lnbs", [1, C], F32)
    b.d("sync", lngs[:], lngr[:], w=["lngs"])
    b.d("sync", lnbs[:], lnbr[:], w=["lnbs"])
    for o2 in range(2):
        b.c("tensor", lambda e, o2=o2: e.matmul(
            psQ[0][:, o2 * 512:(o2 + 1) * 512], ones1f[:], lngs[:, o2 * 512:(o2 + 1) * 512],
            start=True, stop=True),
            r=["ones1f", "lngs"] + (["psQ0r"] if o2 == 0 else []), w=["psQ0"])
    b.c("vector", lambda e: e.tensor_copy(glnB[:], psQ[0][:]), r=["psQ0"], w=["glnB", "psQ0r"])
    for o2 in range(2):
        b.c("tensor", lambda e, o2=o2: e.matmul(
            psQ[0][:, o2 * 512:(o2 + 1) * 512], ones1f[:], lnbs[:, o2 * 512:(o2 + 1) * 512],
            start=True, stop=True),
            r=["ones1f", "lnbs"] + (["psQ0r"] if o2 == 0 else []), w=["psQ0"])
    b.c("vector", lambda e: e.tensor_copy(blnB[:], psQ[0][:]), r=["psQ0"], w=["blnB", "psQ0r"])

    for ti in range(NT):
        b.d("sync", xa[:], g1d[ts(ti), :], r=["xaR"], w=["xa"])
        b.d("gpsimd", scr[:], f2d[ts(ti), :], r=["scrR"], w=["scr"])
        b.c("vector", lambda e: e.tensor_tensor(xa[:], xa[:], scr[:], ALU.add),
            r=["xa", "scr"], w=["xa", "scrR"])
        ln_token_tile(b, xa, xa, mn, scr, "p6ln", out_res=["xa", "xaR"])
        b.d("sync", o_x3[ts(ti), :], xa[:], r=["xa"], w=["x3d"])
        ln_token_tile(b, xa, xn, mn, scr, "p6ln", out_res=["xnb"])
        b.c("vector", lambda e: e.tensor_tensor(xn[:], xn[:], glnB[:], ALU.mult),
            r=["xnb", "glnB"], w=["xnb"])
        b.c("vector", lambda e: e.tensor_tensor(xn[:], xn[:], blnB[:], ALU.add),
            r=["xnb", "blnB"], w=["xnb"])
        b.c("vector", lambda e: e.tensor_copy(xnb16[:], xn[:]), r=["xnb"], w=["xnb16"])
        for g4 in range(2):
            for jj in range(4):
                c1 = g4 * 4 + jj
                b.c("tensor", lambda e, c1=c1, jj=jj: e.matmul(
                    psT[:, jj * 128:(jj + 1) * 128], xnb16[:, c1 * 128:(c1 + 1) * 128],
                    idn[:], start=(jj == 0), stop=(jj == 3), is_transpose=True,
                    skip_group_check=True),
                    r=["xnb16", "idn"] + (["psTr"] if jj == 0 else []), w=["psT"])
            b.c("vector", lambda e, g4=g4: e.tensor_copy(
                xnT[:, g4 * 512:(g4 + 1) * 512], psT[:]),
                r=["psT"], w=["xnT", "psTr"])
        b.d("sync", rc[ti % 2][:], ropeC[ts(ti), :], r=[f"rc{ti % 2}R"], w=[f"rc{ti % 2}"])
        b.d("sync", rs[ti % 2][:], ropeS[ts(ti), :], r=[f"rs{ti % 2}R"], w=[f"rs{ti % 2}"])

        def proj(WS, wsn, pq, pqn, bias_row=None, brn=None):
            for f in range(8):
                for o2 in range(2):
                    b.c("tensor", lambda e, f=f, o2=o2, WS=WS, pq=pq: e.matmul(
                        pq[:, o2 * 512:(o2 + 1) * 512],
                        xnT[:, f * 128:(f + 1) * 128],
                        WS[:, f * 1024 + o2 * 512: f * 1024 + (o2 + 1) * 512],
                        start=(f == 0), stop=(f == 7 and bias_row is None),
                        skip_group_check=True),
                        r=["xnT", wsn] + ([pqn + "r"] if f == 0 and o2 == 0 else []),
                        w=[pqn])
            if bias_row is not None:
                for o2 in range(2):
                    b.c("tensor", lambda e, o2=o2, bias_row=bias_row: e.matmul(
                        pq[:, o2 * 512:(o2 + 1) * 512], b.ones1b[:],
                        bias_row[:, o2 * 512:(o2 + 1) * 512],
                        start=False, stop=True, skip_group_check=True),
                        r=["ones1b", brn], w=[pqn])

        def l2rope(pq, pqn, dst_dram, name):
            b.c("vector", lambda e, pq=pq: e.tensor_copy(tq[:], pq[:]),
                r=[pqn], w=["tq", pqn + "r"])
            b.c("scalar", lambda e: e.activation(out=scr[:], in_=tq[:], func=AF.Square),
                r=["tq", "scrR"], w=["scr"])
            b.c("vector", lambda e: e.tensor_reduce(mn[:, 4:5], scr[:], AX.X, ALU.add),
                r=["scr"], w=["mnl", "scrR"])
            b.c("scalar", lambda e: e.activation(out=mn[:, 5:6], in_=mn[:, 4:5], func=AF.Sqrt),
                r=["mnl"], w=["mnl2"])
            b.c("vector", lambda e: e.tensor_scalar(mn[:, 5:6], mn[:, 5:6], 1e-12, None, ALU.max),
                r=["mnl2"], w=["mnl2"])
            b.c("vector", lambda e: e.reciprocal(mn[:, 5:6], mn[:, 5:6]),
                r=["mnl2"], w=["mnl2"])
            b.c("vector", lambda e: e.tensor_scalar(tq[:], tq[:], mn[:, 5:6], None, ALU.mult),
                r=["tq", "mnl2"], w=["tq"])
            b.c("vector", lambda e: e.tensor_copy(
                bass.AP(tsw, 0, [[1024, 128], [2, 512]]),
                bass.AP(tq, 1, [[1024, 128], [2, 512]])),
                r=["tq"], w=["tsw"])
            b.c("vector", lambda e: e.tensor_copy(
                bass.AP(tsw, 1, [[1024, 128], [2, 512]]),
                bass.AP(tq, 0, [[1024, 128], [2, 512]])),
                r=["tq"], w=["tsw"])
            b.c("vector", lambda e, ti=ti: e.tensor_tensor(tq[:], tq[:], rc[ti % 2][:], ALU.mult),
                r=["tq", f"rc{ti % 2}"], w=["tq"] + ([f"rc{ti % 2}R"] if name == "k" else []))
            b.c("vector", lambda e, ti=ti: e.tensor_tensor(tsw[:], tsw[:], rs[ti % 2][:], ALU.mult),
                r=["tsw", f"rs{ti % 2}"], w=["tsw"] + ([f"rs{ti % 2}R"] if name == "k" else []))
            b.c("vector", lambda e: e.tensor_tensor(outb[:], tq[:], tsw[:], ALU.add),
                r=["tq", "tsw"], w=["outb"])
            b.d("sync", dst_dram[ts(ti), :], outb[:], r=["outb"], w=["outbd"])

        b.ones1b = getattr(b, "ones1b_t", None)
        if b.ones1b is None:
            b.ones1b_t = b.ones1b = A("ones1b", [1, 128], BF16)
            b.d("sync", b.ones1b[:], c_ones[:, 0:128], w=["ones1b"])

        proj(WqS, "WqS", psQ[0], "psQ0")
        l2rope(psQ[0], "psQ0", o_q, "q")
        proj(WkS, "WkS", psQ[1], "psQ1")
        l2rope(psQ[1], "psQ1", o_k, "k")
        proj(WvS, "WvS", psQ[0], "psQ0")
        b.c("vector", lambda e: e.tensor_copy(outb[:], psQ[0][:]),
            r=["psQ0", "outbd"], w=["outb", "psQ0r"])
        b.d("sync", o_v[ts(ti), :], outb[:], r=["outb"], w=["outbd"])
        proj(WgS, "WgS", psQ[1], "psQ1", bias_row=gbrow, brn="gbrow")
        softplus_chain(b, psQ[1], "psQ1", tq, "tq", tq2, tsw, onecf)
        b.c("vector", lambda e: e.tensor_copy(scr[:, 0:1], scr[:, 0:1]),
            r=["tq"], w=["psQ1r", "scrR"])
        for o2 in range(2):
            b.c("tensor", lambda e, o2=o2: e.matmul(
                psQ[0][:, o2 * 512:(o2 + 1) * 512], utriF[:], tq[:, o2 * 512:(o2 + 1) * 512],
                start=True, stop=False, skip_group_check=True),
                r=["utriF", "tq"] + (["psQ0r"] if o2 == 0 else []), w=["psQ0"])
            if ti > 0:
                b.c("tensor", lambda e, o2=o2: e.matmul(
                    psQ[0][:, o2 * 512:(o2 + 1) * 512], ones1f[:],
                    carry[:, o2 * 512:(o2 + 1) * 512],
                    start=False, stop=True, skip_group_check=True),
                    r=["ones1f", "carry"], w=["psQ0"])
        for o2 in range(2):
            b.c("tensor", lambda e, o2=o2, ti=ti: e.matmul(
                psTot[:, o2 * 512:(o2 + 1) * 512], onecf[:],
                tq[:, o2 * 512:(o2 + 1) * 512],
                start=(ti == 0), stop=(ti == NT - 1), skip_group_check=True),
                r=["onecolf", "tq"] + (["psTotr"] if ti == 0 and o2 == 0 else []),
                w=["psTot"])
        b.c("vector", lambda e: e.tensor_copy(carry[:], psTot[:]),
            r=["psTot"], w=["carry"])
        b.c("vector", lambda e: e.tensor_copy(scr[:], psQ[0][:]),
            r=["psQ0", "scrR"], w=["scr", "psQ0r"])
        b.d("sync", o_gcs[ts(ti), :], scr[:], r=["scr"], w=["scrR"])
        proj(WpS, "WpS", psQ[1], "psQ1", bias_row=pbrow, brn="pbrow")
        b.c("vector", lambda e: e.tensor_copy(tq[:], psQ[1][:]),
            r=["psQ1"], w=["tq", "psQ1r"])
        b.d("sync", o_p[ts(ti), :], tq[:], r=["tq"], w=["tqR"])
        if ti == 0:
            b.c("vector", lambda e: e.tensor_copy(pmaxs[:], tq[:]), r=["tq"], w=["pmaxs"])
        else:
            b.c("vector", lambda e: e.tensor_tensor(pmaxs[:], pmaxs[:], tq[:], ALU.max),
                r=["tq", "pmaxs"], w=["pmaxs"])
        for f in range(8):
            b.c("tensor", lambda e, f=f: e.matmul(
                psCk[:], xnT[:, f * 128:(f + 1) * 128], WcS[:, f:f + 1],
                start=(f == 0), stop=(f == 7)),
                r=["xnT", "WcS"] + (["psCkr"] if f == 0 else []), w=["psCk"])
        b.c("vector", lambda e, ti=ti: e.tensor_scalar(mn[:, 6:7], psCk[:], cbv[:], None, ALU.add),
            r=["psCk", "cbv"], w=["mnck", "psCkr"])
        b.c("scalar", lambda e: e.activation(out=mn[:, 7:8], in_=mn[:, 6:7], func=AF.Abs),
            r=["mnck"], w=["mnck2"])
        b.c("scalar", lambda e: e.activation(out=mn[:, 7:8], in_=mn[:, 7:8], func=AF.Exp, scale=-1.0),
            r=["mnck2"], w=["mnck2"])
        b.c("scalar", lambda e: e.activation(out=mn[:, 7:8], in_=mn[:, 7:8], func=AF.Ln,
                                             bias=onecf[:, 0:1], scale=1.0),
            r=["mnck2", "onecolf"], w=["mnck2"])
        b.c("vector", lambda e: e.tensor_scalar(mn[:, 6:7], mn[:, 6:7], 0.0, None, ALU.max),
            r=["mnck"], w=["mnck"])
        b.c("vector", lambda e, ti=ti: e.tensor_tensor(clocks[:, ti:ti + 1], mn[:, 6:7],
                                                       mn[:, 7:8], ALU.add),
            r=["mnck", "mnck2"], w=["clocks"])
    b.d("sync", o_gtot[:], carry[:], r=["carry"])
    b.d("sync", o_pmax[:], pmaxs[:], r=["pmaxs"])
    b.d("sync", o_clock[:], clocks[:], r=["clocks"])
    b.replay()
    return nc


def host_p6_tables():
    half = 512
    freqs = np.exp(-np.arange(half, dtype=np.float64) * (np.log(10000.0) / half))
    t = np.arange(T, dtype=np.float64)
    th = t[:, None] * freqs[None, :]
    c = np.cos(th)
    s = np.sin(th)
    ropeC = np.repeat(c, 2, axis=1).astype(np.float32)
    ropeS = np.empty((T, 1024), np.float64)
    ropeS[:, 0::2] = -s
    ropeS[:, 1::2] = s
    ut = np.triu(np.ones((128, 128), dtype=np.float32))
    return ropeC, ropeS.astype(np.float32), ut


def run_p6(I, g1full, f2full, _cache={}):
    cs = host_consts()
    ropeC, ropeS, ut = host_p6_tables()
    if "p6" not in _cache:
        _cache["p6"] = build_p6()
    nc = _cache["p6"]
    maps = []
    for c in range(8):
        bb, h = c // 2, c % 2
        s = h * NTOK
        maps.append({
            "g1d": np.ascontiguousarray(g1full[bb, s:s + NTOK]),
            "f2d": np.ascontiguousarray(f2full[bb, s:s + NTOK]),
            "Wq": I["L1_attn_Wq"], "Wk": I["L1_attn_Wk"], "Wv": I["L1_attn_Wv"],
            "Wg": I["L1_attn_gate_W"], "Wp": I["L1_attn_p_W"],
            "Wc": I["L1_attn_clock_W"],
            "gbr": I["L1_attn_gate_b"].reshape(1, C),
            "pbr": I["L1_attn_p_b"].reshape(1, C),
            "cbc": np.ascontiguousarray(np.broadcast_to(I["L1_attn_clock_b"].reshape(1, 1), (128, 1))),
            "lngr": I["L1_attn_ln1_g"].reshape(1, C),
            "lnbr": I["L1_attn_ln1_b"].reshape(1, C),
            "ropeC": np.ascontiguousarray(ropeC[s:s + NTOK]),
            "ropeS": np.ascontiguousarray(ropeS[s:s + NTOK]),
            "c_idn": cs["c_idn"], "c_ones": np.ones((1, 512), np.float32),
            "c_onec": np.ones((128, 1), np.float32), "c_utri": ut,
        })
    res = run_bass_kernel_spmd(nc, maps, core_ids=list(range(8)))
    return res.results



def build_p7():
    nc = bass.Bass("TRN2", target_bir_lowering=False)
    pd = nc.dram_tensor("pd", [NTOK, C], F32, kind="ExternalInput").ap()
    pm0 = nc.dram_tensor("pm0", [128, C], F32, kind="ExternalInput").ap()
    pm1 = nc.dram_tensor("pm1", [128, C], F32, kind="ExternalInput").ap()
    clk = nc.dram_tensor("clk", [128, NT], F32, kind="ExternalInput").ap()
    c_idn = nc.dram_tensor("c_idn", [128, 128], F32, kind="ExternalInput").ap()
    c_ones = nc.dram_tensor("c_ones", [1, 512], F32, kind="ExternalInput").ap()
    c_onec = nc.dram_tensor("c_onec", [128, 1], F32, kind="ExternalInput").ap()
    c_utri = nc.dram_tensor("c_utri", [128, 128], F32, kind="ExternalInput").ap()
    sc_pm = nc.dram_tensor("sc_pm", [8, 128], F32, kind="Internal").ap()
    o_pexp = nc.dram_tensor("o_pexp", [NTOK, C], F32, kind="ExternalOutput").ap()
    o_pcs = nc.dram_tensor("o_pcs", [NTOK, C], F32, kind="ExternalOutput").ap()
    o_ptot = nc.dram_tensor("o_ptot", [1, C], F32, kind="ExternalOutput").ap()

    b = Builder(nc)
    A = nc.alloc_sbuf_tensor
    idnf = A("idnf", [128, 128], F32)
    ones1f = A("ones1f", [1, 128], F32)
    onecf = A("onecolf", [128, 1], F32)
    utriF = A("utriF", [128, 128], F32)
    clks = A("clks", [128, NT], F32)
    pma = A("pma", [128, 1024], F32)
    pmb = A("pmb", [128, 1024], F32)
    pmpp = A("pmpp", [128, 8], F32)
    pmrow = A("pmrow", [1, 1024], F32)
    pmB = A("pmB", [128, 1024], F32)
    pt = [A(f"pt{i}", [128, 1024], F32) for i in range(2)]
    tq = A("tq", [128, 1024], F32)
    carry = A("carry", [1, 1024], F32)
    outf = A("outf", [128, 1024], F32)

    P = nc.alloc_psum_tensor
    psQ = [P(f"psQ{i}", [128, 1024], F32) for i in range(2)]
    psT = P("psT", [128, 128], F32)
    psTot = P("psTot", [1, 1024], F32)

    b.d("sync", idnf[:], c_idn[:], w=["idnf"])
    b.d("sync", ones1f[:], c_ones[:, 0:128], w=["ones1f"])
    b.d("sync", onecf[:], c_onec[:], w=["onecolf"])
    b.d("sync", utriF[:], c_utri[:], w=["utriF"])
    b.d("sync", clks[:], clk[:], w=["clks"])
    b.d("sync", pma[:], pm0[:], w=["pma"])
    b.d("sync", pmb[:], pm1[:], w=["pmb"])
    b.c("vector", lambda e: e.tensor_tensor(pma[:], pma[:], pmb[:], ALU.max),
        r=["pma", "pmb"], w=["pma"])
    for c1 in range(8):
        b.c("tensor", lambda e, c1=c1: e.transpose(
            psT[:], pma[:, c1 * 128:(c1 + 1) * 128], idnf[:]),
            r=["pma", "idnf", "psTr"], w=["psT"])
        b.c("vector", lambda e: e.tensor_copy(pmb[:, 0:128], psT[:]),
            r=["psT"], w=["pmb2", "psTr"])
        b.c("vector", lambda e, c1=c1: e.tensor_reduce(
            pmpp[:, c1:c1 + 1], pmb[:, 0:128], AX.X, ALU.max),
            r=["pmb2"], w=["pmpp"])
        b.d("sync", sc_pm[c1:c1 + 1, :], pmpp[:, c1:c1 + 1], r=["pmpp"], w=["scpm"])
    b.d("sync", pmrow[:], sc_pm.rearrange("c p -> (c p)"), r=["scpm"], w=["pmrow"])
    for o2 in range(2):
        b.c("tensor", lambda e, o2=o2: e.matmul(
            psQ[0][:, o2 * 512:(o2 + 1) * 512], ones1f[:], pmrow[:, o2 * 512:(o2 + 1) * 512],
            start=True, stop=True),
            r=["ones1f", "pmrow"] + (["psQ0r"] if o2 == 0 else []), w=["psQ0"])
    b.c("vector", lambda e: e.tensor_copy(pmB[:], psQ[0][:]), r=["psQ0"], w=["pmB", "psQ0r"])

    for ti in range(NT):
        tb = ti % 2
        b.d("sync", pt[tb][:], pd[ts(ti), :], r=[f"pt{tb}R"], w=[f"pt{tb}"])
        b.c("vector", lambda e, tb=tb: e.tensor_tensor(tq[:], pt[tb][:], pmB[:], ALU.subtract),
            r=[f"pt{tb}", "pmB"], w=["tq", f"pt{tb}R"])
        b.c("scalar", lambda e: e.activation(out=tq[:], in_=tq[:], func=AF.Exp),
            r=["tq"], w=["tq"])
        b.c("vector", lambda e, ti=ti: e.tensor_scalar(tq[:], tq[:], clks[:, ti:ti + 1],
                                                       None, ALU.mult),
            r=["tq", "clks"], w=["tq"])
        b.d("sync", o_pexp[ts(ti), :], tq[:], r=["tq"], w=["tqD"])
        for o2 in range(2):
            b.c("tensor", lambda e, o2=o2: e.matmul(
                psQ[0][:, o2 * 512:(o2 + 1) * 512], utriF[:], tq[:, o2 * 512:(o2 + 1) * 512],
                start=True, stop=False, skip_group_check=True),
                r=["utriF", "tq"] + (["psQ0r"] if o2 == 0 else []), w=["psQ0"])
            if ti > 0:
                b.c("tensor", lambda e, o2=o2: e.matmul(
                    psQ[0][:, o2 * 512:(o2 + 1) * 512], ones1f[:],
                    carry[:, o2 * 512:(o2 + 1) * 512],
                    start=False, stop=True, skip_group_check=True),
                    r=["ones1f", "carry"], w=["psQ0"])
        for o2 in range(2):
            b.c("tensor", lambda e, o2=o2, ti=ti: e.matmul(
                psTot[:, o2 * 512:(o2 + 1) * 512], onecf[:], tq[:, o2 * 512:(o2 + 1) * 512],
                start=(ti == 0), stop=(ti == NT - 1), skip_group_check=True),
                r=["onecolf", "tq"] + (["psTotr"] if ti == 0 and o2 == 0 else []),
                w=["psTot"])
        b.c("vector", lambda e: e.tensor_copy(carry[:], psTot[:]), r=["psTot"], w=["carry"])
        b.c("vector", lambda e: e.tensor_copy(outf[:], psQ[0][:]),
            r=["psQ0"], w=["outf", "psQ0r"])
        b.d("sync", o_pcs[ts(ti), :], outf[:], r=["outf"], w=["outfD"])
    b.d("sync", o_ptot[:], carry[:], r=["carry"])
    b.replay()
    return nc



def build_p8():
    nc = bass.Bass("TRN2", target_bir_lowering=False)
    qd = nc.dram_tensor("qd", [NTOK, C], BF16, kind="ExternalInput").ap()
    kd = nc.dram_tensor("kd", [NTOK, C], BF16, kind="ExternalInput").ap()
    gcs = nc.dram_tensor("gcs", [NTOK, C], F32, kind="ExternalInput").ap()
    pexp = nc.dram_tensor("pexp", [NTOK, C], F32, kind="ExternalInput").ap()
    pcs = nc.dram_tensor("pcs", [NTOK, C], F32, kind="ExternalInput").ap()
    gtp = nc.dram_tensor("gtp", [1, C], F32, kind="ExternalInput").ap()
    ptp = nc.dram_tensor("ptp", [1, C], F32, kind="ExternalInput").ap()
    c_ones = nc.dram_tensor("c_ones", [1, 512], F32, kind="ExternalInput").ap()
    c_ln32 = nc.dram_tensor("c_ln32", [128, 1], F32, kind="ExternalInput").ap()
    o_qp = nc.dram_tensor("o_qp", [NTOK, C], BF16, kind="ExternalOutput").ap()
    o_kp = nc.dram_tensor("o_kp", [NTOK, C], BF16, kind="ExternalOutput").ap()

    b = Builder(nc)
    A = nc.alloc_sbuf_tensor
    ones1f = A("ones1f", [1, 128], F32)
    ln32 = A("ln32", [128, 1], F32)
    gtr = A("gtr", [1, 1024], F32)
    ptr = A("ptr", [1, 1024], F32)
    GB = A("GB", [128, 1024], F32)
    PB = A("PB", [128, 1024], F32)
    qt = A("qt", [128, 1024], BF16)
    kt = A("kt", [128, 1024], BF16)
    g1t = A("g1t", [128, 1024], F32)
    pet = A("pet", [128, 1024], F32)
    pct = A("pct", [128, 1024], F32)
    gjc = A("gjc", [128, 1024], F32)
    rcp = A("rcp", [128, 1024], F32)
    tmp = A("tmp", [128, 1024], F32)
    outb = A("outb", [128, 1024], BF16)

    P = nc.alloc_psum_tensor
    psQ = P("psQ", [128, 1024], F32)

    b.d("sync", ones1f[:], c_ones[:, 0:128], w=["ones1f"])
    b.d("sync", ln32[:], c_ln32[:], w=["ln32"])
    b.d("sync", gtr[:], gtp[:], w=["gtr"])
    b.d("sync", ptr[:], ptp[:], w=["ptr"])
    for (row, rn, dst, dn) in ((gtr, "gtr", GB, "GB"), (ptr, "ptr", PB, "PB")):
        for o2 in range(2):
            b.c("tensor", lambda e, o2=o2, row=row: e.matmul(
                psQ[:, o2 * 512:(o2 + 1) * 512], ones1f[:], row[:, o2 * 512:(o2 + 1) * 512],
                start=True, stop=True),
                r=["ones1f", rn] + (["psQr"] if o2 == 0 else []), w=["psQ"])
        b.c("vector", lambda e, dst=dst: e.tensor_copy(dst[:], psQ[:]),
            r=["psQ"], w=[dn, "psQr"])

    for ti in range(NT):
        b.d("sync", qt[:], qd[ts(ti), :], r=["qtR"], w=["qtb"])
        b.d("sync", kt[:], kd[ts(ti), :], r=["ktR"], w=["ktb"])
        b.d("sync", g1t[:], gcs[ts(ti), :], r=["g1tR"], w=["g1tb"])
        b.d("sync", pet[:], pexp[ts(ti), :], r=["petR"], w=["petb"])
        b.d("sync", pct[:], pcs[ts(ti), :], r=["pctR"], w=["pctb"])
        b.c("vector", lambda e: e.tensor_tensor(gjc[:], g1t[:], GB[:], ALU.add),
            r=["g1tb", "GB"], w=["gjc", "g1tR"])
        b.c("vector", lambda e: e.tensor_scalar(gjc[:], gjc[:], 60.0, -50.0, ALU.min,
                                                op1=ALU.max),
            r=["gjc"], w=["gjc"])
        b.c("scalar", lambda e: e.activation(out=gjc[:], in_=gjc[:], func=AF.Exp, scale=-1.0),
            r=["gjc"], w=["gjc"])
        b.c("vector", lambda e: e.tensor_tensor(pct[:], pct[:], PB[:], ALU.add),
            r=["pctb", "PB"], w=["pctb"])
        b.c("vector", lambda e: e.tensor_scalar(pct[:], pct[:], 1e-8, None, ALU.add),
            r=["pctb"], w=["pctb"])
        b.c("scalar", lambda e: e.activation(out=rcp[:], in_=pct[:], func=AF.Ln),
            r=["pctb"], w=["rcp", "pctR"])
        b.c("scalar", lambda e: e.activation(out=rcp[:], in_=rcp[:], func=AF.Exp, scale=-1.0,
                                             bias=ln32[:, 0:1]),
            r=["rcp", "ln32"], w=["rcp"])
        b.c("vector", lambda e: e.tensor_tensor(tmp[:], qt[:], gjc[:], ALU.mult),
            r=["qtb", "gjc"], w=["tmp", "qtR"])
        b.c("vector", lambda e: e.tensor_tensor(outb[:], tmp[:], rcp[:], ALU.mult),
            r=["tmp", "rcp"], w=["outb"])
        b.d("sync", o_qp[ts(ti), :], outb[:], r=["outb"], w=["outbD"])
        b.c("vector", lambda e: e.tensor_scalar(gjc[:], gjc[:], 1e-8, None, ALU.add),
            r=["gjc"], w=["gjc"])
        b.c("scalar", lambda e: e.activation(out=gjc[:], in_=gjc[:], func=AF.Ln),
            r=["gjc"], w=["gjc"])
        b.c("scalar", lambda e: e.activation(out=gjc[:], in_=gjc[:], func=AF.Exp, scale=-1.0),
            r=["gjc"], w=["gjc"])
        b.c("vector", lambda e: e.tensor_tensor(tmp[:], kt[:], pet[:], ALU.mult),
            r=["ktb", "petb"], w=["tmp", "ktR", "petR"])
        b.c("vector", lambda e: e.tensor_tensor(outb[:], tmp[:], gjc[:], ALU.mult),
            r=["tmp", "gjc", "outbD"], w=["outb"])
        b.d("sync", o_kp[ts(ti), :], outb[:], r=["outb"], w=["outbD"])
    b.replay()
    return nc


def run_p7(I, p6res, _cache={}):
    if "p7" not in _cache:
        _cache["p7"] = build_p7()
    nc = _cache["p7"]
    _, _, ut = host_p6_tables()
    maps = []
    for c in range(8):
        bb, h = c // 2, c % 2
        peer = bb * 2 + (1 - h)
        maps.append({
            "pd": np.asarray(p6res[c]["o_p"]),
            "pm0": np.asarray(p6res[c]["o_pmax"]),
            "pm1": np.asarray(p6res[peer]["o_pmax"]),
            "clk": np.asarray(p6res[c]["o_clock"]),
            "c_idn": np.eye(128, dtype=np.float32),
            "c_ones": np.ones((1, 512), np.float32),
            "c_onec": np.ones((128, 1), np.float32), "c_utri": ut,
        })
    res = run_bass_kernel_spmd(nc, maps, core_ids=list(range(8)))
    return res.results


def run_p8(I, p6res, p7res, _cache={}):
    if "p8" not in _cache:
        _cache["p8"] = build_p8()
    nc = _cache["p8"]
    z = np.zeros((1, C), np.float32)
    maps = []
    for c in range(8):
        bb, h = c // 2, c % 2
        maps.append({
            "qd": np.asarray(p6res[c]["o_q"]),
            "kd": np.asarray(p6res[c]["o_k"]),
            "gcs": np.asarray(p6res[c]["o_gcs"]),
            "pexp": np.asarray(p7res[c]["o_pexp"]),
            "pcs": np.asarray(p7res[c]["o_pcs"]),
            "gtp": (np.asarray(p6res[bb * 2]["o_gtot"]) if h == 1 else z),
            "ptp": (np.asarray(p7res[bb * 2]["o_ptot"]) if h == 1 else z),
            "c_ones": np.ones((1, 512), np.float32),
            "c_ln32": np.full((128, 1), np.log(1.0 / 32.0), np.float32),
        })
    res = run_bass_kernel_spmd(nc, maps, core_ids=list(range(8)))
    return res.results



def build_p9():
    nc = bass.Bass("TRN2", target_bir_lowering=False)
    qp = nc.dram_tensor("qp", [NTOK, C], BF16, kind="ExternalInput").ap()
    kp = nc.dram_tensor("kp", [NTOK, C], BF16, kind="ExternalInput").ap()
    vp = nc.dram_tensor("vp", [NTOK, C], BF16, kind="ExternalInput").ap()
    qx = nc.dram_tensor("qx", [2048, C], BF16, kind="ExternalInput").ap()
    kx = nc.dram_tensor("kx", [NTOK, C], BF16, kind="ExternalInput").ap()
    vx = nc.dram_tensor("vx", [NTOK, C], BF16, kind="ExternalInput").ap()
    c_fc = nc.dram_tensor("c_fc", [128, 896], F32, kind="ExternalInput").ap()
    o_O1 = nc.dram_tensor("o_O1", [NTOK, C], F32, kind="ExternalOutput").ap()
    o_ml1 = nc.dram_tensor("o_ml1", [128, 2 * NT], F32, kind="ExternalOutput").ap()
    o_O2 = nc.dram_tensor("o_O2", [2048, C], F32, kind="ExternalOutput").ap()
    o_ml2 = nc.dram_tensor("o_ml2", [128, 2 * 16], F32, kind="ExternalOutput").ap()

    b = Builder(nc)
    A = nc.alloc_sbuf_tensor
    fc = A("fc", [128, 896], F32)
    QT = A("QT", [128, 8 * 2048], BF16)
    KT = [A(f"KT{i}", [128, 8 * 512], BF16) for i in range(2)]
    VB = [A(f"VB{i}", [128, 4 * 1024], BF16) for i in range(2)]
    Oac = A("Oac", [128, 16 * 1024], F32)
    ml = A("ml", [128, 4 * 16], F32)
    sm_ = [A(f"sm{i}", [128, 6], F32) for i in range(2)]
    Ssb_ = [A(f"Ssb{i}", [128, 512], F32) for i in range(2)]
    Pb_ = [A(f"Pb{i}", [128, 512], BF16) for i in range(2)]
    PT_ = [A(f"PT{i}", [128, 512], BF16) for i in range(2)]
    outf = A("outf", [128, 1024], F32)
    idn = A("idn", [128, 128], BF16)

    P = nc.alloc_psum_tensor
    psS = [P(f"psS{i}", [128, 512], F32) for i in range(2)]
    psPT_ = [P(f"psPT{i}", [128, 512], BF16) for i in range(2)]
    psO = P("psO", [128, 1024], F32)

    c_idn = nc.dram_tensor("c_idn", [128, 128], F32, kind="ExternalInput").ap()
    b.d("sync", idn[:], c_idn[:], w=["idn"])
    b.d("sync", fc[:], c_fc[:], w=["fc"])

    def load_q(src, tok0, ntok):
        for c1 in range(8):
            b.d("sync", QT[:, c1 * 2048: c1 * 2048 + ntok],
                src[tok0:tok0 + ntok, c1 * 128:(c1 + 1) * 128],
                w=["QT"], transpose=True)

    def load_kv(ksrc, vsrc, ks, slot):
        for c1 in range(8):
            b.d("sync", KT[slot][:, c1 * 512:(c1 + 1) * 512],
                ksrc[ks:ks + 512, c1 * 128:(c1 + 1) * 128],
                r=[f"KTr{slot}"], w=[f"KT{slot}"], transpose=True)
        for i in range(4):
            b.d("gpsimd", VB[slot][:, i * 1024:(i + 1) * 1024],
                vsrc[ks + i * 128: ks + (i + 1) * 128, :],
                r=[f"VBr{slot}"], w=[f"VB{slot}"])

    def flash_range(tag, qsrc, qtok0, nq, ksrc, vsrc, nk, causal, oO, oML):
        nqt = nq // 128
        load_q(qsrc, qtok0, nq)
        nkb = nk // 512
        first_done = [False] * nqt
        for kb in range(nkb):
            ks = kb * 512
            slot = kb % 2
            load_kv(ksrc, vsrc, ks, slot)
            for qt in range(nqt):
                qs = qt * 128 + (2048 if tag == "R2" else 0)
                if causal and ks > qs + 127:
                    continue
                vp = (kb + qt) % 2
                Ssb, sm, Pb, PT = Ssb_[vp], sm_[vp], Pb_[vp], PT_[vp]
                psPT = psPT_[vp]
                sfx = str(vp)
                for c1 in range(8):
                    b.c("tensor", lambda e, c1=c1, qt=qt, slot=slot: e.matmul(
                        psS[slot][:], QT[:, c1 * 2048 + qt * 128: c1 * 2048 + (qt + 1) * 128],
                        KT[slot][:, c1 * 512:(c1 + 1) * 512],
                        start=(c1 == 0), stop=(c1 == 7)),
                        r=["QT", f"KT{slot}"] + ([f"psS{slot}r"] if c1 == 0 else []),
                        w=[f"psS{slot}"])
                diag = causal and (qs - ks) < 512
                if diag:
                    d = qs - ks
                    b.c("vector", lambda e, d=d, slot=slot: e.tensor_tensor(
                        Ssb[:], psS[slot][:], fc[:, 384 - d: 896 - d], ALU.add),
                        r=[f"psS{slot}", "fc"], w=["Ssb" + sfx, f"psS{slot}r"])
                    sin_, sin_n = Ssb, "Ssb" + sfx
                else:
                    sin_, sin_n = psS[slot], f"psS{slot}"
                b.c("vector", lambda e, sin_=sin_: e.tensor_reduce(
                    sm[:, 0:1], sin_[:], AX.X, ALU.max),
                    r=[sin_n], w=["sm" + sfx])
                fd = first_done[qt]
                if not fd:
                    b.c("vector", lambda e, qt=qt: e.tensor_copy(ml[:, qt:qt + 1], sm[:, 0:1]),
                        r=["sm" + sfx], w=["mlm"])
                else:
                    b.c("vector", lambda e, qt=qt: e.tensor_tensor(
                        sm[:, 1:2], ml[:, qt:qt + 1], sm[:, 0:1], ALU.max),
                        r=["sm" + sfx, "mlm"], w=["sm2" + sfx])
                    b.c("vector", lambda e, qt=qt: e.tensor_tensor(
                        sm[:, 2:3], ml[:, qt:qt + 1], sm[:, 1:2], ALU.subtract),
                        r=["sm2" + sfx, "mlm"], w=["sm3" + sfx])
                    b.c("scalar", lambda e: e.activation(out=sm[:, 2:3], in_=sm[:, 2:3],
                                                         func=AF.Exp),
                        r=["sm3" + sfx], w=["sm3" + sfx])
                    b.c("vector", lambda e, qt=qt: e.tensor_copy(ml[:, qt:qt + 1], sm[:, 1:2]),
                        r=["sm2" + sfx], w=["mlm"])
                b.c("vector", lambda e, qt=qt: e.tensor_scalar(
                    sm[:, 3:4], ml[:, qt:qt + 1], -1.0, None, ALU.mult),
                    r=["mlm"], w=["sm4" + sfx])
                b.c("scalar", lambda e, sin_=sin_: e.activation(
                    out=Pb[:], in_=sin_[:], func=AF.Exp, bias=sm[:, 3:4], scale=1.0,
                    accum_out=sm[:, 4:5]),
                    r=[sin_n, "sm4" + sfx], w=["Pb" + sfx, "sm5" + sfx] + ([] if diag else [f"psS{slot}r"]))
                if not fd:
                    b.c("vector", lambda e, qt=qt: e.tensor_copy(
                        ml[:, 16 + qt: 17 + qt], sm[:, 4:5]),
                        r=["sm5" + sfx], w=["mll"])
                else:
                    b.c("vector", lambda e, qt=qt: e.tensor_tensor(
                        ml[:, 16 + qt: 17 + qt], ml[:, 16 + qt: 17 + qt], sm[:, 2:3], ALU.mult),
                        r=["mll", "sm3" + sfx], w=["mll"])
                    b.c("vector", lambda e, qt=qt: e.tensor_tensor(
                        ml[:, 16 + qt: 17 + qt], ml[:, 16 + qt: 17 + qt], sm[:, 4:5], ALU.add),
                        r=["mll", "sm5" + sfx], w=["mll"])
                for pc in range(4):
                    b.c("tensor", lambda e, pc=pc: e.matmul(
                        psPT[:, pc * 128:(pc + 1) * 128], Pb[:, pc * 128:(pc + 1) * 128],
                        idn[:], start=(pc == 0), stop=(pc == 3), is_transpose=True,
                        skip_group_check=True),
                        r=["Pb" + sfx, "idn"] + (["psPTr" + sfx] if pc == 0 else []), w=["psPT" + sfx])
                b.c("vector", lambda e: e.tensor_copy(PT[:], psPT[:]),
                    r=["psPT" + sfx], w=["PT" + sfx, "psPTr" + sfx])
                for pc in range(4):
                    for o2 in range(2):
                        b.c("tensor", lambda e, pc=pc, o2=o2, slot=slot: e.matmul(
                            psO[:, o2 * 512:(o2 + 1) * 512],
                            PT[:, pc * 128:(pc + 1) * 128],
                            VB[slot][:, pc * 1024 + o2 * 512: pc * 1024 + (o2 + 1) * 512],
                            start=(pc == 0), stop=(pc == 3), skip_group_check=True),
                            r=["PT" + sfx, f"VB{slot}"] + (["psOr"] if pc == 0 and o2 == 0 else []),
                            w=["psO"])
                if not fd:
                    b.c("vector", lambda e, qt=qt: e.tensor_copy(
                        Oac[:, qt * 1024:(qt + 1) * 1024], psO[:]),
                        r=["psO"], w=[f"Oac_{qt}", "psOr"])
                    first_done[qt] = True
                else:
                    b.c("scalar", lambda e, qt=qt: e.activation(
                        out=Oac[:, qt * 1024:(qt + 1) * 1024],
                        in_=Oac[:, qt * 1024:(qt + 1) * 1024],
                        func=AF.Copy, scale=sm[:, 2:3]),
                        r=[f"Oac_{qt}", "sm3" + sfx], w=[f"Oac_{qt}"])
                    b.c("vector", lambda e, qt=qt: e.tensor_tensor(
                        Oac[:, qt * 1024:(qt + 1) * 1024],
                        Oac[:, qt * 1024:(qt + 1) * 1024], psO[:], ALU.add),
                        r=[f"Oac_{qt}", "psO"], w=[f"Oac_{qt}", "psOr"])
            b.c("sync", lambda e: e.nop(), r=[f"KT{slot}", f"VB{slot}"],
                w=[f"KTr{slot}", f"VBr{slot}"])
        for qt in range(nqt):
            b.d("sync", oO[(qtok0 % NTOK) * 0 + (qt * 128) + (2048 if tag == "R2" else 0)
                           if tag != "R3" else qt * 128, :][0:0, :] if False else
                oO[(qt * 128 + (2048 if tag == "R2" else 0)) if tag != "R3" else qt * 128:
                   ((qt + 1) * 128 + (2048 if tag == "R2" else 0)) if tag != "R3" else (qt + 1) * 128, :],
                Oac[:, qt * 1024:(qt + 1) * 1024], r=[f"Oac_{qt}"], w=["OacD"])
        mlcol = (16 if tag == "R2" else 0)
        b.d("sync", oML[:, mlcol:mlcol + 16], ml[:, 0:16], r=["mlm"], w=["mlD"])
        b.d("sync", oML[:, NT + mlcol if tag != "R3" else 16: (NT + mlcol + 16) if tag != "R3" else 32],
            ml[:, 16:32], r=["mll"], w=["mlD"])

    flash_range("R1", qp, 0, 2048, kp, vp, 2048, True, o_O1, o_ml1)
    flash_range("R2", qp, 2048, 2048, kp, vp, 4096, True, o_O1, o_ml1)
    flash_range("R3", qx, 0, 2048, kx, vx, 4096, False, o_O2, o_ml2)
    b.replay()
    return nc


def run_p9(I, p6res, p8res, _cache={}):
    if "p9" not in _cache:
        _cache["p9"] = build_p9()
    nc = _cache["p9"]
    fc = np.zeros((128, 896), np.float32)
    for p in range(128):
        for v_ in range(896):
            if not (v_ - 384 <= p):
                fc[p, v_] = -1e30
    maps = []
    for c in range(8):
        bb, h = c // 2, c % 2
        peer = bb * 2 + (1 - h)
        qp_own = np.asarray(p8res[c]["o_qp"])
        q2 = np.asarray(p8res[bb * 2 + 1]["o_qp"])[h * 2048:(h + 1) * 2048]
        k1 = np.asarray(p8res[bb * 2]["o_kp"])
        v1 = np.asarray(p6res[bb * 2]["o_v"])
        maps.append({
            "qp": qp_own, "kp": np.asarray(p8res[c]["o_kp"]),
            "vp": np.asarray(p6res[c]["o_v"]),
            "qx": np.ascontiguousarray(q2), "kx": k1, "vx": v1,
            "c_fc": fc, "c_idn": np.eye(128, dtype=np.float32),
        })
    res = run_bass_kernel_spmd(nc, maps, core_ids=list(range(8)))
    return res.results



def build_p10():
    nc = bass.Bass("TRN2", target_bir_lowering=False)
    O1d = nc.dram_tensor("O1d", [NTOK, C], F32, kind="ExternalInput").ap()
    O2d = nc.dram_tensor("O2d", [NTOK, C], F32, kind="ExternalInput").ap()
    ml1d = nc.dram_tensor("ml1d", [128, 2 * NT], F32, kind="ExternalInput").ap()
    ml2d = nc.dram_tensor("ml2d", [128, 2 * NT], F32, kind="ExternalInput").ap()
    x3d = nc.dram_tensor("x3d", [NTOK, C], F32, kind="ExternalInput").ap()
    outW = nc.dram_tensor("outW", [C, C], F32, kind="ExternalInput").ap()
    outbr = nc.dram_tensor("outbr", [1, C], F32, kind="ExternalInput").ap()
    ln2gr = nc.dram_tensor("ln2gr", [1, C], F32, kind="ExternalInput").ap()
    ln2br = nc.dram_tensor("ln2br", [1, C], F32, kind="ExternalInput").ap()
    W1d = nc.dram_tensor("W1d", [C, 4096], F32, kind="ExternalInput").ap()
    b1r = nc.dram_tensor("b1r", [1, 4096], F32, kind="ExternalInput").ap()
    W2d = nc.dram_tensor("W2d", [4096, C], F32, kind="ExternalInput").ap()
    b2r = nc.dram_tensor("b2r", [1, C], F32, kind="ExternalInput").ap()
    c_idn = nc.dram_tensor("c_idn", [128, 128], F32, kind="ExternalInput").ap()
    c_ones = nc.dram_tensor("c_ones", [1, 512], F32, kind="ExternalInput").ap()
    o_g3 = nc.dram_tensor("o_g3", [NTOK, C], F32, kind="ExternalOutput").ap()

    b = Builder(nc)
    A = nc.alloc_sbuf_tensor
    idn = A("idn", [128, 128], BF16)
    ones1 = A("ones1", [1, 128], BF16)
    ones1f = A("ones1f", [1, 128], F32)
    outWS = A("outWS", [128, 8 * 1024], BF16)
    obrow = A("obrow", [1, C], BF16)
    b1row = A("b1row", [1, 4096], BF16)
    b2row = A("b2row", [1, C], BF16)
    W1S = A("W1S", [128, 8 * 4096], BF16)
    W2S = A("W2S", [128, 32 * 1024], BF16)
    ml1 = A("ml1", [128, 2 * NT], F32)
    ml2 = A("ml2", [128, 2 * NT], F32)
    g2B = A("g2B", [128, 1024], F32)
    b2B = A("b2B", [128, 1024], F32)
    lngs = A("lngs", [1, C], F32)
    O1t = A("O1t", [128, 1024], F32)
    O2t = A("O2t", [128, 1024], F32)
    x3t = A("x3t", [128, 1024], F32)
    x2t = A("x2t", [128, 1024], F32)
    sm = A("sm", [128, 6], F32)
    mn = A("mn", [128, 4], F32)
    scr = A("scr", [128, 1024], F32)
    hb = A("hb", [128, 1024], BF16)
    hT = A("hT", [128, 8 * 128], BF16)
    mg = [A(f"mg{i}", [128, 512], BF16) for i in range(2)]
    mT = A("mT", [128, 32 * 128], BF16)
    outf = O2t

    P = nc.alloc_psum_tensor
    psA = [P(f"psA{i}", [128, 1024], F32) for i in range(2)]
    psT = P("psT", [128, 512], BF16)
    psW = [P(f"psW{i}", [128, 512], F32) for i in range(2)]

    b.d("sync", idn[:], c_idn[:], w=["idn"])
    b.d("sync", ones1[:], c_ones[:, 0:128], w=["ones1"])
    b.d("sync", ones1f[:], c_ones[:, 0:128], w=["ones1f"])
    b.d("sync", ml1[:], ml1d[:], w=["ml1"])
    b.d("sync", ml2[:], ml2d[:], w=["ml2"])
    b.d("gpsimd", obrow[:], outbr[:], w=["obrow"])
    b.d("gpsimd", b1row[:], b1r[:], w=["b1row"])
    b.d("gpsimd", b2row[:], b2r[:], w=["b2row"])

    for f in range(8):
        b.d("gpsimd", outWS[:, f * 1024:(f + 1) * 1024], outW[ts(f), :], w=["outWS"])
        b.d("gpsimd", W1S[:, f * 4096:(f + 1) * 4096], W1d[ts(f), :], w=["W1S"])
    for hc in range(32):
        b.d("gpsimd", W2S[:, hc * 1024:(hc + 1) * 1024], W2d[ts(hc), :], w=["W2S"])
    for (srcd, srn, dst, dn) in ((ln2gr, "lngs", g2B, "g2B"), (ln2br, "lngs", b2B, "b2B")):
        b.d("sync", lngs[:], srcd[:], r=["lngsR"], w=["lngs"])
        src = lngs
        for o2 in range(2):
            b.c("tensor", lambda e, o2=o2, src=src: e.matmul(
                psA[0][:, o2 * 512:(o2 + 1) * 512], ones1f[:], src[:, o2 * 512:(o2 + 1) * 512],
                start=True, stop=True),
                r=["ones1f", srn] + (["psA0r"] if o2 == 0 else []), w=["psA0"])
        b.c("vector", lambda e, dst=dst: e.tensor_copy(dst[:], psA[0][:]),
            r=["psA0"], w=[dn, "psA0r", "lngsR"])

    for ti in range(NT):
        b.d("sync", O1t[:], O1d[ts(ti), :], r=["O1tR"], w=["O1t"])
        b.d("sync", O2t[:], O2d[ts(ti), :], r=["O2tR", "outfD", "outf"], w=["O2t"])
        b.d("sync", x3t[:], x3d[ts(ti), :], r=["x3tR"], w=["x3t"])
        b.c("vector", lambda e, ti=ti: e.tensor_tensor(
            sm[:, 0:1], ml1[:, ti:ti + 1], ml2[:, ti:ti + 1], ALU.max),
            r=["ml1", "ml2"], w=["sm"])
        b.c("vector", lambda e, ti=ti: e.tensor_tensor(
            sm[:, 1:2], ml1[:, ti:ti + 1], sm[:, 0:1], ALU.subtract),
            r=["ml1", "sm"], w=["sm1"])
        b.c("scalar", lambda e: e.activation(out=sm[:, 1:2], in_=sm[:, 1:2], func=AF.Exp),
            r=["sm1"], w=["sm1"])
        b.c("vector", lambda e, ti=ti: e.tensor_tensor(
            sm[:, 2:3], ml2[:, ti:ti + 1], sm[:, 0:1], ALU.subtract),
            r=["ml2", "sm"], w=["sm2"])
        b.c("scalar", lambda e: e.activation(out=sm[:, 2:3], in_=sm[:, 2:3], func=AF.Exp),
            r=["sm2"], w=["sm2"])
        b.c("vector", lambda e: e.tensor_scalar(O1t[:], O1t[:], sm[:, 1:2], None, ALU.mult),
            r=["O1t", "sm1"], w=["O1t"])
        b.c("vector", lambda e: e.tensor_scalar(O2t[:], O2t[:], sm[:, 2:3], None, ALU.mult),
            r=["O2t", "sm2"], w=["O2t"])
        b.c("vector", lambda e: e.tensor_tensor(O1t[:], O1t[:], O2t[:], ALU.add),
            r=["O1t", "O2t"], w=["O1t", "O2tR"])
        b.c("vector", lambda e, ti=ti: e.tensor_tensor(
            sm[:, 3:4], ml1[:, NT + ti:NT + ti + 1], sm[:, 1:2], ALU.mult),
            r=["ml1", "sm1"], w=["sm3"])
        b.c("vector", lambda e, ti=ti: e.tensor_tensor(
            sm[:, 4:5], ml2[:, NT + ti:NT + ti + 1], sm[:, 2:3], ALU.mult),
            r=["ml2", "sm2"], w=["sm4"])
        b.c("vector", lambda e: e.tensor_tensor(sm[:, 3:4], sm[:, 3:4], sm[:, 4:5], ALU.add),
            r=["sm3", "sm4"], w=["sm3"])
        b.c("vector", lambda e: e.reciprocal(sm[:, 3:4], sm[:, 3:4]), r=["sm3"], w=["sm3"])
        b.c("vector", lambda e: e.tensor_scalar(hb[:], O1t[:], sm[:, 3:4], None, ALU.mult),
            r=["O1t", "sm3"], w=["hb", "O1tR"])
        for g4 in range(2):
            for jj in range(4):
                c1 = g4 * 4 + jj
                b.c("tensor", lambda e, c1=c1, jj=jj: e.matmul(
                    psT[:, jj * 128:(jj + 1) * 128], hb[:, c1 * 128:(c1 + 1) * 128],
                    idn[:], start=(jj == 0), stop=(jj == 3), is_transpose=True,
                    skip_group_check=True),
                    r=["hb", "idn"] + (["psTr"] if jj == 0 else []), w=["psT"])
            b.c("vector", lambda e, g4=g4: e.tensor_copy(
                hT[:, g4 * 512:(g4 + 1) * 512], psT[:]),
                r=["psT"], w=["hT", "psTr"])
        for c1 in range(8):
            for o2 in range(2):
                b.c("tensor", lambda e, c1=c1, o2=o2: e.matmul(
                    psA[0][:, o2 * 512:(o2 + 1) * 512], hT[:, c1 * 128:(c1 + 1) * 128],
                    outWS[:, c1 * 1024 + o2 * 512: c1 * 1024 + (o2 + 1) * 512],
                    start=(c1 == 0), stop=False, skip_group_check=True),
                    r=["hT", "outWS"] + (["psA0r"] if c1 == 0 and o2 == 0 else []),
                    w=["psA0"])
        for o2 in range(2):
            b.c("tensor", lambda e, o2=o2: e.matmul(
                psA[0][:, o2 * 512:(o2 + 1) * 512], ones1[:], obrow[:, o2 * 512:(o2 + 1) * 512],
                start=False, stop=True, skip_group_check=True),
                r=["ones1", "obrow"], w=["psA0"])
        b.c("vector", lambda e: e.tensor_tensor(x2t[:], psA[0][:], x3t[:], ALU.add),
            r=["psA0", "x3t"], w=["x2t", "psA0r"])
        ln_token_tile(b, x2t, scr, mn, scr, "p10ln", in_res=["x2t"],
                      out_res=["scrO"])
        b.c("vector", lambda e: e.tensor_tensor(scr[:], scr[:], g2B[:], ALU.mult),
            r=["scrO", "g2B"], w=["scrO"])
        b.c("vector", lambda e: e.tensor_tensor(scr[:], scr[:], b2B[:], ALU.add),
            r=["scrO", "b2B"], w=["scrO"])
        b.c("vector", lambda e: e.tensor_copy(hb[:], scr[:]), r=["scrO"], w=["hb"])
        for g4 in range(2):
            for jj in range(4):
                c1 = g4 * 4 + jj
                b.c("tensor", lambda e, c1=c1, jj=jj: e.matmul(
                    psT[:, jj * 128:(jj + 1) * 128], hb[:, c1 * 128:(c1 + 1) * 128],
                    idn[:], start=(jj == 0), stop=(jj == 3), is_transpose=True,
                    skip_group_check=True),
                    r=["hb", "idn"] + (["psTr"] if jj == 0 else []), w=["psT"])
            b.c("vector", lambda e, g4=g4: e.tensor_copy(
                hT[:, g4 * 512:(g4 + 1) * 512], psT[:]),
                r=["psT"], w=["hT", "psTr"])
        for og in range(8):
            pw = psW[og % 2]
            pwn = f"psW{og % 2}"
            for f in range(8):
                b.c("tensor", lambda e, og=og, f=f, pw=pw: e.matmul(
                    pw[:], hT[:, f * 128:(f + 1) * 128],
                    W1S[:, f * 4096 + og * 512: f * 4096 + (og + 1) * 512],
                    start=(f == 0), stop=False, skip_group_check=True),
                    r=["hT", "W1S"] + ([pwn + "r"] if f == 0 else []), w=[pwn])
            b.c("tensor", lambda e, og=og, pw=pw: e.matmul(
                pw[:], ones1[:], b1row[:, og * 512:(og + 1) * 512],
                start=False, stop=True, skip_group_check=True),
                r=["ones1", "b1row"], w=[pwn])
            mgx, mgn = mg[og % 2], f"mg{og % 2}"
            b.c("scalar", lambda e, og=og, pw=pw, mgx=mgx: e.activation(
                out=mgx[:], in_=pw[:], func=AF.Gelu),
                r=[pwn, mgn + "R"], w=[mgn, pwn + "r"])
            for j in range(4):
                b.c("tensor", lambda e, j=j, mgx=mgx: e.matmul(
                    psT[:, j * 128:(j + 1) * 128], mgx[:, j * 128:(j + 1) * 128], idn[:],
                    start=(j == 0), stop=(j == 3), is_transpose=True,
                    skip_group_check=True),
                    r=[mgn, "idn"] + (["psTr"] if j == 0 else []),
                    w=["psT"] + ([mgn + "R"] if j == 3 else []))
            b.c("vector", lambda e, og=og: e.tensor_copy(
                mT[:, og * 512:(og + 1) * 512], psT[:]),
                r=["psT"], w=["mT", "psTr"])
        for o2 in range(2):
            for hc in range(32):
                b.c("tensor", lambda e, hc=hc, o2=o2: e.matmul(
                    psA[1][:, o2 * 512:(o2 + 1) * 512], mT[:, hc * 128:(hc + 1) * 128],
                    W2S[:, hc * 1024 + o2 * 512: hc * 1024 + (o2 + 1) * 512],
                    start=(hc == 0), stop=False, skip_group_check=True),
                    r=["mT", "W2S"] + (["psA1r"] if hc == 0 and o2 == 0 else []),
                    w=["psA1"])
            b.c("tensor", lambda e, o2=o2: e.matmul(
                psA[1][:, o2 * 512:(o2 + 1) * 512], ones1[:], b2row[:, o2 * 512:(o2 + 1) * 512],
                start=False, stop=True, skip_group_check=True),
                r=["ones1", "b2row"], w=["psA1"])
        b.c("vector", lambda e: e.tensor_tensor(outf[:], psA[1][:], x2t[:], ALU.add),
            r=["psA1", "x2t", "O2tR"], w=["outf", "psA1r"])
        b.c("vector", lambda e: e.tensor_tensor(outf[:], outf[:], x3t[:], ALU.add),
            r=["outf", "x3t"], w=["outf", "x3tR"])
        ln_token_tile(b, outf, outf, mn, scr, "p10ln", in_res=["outf"],
                      out_res=["outf"])
        b.d("sync", o_g3[ts(ti), :], outf[:], r=["outf"], w=["outfD"])
    b.replay()
    return nc


def run_p10(I, x3full, p9res, _cache={}):
    if "p10" not in _cache:
        _cache["p10"] = build_p10()
    nc = _cache["p10"]
    cs = host_consts()
    maps = []
    for c in range(8):
        bb, h = c // 2, c % 2
        if h == 0:
            O2 = np.zeros((NTOK, C), np.float32)
            ml2 = np.zeros((128, 2 * NT), np.float32)
            ml2[:, 0:NT] = -1e30
        else:
            O2a = np.asarray(p9res[bb * 2]["o_O2"])
            O2b = np.asarray(p9res[bb * 2 + 1]["o_O2"])
            O2 = np.concatenate([O2a, O2b], 0)
            m2a = np.asarray(p9res[bb * 2]["o_ml2"])
            m2b = np.asarray(p9res[bb * 2 + 1]["o_ml2"])
            ml2 = np.concatenate([m2a[:, 0:16], m2b[:, 0:16], m2a[:, 16:32], m2b[:, 16:32]],
                                 axis=1)
        maps.append({
            "O1d": np.asarray(p9res[c]["o_O1"]), "O2d": np.ascontiguousarray(O2),
            "ml1d": np.asarray(p9res[c]["o_ml1"]), "ml2d": np.ascontiguousarray(ml2),
            "x3d": np.ascontiguousarray(x3full[bb, h * NTOK:(h + 1) * NTOK]),
            "outW": I["L1_attn_out_W"], "outbr": I["L1_attn_out_b"].reshape(1, C),
            "ln2gr": I["L1_attn_ln2_g"].reshape(1, C),
            "ln2br": I["L1_attn_ln2_b"].reshape(1, C),
            "W1d": I["L1_attn_mlp_W1"], "b1r": I["L1_attn_mlp_b1"].reshape(1, 4096),
            "W2d": I["L1_attn_mlp_W2"], "b2r": I["L1_attn_mlp_b2"].reshape(1, C),
            "c_idn": cs["c_idn"], "c_ones": np.ones((1, 512), np.float32),
        })
    res = run_bass_kernel_spmd(nc, maps, core_ids=list(range(8)))
    return res.results



def build_p11():
    nc = bass.Bass("TRN2", target_bir_lowering=False)
    g3d = nc.dram_tensor("g3d", [NTOK, C], F32, kind="ExternalInput").ap()
    wred = nc.dram_tensor("wred", [128, 128], F32, kind="ExternalInput").ap()
    wgd = nc.dram_tensor("wgd", [8, 16], F32, kind="ExternalInput").ap()
    W1d = nc.dram_tensor("W1d", [8 * 128, 8 * 2048], F32, kind="ExternalInput").ap()
    b1d = nc.dram_tensor("b1d", [8, 2048], F32, kind="ExternalInput").ap()
    W2d = nc.dram_tensor("W2d", [8 * 128, 16 * 1024], F32, kind="ExternalInput").ap()
    b2d = nc.dram_tensor("b2d", [8, C], F32, kind="ExternalInput").ap()
    c_idn = nc.dram_tensor("c_idn", [128, 128], F32, kind="ExternalInput").ap()
    c_ones = nc.dram_tensor("c_ones", [1, 512], F32, kind="ExternalInput").ap()
    sc_xT = nc.dram_tensor("sc_xT", [C, NTOK], BF16, kind="Internal").ap()
    o_g4 = nc.dram_tensor("o_g4", [NTOK, C], F32, kind="ExternalOutput").ap()

    b = Builder(nc)
    A = nc.alloc_sbuf_tensor
    idn = A("idn", [128, 128], BF16)
    ones1 = A("ones1", [1, 512], BF16)
    sm8 = A("sm8", [8, 4], F32)
    W1S_ = [A(f"W1S{i}", [128, 8 * 2048], BF16) for i in range(2)]
    W2S = A("W2S", [128, 16 * 1024], BF16)
    b1row = A("b1row", [1, 2048], BF16)
    b2S = A("b2S", [8, 1024], BF16)
    gt = A("gt", [128, 1024], F32)
    xTg = [A(f"xTg{i}", [128, 8 * 512], BF16) for i in range(2)]
    msc_all = A("msc_all", [128, NT * 8], BF16)
    mscR = A("mscR", [1, 512], BF16)
    msc8g = A("msc8g", [8, 512], BF16)
    mgb = [A(f"mgb{i}", [128, 512], BF16) for i in range(2)]
    _grp = [nc.sbuf_tensor("idnf", [128, 128], F32),
            nc.sbuf_tensor("wredS", [128, 8 * 16], F32),
            nc.sbuf_tensor("wgs", [8, 16], F32),
            nc.sbuf_tensor("sq8", [8, 16], F32),
            nc.sbuf_tensor("wgnT", [16, 8], F32),
            nc.sbuf_tensor("lgf", [8, 512], F32),
            nc.sbuf_tensor("ltok", [128, 8], F32),
            nc.sbuf_tensor("gts", [128, 8], F32),
            nc.sbuf_tensor("eq", [128, 8], F32),
            nc.sbuf_tensor("rem", [128, 2], F32),
            nc.sbuf_tensor("xredf", [16, 512], F32),
            nc.sbuf_tensor("xTgF", [128, 8 * 512], F32)]
    idnf, wredS, wgs, sq8, wgnT, lgf, ltok, gts, eq, rem, xredf, xTgF = [
        g.__enter__() for g in _grp]
    for g in reversed(_grp):
        g.__exit__(None, None, None)
    hsc = A("hsc", [128, 16 * 512], BF16)
    out_all = A("out_all", [128, NT * 1024], BF16)
    mn = A("mn", [128, 4], F32)

    P = nc.alloc_psum_tensor
    psW = [P(f"psW{i}", [128, 512], F32) for i in range(2)]
    psO = P("psO", [128, 1024], F32)
    psT = P("psT", [128, 512], BF16)
    psTf = P("psTf", [128, 512], F32)
    psM = P("psM", [128, 512], F32)

    b.d("sync", idn[:], c_idn[:], w=["idn"])
    b.d("sync", idnf[:], c_idn[:], w=["idnf"])
    b.d("sync", ones1[:], c_ones[:], w=["ones1"])
    b.d("sync", wredS[:], wred[:], w=["wredS"])
    b.d("sync", wgs[:], wgd[:], w=["wgs"])
    b.d("gpsimd", b2S[:], b2d[:], w=["b2S"])
    b.c("scalar", lambda e: e.activation(out=sq8[:], in_=wgs[:], func=AF.Square,
                                         accum_out=sm8[:, 1:2]),
        r=["wgs"], w=["sm8", "sq8"])
    b.c("scalar", lambda e: e.activation(out=sm8[:, 1:2], in_=sm8[:, 1:2], func=AF.Sqrt),
        r=["sm8"], w=["sm8b"])
    b.c("vector", lambda e: e.tensor_scalar(sm8[:, 1:2], sm8[:, 1:2], 1e-4, None, ALU.max),
        r=["sm8b"], w=["sm8b"])
    b.c("vector", lambda e: e.reciprocal(sm8[:, 1:2], sm8[:, 1:2]), r=["sm8b"], w=["sm8b"])
    b.c("vector", lambda e: e.tensor_scalar(wgs[:], wgs[:], sm8[:, 1:2], None, ALU.mult),
        r=["wgs", "sm8b"], w=["wgs"])
    b.c("tensor", lambda e: e.transpose(psTf[0:16, 0:8], wgs[:], idnf[0:8, 0:8]),
        r=["wgs", "idnf", "psTfr"], w=["psTf"])
    b.c("vector", lambda e: e.tensor_copy(wgnT[:], psTf[0:16, 0:8]),
        r=["psTf"], w=["wgnT", "psTfr"])

    for g in range(NG):
        for i in range(4):
            ti = g * 4 + i
            b.d("sync", gt[:], g3d[ts(ti), :], r=["gtR"], w=["gt"])
            for g4 in range(2):
                for jj in range(4):
                    c1 = g4 * 4 + jj
                    b.c("tensor", lambda e, c1=c1, jj=jj: e.matmul(
                        psTf[:, jj * 128:(jj + 1) * 128], gt[:, c1 * 128:(c1 + 1) * 128],
                        idnf[:], start=(jj == 0), stop=(jj == 3), is_transpose=True,
                        skip_group_check=True),
                        r=["gt", "idnf"] + (["psTfr"] if jj == 0 else []), w=["psTf"])
                for jj in range(4):
                    c1 = g4 * 4 + jj
                    b.c("vector", lambda e, c1=c1, i=i, jj=jj: e.tensor_copy(
                        xTgF[:, c1 * 512 + i * 128: c1 * 512 + (i + 1) * 128],
                        psTf[:, jj * 128:(jj + 1) * 128]),
                        r=["psTf"], w=["xTgF"] + (["psTfr"] if jj == 3 else [])
                        + (["gtR"] if g4 == 1 and jj == 3 else []))
        for c1 in range(8):
            b.c("vector", lambda e, c1=c1: e.tensor_copy(
                xTg[0][:, c1 * 512:(c1 + 1) * 512], xTgF[:, c1 * 512:(c1 + 1) * 512]),
                r=["xTgF"], w=["xTg0s"])
            b.d("sync", sc_xT[ts(c1), g * 512:(g + 1) * 512],
                xTg[0][:, c1 * 512:(c1 + 1) * 512], r=["xTg0s"], w=["scxT"])
        for c1 in range(8):
            b.c("tensor", lambda e, c1=c1: e.matmul(
                psW[0][0:16, :], wredS[:, c1 * 16:(c1 + 1) * 16],
                xTgF[:, c1 * 512:(c1 + 1) * 512],
                start=(c1 == 0), stop=(c1 == 7)),
                r=["wredS", "xTgF"] + (["psW0r"] if c1 == 0 else []), w=["psW0"])
        b.c("vector", lambda e: e.tensor_copy(xredf[:], psW[0][0:16, :]),
            r=["psW0"], w=["xredf", "psW0r"])
        b.c("tensor", lambda e: e.matmul(psW[1][0:8, :], wgnT[:], xredf[:],
                                         start=True, stop=True),
            r=["wgnT", "xredf", "psW1r"], w=["psW1"])
        b.c("vector", lambda e: e.tensor_copy(lgf[:], psW[1][0:8, :]),
            r=["psW1"], w=["lgf", "psW1r"])
        for i in range(4):
            b.c("tensor", lambda e, i=i: e.transpose(
                psTf[:, 0:8], lgf[:, i * 128:(i + 1) * 128], idnf[0:8, 0:8]),
                r=["lgf", "idnf", "psTfr"], w=["psTf"])
            b.c("vector", lambda e: e.tensor_copy(ltok[:], psTf[:, 0:8]),
                r=["psTf"], w=["ltok", "psTfr"])
            b.c("vector", lambda e: e.tensor_reduce(mn[:, 0:1], ltok[:], AX.X, ALU.max,
                                                    negate=True),
                r=["ltok"], w=["mnmax"])
            b.c("scalar", lambda e: e.activation(out=gts[:], in_=ltok[:], func=AF.Exp,
                                                 bias=mn[:, 0:1], scale=1.0),
                r=["ltok", "mnmax"], w=["gts"])
            b.c("vector", lambda e: e.tensor_reduce(mn[:, 1:2], gts[:], AX.X, ALU.add),
                r=["gts"], w=["mnsum"])
            b.c("vector", lambda e: e.reciprocal(mn[:, 1:2], mn[:, 1:2]),
                r=["mnsum"], w=["mnsum"])
            b.c("vector", lambda e: e.tensor_scalar(gts[:], gts[:], mn[:, 1:2], None, ALU.mult),
                r=["gts", "mnsum"], w=["gts"])
            b.c("vector", lambda e: e.tensor_reduce(mn[:, 2:3], gts[:], AX.X, ALU.max),
                r=["gts"], w=["mngm"])
            b.c("vector", lambda e: e.tensor_scalar(eq[:], gts[:], mn[:, 2:3], None, ALU.is_ge),
                r=["gts", "mngm"], w=["eq"])
            b.c("vector", lambda e: e.tensor_scalar(rem[:, 0:1], mn[:, 2:3], 0.0, None,
                                                    ALU.mult),
                r=["mngm"], w=["rem"])
            b.c("vector", lambda e: e.tensor_scalar(rem[:, 0:1], rem[:, 0:1], 1.0, None,
                                                    ALU.add),
                r=["rem"], w=["rem"])
            ti = g * 4 + i
            for ee in range(8):
                b.c("vector", lambda e, ee=ee: e.tensor_tensor(
                    eq[:, ee:ee + 1], eq[:, ee:ee + 1], rem[:, 0:1], ALU.mult),
                    r=["eq", "rem"], w=["eq"])
                if ee < 7:
                    b.c("vector", lambda e, ee=ee: e.tensor_tensor(
                        rem[:, 0:1], rem[:, 0:1], eq[:, ee:ee + 1], ALU.subtract),
                        r=["eq", "rem"], w=["rem"])
            b.c("vector", lambda e, ti=ti: e.tensor_scalar(
                msc_all[:, ti * 8:(ti + 1) * 8], eq[:], mn[:, 2:3], None, ALU.mult),
                r=["eq", "mngm"], w=["msc_all"])

    for e_ in range(8):
        W1S = W1S_[e_ % 2]
        w1n = f"W1S{e_ % 2}"
        b.d("gpsimd", W1S[:], W1d[e_ * 128:(e_ + 1) * 128, :],
            r=[w1n + "r"], w=[w1n])
        b.d("gpsimd", W2S[:], W2d[e_ * 128:(e_ + 1) * 128, :],
            r=["W2Sr"], w=["W2S"])
        b.d("gpsimd", b1row[:], b1d[e_:e_ + 1, :], r=["b1rowR"], w=["b1row"])
        for g in range(NG):
            gsl = g % 2
            b.d("sync", xTg[gsl][:].rearrange("p (c t) -> p c t", c=8),
                sc_xT[:, g * 512:(g + 1) * 512].rearrange("(c p) t -> p c t", p=128),
                r=["scxT", f"xTg{gsl}R"], w=[f"xTg{gsl}"])
            for i in range(4):
                ti = g * 4 + i
                b.c("tensor", lambda e, ti=ti, e_=e_, i=i: e.matmul(
                    psT[0:1, i * 128:(i + 1) * 128],
                    msc_all[:, ti * 8 + e_: ti * 8 + e_ + 1], idn[:],
                    start=(i == 0), stop=(i == 3), is_transpose=True,
                    skip_group_check=True),
                    r=["msc_all", "idn"] + (["psTr"] if i == 0 else []), w=["psT"])
            b.c("vector", lambda e: e.tensor_copy(mscR[:], psT[0:1, 0:512]),
                r=["psT"], w=["mscR", "psTr"])
            b.c("tensor", lambda e: e.matmul(psM[:], ones1[:, 0:128], mscR[:],
                                             start=True, stop=True),
                r=["ones1", "mscR", "psMr"], w=["psM"])
            if e_ == 0:
                for i in range(4):
                    ti = g * 4 + i
                    b.c("tensor", lambda e, ti=ti, i=i: e.matmul(
                        psT[0:8, i * 128:(i + 1) * 128],
                        msc_all[:, ti * 8:(ti + 1) * 8], idn[:],
                        start=(i == 0), stop=(i == 3), is_transpose=True,
                        skip_group_check=True),
                        r=["msc_all", "idn"] + (["psTr"] if i == 0 else []), w=["psT"])
                b.c("vector", lambda e: e.tensor_copy(msc8g[:], psT[0:8, 0:512]),
                    r=["psT"], w=["msc8g", "psTr"])
            for hc in range(16):
                pw = psW[hc % 2]
                pwn = f"psW{hc % 2}"
                for f in range(8):
                    b.c("tensor", lambda e, hc=hc, f=f, pw=pw, gsl=gsl, W1S=W1S: e.matmul(
                        pw[:], W1S[:, f * 2048 + hc * 128: f * 2048 + (hc + 1) * 128],
                        xTg[gsl][:, f * 512:(f + 1) * 512],
                        start=(f == 0), stop=False, skip_group_check=True),
                        r=[w1n, f"xTg{gsl}"] + ([pwn + "r"] if f == 0 else []), w=[pwn])
                b.c("tensor", lambda e, hc=hc, pw=pw: e.matmul(
                    pw[:], b1row[:, hc * 128:(hc + 1) * 128], ones1[:],
                    start=False, stop=True, skip_group_check=True),
                    r=["b1row", "ones1"], w=[pwn])
                mg_ = mgb[hc % 2]
                mgn = f"mgb{hc % 2}"
                b.c("scalar", lambda e, pw=pw, mg_=mg_: e.activation(out=mg_[:], in_=pw[:],
                                                                     func=AF.Gelu),
                    r=[pwn, mgn + "R"], w=[mgn, pwn + "r"])
                b.c("vector", lambda e, hc=hc, mg_=mg_: e.tensor_tensor(
                    hsc[:, hc * 512:(hc + 1) * 512], mg_[:], psM[:], ALU.mult),
                    r=[mgn, "psM"],
                    w=[f"hsc_{hc}", mgn + "R"] + (["xTgF", "idnf", "wredS", "wgs", "sq8", "wgnT", "lgf", "ltok", "gts", "eq", "rem"] if e_ == 0 and g == 0 and hc == 0 else []))
            for i in range(4):
                ti = g * 4 + i
                if e_ == 0:
                    for o2 in range(2):
                        b.c("tensor", lambda e, i=i, o2=o2: e.matmul(
                            psO[:, o2 * 512:(o2 + 1) * 512], msc8g[:, i * 128:(i + 1) * 128],
                            b2S[:, o2 * 512:(o2 + 1) * 512],
                            start=True, stop=False, skip_group_check=True),
                            r=["msc8g", "b2S"] + (["psOr"] if o2 == 0 else []), w=["psO"])
                for hc in range(16):
                    for o2 in range(2):
                        b.c("tensor", lambda e, i=i, hc=hc, o2=o2: e.matmul(
                            psO[:, o2 * 512:(o2 + 1) * 512],
                            hsc[:, hc * 512 + i * 128: hc * 512 + (i + 1) * 128],
                            W2S[:, hc * 1024 + o2 * 512: hc * 1024 + (o2 + 1) * 512],
                            start=(hc == 0 and e_ != 0), stop=(hc == 15),
                            skip_group_check=True),
                            r=[f"hsc_{hc}", "W2S"] + (["psOr"] if hc == 0 and o2 == 0 and e_ != 0 else []),
                            w=["psO"])
                if e_ == 0:
                    b.c("vector", lambda e, ti=ti: e.tensor_copy(
                        out_all[:, ti * 1024:(ti + 1) * 1024], psO[:]),
                        r=["psO"],
                        w=[f"oa_{ti}", "psOr"] + (["xTgF", "idnf", "wredS", "wgs", "sq8", "wgnT", "lgf", "ltok", "gts", "eq", "rem"] if g == 0 and i == 0 else []))
                else:
                    b.c("vector", lambda e, ti=ti: e.tensor_tensor(
                        out_all[:, ti * 1024:(ti + 1) * 1024],
                        out_all[:, ti * 1024:(ti + 1) * 1024], psO[:], ALU.add),
                        r=[f"oa_{ti}", "psO"], w=[f"oa_{ti}", "psOr"])
        b.c("sync", lambda e: e.nop(), r=[w1n, "W2S", "b1row"],
            w=[w1n + "r", "W2Sr", "b1rowR"])

    for ti in range(NT):
        b.d("sync", gt[:], g3d[ts(ti), :], r=["gtR"], w=["gt"])
        b.c("vector", lambda e, ti=ti: e.tensor_tensor(
            gt[:], gt[:], out_all[:, ti * 1024:(ti + 1) * 1024], ALU.add),
            r=["gt", f"oa_{ti}"], w=["gt"])
        ln_token_tile(b, gt, gt, mn, xTg[0], "p11ln", in_res=["gt"], out_res=["gt", "gtR"])
        b.d("sync", o_g4[ts(ti), :], gt[:], r=["gt"], w=["g4D"])
    b.replay()
    return nc


def run_p11(I, g3full, _cache={}):
    if "p11" not in _cache:
        _cache["p11"] = build_p11()
    nc = _cache["p11"]
    cs = host_consts()
    maps = []
    for c in range(8):
        bb, h = c // 2, c % 2
        maps.append({
            "g3d": np.ascontiguousarray(g3full[bb, h * NTOK:(h + 1) * NTOK]),
            "wred": np.ascontiguousarray(
                I["L1_ffn_w_red"].reshape(8, 128, 16).transpose(1, 0, 2).reshape(128, 128)),
            "wgd": I["L1_ffn_wg"],
            "W1d": np.ascontiguousarray(
                I["L1_ffn_W1"].reshape(8, 8, 128, 2048).transpose(0, 2, 1, 3).reshape(8 * 128, 8 * 2048)),
            "b1d": I["L1_ffn_b1"],
            "W2d": np.ascontiguousarray(
                I["L1_ffn_W2"].reshape(8, 16, 128, 1024).transpose(0, 2, 1, 3).reshape(8 * 128, 16 * 1024)),
            "b2d": I["L1_ffn_b2"],
            "c_idn": cs["c_idn"], "c_ones": np.ones((1, 512), np.float32),
        })
    res = run_bass_kernel_spmd(nc, maps, core_ids=list(range(8)))
    return res.results



def _full(res, key):
    return np.stack([np.asarray(res[c][key]) for c in range(8)]).reshape(B, T, -1)


def kernel(**I):
    p1 = run_p1(I)
    p2 = run_p2(I, p1)
    g1 = _full(p2, "o_g1")
    x = g1
    for l in range(2):
        pa = run_p3a(I, x, l)
        pb = run_p3b(I, pa, l)
        x = _full(pb, "o_f")
    f2 = x
    p6 = run_p6(I, g1, f2)
    p7 = run_p7(I, p6)
    p8 = run_p8(I, p6, p7)
    p9 = run_p9(I, p6, p8)
    x3 = _full(p6, "o_x3")
    p10 = run_p10(I, x3, p9)
    g3 = _full(p10, "o_g3")
    p11 = run_p11(I, g3)
    g4 = _full(p11, "o_g4")
    return g4.astype(np.float32)
